# version 03
# baseline (speedup 1.0000x reference)
import math
from contextlib import ExitStack

import numpy as np
import ml_dtypes
import concourse.bass as bass
import concourse.mybir as mybir
from concourse.bass_utils import run_bass_kernel_spmd

F32 = mybir.dt.float32
BF16 = mybir.dt.bfloat16
AF = mybir.ActivationFunctionType
ALU = mybir.AluOpType
AX = mybir.AxisListType

P = 128
NCORES = 8
B, S, D = 4, 4096, 2048
NTOK = B * S
TLOC = NTOK // NCORES
TCH = 512
EPS = 1e-5
NH = 8
NSUP = 8
SCL = 1.0 / math.sqrt(128.0)
DBG = False

_CACHE = {}
_UIDC = [0]


def _uid():
    _UIDC[0] += 1
    return _UIDC[0]


class SemT:
    def __init__(self, h):
        self.h = h
        self.n = 0


def barrier(block, SS):
    bar = SS["bar"]
    base = bar.n
    for name in ("sync", "gpsimd", "tensor", "scalar", "vector"):
        def mk(eng):
            eng.nop().then_inc(bar.h, 1)
            eng.wait_ge(bar.h, base + 5)
        getattr(block, name)(mk)
    bar.n += 5



def emit_ln(nc, block, SS, G, src_ap, dst_ap):
    nk = D // P
    ntc = TLOC // TCH
    VPT = 10 + 2 * nk
    SPT = nk + 1
    PPT = nk + 3
    b_ldx, b_pe, b_ev, b_ev2, b_st = SS["ldx"].n, SS["pe"].n, SS["ev"].n, SS["ev2"].n, SS["st0"].n
    with ExitStack() as cx:
        uid = _uid()
        ec = cx.enter_context
        xb = ec(nc.sbuf_tensor(f"ln_xb_{uid}", [P, nk, TCH], BF16))
        sqb = ec(nc.sbuf_tensor(f"ln_sqb_{uid}", [P, TCH], BF16))
        xnb = ec(nc.sbuf_tensor(f"ln_xnb_{uid}", [P, nk, TCH], BF16))
        mrow = ec(nc.sbuf_tensor(f"ln_m_{uid}", [1, TCH], F32))
        erow = ec(nc.sbuf_tensor(f"ln_e_{uid}", [1, TCH], F32))
        msq = ec(nc.sbuf_tensor(f"ln_msq_{uid}", [1, TCH], F32))
        var = ec(nc.sbuf_tensor(f"ln_var_{uid}", [1, TCH], F32))
        srt = ec(nc.sbuf_tensor(f"ln_srt_{uid}", [1, TCH], F32))
        rr = ec(nc.sbuf_tensor(f"ln_rr_{uid}", [1, TCH], F32))
        nmr = ec(nc.sbuf_tensor(f"ln_nmr_{uid}", [1, TCH], F32))
        a_bf = ec(nc.sbuf_tensor(f"ln_abf_{uid}", [1, TCH], BF16))
        bb_bf = ec(nc.sbuf_tensor(f"ln_bbf_{uid}", [1, TCH], BF16))
        a_sb = ec(nc.sbuf_tensor(f"ln_asb_{uid}", [P, TCH], BF16))
        b_sb = ec(nc.sbuf_tensor(f"ln_bsb_{uid}", [P, TCH], BF16))
        ps_m = ec(nc.psum_tensor(f"ln_psm_{uid}", [1, TCH], F32))
        ps_s = ec(nc.psum_tensor(f"ln_pss_{uid}", [1, TCH], F32))
        ps_a = ec(nc.psum_tensor(f"ln_psa_{uid}", [P, TCH], F32))
        ps_b = ec(nc.psum_tensor(f"ln_psb_{uid}", [P, TCH], F32))

        def gp(g):
            for t in range(ntc):
                if t >= 1:
                    g.wait_ge(SS["ev2"].h, b_ev2 + VPT * t)
                g.dma_start(out=xb[:], in_=src_ap[:, t * TCH:(t + 1) * TCH]
                            .rearrange("(k p) n -> p k n", p=P)).then_inc(SS["ldx"].h, 16)
        block.gpsimd(gp)

        def tns(tn):
            tn.wait_ge(SS["gldc"].h, G["ldv_const"])
            for t in range(ntc):
                pb = b_pe + PPT * t
                vbp = b_ev2 + VPT * (t - 1)
                tn.wait_ge(SS["ldx"].h, b_ldx + 16 * (t + 1))
                if t >= 1:
                    tn.wait_ge(SS["ev2"].h, vbp + 1)
                ins = None
                for k in range(nk):
                    ins = tn.matmul(ps_m[:], G["ones_bf"][:], xb[:, k, :],
                                    start=(k == 0), stop=(k == nk - 1))
                ins.then_inc(SS["pe"].h, 1)
                for k in range(nk):
                    tn.wait_ge(SS["ev"].h, b_ev + SPT * t + k + 1)
                    if t >= 1 and k == 0:
                        tn.wait_ge(SS["ev2"].h, vbp + 2)
                    tn.matmul(ps_s[:], G["ones_bf"][:], sqb[:],
                              start=(k == 0), stop=(k == nk - 1)).then_inc(SS["pe"].h, 1)
                vb = b_ev2 + VPT * t
                tn.wait_ge(SS["ev2"].h, vb + 7)
                if t >= 1:
                    tn.wait_ge(SS["ev2"].h, vbp + 9)
                tn.matmul(ps_a[:], G["onesb_bf"][:], a_bf[:], start=True, stop=True).then_inc(SS["pe"].h, 1)
                tn.wait_ge(SS["ev2"].h, vb + 8)
                if t >= 1:
                    tn.wait_ge(SS["ev2"].h, vbp + 10)
                tn.matmul(ps_b[:], G["onesb_bf"][:], bb_bf[:], start=True, stop=True).then_inc(SS["pe"].h, 1)
        block.tensor(tns)

        def sc(s_):
            for t in range(ntc):
                s_.wait_ge(SS["ldx"].h, b_ldx + 16 * (t + 1))
                for k in range(nk):
                    if k >= 1:
                        s_.wait_ge(SS["pe"].h, b_pe + PPT * t + 1 + k)
                    elif t >= 1:
                        s_.wait_ge(SS["pe"].h, b_pe + PPT * t)
                    s_.activation(sqb[:], xb[:, k, :], AF.Square).then_inc(SS["ev"].h, 1)
                s_.wait_ge(SS["ev2"].h, b_ev2 + VPT * t + 4)
                s_.activation(srt[:], var[:], AF.Sqrt, bias=G["epst"][0:1, :]).then_inc(SS["ev"].h, 1)
        block.scalar(sc)

        def vc(v):
            for t in range(ntc):
                vb = b_ev2 + VPT * t
                v.wait_ge(SS["pe"].h, b_pe + PPT * t + 1)
                v.tensor_scalar(mrow[:], ps_m[:], 1.0 / D, None, ALU.mult).then_inc(SS["ev2"].h, 1)
                v.wait_ge(SS["pe"].h, b_pe + PPT * t + 1 + nk)
                v.tensor_scalar(erow[:], ps_s[:], 1.0 / D, None, ALU.mult).then_inc(SS["ev2"].h, 1)
                v.wait_ge(SS["ev2"].h, vb + 2)
                v.tensor_mul(msq[:], mrow[:], mrow[:]).then_inc(SS["ev2"].h, 1)
                v.wait_ge(SS["ev2"].h, vb + 3)
                v.tensor_sub(var[:], erow[:], msq[:]).then_inc(SS["ev2"].h, 1)
                v.wait_ge(SS["ev"].h, b_ev + SPT * t + nk + 1)
                v.reciprocal(rr[:], srt[:]).then_inc(SS["ev2"].h, 1)
                v.wait_ge(SS["ev2"].h, vb + 5)
                v.tensor_mul(nmr[:], mrow[:], rr[:]).then_inc(SS["ev2"].h, 1)
                v.tensor_copy(a_bf[:], rr[:]).then_inc(SS["ev2"].h, 1)
                v.wait_ge(SS["ev2"].h, vb + 6)
                v.tensor_scalar(bb_bf[:], nmr[:], -1.0, None, ALU.mult).then_inc(SS["ev2"].h, 1)
                v.wait_ge(SS["pe"].h, b_pe + PPT * t + nk + 2)
                if t >= 1:
                    v.wait_ge(SS["ev2"].h, b_ev2 + VPT * t)
                v.tensor_copy(a_sb[:], ps_a[:]).then_inc(SS["ev2"].h, 1)
                v.wait_ge(SS["pe"].h, b_pe + PPT * t + nk + 3)
                v.tensor_copy(b_sb[:], ps_b[:]).then_inc(SS["ev2"].h, 1)
                v.wait_ge(SS["ev2"].h, vb + 10)
                if t >= 2:
                    v.wait_ge(SS["st0"].h, b_st + 16 * (t - 1))
                if t >= 1:
                    v.wait_ge(SS["st0"].h, b_st + 16 * t)
                for k in range(nk):
                    v.tensor_mul(xnb[:, k, :], xb[:, k, :], a_sb[:]).then_inc(SS["ev2"].h, 1)
                    v.wait_ge(SS["ev2"].h, vb + 11 + 2 * k)
                    v.tensor_add(xnb[:, k, :], xnb[:, k, :], b_sb[:]).then_inc(SS["ev2"].h, 1)
        block.vector(vc)

        def sy(sy_):
            for t in range(ntc):
                sy_.wait_ge(SS["ev2"].h, b_ev2 + VPT * (t + 1))
                sy_.dma_start(out=dst_ap[:, t * TCH:(t + 1) * TCH].rearrange("(k p) n -> p k n", p=P),
                              in_=xnb[:]).then_inc(SS["st0"].h, 16)
            sy_.wait_ge(SS["st0"].h, b_st + 16 * ntc)
        block.sync(sy)

        SS["ldx"].n = b_ldx + 16 * ntc
        SS["pe"].n = b_pe + PPT * ntc
        SS["ev"].n = b_ev + SPT * ntc
        SS["ev2"].n = b_ev2 + VPT * ntc
        SS["st0"].n = b_st + 16 * ntc
    barrier(block, SS)



def emit_matmul(nc, block, SS, G, K, M, ntok, xchunk, w_ap, w_mode, out_tile_ap,
                out_dtype="bf16", has_bias=False, bvec=None, act="copy",
                has_scale=False, svec_ap=None, cmul=1.0, sfold_ap=None,
                resid_tile_ap=None, sign_w=False, mult_tile_ap=None, psc_ap=None,
                xpm_ap=None, wslab_ap=None, slab_m=4, xnparts=1):
    nk, nm = K // P, M // P
    ntc = ntok // TCH
    if K >= 8192:
        nth = 4 if ntc >= 4 else ntc
    else:
        nth = 1
    ntcl = ntc // nth
    if K >= 8192:
        slab_m = 2
    nslab = nm // slab_m
    use_act = act != "copy"
    afunc = {"silu": AF.Silu, "relu": AF.Relu}.get(act)
    has_resid = resid_tile_ap is not None
    has_mult = mult_tile_ap is not None
    has_psc = psc_ap is not None
    has_xpm = xpm_ap is not None
    has_vec_op = (not use_act) or has_mult or has_resid or has_psc
    n_rm = int(has_resid) + int(has_mult)
    if has_psc or sfold_ap is not None:
        assert has_scale
    if use_act:
        assert not has_bias and not has_resid and not has_psc
    scale_pre = has_scale and (cmul != 1.0 or has_psc or sfold_ap is not None)
    n_vini = int(has_scale and cmul != 1.0) + int(has_psc) + int(sfold_ap is not None)
    nidx = ntc * nm

    b_ldx = SS["ldx"].n
    b_ldw = [SS["ldw0"].n, SS["ldw1"].n]
    b_ldv = SS["ldv"].n
    b_ldr = [SS["ldr0"].n, SS["ldr1"].n]
    b_pe = SS["pe"].n
    b_ev = SS["ev"].n
    b_ev2 = SS["ev2"].n
    b_sg = SS["sg"].n
    b_st = [SS["st0"].n, SS["st1"].n]
    b_xpms = SS["xpms"].n
    b_vini = SS["vini"].n
    odt = BF16 if out_dtype == "bf16" else F32

    NX = xnparts

    def order():
        gi = 0
        for th in range(nth):
            for m in range(nm):
                for tc in range(ntcl):
                    yield gi, th, m, tc, th * ntcl + tc
                    gi += 1

    with ExitStack() as cx:
        uid = _uid()
        ec = cx.enter_context
        xb = ec(nc.sbuf_tensor(f"mm_xb_{uid}", [P, nk, ntcl * TCH], BF16))
        wb = [ec(nc.sbuf_tensor(f"mm_wb{j}_{uid}", [P, nk, slab_m * P], BF16)) for j in range(2)]
        eb = [ec(nc.sbuf_tensor(f"mm_e{j}_{uid}", [P, TCH], odt)) for j in range(4)]
        rb = [ec(nc.sbuf_tensor(f"mm_r{j}_{uid}", [P, TCH], F32)) for j in range(2)]
        mb = [ec(nc.sbuf_tensor(f"mm_m{j}_{uid}", [P, TCH], BF16)) for j in range(2)]
        vecs = ec(nc.sbuf_tensor(f"mm_vecs_{uid}", [P, 3 * nm + nk + 2], F32))
        pb = [ec(nc.psum_tensor(f"mm_pb{j}_{uid}", [P, TCH], F32)) for j in range(4)]
        off_b, off_s, off_w, off_p, off_f = 0, nm, 2 * nm, 2 * nm + nk, 2 * nm + nk + 1
        nvec_dma = int(has_bias) + int(has_scale) + int(has_xpm) + int(has_psc) + int(sfold_ap is not None)

        def sy(sy_):
            if has_bias:
                sy_.dma_start(out=vecs[:, off_b:off_b + nm], in_=bvec).then_inc(SS["ldv"].h, 16)
            if has_scale:
                sy_.dma_start(out=vecs[:, off_s:off_s + nm], in_=svec_ap).then_inc(SS["ldv"].h, 16)
            if has_xpm:
                sy_.dma_start(out=vecs[:, off_w:off_w + nk], in_=xpm_ap).then_inc(SS["ldv"].h, 16)
            if has_psc:
                sy_.dma_start(out=vecs[:, off_p:off_p + 1], in_=psc_ap).then_inc(SS["ldv"].h, 16)
            if sfold_ap is not None:
                sy_.dma_start(out=vecs[:, off_f:off_f + nm], in_=sfold_ap).then_inc(SS["ldv"].h, 16)
            if n_rm:
                for gi, th, m, tc, t in order():
                    if gi >= 2:
                        sy_.wait_ge(SS[f"st{gi % 2}"].h, b_st[gi % 2] + 16 * ((gi - 2) // 2 + 1))
                    if has_resid:
                        sy_.dma_start(out=rb[gi % 2][:], in_=resid_tile_ap(t, m)).then_inc(SS[f"ldr{gi % 2}"].h, 16)
                    if has_mult:
                        sy_.dma_start(out=mb[gi % 2][:], in_=mult_tile_ap(t, m)).then_inc(SS[f"ldr{gi % 2}"].h, 16)
        block.sync(sy)

        def gp(g):
            for th in range(nth):
                if th >= 1:
                    g.wait_ge(SS["pe"].h, b_pe + th * ntcl * nm)
                parts = xchunk2(th)
                if not isinstance(parts, list):
                    parts = [(xb[:], parts)]
                for xo, xi in parts:
                    g.dma_start(out=xo, in_=xi).then_inc(SS["ldx"].h, 16)
                for si in range(nslab):
                    sidx = th * nslab + si
                    if sidx >= 2:
                        g.wait_ge(SS["pe"].h, b_pe + (sidx - 1) * slab_m * ntcl)
                    g.dma_start(out=wb[sidx % 2][:], in_=wslab_ap(si)).then_inc(SS[f"ldw{sidx % 2}"].h, 16)
        def xchunk2(th):
            try:
                return xchunk(th, xb)
            except TypeError:
                return xchunk(th)
        block.gpsimd(gp)

        def tns(tn):
            for gi, th, m, tc, t in order():
                sidx = th * nslab + m // slab_m
                if m == 0 and tc == 0:
                    if has_xpm:
                        tn.wait_ge(SS["xpms"].h, b_xpms + th + 1)
                    else:
                        tn.wait_ge(SS["ldx"].h, b_ldx + 16 * NX * (th + 1))
                if m % slab_m == 0 and tc == 0:
                    if sign_w:
                        tn.wait_ge(SS["sg"].h, b_sg + sidx + 1)
                    else:
                        tn.wait_ge(SS[f"ldw{sidx % 2}"].h, b_ldw[sidx % 2] + 16 * (sidx // 2 + 1))
                if gi >= 4:
                    if use_act:
                        tn.wait_ge(SS["ev"].h, b_ev + gi - 3)
                    else:
                        tn.wait_ge(SS["ev2"].h, b_ev2 + gi - 3)
                ins = None
                wloc = (m % slab_m) * P
                for k in range(nk):
                    ins = tn.matmul(pb[gi % 4][:], wb[sidx % 2][:, k, wloc:wloc + P],
                                    xb[:, k, tc * TCH:(tc + 1) * TCH],
                                    start=(k == 0), stop=(k == nk - 1))
                ins.then_inc(SS["pe"].h, 1)
        block.tensor(tns)

        def sc(s_):
            if use_act and has_scale:
                s_.wait_ge(SS["ldv"].h, b_ldv + 16 * nvec_dma)
                if scale_pre:
                    s_.wait_ge(SS["vini"].h, b_vini + n_vini)
            for gi, th, m, tc, t in order():
                sidx = th * nslab + m // slab_m
                if sign_w and m % slab_m == 0 and tc == 0:
                    s_.wait_ge(SS[f"ldw{sidx % 2}"].h, b_ldw[sidx % 2] + 16 * (sidx // 2 + 1))
                    s_.activation(wb[sidx % 2][:, :, :], wb[sidx % 2][:, :, :], AF.Sign).then_inc(SS["sg"].h, 1)
                if use_act:
                    s_.wait_ge(SS["pe"].h, b_pe + gi + 1)
                    if gi >= 4:
                        s_.wait_ge(SS[f"st{gi % 2}"].h, b_st[gi % 2] + 16 * ((gi - 4) // 2 + 1))
                    kw = {}
                    if has_scale:
                        kw["scale"] = vecs[:, off_s + m:off_s + m + 1]
                    elif cmul != 1.0:
                        kw["scale"] = float(cmul)
                    s_.activation(eb[gi % 4][:], pb[gi % 4][:], afunc, **kw).then_inc(SS["ev"].h, 1)
                if has_vec_op:
                    s_.wait_ge(SS["ev2"].h, b_ev2 + gi + 1)
                else:
                    s_.wait_ge(SS["ev"].h, b_ev + gi + 1)
                s_.dma_start(out=out_tile_ap(t, m), in_=eb[gi % 4][:]).then_inc(SS[f"st{gi % 2}"].h, 16)
            s_.wait_ge(SS["st0"].h, b_st[0] + 16 * ((nidx + 1) // 2))
            s_.wait_ge(SS["st1"].h, b_st[1] + 16 * (nidx // 2))
        block.scalar(sc)

        def vc(v):
            nvi = 0
            if scale_pre:
                v.wait_ge(SS["ldv"].h, b_ldv + 16 * nvec_dma)
                if cmul != 1.0:
                    v.tensor_scalar(vecs[:, off_s:off_s + nm], vecs[:, off_s:off_s + nm],
                                    float(cmul), None, ALU.mult).then_inc(SS["vini"].h, 1)
                    nvi += 1
                    v.wait_ge(SS["vini"].h, b_vini + nvi)
                if has_psc:
                    v.tensor_scalar(vecs[:, off_s:off_s + nm], vecs[:, off_s:off_s + nm],
                                    vecs[:, off_p:off_p + 1], None, ALU.mult).then_inc(SS["vini"].h, 1)
                    nvi += 1
                    v.wait_ge(SS["vini"].h, b_vini + nvi)
                if sfold_ap is not None:
                    v.tensor_mul(vecs[:, off_s:off_s + nm], vecs[:, off_s:off_s + nm],
                                 vecs[:, off_f:off_f + nm]).then_inc(SS["vini"].h, 1)
                    nvi += 1
                    v.wait_ge(SS["vini"].h, b_vini + nvi)
            for gi, th, m, tc, t in order():
                if has_xpm and m == 0 and tc == 0:
                    v.wait_ge(SS["ldv"].h, b_ldv + 16 * nvec_dma)
                    v.wait_ge(SS["ldx"].h, b_ldx + 16 * NX * (th + 1))
                    ins = None
                    for k in range(nk):
                        ins = v.tensor_scalar(xb[:, k, :], xb[:, k, :],
                                              vecs[:, off_w + k:off_w + k + 1], None, ALU.mult)
                    ins.then_inc(SS["xpms"].h, 1)
                if not has_vec_op:
                    continue
                if use_act:
                    v.wait_ge(SS["ev"].h, b_ev + gi + 1)
                    v.wait_ge(SS[f"ldr{gi % 2}"].h, b_ldr[gi % 2] + 16 * n_rm * (gi // 2 + 1))
                    v.tensor_mul(eb[gi % 4][:], eb[gi % 4][:], mb[gi % 2][:]).then_inc(SS["ev2"].h, 1)
                    continue
                v.wait_ge(SS["pe"].h, b_pe + gi + 1)
                if has_bias or has_scale:
                    v.wait_ge(SS["ldv"].h, b_ldv + 16 * nvec_dma)
                if scale_pre:
                    v.wait_ge(SS["vini"].h, b_vini + n_vini)
                if n_rm:
                    v.wait_ge(SS[f"ldr{gi % 2}"].h, b_ldr[gi % 2] + 16 * n_rm * (gi // 2 + 1))
                if gi >= 4:
                    v.wait_ge(SS[f"st{gi % 2}"].h, b_st[gi % 2] + 16 * ((gi - 4) // 2 + 1))
                pbank = pb[gi % 4]
                cur = eb[gi % 4]
                if has_resid:
                    if has_scale:
                        last = v.scalar_tensor_tensor(cur[:], pbank[:], vecs[:, off_s + m:off_s + m + 1],
                                                      rb[gi % 2][:], ALU.mult, ALU.add)
                    elif has_bias:
                        last = v.scalar_tensor_tensor(cur[:], pbank[:], vecs[:, off_b + m:off_b + m + 1],
                                                      rb[gi % 2][:], ALU.add, ALU.add)
                    else:
                        last = v.scalar_tensor_tensor(cur[:], pbank[:], float(cmul),
                                                      rb[gi % 2][:], ALU.mult, ALU.add)
                else:
                    if has_scale and has_bias:
                        last = v.tensor_scalar(cur[:], pbank[:], vecs[:, off_s + m:off_s + m + 1],
                                               vecs[:, off_b + m:off_b + m + 1], ALU.mult, ALU.add)
                    elif has_scale:
                        last = v.tensor_scalar(cur[:], pbank[:], vecs[:, off_s + m:off_s + m + 1],
                                               None, ALU.mult)
                    elif has_bias:
                        last = v.tensor_scalar(cur[:], pbank[:], vecs[:, off_b + m:off_b + m + 1],
                                               None, ALU.add)
                    elif cmul != 1.0:
                        last = v.tensor_scalar(cur[:], pbank[:], float(cmul), None, ALU.mult)
                    else:
                        last = v.tensor_copy(cur[:], pbank[:])
                last.then_inc(SS["ev2"].h, 1)
        block.vector(vc)

        SS["ldx"].n = b_ldx + 16 * NX * nth
        nslabs_tot = nth * nslab
        SS["ldw0"].n = b_ldw[0] + 16 * ((nslabs_tot + 1) // 2)
        SS["ldw1"].n = b_ldw[1] + 16 * (nslabs_tot // 2)
        SS["ldv"].n = b_ldv + 16 * nvec_dma
        SS["ldr0"].n = b_ldr[0] + 16 * n_rm * ((nidx + 1) // 2)
        SS["ldr1"].n = b_ldr[1] + 16 * n_rm * (nidx // 2)
        SS["pe"].n = b_pe + nidx
        SS["ev"].n = b_ev + (nidx if use_act else 0)
        SS["ev2"].n = b_ev2 + (nidx if has_vec_op else 0)
        SS["sg"].n = b_sg + (nslabs_tot if sign_w else 0)
        SS["st0"].n = b_st[0] + 16 * ((nidx + 1) // 2)
        SS["st1"].n = b_st[1] + 16 * (nidx // 2)
        SS["xpms"].n = b_xpms + (nth if has_xpm else 0)
        SS["vini"].n = b_vini + n_vini
    barrier(block, SS)



def emit_matmul_tm(nc, block, SS, G, K, M, ntok, xtile, w_ap_full, out_tile_ap,
                   bias_row_ap=None):
    nk, nmc = K // P, M // TCH
    ntt = ntok // P
    has_bias = bias_row_ap is not None
    b_ldx, b_ldw, b_ldv = SS["ldx"].n, SS["ldw0"].n, SS["ldw1"].n
    b_pe, b_ev2 = SS["pe"].n, SS["ev2"].n
    b_st = [SS["st0"].n, SS["st1"].n]
    with ExitStack() as cx:
        uid = _uid()
        ec = cx.enter_context
        xt = [ec(nc.sbuf_tensor(f"vt_x{j}_{uid}", [P, nk, P], BF16)) for j in range(2)]
        wsb = ec(nc.sbuf_tensor(f"vt_w_{uid}", [P, nk, M], BF16))
        brow = ec(nc.sbuf_tensor(f"vt_br_{uid}", [1, M], F32))
        bsb = ec(nc.sbuf_tensor(f"vt_bs_{uid}", [P, M], F32))
        eb = [ec(nc.sbuf_tensor(f"vt_e{j}_{uid}", [P, TCH], BF16)) for j in range(2)]
        brbf = ec(nc.sbuf_tensor(f"vt_brbf_{uid}", [1, M], BF16))
        pb = [ec(nc.psum_tensor(f"vt_pb{j}_{uid}", [P, TCH], F32)) for j in range(2)]
        ps_bc = ec(nc.psum_tensor(f"vt_psbc_{uid}", [P, TCH], F32))

        def gp(g):
            g.dma_start(out=wsb[:], in_=w_ap_full.rearrange("(k p) m -> p k m", p=P)).then_inc(SS["ldw0"].h, 16)
            if has_bias:
                g.dma_start(out=brow[:], in_=bias_row_ap).then_inc(SS["ldw1"].h, 16)
            for tt in range(ntt):
                if tt >= 2:
                    g.wait_ge(SS["pe"].h, b_pe + nbc + (tt - 1) * nmc)
                g.dma_start(out=xt[tt % 2][:], in_=xtile(tt)).then_inc(SS[f"gq{tt % 2}"].h, 16)
        nbc = (M // TCH) if has_bias else 0

        b_ldr = [SS["gq0"].n, SS["gq1"].n]
        block.gpsimd(gp)

        def tns(tn):
            tn.wait_ge(SS["ldw0"].h, b_ldw + 16)
            if has_bias:
                tn.wait_ge(SS["ldw1"].h, b_ldv + 16)
                for mc in range(M // TCH):
                    tn.wait_ge(SS["ev2"].h, b_ev2 + 1)
                    if mc > 0:
                        tn.wait_ge(SS["ev2"].h, b_ev2 + 1 + mc)
                    tn.matmul(ps_bc[:], G["onesb_bf"][:], brbf[:, mc * TCH:(mc + 1) * TCH],
                              start=True, stop=True).then_inc(SS["pe"].h, 1)
            for tt in range(ntt):
                for mc in range(nmc):
                    idx = tt * nmc + mc
                    if mc == 0:
                        tn.wait_ge(SS[f"gq{tt % 2}"].h, b_ldr[tt % 2] + 16 * (tt // 2 + 1))
                    if idx >= 2:
                        tn.wait_ge(SS["ev2"].h, b_ev2 + 1 + nbc + idx - 1)
                    ins = None
                    for k in range(nk):
                        ins = tn.matmul(pb[idx % 2][:], xt[tt % 2][:, k, :],
                                        wsb[:, k, mc * TCH:(mc + 1) * TCH],
                                        start=(k == 0), stop=(k == nk - 1))
                    ins.then_inc(SS["pe"].h, 1)
        block.tensor(tns)

        def vc(v):
            nb = 0
            if has_bias:
                v.wait_ge(SS["ldw1"].h, b_ldv + 16)
                v.tensor_copy(brbf[:], brow[:]).then_inc(SS["ev2"].h, 1)
                nb = 1
                for mc in range(M // TCH):
                    v.wait_ge(SS["pe"].h, b_pe + mc + 1)
                    v.tensor_copy(bsb[:, mc * TCH:(mc + 1) * TCH], ps_bc[:]).then_inc(SS["ev2"].h, 1)
                nb += M // TCH
            for tt in range(ntt):
                for mc in range(nmc):
                    idx = tt * nmc + mc
                    v.wait_ge(SS["pe"].h, b_pe + nbc + idx + 1)
                    if idx >= 2:
                        v.wait_ge(SS[f"st{idx % 2}"].h, b_st[idx % 2] + 16 * ((idx - 2) // 2 + 1))
                    if has_bias:
                        v.wait_ge(SS["ev2"].h, b_ev2 + 1 + nbc)
                        v.tensor_add(eb[idx % 2][:], pb[idx % 2][:], bsb[:, mc * TCH:(mc + 1) * TCH]).then_inc(SS["ev2"].h, 1)
                    else:
                        v.tensor_copy(eb[idx % 2][:], pb[idx % 2][:]).then_inc(SS["ev2"].h, 1)
        block.vector(vc)

        def sc(s_):
            for tt in range(ntt):
                for mc in range(nmc):
                    idx = tt * nmc + mc
                    s_.wait_ge(SS["ev2"].h, b_ev2 + (1 + nbc if has_bias else 0) + idx + 1)
                    s_.dma_start(out=out_tile_ap(tt, mc), in_=eb[idx % 2][:]).then_inc(SS[f"st{idx % 2}"].h, 16)
            s_.wait_ge(SS["st0"].h, b_st[0] + 16 * ((ntt * nmc + 1) // 2))
            s_.wait_ge(SS["st1"].h, b_st[1] + 16 * ((ntt * nmc) // 2))
        block.scalar(sc)

        nidx = ntt * nmc
        SS["ldw0"].n = b_ldw + 16
        SS["ldw1"].n = b_ldv + (16 if has_bias else 0)
        SS["gq0"].n = b_ldr[0] + 16 * ((ntt + 1) // 2)
        SS["gq1"].n = b_ldr[1] + 16 * (ntt // 2)
        SS["pe"].n = b_pe + nbc + nidx
        SS["ev2"].n = b_ev2 + (1 + nbc if has_bias else 0) + nidx
        SS["st0"].n = b_st[0] + 16 * ((nidx + 1) // 2)
        SS["st1"].n = b_st[1] + 16 * (nidx // 2)
    barrier(block, SS)



def _nkb(s):
    return 4 * (s + 1)


KBS_HEAD = sum(_nkb(s) for s in range(NSUP))
A_PE_HEAD = 8 + sum(3 + 3 * _nkb(s) for s in range(NSUP))
A_SC_HEAD = 9 + sum(3 + _nkb(s) for s in range(NSUP))
A_VC_HEAD = 9 + 14 * NSUP


def _a_pe_base_s(s):
    return 8 + sum(3 + 3 * _nkb(x) for x in range(s))


def _a_sc_base_s(s):
    return 9 + sum(3 + _nkb(x) for x in range(s))


def _a_vc_base_s(s):
    return 9 + 14 * s


def _a_kbg(s, kb):
    return sum(_nkb(x) for x in range(s)) + kb


def _a_kbg_to_skb(kbg):
    s = 0
    while kbg >= _nkb(s):
        kbg -= _nkb(s)
        s += 1
    return s, kbg


def emit_attention(nc, block, SS, G, qk_t, vt_t, cc2_t):
    NKB = S // P
    b_pe, b_sc, b_vc = SS["pe"].n, SS["ev"].n, SS["ev2"].n
    b_kk, b_kv = SS["ldw0"].n, SS["ldw1"].n
    b_q = [SS["gq0"].n, SS["gq1"].n]
    b_st = [SS["st0"].n, SS["st1"].n]

    def pe_st(h, s, kb):
        return b_pe + A_PE_HEAD * h + _a_pe_base_s(s) + 2 + 3 * kb + 1

    def pe_l(h, s, kb):
        return b_pe + A_PE_HEAD * h + _a_pe_base_s(s) + 2 + 3 * kb + 2

    def pe_pv(h, s, kb):
        return b_pe + A_PE_HEAD * h + _a_pe_base_s(s) + 2 + 3 * kb + 3

    def sc_exp(h, s, kb):
        return b_sc + A_SC_HEAD * h + _a_sc_base_s(s) + 3 + kb + 1

    with ExitStack() as cx:
        uid = _uid()
        ec = cx.enter_context
        kbuf = ec(nc.sbuf_tensor(f"at_kbuf_{uid}", [P, S], BF16))
        vbuf = ec(nc.sbuf_tensor(f"at_vbuf_{uid}", [P, NKB, P], BF16))
        qbs = [ec(nc.sbuf_tensor(f"at_qb{j}_{uid}", [P, TCH], BF16)) for j in range(2)]
        ksq = ec(nc.sbuf_tensor(f"at_ksq_{uid}", [P, TCH], BF16))
        qsq = ec(nc.sbuf_tensor(f"at_qsq_{uid}", [P, TCH], BF16))
        ksn = ec(nc.sbuf_tensor(f"at_ksn_{uid}", [1, S], F32))
        qsn = ec(nc.sbuf_tensor(f"at_qsn_{uid}", [1, TCH], F32))
        km2 = ec(nc.sbuf_tensor(f"at_km2_{uid}", [1, 1], F32))
        qm2 = ec(nc.sbuf_tensor(f"at_qm2_{uid}", [1, 1], F32))
        km = ec(nc.sbuf_tensor(f"at_km_{uid}", [1, 1], F32))
        qm = ec(nc.sbuf_tensor(f"at_qm_{uid}", [1, 1], F32))
        cprod = ec(nc.sbuf_tensor(f"at_cprod_{uid}", [1, 1], F32))
        negc = ec(nc.sbuf_tensor(f"at_negc_{uid}", [1, 1], F32))
        negc_bf = ec(nc.sbuf_tensor(f"at_negcbf_{uid}", [1, 1], BF16))
        bias_sb = ec(nc.sbuf_tensor(f"at_bias_{uid}", [P, 1], F32))
        pts = [ec(nc.sbuf_tensor(f"at_pt{j}_{uid}", [P, TCH], BF16)) for j in range(4)]
        lsb = ec(nc.sbuf_tensor(f"at_lsb_{uid}", [1, TCH], F32))
        rlv = ec(nc.sbuf_tensor(f"at_rlv_{uid}", [1, TCH], F32))
        rl_bf = ec(nc.sbuf_tensor(f"at_rlbf_{uid}", [1, TCH], BF16))
        rb_s = ec(nc.sbuf_tensor(f"at_rbs_{uid}", [P, TCH], F32))
        cxs = [ec(nc.sbuf_tensor(f"at_cxs{j}_{uid}", [P, TCH], BF16)) for j in range(2)]
        pss = [ec(nc.psum_tensor(f"at_pss{j}_{uid}", [P, TCH], F32)) for j in range(2)]
        ps_o = ec(nc.psum_tensor(f"at_pso_{uid}", [P, TCH], F32))
        ps_l = ec(nc.psum_tensor(f"at_psl_{uid}", [1, TCH], F32))
        ps_sm = ec(nc.psum_tensor(f"at_pssm_{uid}", [1, TCH], F32))
        ps_b = ec(nc.psum_tensor(f"at_psb_{uid}", [P, 1], F32))
        ps_b2 = ec(nc.psum_tensor(f"at_psb2_{uid}", [P, TCH], F32))

        def gp(g):
            for h in range(NH):
                if h >= 1:
                    g.wait_ge(SS["pe"].h, b_pe + A_PE_HEAD * h)
                g.dma_start(out=kbuf[:], in_=qk_t.ap()[(NH + h) * P:(NH + h + 1) * P, :]).then_inc(SS["ldw0"].h, 16)
                g.dma_start(out=vbuf[:], in_=vt_t.ap()[:, h * P:(h + 1) * P]
                            .rearrange("(b p) d -> p b d", p=P)).then_inc(SS["ldw1"].h, 16)
                for s in range(NSUP):
                    gq = h * NSUP + s
                    if gq >= 2:
                        hp, sp = divmod(gq - 2, NSUP)
                        g.wait_ge(SS["pe"].h, pe_st(hp, sp, _nkb(sp) - 1))
                    g.dma_start(out=qbs[gq % 2][:],
                                in_=qk_t.ap()[h * P:(h + 1) * P, s * TCH:(s + 1) * TCH]).then_inc(SS[f"gq{gq % 2}"].h, 16)
        block.gpsimd(gp)

        def tns(tn):
            for h in range(NH):
                tn.wait_ge(SS["ldw0"].h, b_kk + 16 * (h + 1))
                tn.wait_ge(SS["ldw1"].h, b_kv + 16 * (h + 1))
                for c in range(8):
                    tn.wait_ge(SS["ev"].h, b_sc + A_SC_HEAD * h + c + 1)
                    if c > 0:
                        tn.wait_ge(SS["ev2"].h, b_vc + A_VC_HEAD * h + c)
                    elif h > 0:
                        tn.wait_ge(SS["ev2"].h, b_vc + A_VC_HEAD * h)
                    tn.matmul(ps_sm[:], G["ones_bf"][:], ksq[:], start=True, stop=True).then_inc(SS["pe"].h, 1)
                for s in range(NSUP):
                    gq = h * NSUP + s
                    tn.wait_ge(SS["ev"].h, b_sc + A_SC_HEAD * h + _a_sc_base_s(s) + 1)
                    if s == 0:
                        tn.wait_ge(SS["ev2"].h, b_vc + A_VC_HEAD * h + 8)
                    else:
                        tn.wait_ge(SS["ev2"].h, b_vc + A_VC_HEAD * h + _a_vc_base_s(s - 1) + 1)
                    tn.matmul(ps_sm[:], G["ones_bf"][:], qsq[:], start=True, stop=True).then_inc(SS["pe"].h, 1)
                    tn.wait_ge(SS["ev2"].h, b_vc + A_VC_HEAD * h + _a_vc_base_s(s) + 4)
                    tn.matmul(ps_b[:], G["onesb_bf"][:], negc_bf[:], start=True, stop=True).then_inc(SS["pe"].h, 1)
                    for kb in range(_nkb(s)):
                        kbg = _a_kbg(s, kb)
                        if kbg >= 2:
                            s2, k2 = _a_kbg_to_skb(kbg - 2)
                            tn.wait_ge(SS["ev"].h, sc_exp(h, s2, k2))
                        elif h > 0:
                            s2, k2 = _a_kbg_to_skb(KBS_HEAD - 2 + kbg)
                            tn.wait_ge(SS["ev"].h, sc_exp(h - 1, s2, k2))
                        if kb == 0:
                            tn.wait_ge(SS[f"gq{gq % 2}"].h, b_q[gq % 2] + 16 * (gq // 2 + 1))
                        tn.matmul(pss[kbg % 2][:], kbuf[:, kb * P:(kb + 1) * P],
                                  qbs[gq % 2][:], start=True, stop=True).then_inc(SS["pe"].h, 1)
                        tn.wait_ge(SS["ev"].h, sc_exp(h, s, kb))
                        if kb >= 4 * s:
                            tn.wait_ge(SS["ev2"].h, b_vc + A_VC_HEAD * h + _a_vc_base_s(s) + 5 + (kb - 4 * s) + 1)
                        if kb == 0:
                            if s > 0:
                                tn.wait_ge(SS["ev2"].h, b_vc + A_VC_HEAD * h + _a_vc_base_s(s - 1) + 14)
                            elif h > 0:
                                tn.wait_ge(SS["ev2"].h, b_vc + A_VC_HEAD * h)
                        tn.matmul(ps_l[:], G["ones_bf"][:], pts[kbg % 4][:],
                                  start=(kb == 0), stop=(kb == _nkb(s) - 1)).then_inc(SS["pe"].h, 1)
                        tn.matmul(ps_o[:], vbuf[:, kb, :], pts[kbg % 4][:],
                                  start=(kb == 0), stop=(kb == _nkb(s) - 1)).then_inc(SS["pe"].h, 1)
                    tn.wait_ge(SS["ev2"].h, b_vc + A_VC_HEAD * h + _a_vc_base_s(s) + 12)
                    tn.matmul(ps_b2[:], G["onesb_bf"][:], rl_bf[:], start=True, stop=True).then_inc(SS["pe"].h, 1)
        block.tensor(tns)

        def sc(s_):
            for h in range(NH):
                s_.wait_ge(SS["ldw0"].h, b_kk + 16 * (h + 1))
                for c in range(8):
                    if c > 0:
                        s_.wait_ge(SS["pe"].h, b_pe + A_PE_HEAD * h + c)
                    s_.activation(ksq[:], kbuf[:, c * TCH:(c + 1) * TCH], AF.Square).then_inc(SS["ev"].h, 1)
                s_.wait_ge(SS["ev2"].h, b_vc + A_VC_HEAD * h + 9)
                s_.activation(km[:], km2[:], AF.Sqrt).then_inc(SS["ev"].h, 1)
                for s in range(NSUP):
                    gq = h * NSUP + s
                    s_.wait_ge(SS[f"gq{gq % 2}"].h, b_q[gq % 2] + 16 * (gq // 2 + 1))
                    s_.activation(qsq[:], qbs[gq % 2][:], AF.Square).then_inc(SS["ev"].h, 1)
                    s_.wait_ge(SS["ev2"].h, b_vc + A_VC_HEAD * h + _a_vc_base_s(s) + 2)
                    s_.activation(qm[:], qm2[:], AF.Sqrt).then_inc(SS["ev"].h, 1)
                    s_.wait_ge(SS["ev2"].h, b_vc + A_VC_HEAD * h + _a_vc_base_s(s) + 3)
                    s_.mul(negc[:], cprod[:], -SCL).then_inc(SS["ev"].h, 1)
                    for kb in range(_nkb(s)):
                        kbg = _a_kbg(s, kb)
                        s_.wait_ge(SS["pe"].h, pe_st(h, s, kb))
                        s_.wait_ge(SS["ev2"].h, b_vc + A_VC_HEAD * h + _a_vc_base_s(s) + 5)
                        if kbg >= 4:
                            s4, k4 = _a_kbg_to_skb(kbg - 4)
                            s_.wait_ge(SS["pe"].h, pe_pv(h, s4, k4))
                        elif h > 0:
                            s4, k4 = _a_kbg_to_skb(KBS_HEAD - 4 + kbg)
                            s_.wait_ge(SS["pe"].h, pe_pv(h - 1, s4, k4))
                        s_.activation(pts[kbg % 4][:], pss[kbg % 2][:], AF.Exp,
                                      bias=bias_sb[:], scale=SCL).then_inc(SS["ev"].h, 1)
        block.scalar(sc)

        def vc(v):
            for h in range(NH):
                for c in range(8):
                    v.wait_ge(SS["pe"].h, b_pe + A_PE_HEAD * h + c + 1)
                    v.tensor_copy(ksn[:, c * TCH:(c + 1) * TCH], ps_sm[:]).then_inc(SS["ev2"].h, 1)
                v.wait_ge(SS["ev2"].h, b_vc + A_VC_HEAD * h + 8)
                v.reduce_max(km2[:], ksn[:], axis=AX.X).then_inc(SS["ev2"].h, 1)
                for s in range(NSUP):
                    gq = h * NSUP + s
                    vb = b_vc + A_VC_HEAD * h + _a_vc_base_s(s)
                    v.wait_ge(SS["pe"].h, b_pe + A_PE_HEAD * h + _a_pe_base_s(s) + 1)
                    v.tensor_copy(qsn[:], ps_sm[:]).then_inc(SS["ev2"].h, 1)
                    v.wait_ge(SS["ev2"].h, vb + 1)
                    v.reduce_max(qm2[:], qsn[:], axis=AX.X).then_inc(SS["ev2"].h, 1)
                    v.wait_ge(SS["ev"].h, b_sc + A_SC_HEAD * h + _a_sc_base_s(s) + 2)
                    v.tensor_mul(cprod[:], km[:], qm[:]).then_inc(SS["ev2"].h, 1)
                    v.wait_ge(SS["ev"].h, b_sc + A_SC_HEAD * h + _a_sc_base_s(s) + 3)
                    v.tensor_copy(negc_bf[:], negc[:]).then_inc(SS["ev2"].h, 1)
                    v.wait_ge(SS["pe"].h, b_pe + A_PE_HEAD * h + _a_pe_base_s(s) + 2)
                    v.tensor_copy(bias_sb[:], ps_b[:]).then_inc(SS["ev2"].h, 1)
                    for j in range(4):
                        kb = 4 * s + j
                        v.wait_ge(SS["ev"].h, sc_exp(h, s, kb))
                        kbg = _a_kbg(s, kb)
                        v.tensor_mul(pts[kbg % 4][:], pts[kbg % 4][:], G["mbuf"][:, j, :]).then_inc(SS["ev2"].h, 1)
                    v.wait_ge(SS["pe"].h, pe_l(h, s, _nkb(s) - 1))
                    v.tensor_copy(lsb[:], ps_l[:]).then_inc(SS["ev2"].h, 1)
                    v.wait_ge(SS["ev2"].h, vb + 10)
                    v.reciprocal(rlv[:], lsb[:]).then_inc(SS["ev2"].h, 1)
                    v.wait_ge(SS["ev2"].h, vb + 11)
                    v.tensor_copy(rl_bf[:], rlv[:]).then_inc(SS["ev2"].h, 1)
                    v.wait_ge(SS["pe"].h, b_pe + A_PE_HEAD * h + _a_pe_base_s(s) + 3 + 3 * _nkb(s))
                    v.tensor_copy(rb_s[:], ps_b2[:]).then_inc(SS["ev2"].h, 1)
                    v.wait_ge(SS["pe"].h, pe_pv(h, s, _nkb(s) - 1))
                    v.wait_ge(SS["ev2"].h, vb + 13)
                    if gq >= 2:
                        v.wait_ge(SS[f"st{gq % 2}"].h, b_st[gq % 2] + 16 * ((gq - 2) // 2 + 1))
                    v.tensor_mul(cxs[gq % 2][:], ps_o[:], rb_s[:]).then_inc(SS["ev2"].h, 1)
        block.vector(vc)

        def sy(sy_):
            for h in range(NH):
                for s in range(NSUP):
                    gq = h * NSUP + s
                    sy_.wait_ge(SS["ev2"].h, b_vc + A_VC_HEAD * h + _a_vc_base_s(s) + 14)
                    sy_.dma_start(out=cc2_t.ap()[h * P:(h + 1) * P, s * TCH:(s + 1) * TCH],
                                  in_=cxs[gq % 2][:]).then_inc(SS[f"st{gq % 2}"].h, 16)
            sy_.wait_ge(SS["st0"].h, b_st[0] + 16 * 32)
            sy_.wait_ge(SS["st1"].h, b_st[1] + 16 * 32)
        block.sync(sy)

        SS["pe"].n = b_pe + A_PE_HEAD * NH
        SS["ev"].n = b_sc + A_SC_HEAD * NH
        SS["ev2"].n = b_vc + A_VC_HEAD * NH
        SS["ldw0"].n = b_kk + 16 * NH
        SS["ldw1"].n = b_kv + 16 * NH
        SS["gq0"].n = b_q[0] + 16 * 32
        SS["gq1"].n = b_q[1] + 16 * 32
        SS["st0"].n = b_st[0] + 16 * 32
        SS["st1"].n = b_st[1] + 16 * 32
    barrier(block, SS)



def emit_fwht_T(nc, block, SS, G, wdn_t, tdc_t):
    n = 8192
    nst = 13
    b_ldx = SS["ldx"].n
    b_pe = SS["pe"].n
    b_vc = SS["ev2"].n
    b_st = SS["st0"].n
    VPT = 2 * nst + 1 + 8
    PPT = 64
    with ExitStack() as cx:
        uid = _uid()
        ec = cx.enter_context
        ba = ec(nc.sbuf_tensor(f"ft_ba_{uid}", [P, n], F32))
        bb = ec(nc.sbuf_tensor(f"ft_bb_{uid}", [P, n], F32))
        tcb = ec(nc.sbuf_tensor(f"ft_tcb_{uid}", [P, n], BF16))
        tout = ec(nc.sbuf_tensor(f"ft_tout_{uid}", [P, 8, P], BF16))
        ptr = [ec(nc.psum_tensor(f"ft_ptr{j}_{uid}", [P, 8 * P], BF16)) for j in range(2)]

        def gp(g):
            for i in range(2):
                if i >= 1:
                    g.wait_ge(SS["ev2"].h, b_vc + VPT * i - 8)
                g.dma_start(out=ba[:], in_=wdn_t.ap()[i * P:(i + 1) * P, :]).then_inc(SS["ldx"].h, 16)
        block.gpsimd(gp)

        def vc(v):
            for i in range(2):
                vb = b_vc + VPT * i
                cnt = 0
                v.wait_ge(SS["ldx"].h, b_ldx + 16 * (i + 1))
                if i >= 1:
                    v.wait_ge(SS["st0"].h, b_st + 16 * 8 * i)
                src_, dst = ba, bb
                h = 1
                for stg in range(nst):
                    sv = src_[:].rearrange("p (g two h) -> p g two h", two=2, h=h)
                    dv = dst[:].rearrange("p (g two h) -> p g two h", two=2, h=h)
                    v.tensor_add(dv[:, :, 0, :], sv[:, :, 0, :], sv[:, :, 1, :]).then_inc(SS["ev2"].h, 1)
                    v.tensor_sub(dv[:, :, 1, :], sv[:, :, 0, :], sv[:, :, 1, :]).then_inc(SS["ev2"].h, 1)
                    cnt += 2
                    v.wait_ge(SS["ev2"].h, vb + cnt)
                    src_, dst = dst, src_
                    h *= 2
                v.tensor_copy(tcb[:], src_[:]).then_inc(SS["ev2"].h, 1)
                cnt += 1
                v.wait_ge(SS["ev2"].h, vb + cnt)
                for grp in range(8):
                    v.wait_ge(SS["pe"].h, b_pe + PPT * i + grp * 8 + 8)
                    if i > 0 or grp > 0:
                        v.wait_ge(SS["st0"].h, b_st + 16 * (8 * i + grp))
                    v.tensor_copy(tout[:], ptr[grp % 2][:].rearrange("p (j m) -> p j m", m=P)).then_inc(SS["ev2"].h, 1)
        block.vector(vc)

        def tns(tn):
            for i in range(2):
                tn.wait_ge(SS["ev2"].h, b_vc + VPT * i + 2 * nst + 1)
                for grp in range(8):
                    if i > 0 or grp > 1:
                        gidx = 8 * i + grp
                        pv, gv = divmod(gidx - 2, 8)
                        tn.wait_ge(SS["ev2"].h, b_vc + VPT * pv + 2 * nst + 1 + gv + 1)
                    for j in range(8):
                        tn.transpose(ptr[grp % 2][:, j * P:(j + 1) * P],
                                     tcb[:, (grp * 8 + j) * P:(grp * 8 + j + 1) * P],
                                     G["eye_bf"][:]).then_inc(SS["pe"].h, 1)
        block.tensor(tns)

        def sy(sy_):
            for i in range(2):
                for grp in range(8):
                    sy_.wait_ge(SS["ev2"].h, b_vc + VPT * i + 2 * nst + 1 + grp + 1)
                    sy_.dma_start(out=tdc_t.ap()[grp * 1024:(grp + 1) * 1024, i * P:(i + 1) * P]
                                  .rearrange("(j p) m -> p j m", p=P),
                                  in_=tout[:]).then_inc(SS["st0"].h, 16)
            sy_.wait_ge(SS["st0"].h, b_st + 16 * 16)
        block.sync(sy)

        SS["ldx"].n = b_ldx + 32
        SS["pe"].n = b_pe + 2 * PPT
        SS["ev2"].n = b_vc + 2 * VPT
        SS["st0"].n = b_st + 16 * 16
    barrier(block, SS)


def emit_collective(nc, block, SS, kind, groups, ins, outs, op=None, sem="cc0"):
    from concourse.aot_env import allow_ici_all_to_all

    if op is None:
        op = ALU.bypass

    def gp(g):
        with allow_ici_all_to_all():
            g.collective_compute(kind, op, replica_groups=groups,
                                 ins=ins, outs=outs).then_inc(SS[sem].h, 1)
        g.wait_ge(SS[sem].h, 1)
    block.gpsimd(gp)
    SS[sem].n = 1
    barrier(block, SS)



def emit_dbgcopy(nc, block, SS, dst_t, src_t):
    b_cc = SS["cc"].n

    def gp(g):
        g.dma_start(out=dst_t.ap(), in_=src_t.ap()).then_inc(SS["cc"].h, 16)
        g.wait_ge(SS["cc"].h, b_cc + 16)
    block.gpsimd(gp)
    SS["cc"].n = b_cc + 16
    barrier(block, SS)


def emit_resadd(nc, block, SS, G, res_t, delta_t, bvec_ap=None, src_t=None):
    if src_t is None:
        src_t = res_t
    nm, ntc = D // P, TLOC // TCH
    nidx = nm * ntc
    b_ldr = [SS["ldr0"].n, SS["ldr1"].n]
    b_ldv = SS["ldv"].n
    b_ev2 = SS["ev2"].n
    b_st = [SS["st0"].n, SS["st1"].n]
    has_bias = bvec_ap is not None
    with ExitStack() as cx:
        uid = _uid()
        ec = cx.enter_context
        rb = [ec(nc.sbuf_tensor(f"ra_r{j}_{uid}", [P, TCH], F32)) for j in range(2)]
        db = [ec(nc.sbuf_tensor(f"ra_d{j}_{uid}", [P, TCH], F32)) for j in range(2)]
        vb = ec(nc.sbuf_tensor(f"ra_vb_{uid}", [P, nm], F32))

        def sy(sy_):
            if has_bias:
                sy_.dma_start(out=vb[:], in_=bvec_ap).then_inc(SS["ldv"].h, 16)
            for idx in range(nidx):
                t, m = divmod(idx, nm)
                if idx >= 2:
                    sy_.wait_ge(SS[f"st{idx % 2}"].h, b_st[idx % 2] + 16 * ((idx - 2) // 2 + 1))
                sy_.dma_start(out=rb[idx % 2][:],
                              in_=src_t.ap()[m * P:(m + 1) * P, t * TCH:(t + 1) * TCH]).then_inc(SS[f"ldr{idx % 2}"].h, 16)
                sy_.dma_start(out=db[idx % 2][:],
                              in_=delta_t.ap()[m * P:(m + 1) * P, t * TCH:(t + 1) * TCH]).then_inc(SS[f"ldr{idx % 2}"].h, 16)
        block.sync(sy)

        def vc(v):
            if has_bias:
                v.wait_ge(SS["ldv"].h, b_ldv + 16)
            for idx in range(nidx):
                t, m = divmod(idx, nm)
                v.wait_ge(SS[f"ldr{idx % 2}"].h, b_ldr[idx % 2] + 32 * (idx // 2 + 1))
                if idx >= 2:
                    v.wait_ge(SS[f"st{idx % 2}"].h, b_st[idx % 2] + 16 * ((idx - 2) // 2 + 1))
                if has_bias:
                    v.scalar_tensor_tensor(rb[idx % 2][:], rb[idx % 2][:], vb[:, m:m + 1],
                                           db[idx % 2][:], ALU.add, ALU.add).then_inc(SS["ev2"].h, 1)
                else:
                    v.tensor_add(rb[idx % 2][:], rb[idx % 2][:], db[idx % 2][:]).then_inc(SS["ev2"].h, 1)
        block.vector(vc)

        def sc(s_):
            for idx in range(nidx):
                t, m = divmod(idx, nm)
                s_.wait_ge(SS["ev2"].h, b_ev2 + idx + 1)
                s_.dma_start(out=res_t.ap()[m * P:(m + 1) * P, t * TCH:(t + 1) * TCH],
                             in_=rb[idx % 2][:]).then_inc(SS[f"st{idx % 2}"].h, 16)
            s_.wait_ge(SS["st0"].h, b_st[0] + 16 * ((nidx + 1) // 2))
            s_.wait_ge(SS["st1"].h, b_st[1] + 16 * (nidx // 2))
        block.scalar(sc)

        SS["ldv"].n = b_ldv + (16 if has_bias else 0)
        SS["ldr0"].n = b_ldr[0] + 32 * ((nidx + 1) // 2)
        SS["ldr1"].n = b_ldr[1] + 32 * (nidx // 2)
        SS["ev2"].n = b_ev2 + nidx
        SS["st0"].n = b_st[0] + 16 * ((nidx + 1) // 2)
        SS["st1"].n = b_st[1] + 16 * (nidx // 2)
    barrier(block, SS)



def build_megakernel():
    nc = bass.Bass("TRN2", target_bir_lowering=False)
    x_fm = nc.dram_tensor("x_fm", [D, TLOC], F32, kind="ExternalInput")
    w_qkv0 = nc.dram_tensor("w_qkv0", [D, 3072], F32, kind="ExternalInput")
    b_qk0 = nc.dram_tensor("b_qk0", [P, 16], F32, kind="ExternalInput")
    b_v0 = nc.dram_tensor("b_v0", [1, 1024], F32, kind="ExternalInput")
    w_out0 = nc.dram_tensor("w_out0", [NH * P, D], F32, kind="ExternalInput")
    b_out0 = nc.dram_tensor("b_out0", [P, 16], F32, kind="ExternalInput")
    h2048 = nc.dram_tensor("h2048", [D, D], F32, kind="ExternalInput")
    su_ug = nc.dram_tensor("su_ug", [P, 16], F32, kind="ExternalInput")
    w_upT = nc.dram_tensor("w_upT", [D, 8192], F32, kind="ExternalInput")
    ws_u = nc.dram_tensor("ws_u", [P, 64], F32, kind="ExternalInput")
    su_dp = nc.dram_tensor("su_dp", [P, 64], F32, kind="ExternalInput")
    w_gateT = nc.dram_tensor("w_gateT", [D, 8192], F32, kind="ExternalInput")
    ws_g = nc.dram_tensor("ws_g", [P, 64], F32, kind="ExternalInput")
    wdn = nc.dram_tensor("wdn", [256, 8192], F32, kind="ExternalInput")
    ws_dp = nc.dram_tensor("ws_dp", [P, 16], F32, kind="ExternalInput")
    w_qkv1 = nc.dram_tensor("w_qkv1", [D, 3072], F32, kind="ExternalInput")
    w_o1 = nc.dram_tensor("w_o1", [NH * P, D], F32, kind="ExternalInput")
    b_o1 = nc.dram_tensor("b_o1", [P, 16], F32, kind="ExternalInput")
    w_up1T = nc.dram_tensor("w_up1T", [D, 8192], F32, kind="ExternalInput")
    us1 = nc.dram_tensor("us1", [P, 64], F32, kind="ExternalInput")
    w_dn1T = nc.dram_tensor("w_dn1T", [8192, D], F32, kind="ExternalInput")
    ds1 = nc.dram_tensor("ds1", [P, 16], F32, kind="ExternalInput")
    osc = nc.dram_tensor("osc", [P, 1], F32, kind="ExternalInput")
    masks_in = nc.dram_tensor("masks_in", [4 * P, TCH], F32, kind="ExternalInput")
    onesk_in = nc.dram_tensor("onesk_in", [P, 1], F32, kind="ExternalInput")
    onesb_in = nc.dram_tensor("onesb_in", [1, P], F32, kind="ExternalInput")
    eye_in = nc.dram_tensor("eye_in", [P, P], F32, kind="ExternalInput")
    eps_in = nc.dram_tensor("eps_in", [P, 1], F32, kind="ExternalInput")
    y = nc.dram_tensor("y", [D, TLOC], F32, kind="ExternalOutput")
    res = nc.dram_tensor("res", [D, TLOC], F32)
    xnc = nc.dram_tensor("xnc", [D, TLOC], BF16)
    xng = nc.dram_tensor("xng", [2 * D, TLOC], BF16)
    qk = nc.dram_tensor("qk", [2 * NH * P, S], BF16)
    vt = nc.dram_tensor("vt", [S, NH * P], BF16)
    ctxl = nc.dram_tensor("ctxl", [NH * P, S], BF16)
    prt = nc.dram_tensor("prt", [2 * D, TLOC], F32)
    rso = nc.dram_tensor("rso", [D, TLOC], F32)
    x2t = nc.dram_tensor("x2t", [D, TLOC], BF16)
    xupt = nc.dram_tensor("xupt", [8192, TLOC], BF16)
    x3t = nc.dram_tensor("x3t", [8192, TLOC], BF16)
    hidt = nc.dram_tensor("hidt", [8192, TLOC], BF16)
    tdc = nc.dram_tensor("tdc", [8192, 256], BF16)
    tdg = nc.dram_tensor("tdg", [8 * 8192, 256], BF16)

    PAIRS = [[0, 1], [2, 3], [4, 5], [6, 7]]
    ALL8 = [list(range(8))]

    with ExitStack() as cx:
        uid = _uid()
        ec = cx.enter_context
        ones_bf = ec(nc.sbuf_tensor(f"g_ones_{uid}", [P, 1], BF16))
        onesb_bf = ec(nc.sbuf_tensor(f"g_onesb_{uid}", [1, P], BF16))
        mbuf = ec(nc.sbuf_tensor(f"g_mbuf_{uid}", [P, 4, TCH], BF16))
        eye_bf = ec(nc.sbuf_tensor(f"g_eye_{uid}", [P, P], BF16))
        epst = ec(nc.sbuf_tensor(f"g_eps_{uid}", [P, 1], F32))
        sems = {}
        for nm in ("bar", "ldx", "ldw0", "ldw1", "ldv", "ldr0", "ldr1", "gq0", "gq1", "gldc", "xpms",
                   "vini", "pe", "ev", "ev2", "sg", "st0", "st1", "cc",
                   "cc0", "cc1", "cc2s", "cc3", "cc4"):
            sems[nm] = SemT(ec(nc.semaphore(nm)))
        block = ec(nc.Block())
        SS = sems
        G = {"ones_bf": ones_bf, "onesb_bf": onesb_bf, "mbuf": mbuf,
             "eye_bf": eye_bf, "epst": epst, "ldv_const": 80}

        def pro(g):
            g.dma_start(out=ones_bf[:], in_=onesk_in.ap()).then_inc(SS["gldc"].h, 16)
            g.dma_start(out=onesb_bf[:], in_=onesb_in.ap()).then_inc(SS["gldc"].h, 16)
            g.dma_start(out=mbuf[:], in_=masks_in.ap().rearrange("(i p) n -> p i n", p=P)).then_inc(SS["gldc"].h, 16)
            g.dma_start(out=eye_bf[:], in_=eye_in.ap()).then_inc(SS["gldc"].h, 16)
            g.dma_start(out=epst[:], in_=eps_in.ap()).then_inc(SS["gldc"].h, 16)
            g.wait_ge(SS["gldc"].h, 80)
        block.gpsimd(pro)
        SS["gldc"].n = 80
        barrier(block, SS)

        def fm_chunk(tensor):
            return lambda th: tensor.ap().rearrange("(k p) n -> p k n", p=P)

        def fm_chunk_q(tensor):
            return lambda th: tensor.ap()[:, th * TCH:(th + 1) * TCH].rearrange("(k p) n -> p k n", p=P)

        def xng_chunk(th, xb):
            return [(xb[:, :, r * TLOC:(r + 1) * TLOC],
                     xng.ap()[r * D:(r + 1) * D, :].rearrange("(k p) n -> p k n", p=P))
                    for r in range(2)]

        def xng_tile(tt):
            r, tl = divmod(tt, 16)
            return xng.ap()[r * D:(r + 1) * D, tl * P:(tl + 1) * P].rearrange("(k p) n -> p k n", p=P)

        def w_slab(w, c0=0):
            return lambda si: w.ap()[:, c0 + si * 512:c0 + (si + 1) * 512].rearrange("(k p) n -> p k n", p=P)

        def w_slab_dn1(si):
            return w_dn1T.ap()[:, si * 256:(si + 1) * 256].rearrange("(k p) n -> p k n", p=P)

        def ctxl_chunk(th):
            return ctxl.ap().rearrange("(k p) n -> p k n", p=P)

        def prt_tile(t, m):
            return prt.ap()[(t // 4) * D + m * P:(t // 4) * D + (m + 1) * P,
                            (t % 4) * TCH:(t % 4 + 1) * TCH]

        def w_tdg_slab(si):
            return tdg.ap()[si * 8192:(si + 1) * 8192, :].rearrange("(k p) n -> p k n", p=P)

        def tile_of(tensor):
            return lambda t, m: tensor.ap()[m * P:(m + 1) * P, t * TCH:(t + 1) * TCH]

        def qk_tile(t, m):
            return qk.ap()[m * P:(m + 1) * P, t * TCH:(t + 1) * TCH]

        def vt_tile(tt, mc):
            return vt.ap()[tt * P:(tt + 1) * P, mc * TCH:(mc + 1) * TCH]

        emit_ln(nc, block, SS, G, x_fm.ap(), xnc.ap())
        emit_collective(nc, block, SS, "AllGather", PAIRS, [xnc.ap()], [xng.ap()], sem="cc0")
        emit_matmul(nc, block, SS, G, D, 2048, S, xng_chunk, None, "cast",
                    qk_tile, out_dtype="bf16", has_bias=True, bvec=b_qk0.ap(),
                    wslab_ap=w_slab(w_qkv0), xnparts=2)
        emit_matmul_tm(nc, block, SS, G, D, 1024, S, xng_tile, w_qkv0.ap()[:, 2048:3072],
                       vt_tile, bias_row_ap=b_v0.ap())
        emit_attention(nc, block, SS, G, qk, vt, ctxl)
        emit_matmul(nc, block, SS, G, NH * P, D, S, ctxl_chunk, None, "cast",
                    prt_tile, out_dtype="f32", wslab_ap=w_slab(w_out0))
        emit_collective(nc, block, SS, "ReduceScatter", PAIRS, [prt.ap()], [rso.ap()], op=ALU.add, sem="cc1")
        emit_resadd(nc, block, SS, G, res, rso, bvec_ap=b_out0.ap(), src_t=x_fm)
        if DBG:
            d_res1 = nc.dram_tensor("d_res1", [D, TLOC], F32, kind="ExternalOutput")
            emit_dbgcopy(nc, block, SS, d_res1, res)
        emit_ln(nc, block, SS, G, res.ap(), xnc.ap())
        emit_matmul(nc, block, SS, G, D, D, TLOC, fm_chunk(xnc), None, "cast",
                    tile_of(x2t), out_dtype="bf16", cmul=1.0 / (64.0 * math.sqrt(D)),
                    xpm_ap=su_ug.ap(), wslab_ap=w_slab(h2048))
        emit_matmul(nc, block, SS, G, D, 8192, TLOC, fm_chunk(x2t), None, "cast",
                    tile_of(xupt), out_dtype="bf16", has_scale=True, svec_ap=ws_u.ap(),
                    cmul=64.0, sfold_ap=su_dp.ap(), wslab_ap=w_slab(w_upT))
        emit_matmul(nc, block, SS, G, D, 8192, TLOC, fm_chunk(x2t), None, "cast",
                    tile_of(x3t), out_dtype="bf16", act="silu", has_scale=True,
                    svec_ap=ws_g.ap(), cmul=64.0, mult_tile_ap=tile_of(xupt),
                    wslab_ap=w_slab(w_gateT))
        emit_fwht_T(nc, block, SS, G, wdn, tdc)
        emit_collective(nc, block, SS, "AllGather", ALL8, [tdc.ap()], [tdg.ap()], sem="cc2s")
        emit_matmul(nc, block, SS, G, 8192, D, TLOC, fm_chunk_q(x3t), None, "bf16",
                    tile_of(res), out_dtype="f32", has_scale=True, svec_ap=ws_dp.ap(),
                    cmul=1.0 / math.sqrt(8192.0), resid_tile_ap=tile_of(res),
                    wslab_ap=w_tdg_slab)
        if DBG:
            d_res2 = nc.dram_tensor("d_res2", [D, TLOC], F32, kind="ExternalOutput")
            emit_dbgcopy(nc, block, SS, d_res2, res)
        emit_ln(nc, block, SS, G, res.ap(), xnc.ap())
        emit_collective(nc, block, SS, "AllGather", PAIRS, [xnc.ap()], [xng.ap()], sem="cc3")
        emit_matmul(nc, block, SS, G, D, 2048, S, xng_chunk, None, "cast",
                    qk_tile, out_dtype="bf16", wslab_ap=w_slab(w_qkv1), xnparts=2)
        emit_matmul_tm(nc, block, SS, G, D, 1024, S, xng_tile, w_qkv1.ap()[:, 2048:3072],
                       vt_tile)
        emit_attention(nc, block, SS, G, qk, vt, ctxl)
        emit_matmul(nc, block, SS, G, NH * P, D, S, ctxl_chunk, None, "cast",
                    prt_tile, out_dtype="f32", wslab_ap=w_slab(w_o1))
        emit_collective(nc, block, SS, "ReduceScatter", PAIRS, [prt.ap()], [rso.ap()], op=ALU.add, sem="cc4")
        emit_resadd(nc, block, SS, G, res, rso, bvec_ap=b_o1.ap())
        emit_ln(nc, block, SS, G, res.ap(), xnc.ap())
        emit_matmul(nc, block, SS, G, D, 8192, TLOC, fm_chunk(xnc), None, "cast",
                    tile_of(hidt), out_dtype="bf16", act="relu", has_scale=True,
                    svec_ap=us1.ap(), sign_w=True, wslab_ap=w_slab(w_up1T))
        emit_matmul(nc, block, SS, G, 8192, D, TLOC, fm_chunk_q(hidt), None, "cast",
                    tile_of(y), out_dtype="f32", has_scale=True, svec_ap=ds1.ap(),
                    sign_w=True, psc_ap=osc.ap(), resid_tile_ap=tile_of(res),
                    wslab_ap=w_slab_dn1)
        if DBG:
            dbgs = [("d_res", res), ("d_rso", rso), ("d_x2", x2t), ("d_xup", xupt),
                    ("d_x3", x3t), ("d_qk", qk), ("d_vt", vt), ("d_ctxl", ctxl),
                    ("d_xnc", xnc), ("d_hid", hidt), ("d_tdg", tdg), ("d_xng", xng)]
            douts = {}
            for nm, t in dbgs:
                douts[nm] = nc.dram_tensor(nm, list(t.shape), t.dtype, kind="ExternalOutput")
            b_cc = SS["cc"].n

            def dump(g):
                for i, (nm, t) in enumerate(dbgs):
                    g.dma_start(out=douts[nm].ap(), in_=t.ap()).then_inc(SS["cc"].h, 16)
                g.wait_ge(SS["cc"].h, b_cc + 16 * len(dbgs))
            block.gpsimd(dump)
            SS["cc"].n = b_cc + 16 * len(dbgs)
            barrier(block, SS)
    return nc


def _hadamard(n):
    h = np.array([[1.0]], dtype=np.float32)
    while h.shape[0] < n:
        h = np.kron(h, np.array([[1.0, 1.0], [1.0, -1.0]], dtype=np.float32))
    return h


def _prelay(v, n):
    return np.ascontiguousarray(np.asarray(v, dtype=np.float32).reshape(n, P).T)


def make_inmaps(inputs):
    ac = np.ascontiguousarray
    f32 = lambda a: np.asarray(a, dtype=np.float32)
    x = f32(inputs["L0_attn_x"]).reshape(NTOK, D)
    H = _hadamard(D)
    mask = np.zeros((4 * P, TCH), np.float32)
    for j in range(4):
        for kp in range(P):
            mask[j * P + kp, j * P + kp:] = 1.0
    W_in = f32(inputs["L0_attn_W_in"])
    b_in = f32(inputs["L0_attn_b_in"])
    Wq, Wk, Wv = f32(inputs["L1_attn_Wq"]), f32(inputs["L1_attn_Wk"]), f32(inputs["L1_attn_Wv"])
    ws_ug = f32(inputs["L0_ffn_Wscale_ug"])
    in_maps = []
    for c in range(NCORES):
        b, r = divmod(c, 2)
        hs = r * NH
        qcols = slice(hs * P, (hs + NH) * P)
        kcols = slice(D + hs * P, D + (hs + NH) * P)
        vcols = slice(2 * D + hs * P, 2 * D + (hs + NH) * P)
        m = {
            "x_fm": ac(x[b * S + r * TLOC:(b * S + (r + 1) * TLOC)].T),
            "w_qkv0": ac(np.concatenate([W_in[:, qcols], W_in[:, kcols], W_in[:, vcols]], axis=1)),
            "b_qk0": _prelay(np.concatenate([b_in[qcols], b_in[kcols]]), 16),
            "b_v0": ac(b_in[vcols].reshape(1, 1024)),
            "w_out0": ac(f32(inputs["L0_attn_W_out"])[hs * P:(hs + NH) * P, :]),
            "b_out0": _prelay(f32(inputs["L0_attn_b_out"]), 16),
            "h2048": H,
            "su_ug": _prelay(f32(inputs["L0_ffn_SU_ug"]), 16),
            "w_upT": ac(f32(inputs["L0_ffn_W_up"]).T),
            "ws_u": _prelay(ws_ug[:8192], 64),
            "su_dp": _prelay(f32(inputs["L0_ffn_SU_dp"]), 64),
            "w_gateT": ac(f32(inputs["L0_ffn_W_gate"]).T),
            "ws_g": _prelay(ws_ug[8192:], 64),
            "wdn": ac(f32(inputs["L0_ffn_W_down"])[c * 256:(c + 1) * 256]),
            "ws_dp": _prelay(f32(inputs["L0_ffn_Wscale_dp"]), 16),
            "w_qkv1": ac(np.concatenate([Wq[:, qcols.start:qcols.stop],
                                         Wk[:, qcols.start:qcols.stop],
                                         Wv[:, qcols.start:qcols.stop]], axis=1)),
            "w_o1": ac(f32(inputs["L1_attn_Wo"])[hs * P:(hs + NH) * P, :]),
            "b_o1": _prelay(f32(inputs["L1_attn_bo"]), 16),
            "w_up1T": ac(f32(inputs["L1_ffn_up_weight"]).T),
            "us1": _prelay(f32(inputs["L1_ffn_up_scale"]), 64),
            "w_dn1T": ac(f32(inputs["L1_ffn_down_weight"]).T),
            "ds1": _prelay(f32(inputs["L1_ffn_down_scale"]), 16),
            "osc": ac(np.broadcast_to(f32(inputs["L1_ffn_output_scale"]).reshape(1, 1), (P, 1))),
            "masks_in": mask,
            "onesk_in": np.ones((P, 1), np.float32),
            "onesb_in": np.ones((1, P), np.float32),
            "eye_in": np.eye(P, dtype=np.float32),
            "eps_in": np.full((P, 1), EPS, np.float32),
        }
        in_maps.append(m)
    return in_maps


def kernel(**inputs):
    if "nc" not in _CACHE:
        _CACHE["nc"] = build_megakernel()
    nc = _CACHE["nc"]
    in_maps = make_inmaps(inputs)
    res = run_bass_kernel_spmd(nc, in_maps, core_ids=list(range(NCORES)))
    _CACHE["last_results"] = res
    out = np.empty((NTOK, D), np.float32)
    for c in range(NCORES):
        out[c * TLOC:(c + 1) * TLOC] = res.results[c]["y"].T
    return out.reshape(B, S, D)


# version 04
# speedup vs baseline: 1.0332x; 1.0332x over previous
import math
from contextlib import ExitStack

import numpy as np
import ml_dtypes
import concourse.bass as bass
import concourse.mybir as mybir
from concourse.bass_utils import run_bass_kernel_spmd

F32 = mybir.dt.float32
BF16 = mybir.dt.bfloat16
AF = mybir.ActivationFunctionType
ALU = mybir.AluOpType
AX = mybir.AxisListType

P = 128
NCORES = 8
B, S, D = 4, 4096, 2048
NTOK = B * S
TLOC = NTOK // NCORES
TCH = 512
EPS = 1e-5
NH = 8
NSUP = 8
SCL = 1.0 / math.sqrt(128.0)
DBG = False

_CACHE = {}
_UIDC = [0]


def _uid():
    _UIDC[0] += 1
    return _UIDC[0]


class SemT:
    def __init__(self, h):
        self.h = h
        self.n = 0


def barrier(block, SS):
    bar = SS["bar"]
    base = bar.n
    for name in ("sync", "gpsimd", "tensor", "scalar", "vector"):
        def mk(eng):
            eng.nop().then_inc(bar.h, 1)
            eng.wait_ge(bar.h, base + 5)
        getattr(block, name)(mk)
    bar.n += 5



def emit_ln(nc, block, SS, G, src_ap, dst_ap):
    nk = D // P
    ntc = TLOC // TCH
    VPT = 10 + 2 * nk
    SPT = nk + 1
    PPT = nk + 3
    b_ldx, b_pe, b_ev, b_ev2, b_st = SS["ldx"].n, SS["pe"].n, SS["ev"].n, SS["ev2"].n, SS["st0"].n
    with ExitStack() as cx:
        uid = _uid()
        ec = cx.enter_context
        xb = ec(nc.sbuf_tensor(f"ln_xb_{uid}", [P, nk, TCH], BF16))
        sqb = ec(nc.sbuf_tensor(f"ln_sqb_{uid}", [P, TCH], BF16))
        xnb = ec(nc.sbuf_tensor(f"ln_xnb_{uid}", [P, nk, TCH], BF16))
        mrow = ec(nc.sbuf_tensor(f"ln_m_{uid}", [1, TCH], F32))
        erow = ec(nc.sbuf_tensor(f"ln_e_{uid}", [1, TCH], F32))
        msq = ec(nc.sbuf_tensor(f"ln_msq_{uid}", [1, TCH], F32))
        var = ec(nc.sbuf_tensor(f"ln_var_{uid}", [1, TCH], F32))
        srt = ec(nc.sbuf_tensor(f"ln_srt_{uid}", [1, TCH], F32))
        rr = ec(nc.sbuf_tensor(f"ln_rr_{uid}", [1, TCH], F32))
        nmr = ec(nc.sbuf_tensor(f"ln_nmr_{uid}", [1, TCH], F32))
        a_bf = ec(nc.sbuf_tensor(f"ln_abf_{uid}", [1, TCH], BF16))
        bb_bf = ec(nc.sbuf_tensor(f"ln_bbf_{uid}", [1, TCH], BF16))
        a_sb = ec(nc.sbuf_tensor(f"ln_asb_{uid}", [P, TCH], BF16))
        b_sb = ec(nc.sbuf_tensor(f"ln_bsb_{uid}", [P, TCH], BF16))
        ps_m = ec(nc.psum_tensor(f"ln_psm_{uid}", [1, TCH], F32))
        ps_s = ec(nc.psum_tensor(f"ln_pss_{uid}", [1, TCH], F32))
        ps_a = ec(nc.psum_tensor(f"ln_psa_{uid}", [P, TCH], F32))
        ps_b = ec(nc.psum_tensor(f"ln_psb_{uid}", [P, TCH], F32))

        def gp(g):
            for t in range(ntc):
                if t >= 1:
                    g.wait_ge(SS["ev2"].h, b_ev2 + VPT * t)
                g.dma_start(out=xb[:], in_=src_ap[:, t * TCH:(t + 1) * TCH]
                            .rearrange("(k p) n -> p k n", p=P)).then_inc(SS["ldx"].h, 16)
        block.gpsimd(gp)

        def tns(tn):
            tn.wait_ge(SS["gldc"].h, G["ldv_const"])
            for t in range(ntc):
                pb = b_pe + PPT * t
                vbp = b_ev2 + VPT * (t - 1)
                tn.wait_ge(SS["ldx"].h, b_ldx + 16 * (t + 1))
                if t >= 1:
                    tn.wait_ge(SS["ev2"].h, vbp + 1)
                ins = None
                for k in range(nk):
                    ins = tn.matmul(ps_m[:], G["ones_bf"][:], xb[:, k, :],
                                    start=(k == 0), stop=(k == nk - 1))
                ins.then_inc(SS["pe"].h, 1)
                for k in range(nk):
                    tn.wait_ge(SS["ev"].h, b_ev + SPT * t + k + 1)
                    if t >= 1 and k == 0:
                        tn.wait_ge(SS["ev2"].h, vbp + 2)
                    tn.matmul(ps_s[:], G["ones_bf"][:], sqb[:],
                              start=(k == 0), stop=(k == nk - 1)).then_inc(SS["pe"].h, 1)
                vb = b_ev2 + VPT * t
                tn.wait_ge(SS["ev2"].h, vb + 7)
                if t >= 1:
                    tn.wait_ge(SS["ev2"].h, vbp + 9)
                tn.matmul(ps_a[:], G["onesb_bf"][:], a_bf[:], start=True, stop=True).then_inc(SS["pe"].h, 1)
                tn.wait_ge(SS["ev2"].h, vb + 8)
                if t >= 1:
                    tn.wait_ge(SS["ev2"].h, vbp + 10)
                tn.matmul(ps_b[:], G["onesb_bf"][:], bb_bf[:], start=True, stop=True).then_inc(SS["pe"].h, 1)
        block.tensor(tns)

        def sc(s_):
            for t in range(ntc):
                s_.wait_ge(SS["ldx"].h, b_ldx + 16 * (t + 1))
                for k in range(nk):
                    if k >= 1:
                        s_.wait_ge(SS["pe"].h, b_pe + PPT * t + 1 + k)
                    elif t >= 1:
                        s_.wait_ge(SS["pe"].h, b_pe + PPT * t)
                    s_.activation(sqb[:], xb[:, k, :], AF.Square).then_inc(SS["ev"].h, 1)
                s_.wait_ge(SS["ev2"].h, b_ev2 + VPT * t + 4)
                s_.activation(srt[:], var[:], AF.Sqrt, bias=G["epst"][0:1, :]).then_inc(SS["ev"].h, 1)
        block.scalar(sc)

        def vc(v):
            for t in range(ntc):
                vb = b_ev2 + VPT * t
                v.wait_ge(SS["pe"].h, b_pe + PPT * t + 1)
                v.tensor_scalar(mrow[:], ps_m[:], 1.0 / D, None, ALU.mult).then_inc(SS["ev2"].h, 1)
                v.wait_ge(SS["pe"].h, b_pe + PPT * t + 1 + nk)
                v.tensor_scalar(erow[:], ps_s[:], 1.0 / D, None, ALU.mult).then_inc(SS["ev2"].h, 1)
                v.wait_ge(SS["ev2"].h, vb + 2)
                v.tensor_mul(msq[:], mrow[:], mrow[:]).then_inc(SS["ev2"].h, 1)
                v.wait_ge(SS["ev2"].h, vb + 3)
                v.tensor_sub(var[:], erow[:], msq[:]).then_inc(SS["ev2"].h, 1)
                v.wait_ge(SS["ev"].h, b_ev + SPT * t + nk + 1)
                v.reciprocal(rr[:], srt[:]).then_inc(SS["ev2"].h, 1)
                v.wait_ge(SS["ev2"].h, vb + 5)
                v.tensor_mul(nmr[:], mrow[:], rr[:]).then_inc(SS["ev2"].h, 1)
                v.tensor_copy(a_bf[:], rr[:]).then_inc(SS["ev2"].h, 1)
                v.wait_ge(SS["ev2"].h, vb + 6)
                v.tensor_scalar(bb_bf[:], nmr[:], -1.0, None, ALU.mult).then_inc(SS["ev2"].h, 1)
                v.wait_ge(SS["pe"].h, b_pe + PPT * t + nk + 2)
                if t >= 1:
                    v.wait_ge(SS["ev2"].h, b_ev2 + VPT * t)
                v.tensor_copy(a_sb[:], ps_a[:]).then_inc(SS["ev2"].h, 1)
                v.wait_ge(SS["pe"].h, b_pe + PPT * t + nk + 3)
                v.tensor_copy(b_sb[:], ps_b[:]).then_inc(SS["ev2"].h, 1)
                v.wait_ge(SS["ev2"].h, vb + 10)
                if t >= 2:
                    v.wait_ge(SS["st0"].h, b_st + 16 * (t - 1))
                if t >= 1:
                    v.wait_ge(SS["st0"].h, b_st + 16 * t)
                for k in range(nk):
                    v.tensor_mul(xnb[:, k, :], xb[:, k, :], a_sb[:]).then_inc(SS["ev2"].h, 1)
                    v.wait_ge(SS["ev2"].h, vb + 11 + 2 * k)
                    v.tensor_add(xnb[:, k, :], xnb[:, k, :], b_sb[:]).then_inc(SS["ev2"].h, 1)
        block.vector(vc)

        def sy(sy_):
            for t in range(ntc):
                sy_.wait_ge(SS["ev2"].h, b_ev2 + VPT * (t + 1))
                sy_.dma_start(out=dst_ap[:, t * TCH:(t + 1) * TCH].rearrange("(k p) n -> p k n", p=P),
                              in_=xnb[:]).then_inc(SS["st0"].h, 16)
            sy_.wait_ge(SS["st0"].h, b_st + 16 * ntc)
        block.sync(sy)

        SS["ldx"].n = b_ldx + 16 * ntc
        SS["pe"].n = b_pe + PPT * ntc
        SS["ev"].n = b_ev + SPT * ntc
        SS["ev2"].n = b_ev2 + VPT * ntc
        SS["st0"].n = b_st + 16 * ntc
    barrier(block, SS)



def emit_matmul(nc, block, SS, G, K, M, ntok, xchunk, w_ap, w_mode, out_tile_ap,
                out_dtype="bf16", has_bias=False, bvec=None, act="copy",
                has_scale=False, svec_ap=None, cmul=1.0, sfold_ap=None,
                resid_tile_ap=None, sign_w=False, mult_tile_ap=None, psc_ap=None,
                xpm_ap=None, wslab_ap=None, slab_m=4, xnparts=1):
    nk, nm = K // P, M // P
    ntc = ntok // TCH
    if K >= 8192:
        nth = 4 if ntc >= 4 else ntc
    else:
        nth = 1
    ntcl = ntc // nth
    if K >= 8192:
        slab_m = 2
    nslab = nm // slab_m
    use_act = act != "copy"
    afunc = {"silu": AF.Silu, "relu": AF.Relu}.get(act)
    has_resid = resid_tile_ap is not None
    has_mult = mult_tile_ap is not None
    has_psc = psc_ap is not None
    has_xpm = xpm_ap is not None
    has_vec_op = (not use_act) or has_mult or has_resid or has_psc
    n_rm = int(has_resid) + int(has_mult)
    if has_psc or sfold_ap is not None:
        assert has_scale
    if use_act:
        assert not has_bias and not has_resid and not has_psc
    scale_pre = has_scale and (cmul != 1.0 or has_psc or sfold_ap is not None)
    n_vini = int(has_scale and cmul != 1.0) + int(has_psc) + int(sfold_ap is not None)
    nidx = ntc * nm

    b_ldx = SS["ldx"].n
    b_ldw = [SS["ldw0"].n, SS["ldw1"].n]
    b_ldv = SS["ldv"].n
    b_ldr = [SS["ldr0"].n, SS["ldr1"].n]
    b_pe = SS["pe"].n
    b_ev = SS["ev"].n
    b_ev2 = SS["ev2"].n
    b_sg = SS["sg"].n
    b_st = [SS["st0"].n, SS["st1"].n]
    b_xpms = SS["xpms"].n
    b_vini = SS["vini"].n
    odt = BF16 if out_dtype == "bf16" else F32

    NX = xnparts

    def order():
        gi = 0
        for th in range(nth):
            for m in range(nm):
                for tc in range(ntcl):
                    yield gi, th, m, tc, th * ntcl + tc
                    gi += 1

    with ExitStack() as cx:
        uid = _uid()
        ec = cx.enter_context
        xb = ec(nc.sbuf_tensor(f"mm_xb_{uid}", [P, nk, ntcl * TCH], BF16))
        wb = [ec(nc.sbuf_tensor(f"mm_wb{j}_{uid}", [P, nk, slab_m * P], BF16)) for j in range(2)]
        eb = [ec(nc.sbuf_tensor(f"mm_e{j}_{uid}", [P, TCH], odt)) for j in range(4)]
        rb = [ec(nc.sbuf_tensor(f"mm_r{j}_{uid}", [P, TCH], F32)) for j in range(2)]
        mb = [ec(nc.sbuf_tensor(f"mm_m{j}_{uid}", [P, TCH], BF16)) for j in range(2)]
        vecs = ec(nc.sbuf_tensor(f"mm_vecs_{uid}", [P, 3 * nm + nk + 2], F32))
        pb = [ec(nc.psum_tensor(f"mm_pb{j}_{uid}", [P, TCH], F32)) for j in range(4)]
        off_b, off_s, off_w, off_p, off_f = 0, nm, 2 * nm, 2 * nm + nk, 2 * nm + nk + 1
        nvec_dma = int(has_bias) + int(has_scale) + int(has_xpm) + int(has_psc) + int(sfold_ap is not None)

        def sy(sy_):
            if has_bias:
                sy_.dma_start(out=vecs[:, off_b:off_b + nm], in_=bvec).then_inc(SS["ldv"].h, 16)
            if has_scale:
                sy_.dma_start(out=vecs[:, off_s:off_s + nm], in_=svec_ap).then_inc(SS["ldv"].h, 16)
            if has_xpm:
                sy_.dma_start(out=vecs[:, off_w:off_w + nk], in_=xpm_ap).then_inc(SS["ldv"].h, 16)
            if has_psc:
                sy_.dma_start(out=vecs[:, off_p:off_p + 1], in_=psc_ap).then_inc(SS["ldv"].h, 16)
            if sfold_ap is not None:
                sy_.dma_start(out=vecs[:, off_f:off_f + nm], in_=sfold_ap).then_inc(SS["ldv"].h, 16)
            if n_rm:
                for gi, th, m, tc, t in order():
                    if gi >= 2:
                        sy_.wait_ge(SS[f"st{gi % 2}"].h, b_st[gi % 2] + 16 * ((gi - 2) // 2 + 1))
                    if has_resid:
                        sy_.dma_start(out=rb[gi % 2][:], in_=resid_tile_ap(t, m)).then_inc(SS[f"ldr{gi % 2}"].h, 16)
                    if has_mult:
                        sy_.dma_start(out=mb[gi % 2][:], in_=mult_tile_ap(t, m)).then_inc(SS[f"ldr{gi % 2}"].h, 16)
        block.sync(sy)

        def gp(g):
            for th in range(nth):
                if th >= 1:
                    g.wait_ge(SS["pe"].h, b_pe + th * ntcl * nm)
                parts = xchunk2(th)
                if not isinstance(parts, list):
                    parts = [(xb[:], parts)]
                for xo, xi in parts:
                    g.dma_start(out=xo, in_=xi).then_inc(SS["ldx"].h, 16)
                for si in range(nslab):
                    sidx = th * nslab + si
                    if sidx >= 2:
                        g.wait_ge(SS["pe"].h, b_pe + (sidx - 1) * slab_m * ntcl)
                    g.dma_start(out=wb[sidx % 2][:], in_=wslab_ap(si)).then_inc(SS[f"ldw{sidx % 2}"].h, 16)
        def xchunk2(th):
            try:
                return xchunk(th, xb)
            except TypeError:
                return xchunk(th)
        block.gpsimd(gp)

        def tns(tn):
            for gi, th, m, tc, t in order():
                sidx = th * nslab + m // slab_m
                if m == 0 and tc == 0:
                    if has_xpm:
                        tn.wait_ge(SS["xpms"].h, b_xpms + th + 1)
                    else:
                        tn.wait_ge(SS["ldx"].h, b_ldx + 16 * NX * (th + 1))
                if m % slab_m == 0 and tc == 0:
                    if sign_w:
                        tn.wait_ge(SS["sg"].h, b_sg + sidx + 1)
                    else:
                        tn.wait_ge(SS[f"ldw{sidx % 2}"].h, b_ldw[sidx % 2] + 16 * (sidx // 2 + 1))
                if gi >= 4:
                    if use_act:
                        tn.wait_ge(SS["ev"].h, b_ev + gi - 3)
                    else:
                        tn.wait_ge(SS["ev2"].h, b_ev2 + gi - 3)
                ins = None
                wloc = (m % slab_m) * P
                for k in range(nk):
                    ins = tn.matmul(pb[gi % 4][:], wb[sidx % 2][:, k, wloc:wloc + P],
                                    xb[:, k, tc * TCH:(tc + 1) * TCH],
                                    start=(k == 0), stop=(k == nk - 1))
                ins.then_inc(SS["pe"].h, 1)
        block.tensor(tns)

        def sc(s_):
            if use_act and has_scale:
                s_.wait_ge(SS["ldv"].h, b_ldv + 16 * nvec_dma)
                if scale_pre:
                    s_.wait_ge(SS["vini"].h, b_vini + n_vini)
            for gi, th, m, tc, t in order():
                sidx = th * nslab + m // slab_m
                if sign_w and m % slab_m == 0 and tc == 0:
                    s_.wait_ge(SS[f"ldw{sidx % 2}"].h, b_ldw[sidx % 2] + 16 * (sidx // 2 + 1))
                    s_.activation(wb[sidx % 2][:, :, :], wb[sidx % 2][:, :, :], AF.Sign).then_inc(SS["sg"].h, 1)
                if use_act:
                    s_.wait_ge(SS["pe"].h, b_pe + gi + 1)
                    if gi >= 4:
                        s_.wait_ge(SS[f"st{gi % 2}"].h, b_st[gi % 2] + 16 * ((gi - 4) // 2 + 1))
                    kw = {}
                    if has_scale:
                        kw["scale"] = vecs[:, off_s + m:off_s + m + 1]
                    elif cmul != 1.0:
                        kw["scale"] = float(cmul)
                    s_.activation(eb[gi % 4][:], pb[gi % 4][:], afunc, **kw).then_inc(SS["ev"].h, 1)
                if has_vec_op:
                    s_.wait_ge(SS["ev2"].h, b_ev2 + gi + 1)
                else:
                    s_.wait_ge(SS["ev"].h, b_ev + gi + 1)
                s_.dma_start(out=out_tile_ap(t, m), in_=eb[gi % 4][:]).then_inc(SS[f"st{gi % 2}"].h, 16)
            s_.wait_ge(SS["st0"].h, b_st[0] + 16 * ((nidx + 1) // 2))
            s_.wait_ge(SS["st1"].h, b_st[1] + 16 * (nidx // 2))
        block.scalar(sc)

        def vc(v):
            nvi = 0
            if scale_pre:
                v.wait_ge(SS["ldv"].h, b_ldv + 16 * nvec_dma)
                if cmul != 1.0:
                    v.tensor_scalar(vecs[:, off_s:off_s + nm], vecs[:, off_s:off_s + nm],
                                    float(cmul), None, ALU.mult).then_inc(SS["vini"].h, 1)
                    nvi += 1
                    v.wait_ge(SS["vini"].h, b_vini + nvi)
                if has_psc:
                    v.tensor_scalar(vecs[:, off_s:off_s + nm], vecs[:, off_s:off_s + nm],
                                    vecs[:, off_p:off_p + 1], None, ALU.mult).then_inc(SS["vini"].h, 1)
                    nvi += 1
                    v.wait_ge(SS["vini"].h, b_vini + nvi)
                if sfold_ap is not None:
                    v.tensor_mul(vecs[:, off_s:off_s + nm], vecs[:, off_s:off_s + nm],
                                 vecs[:, off_f:off_f + nm]).then_inc(SS["vini"].h, 1)
                    nvi += 1
                    v.wait_ge(SS["vini"].h, b_vini + nvi)
            for gi, th, m, tc, t in order():
                if has_xpm and m == 0 and tc == 0:
                    v.wait_ge(SS["ldv"].h, b_ldv + 16 * nvec_dma)
                    v.wait_ge(SS["ldx"].h, b_ldx + 16 * NX * (th + 1))
                    ins = None
                    for k in range(nk):
                        ins = v.tensor_scalar(xb[:, k, :], xb[:, k, :],
                                              vecs[:, off_w + k:off_w + k + 1], None, ALU.mult)
                    ins.then_inc(SS["xpms"].h, 1)
                if not has_vec_op:
                    continue
                if use_act:
                    v.wait_ge(SS["ev"].h, b_ev + gi + 1)
                    v.wait_ge(SS[f"ldr{gi % 2}"].h, b_ldr[gi % 2] + 16 * n_rm * (gi // 2 + 1))
                    v.tensor_mul(eb[gi % 4][:], eb[gi % 4][:], mb[gi % 2][:]).then_inc(SS["ev2"].h, 1)
                    continue
                v.wait_ge(SS["pe"].h, b_pe + gi + 1)
                if has_bias or has_scale:
                    v.wait_ge(SS["ldv"].h, b_ldv + 16 * nvec_dma)
                if scale_pre:
                    v.wait_ge(SS["vini"].h, b_vini + n_vini)
                if n_rm:
                    v.wait_ge(SS[f"ldr{gi % 2}"].h, b_ldr[gi % 2] + 16 * n_rm * (gi // 2 + 1))
                if gi >= 4:
                    v.wait_ge(SS[f"st{gi % 2}"].h, b_st[gi % 2] + 16 * ((gi - 4) // 2 + 1))
                pbank = pb[gi % 4]
                cur = eb[gi % 4]
                if has_resid:
                    if has_scale:
                        last = v.scalar_tensor_tensor(cur[:], pbank[:], vecs[:, off_s + m:off_s + m + 1],
                                                      rb[gi % 2][:], ALU.mult, ALU.add)
                    elif has_bias:
                        last = v.scalar_tensor_tensor(cur[:], pbank[:], vecs[:, off_b + m:off_b + m + 1],
                                                      rb[gi % 2][:], ALU.add, ALU.add)
                    else:
                        last = v.scalar_tensor_tensor(cur[:], pbank[:], float(cmul),
                                                      rb[gi % 2][:], ALU.mult, ALU.add)
                else:
                    if has_scale and has_bias:
                        last = v.tensor_scalar(cur[:], pbank[:], vecs[:, off_s + m:off_s + m + 1],
                                               vecs[:, off_b + m:off_b + m + 1], ALU.mult, ALU.add)
                    elif has_scale:
                        last = v.tensor_scalar(cur[:], pbank[:], vecs[:, off_s + m:off_s + m + 1],
                                               None, ALU.mult)
                    elif has_bias:
                        last = v.tensor_scalar(cur[:], pbank[:], vecs[:, off_b + m:off_b + m + 1],
                                               None, ALU.add)
                    elif cmul != 1.0:
                        last = v.tensor_scalar(cur[:], pbank[:], float(cmul), None, ALU.mult)
                    else:
                        last = v.tensor_copy(cur[:], pbank[:])
                last.then_inc(SS["ev2"].h, 1)
        block.vector(vc)

        SS["ldx"].n = b_ldx + 16 * NX * nth
        nslabs_tot = nth * nslab
        SS["ldw0"].n = b_ldw[0] + 16 * ((nslabs_tot + 1) // 2)
        SS["ldw1"].n = b_ldw[1] + 16 * (nslabs_tot // 2)
        SS["ldv"].n = b_ldv + 16 * nvec_dma
        SS["ldr0"].n = b_ldr[0] + 16 * n_rm * ((nidx + 1) // 2)
        SS["ldr1"].n = b_ldr[1] + 16 * n_rm * (nidx // 2)
        SS["pe"].n = b_pe + nidx
        SS["ev"].n = b_ev + (nidx if use_act else 0)
        SS["ev2"].n = b_ev2 + (nidx if has_vec_op else 0)
        SS["sg"].n = b_sg + (nslabs_tot if sign_w else 0)
        SS["st0"].n = b_st[0] + 16 * ((nidx + 1) // 2)
        SS["st1"].n = b_st[1] + 16 * (nidx // 2)
        SS["xpms"].n = b_xpms + (nth if has_xpm else 0)
        SS["vini"].n = b_vini + n_vini
    barrier(block, SS)



def emit_matmul_tm(nc, block, SS, G, K, M, ntok, xtile, w_ap_full, out_tile_ap,
                   bias_row_ap=None):
    nk, nmc = K // P, M // TCH
    ntt = ntok // P
    has_bias = bias_row_ap is not None
    b_ldx, b_ldw, b_ldv = SS["ldx"].n, SS["ldw0"].n, SS["ldw1"].n
    b_pe, b_ev2 = SS["pe"].n, SS["ev2"].n
    b_st = [SS["st0"].n, SS["st1"].n]
    with ExitStack() as cx:
        uid = _uid()
        ec = cx.enter_context
        xt = [ec(nc.sbuf_tensor(f"vt_x{j}_{uid}", [P, nk, P], BF16)) for j in range(2)]
        wsb = ec(nc.sbuf_tensor(f"vt_w_{uid}", [P, nk, M], BF16))
        brow = ec(nc.sbuf_tensor(f"vt_br_{uid}", [1, M], F32))
        bsb = ec(nc.sbuf_tensor(f"vt_bs_{uid}", [P, M], F32))
        eb = [ec(nc.sbuf_tensor(f"vt_e{j}_{uid}", [P, TCH], BF16)) for j in range(2)]
        brbf = ec(nc.sbuf_tensor(f"vt_brbf_{uid}", [1, M], BF16))
        pb = [ec(nc.psum_tensor(f"vt_pb{j}_{uid}", [P, TCH], F32)) for j in range(2)]
        ps_bc = ec(nc.psum_tensor(f"vt_psbc_{uid}", [P, TCH], F32))

        def gp(g):
            g.dma_start(out=wsb[:], in_=w_ap_full.rearrange("(k p) m -> p k m", p=P)).then_inc(SS["ldw0"].h, 16)
            if has_bias:
                g.dma_start(out=brow[:], in_=bias_row_ap).then_inc(SS["ldw1"].h, 16)
            for tt in range(ntt):
                if tt >= 2:
                    g.wait_ge(SS["pe"].h, b_pe + nbc + (tt - 1) * nmc)
                g.dma_start(out=xt[tt % 2][:], in_=xtile(tt)).then_inc(SS[f"gq{tt % 2}"].h, 16)
        nbc = (M // TCH) if has_bias else 0

        b_ldr = [SS["gq0"].n, SS["gq1"].n]
        block.gpsimd(gp)

        def tns(tn):
            tn.wait_ge(SS["ldw0"].h, b_ldw + 16)
            if has_bias:
                tn.wait_ge(SS["ldw1"].h, b_ldv + 16)
                for mc in range(M // TCH):
                    tn.wait_ge(SS["ev2"].h, b_ev2 + 1)
                    if mc > 0:
                        tn.wait_ge(SS["ev2"].h, b_ev2 + 1 + mc)
                    tn.matmul(ps_bc[:], G["onesb_bf"][:], brbf[:, mc * TCH:(mc + 1) * TCH],
                              start=True, stop=True).then_inc(SS["pe"].h, 1)
            for tt in range(ntt):
                for mc in range(nmc):
                    idx = tt * nmc + mc
                    if mc == 0:
                        tn.wait_ge(SS[f"gq{tt % 2}"].h, b_ldr[tt % 2] + 16 * (tt // 2 + 1))
                    if idx >= 2:
                        tn.wait_ge(SS["ev2"].h, b_ev2 + 1 + nbc + idx - 1)
                    ins = None
                    for k in range(nk):
                        ins = tn.matmul(pb[idx % 2][:], xt[tt % 2][:, k, :],
                                        wsb[:, k, mc * TCH:(mc + 1) * TCH],
                                        start=(k == 0), stop=(k == nk - 1))
                    ins.then_inc(SS["pe"].h, 1)
        block.tensor(tns)

        def vc(v):
            nb = 0
            if has_bias:
                v.wait_ge(SS["ldw1"].h, b_ldv + 16)
                v.tensor_copy(brbf[:], brow[:]).then_inc(SS["ev2"].h, 1)
                nb = 1
                for mc in range(M // TCH):
                    v.wait_ge(SS["pe"].h, b_pe + mc + 1)
                    v.tensor_copy(bsb[:, mc * TCH:(mc + 1) * TCH], ps_bc[:]).then_inc(SS["ev2"].h, 1)
                nb += M // TCH
            for tt in range(ntt):
                for mc in range(nmc):
                    idx = tt * nmc + mc
                    v.wait_ge(SS["pe"].h, b_pe + nbc + idx + 1)
                    if idx >= 2:
                        v.wait_ge(SS[f"st{idx % 2}"].h, b_st[idx % 2] + 16 * ((idx - 2) // 2 + 1))
                    if has_bias:
                        v.wait_ge(SS["ev2"].h, b_ev2 + 1 + nbc)
                        v.tensor_add(eb[idx % 2][:], pb[idx % 2][:], bsb[:, mc * TCH:(mc + 1) * TCH]).then_inc(SS["ev2"].h, 1)
                    else:
                        v.tensor_copy(eb[idx % 2][:], pb[idx % 2][:]).then_inc(SS["ev2"].h, 1)
        block.vector(vc)

        def sc(s_):
            for tt in range(ntt):
                for mc in range(nmc):
                    idx = tt * nmc + mc
                    s_.wait_ge(SS["ev2"].h, b_ev2 + (1 + nbc if has_bias else 0) + idx + 1)
                    s_.dma_start(out=out_tile_ap(tt, mc), in_=eb[idx % 2][:]).then_inc(SS[f"st{idx % 2}"].h, 16)
            s_.wait_ge(SS["st0"].h, b_st[0] + 16 * ((ntt * nmc + 1) // 2))
            s_.wait_ge(SS["st1"].h, b_st[1] + 16 * ((ntt * nmc) // 2))
        block.scalar(sc)

        nidx = ntt * nmc
        SS["ldw0"].n = b_ldw + 16
        SS["ldw1"].n = b_ldv + (16 if has_bias else 0)
        SS["gq0"].n = b_ldr[0] + 16 * ((ntt + 1) // 2)
        SS["gq1"].n = b_ldr[1] + 16 * (ntt // 2)
        SS["pe"].n = b_pe + nbc + nidx
        SS["ev2"].n = b_ev2 + (1 + nbc if has_bias else 0) + nidx
        SS["st0"].n = b_st[0] + 16 * ((nidx + 1) // 2)
        SS["st1"].n = b_st[1] + 16 * (nidx // 2)
    barrier(block, SS)



def _nkb(s):
    return 4 * (s + 1)


KBS_HEAD = sum(_nkb(s) for s in range(NSUP))
A_PE_HEAD = 17 + sum(3 * _nkb(s) + 1 for s in range(NSUP))
A_SC_HEAD = 4 + KBS_HEAD
A_VC_HEAD = 28 + 9 * NSUP


def _a_pe_s(s):
    return 17 + sum(3 * _nkb(x) + 1 for x in range(s))


def _a_sc_s(s):
    return 4 + sum(_nkb(x) for x in range(s))


def _a_vc_s(s):
    return 28 + 9 * s


def _a_kbg(s, kb):
    return sum(_nkb(x) for x in range(s)) + kb


def _a_kbg_to_skb(kbg):
    s = 0
    while kbg >= _nkb(s):
        kbg -= _nkb(s)
        s += 1
    return s, kbg


def emit_attention(nc, block, SS, G, qk_t, vt_t, cc2_t):
    NKB = S // P
    b_pe, b_sc, b_vc = SS["pe"].n, SS["ev"].n, SS["ev2"].n
    b_kk, b_kv, b_q = SS["ldw0"].n, SS["ldw1"].n, SS["gq0"].n
    b_st = [SS["st0"].n, SS["st1"].n]

    def pe_st(h, s, kb):
        return b_pe + A_PE_HEAD * h + _a_pe_s(s) + 3 * kb + 1

    def pe_l(h, s, kb):
        return b_pe + A_PE_HEAD * h + _a_pe_s(s) + 3 * kb + 2

    def pe_pv(h, s, kb):
        return b_pe + A_PE_HEAD * h + _a_pe_s(s) + 3 * kb + 3

    def sc_exp(h, s, kb):
        return b_sc + A_SC_HEAD * h + _a_sc_s(s) + kb + 1

    with ExitStack() as cx:
        uid = _uid()
        ec = cx.enter_context
        kbuf = [ec(nc.sbuf_tensor(f"at_kbuf{j}_{uid}", [P, S], BF16)) for j in range(2)]
        vbuf = [ec(nc.sbuf_tensor(f"at_vbuf{j}_{uid}", [P, NKB, P], BF16)) for j in range(2)]
        qh = [ec(nc.sbuf_tensor(f"at_qh{j}_{uid}", [P, S], BF16)) for j in range(2)]
        ksqf = ec(nc.sbuf_tensor(f"at_ksqf_{uid}", [P, S], BF16))
        qsqf = ec(nc.sbuf_tensor(f"at_qsqf_{uid}", [P, S], BF16))
        ksn = ec(nc.sbuf_tensor(f"at_ksn_{uid}", [1, S], F32))
        qsn = ec(nc.sbuf_tensor(f"at_qsn_{uid}", [1, S], F32))
        km2 = ec(nc.sbuf_tensor(f"at_km2_{uid}", [1, 1], F32))
        kms = ec(nc.sbuf_tensor(f"at_kms_{uid}", [1, 1], F32))
        qm2 = ec(nc.sbuf_tensor(f"at_qm2_{uid}", [1, NSUP], F32))
        qmv = ec(nc.sbuf_tensor(f"at_qmv_{uid}", [1, NSUP], F32))
        negcb = ec(nc.sbuf_tensor(f"at_negcb_{uid}", [1, NSUP], F32))
        negc_bf = ec(nc.sbuf_tensor(f"at_negcbf_{uid}", [1, NSUP], BF16))
        bias_all = ec(nc.sbuf_tensor(f"at_bias_{uid}", [P, NSUP], F32))
        pts = [ec(nc.sbuf_tensor(f"at_pt{j}_{uid}", [P, TCH], BF16)) for j in range(4)]
        lsb = ec(nc.sbuf_tensor(f"at_lsb_{uid}", [1, TCH], F32))
        rlv = ec(nc.sbuf_tensor(f"at_rlv_{uid}", [1, TCH], F32))
        rl_bf = ec(nc.sbuf_tensor(f"at_rlbf_{uid}", [1, TCH], BF16))
        rb_s = ec(nc.sbuf_tensor(f"at_rbs_{uid}", [P, TCH], F32))
        cxs = [ec(nc.sbuf_tensor(f"at_cxs{j}_{uid}", [P, TCH], BF16)) for j in range(2)]
        pss = [ec(nc.psum_tensor(f"at_pss{j}_{uid}", [P, TCH], F32)) for j in range(2)]
        ps_o = [ec(nc.psum_tensor(f"at_pso{j}_{uid}", [P, TCH], F32)) for j in range(2)]
        ps_l = [ec(nc.psum_tensor(f"at_psl{j}_{uid}", [1, TCH], F32)) for j in range(2)]
        ps_sm = ec(nc.psum_tensor(f"at_pssm_{uid}", [P, TCH], F32))
        ps_b = ec(nc.psum_tensor(f"at_psb_{uid}", [P, NSUP], F32))

        def gp(g):
            for h in range(NH):
                if h >= 2:
                    g.wait_ge(SS["pe"].h, b_pe + A_PE_HEAD * (h - 1))
                g.dma_start(out=kbuf[h % 2][:], in_=qk_t.ap()[(NH + h) * P:(NH + h + 1) * P, :]).then_inc(SS["ldw0"].h, 16)
                g.dma_start(out=vbuf[h % 2][:], in_=vt_t.ap()[:, h * P:(h + 1) * P]
                            .rearrange("(b p) d -> p b d", p=P)).then_inc(SS["ldw1"].h, 16)
                g.dma_start(out=qh[h % 2][:], in_=qk_t.ap()[h * P:(h + 1) * P, :]).then_inc(SS["gq0"].h, 16)
        block.gpsimd(gp)

        def tns(tn):
            for h in range(NH):
                hb = b_pe + A_PE_HEAD * h
                vcb = b_vc + A_VC_HEAD * h
                tn.wait_ge(SS["ldw0"].h, b_kk + 16 * (h + 1))
                tn.wait_ge(SS["ldw1"].h, b_kv + 16 * (h + 1))
                tn.wait_ge(SS["gq0"].h, b_q + 16 * (h + 1))
                for c in range(8):
                    if c == 0:
                        tn.wait_ge(SS["ev"].h, b_sc + A_SC_HEAD * h + 1)
                        if h > 0:
                            tn.wait_ge(SS["ev2"].h, b_vc + A_VC_HEAD * h)
                    else:
                        tn.wait_ge(SS["ev2"].h, vcb + c)
                    tn.matmul(ps_sm[0:1, :], G["ones_bf"][:], ksqf[:, c * TCH:(c + 1) * TCH],
                              start=True, stop=True).then_inc(SS["pe"].h, 1)
                for c in range(8):
                    if c == 0:
                        tn.wait_ge(SS["ev"].h, b_sc + A_SC_HEAD * h + 2)
                        tn.wait_ge(SS["ev2"].h, vcb + 8)
                    else:
                        tn.wait_ge(SS["ev2"].h, vcb + 9 + c)
                    tn.matmul(ps_sm[0:1, :], G["ones_bf"][:], qsqf[:, c * TCH:(c + 1) * TCH],
                              start=True, stop=True).then_inc(SS["pe"].h, 1)
                tn.wait_ge(SS["ev2"].h, vcb + 27)
                tn.matmul(ps_b[:], G["onesb_bf"][:], negc_bf[:], start=True, stop=True).then_inc(SS["pe"].h, 1)
                for s in range(NSUP):
                    sb = hb + _a_pe_s(s)
                    for kb in range(_nkb(s)):
                        kbg = _a_kbg(s, kb)
                        if kbg >= 2:
                            s2, k2 = _a_kbg_to_skb(kbg - 2)
                            tn.wait_ge(SS["ev"].h, sc_exp(h, s2, k2))
                        elif h > 0:
                            s2, k2 = _a_kbg_to_skb(KBS_HEAD - 2 + kbg)
                            tn.wait_ge(SS["ev"].h, sc_exp(h - 1, s2, k2))
                        tn.matmul(pss[kbg % 2][:], kbuf[h % 2][:, kb * P:(kb + 1) * P],
                                  qh[h % 2][:, s * TCH:(s + 1) * TCH], start=True, stop=True).then_inc(SS["pe"].h, 1)
                        tn.wait_ge(SS["ev"].h, sc_exp(h, s, kb))
                        if kb >= 4 * s:
                            tn.wait_ge(SS["ev2"].h, vcb + _a_vc_s(s) + (kb - 4 * s) + 1)
                        if kb == 0:
                            if s >= 2:
                                tn.wait_ge(SS["ev2"].h, vcb + _a_vc_s(s - 2) + 9)
                            elif h > 0:
                                tn.wait_ge(SS["ev2"].h, b_vc + A_VC_HEAD * (h - 1) + _a_vc_s(NSUP - 2 + s) + 9)
                        tn.matmul(ps_l[s % 2][:], G["ones_bf"][:], pts[kbg % 4][:],
                                  start=(kb == 0), stop=(kb == _nkb(s) - 1)).then_inc(SS["pe"].h, 1)
                        tn.matmul(ps_o[s % 2][:], vbuf[h % 2][:, kb, :], pts[kbg % 4][:],
                                  start=(kb == 0), stop=(kb == _nkb(s) - 1)).then_inc(SS["pe"].h, 1)
                    tn.wait_ge(SS["ev2"].h, vcb + _a_vc_s(s) + 7)
                    if s > 0:
                        tn.wait_ge(SS["ev2"].h, vcb + _a_vc_s(s - 1) + 8)
                    tn.matmul(ps_sm[:], G["onesb_bf"][:], rl_bf[:], start=True, stop=True).then_inc(SS["pe"].h, 1)
        block.tensor(tns)

        def sc(s_):
            for h in range(NH):
                s_.wait_ge(SS["ldw0"].h, b_kk + 16 * (h + 1))
                if h >= 2:
                    s_.wait_ge(SS["pe"].h, b_pe + A_PE_HEAD * (h - 1))
                s_.activation(ksqf[:], kbuf[h % 2][:], AF.Square).then_inc(SS["ev"].h, 1)
                s_.wait_ge(SS["gq0"].h, b_q + 16 * (h + 1))
                s_.activation(qsqf[:], qh[h % 2][:], AF.Square).then_inc(SS["ev"].h, 1)
                s_.wait_ge(SS["ev2"].h, b_vc + A_VC_HEAD * h + 9)
                s_.activation(kms[:], km2[:], AF.Sqrt).then_inc(SS["ev"].h, 1)
                s_.wait_ge(SS["ev2"].h, b_vc + A_VC_HEAD * h + 25)
                s_.activation(qmv[:], qm2[:], AF.Sqrt).then_inc(SS["ev"].h, 1)
                for s in range(NSUP):
                    for kb in range(_nkb(s)):
                        kbg = _a_kbg(s, kb)
                        s_.wait_ge(SS["pe"].h, pe_st(h, s, kb))
                        s_.wait_ge(SS["ev2"].h, b_vc + A_VC_HEAD * h + 28)
                        if kbg >= 4:
                            s4, k4 = _a_kbg_to_skb(kbg - 4)
                            s_.wait_ge(SS["pe"].h, pe_pv(h, s4, k4))
                        elif h > 0:
                            s4, k4 = _a_kbg_to_skb(KBS_HEAD - 4 + kbg)
                            s_.wait_ge(SS["pe"].h, pe_pv(h - 1, s4, k4))
                        s_.activation(pts[kbg % 4][:], pss[kbg % 2][:], AF.Exp,
                                      bias=bias_all[:, s:s + 1], scale=SCL).then_inc(SS["ev"].h, 1)
        block.scalar(sc)

        def vc(v):
            for h in range(NH):
                vcb = b_vc + A_VC_HEAD * h
                hb = b_pe + A_PE_HEAD * h
                for c in range(8):
                    v.wait_ge(SS["pe"].h, hb + c + 1)
                    v.tensor_copy(ksn[:, c * TCH:(c + 1) * TCH], ps_sm[0:1, :]).then_inc(SS["ev2"].h, 1)
                v.wait_ge(SS["ev2"].h, vcb + 8)
                v.reduce_max(km2[:], ksn[:], axis=AX.X).then_inc(SS["ev2"].h, 1)
                for c in range(8):
                    v.wait_ge(SS["pe"].h, hb + 8 + c + 1)
                    v.tensor_copy(qsn[:, c * TCH:(c + 1) * TCH], ps_sm[0:1, :]).then_inc(SS["ev2"].h, 1)
                for c in range(8):
                    v.wait_ge(SS["ev2"].h, vcb + 9 + c + 1)
                    v.reduce_max(qm2[:, c:c + 1], qsn[:, c * TCH:(c + 1) * TCH], axis=AX.X).then_inc(SS["ev2"].h, 1)
                v.wait_ge(SS["ev"].h, b_sc + A_SC_HEAD * h + 4)
                v.tensor_scalar(negcb[:], qmv[:], kms[0:1, :], -SCL, ALU.mult, ALU.mult).then_inc(SS["ev2"].h, 1)
                v.wait_ge(SS["ev2"].h, vcb + 26)
                v.tensor_copy(negc_bf[:], negcb[:]).then_inc(SS["ev2"].h, 1)
                v.wait_ge(SS["pe"].h, hb + 17)
                if h > 0:
                    v.wait_ge(SS["ev"].h, b_sc + A_SC_HEAD * h)
                v.tensor_copy(bias_all[:], ps_b[:]).then_inc(SS["ev2"].h, 1)
                for s in range(NSUP):
                    vb = vcb + _a_vc_s(s)
                    for j in range(4):
                        kb = 4 * s + j
                        v.wait_ge(SS["ev"].h, sc_exp(h, s, kb))
                        kbg = _a_kbg(s, kb)
                        v.tensor_mul(pts[kbg % 4][:], pts[kbg % 4][:], G["mbuf"][:, j, :]).then_inc(SS["ev2"].h, 1)
                    v.wait_ge(SS["pe"].h, pe_l(h, s, _nkb(s) - 1))
                    v.tensor_copy(lsb[:], ps_l[s % 2][0:1, :]).then_inc(SS["ev2"].h, 1)
                    v.wait_ge(SS["ev2"].h, vb + 5)
                    v.reciprocal(rlv[:], lsb[:]).then_inc(SS["ev2"].h, 1)
                    v.wait_ge(SS["ev2"].h, vb + 6)
                    v.tensor_copy(rl_bf[:], rlv[:]).then_inc(SS["ev2"].h, 1)
                    v.wait_ge(SS["pe"].h, hb + _a_pe_s(s) + 3 * _nkb(s) + 1)
                    v.tensor_copy(rb_s[:], ps_sm[:]).then_inc(SS["ev2"].h, 1)
                    gq = h * NSUP + s
                    v.wait_ge(SS["pe"].h, pe_pv(h, s, _nkb(s) - 1))
                    v.wait_ge(SS["ev2"].h, vb + 8)
                    if gq >= 2:
                        v.wait_ge(SS[f"st{gq % 2}"].h, b_st[gq % 2] + 16 * ((gq - 2) // 2 + 1))
                    v.tensor_mul(cxs[gq % 2][:], ps_o[s % 2][:], rb_s[:]).then_inc(SS["ev2"].h, 1)
        block.vector(vc)

        def sy(sy_):
            for h in range(NH):
                for s in range(NSUP):
                    gq = h * NSUP + s
                    sy_.wait_ge(SS["ev2"].h, b_vc + A_VC_HEAD * h + _a_vc_s(s) + 9)
                    sy_.dma_start(out=cc2_t.ap()[h * P:(h + 1) * P, s * TCH:(s + 1) * TCH],
                                  in_=cxs[gq % 2][:]).then_inc(SS[f"st{gq % 2}"].h, 16)
            sy_.wait_ge(SS["st0"].h, b_st[0] + 16 * 32)
            sy_.wait_ge(SS["st1"].h, b_st[1] + 16 * 32)
        block.sync(sy)

        SS["pe"].n = b_pe + A_PE_HEAD * NH
        SS["ev"].n = b_sc + A_SC_HEAD * NH
        SS["ev2"].n = b_vc + A_VC_HEAD * NH
        SS["ldw0"].n = b_kk + 16 * NH
        SS["ldw1"].n = b_kv + 16 * NH
        SS["gq0"].n = b_q + 16 * NH
        SS["st0"].n = b_st[0] + 16 * 32
        SS["st1"].n = b_st[1] + 16 * 32
    barrier(block, SS)



def emit_fwht_T(nc, block, SS, G, wdn_t, tdc_t):
    n = 8192
    nst = 13
    b_ldx = SS["ldx"].n
    b_pe = SS["pe"].n
    b_vc = SS["ev2"].n
    b_st = SS["st0"].n
    VPT = 2 * nst + 1 + 8
    PPT = 64
    with ExitStack() as cx:
        uid = _uid()
        ec = cx.enter_context
        ba = ec(nc.sbuf_tensor(f"ft_ba_{uid}", [P, n], F32))
        bb = ec(nc.sbuf_tensor(f"ft_bb_{uid}", [P, n], F32))
        tcb = ec(nc.sbuf_tensor(f"ft_tcb_{uid}", [P, n], BF16))
        tout = ec(nc.sbuf_tensor(f"ft_tout_{uid}", [P, 8, P], BF16))
        ptr = [ec(nc.psum_tensor(f"ft_ptr{j}_{uid}", [P, 8 * P], BF16)) for j in range(2)]

        def gp(g):
            for i in range(2):
                if i >= 1:
                    g.wait_ge(SS["ev2"].h, b_vc + VPT * i - 8)
                g.dma_start(out=ba[:], in_=wdn_t.ap()[i * P:(i + 1) * P, :]).then_inc(SS["ldx"].h, 16)
        block.gpsimd(gp)

        def vc(v):
            for i in range(2):
                vb = b_vc + VPT * i
                cnt = 0
                v.wait_ge(SS["ldx"].h, b_ldx + 16 * (i + 1))
                if i >= 1:
                    v.wait_ge(SS["st0"].h, b_st + 16 * 8 * i)
                src_, dst = ba, bb
                h = 1
                for stg in range(nst):
                    sv = src_[:].rearrange("p (g two h) -> p g two h", two=2, h=h)
                    dv = dst[:].rearrange("p (g two h) -> p g two h", two=2, h=h)
                    v.tensor_add(dv[:, :, 0, :], sv[:, :, 0, :], sv[:, :, 1, :]).then_inc(SS["ev2"].h, 1)
                    v.tensor_sub(dv[:, :, 1, :], sv[:, :, 0, :], sv[:, :, 1, :]).then_inc(SS["ev2"].h, 1)
                    cnt += 2
                    v.wait_ge(SS["ev2"].h, vb + cnt)
                    src_, dst = dst, src_
                    h *= 2
                v.tensor_copy(tcb[:], src_[:]).then_inc(SS["ev2"].h, 1)
                cnt += 1
                v.wait_ge(SS["ev2"].h, vb + cnt)
                for grp in range(8):
                    v.wait_ge(SS["pe"].h, b_pe + PPT * i + grp * 8 + 8)
                    if i > 0 or grp > 0:
                        v.wait_ge(SS["st0"].h, b_st + 16 * (8 * i + grp))
                    v.tensor_copy(tout[:], ptr[grp % 2][:].rearrange("p (j m) -> p j m", m=P)).then_inc(SS["ev2"].h, 1)
        block.vector(vc)

        def tns(tn):
            for i in range(2):
                tn.wait_ge(SS["ev2"].h, b_vc + VPT * i + 2 * nst + 1)
                for grp in range(8):
                    if i > 0 or grp > 1:
                        gidx = 8 * i + grp
                        pv, gv = divmod(gidx - 2, 8)
                        tn.wait_ge(SS["ev2"].h, b_vc + VPT * pv + 2 * nst + 1 + gv + 1)
                    for j in range(8):
                        tn.transpose(ptr[grp % 2][:, j * P:(j + 1) * P],
                                     tcb[:, (grp * 8 + j) * P:(grp * 8 + j + 1) * P],
                                     G["eye_bf"][:]).then_inc(SS["pe"].h, 1)
        block.tensor(tns)

        def sy(sy_):
            for i in range(2):
                for grp in range(8):
                    sy_.wait_ge(SS["ev2"].h, b_vc + VPT * i + 2 * nst + 1 + grp + 1)
                    sy_.dma_start(out=tdc_t.ap()[grp * 1024:(grp + 1) * 1024, i * P:(i + 1) * P]
                                  .rearrange("(j p) m -> p j m", p=P),
                                  in_=tout[:]).then_inc(SS["st0"].h, 16)
            sy_.wait_ge(SS["st0"].h, b_st + 16 * 16)
        block.sync(sy)

        SS["ldx"].n = b_ldx + 32
        SS["pe"].n = b_pe + 2 * PPT
        SS["ev2"].n = b_vc + 2 * VPT
        SS["st0"].n = b_st + 16 * 16
    barrier(block, SS)


def emit_collective(nc, block, SS, kind, groups, ins, outs, op=None, sem="cc0"):
    from concourse.aot_env import allow_ici_all_to_all

    if op is None:
        op = ALU.bypass

    def gp(g):
        with allow_ici_all_to_all():
            g.collective_compute(kind, op, replica_groups=groups,
                                 ins=ins, outs=outs).then_inc(SS[sem].h, 1)
        g.wait_ge(SS[sem].h, 1)
    block.gpsimd(gp)
    SS[sem].n = 1
    barrier(block, SS)


def emit_dbgcopy(nc, block, SS, dst_t, src_t):
    b_cc = SS["cc"].n

    def gp(g):
        g.dma_start(out=dst_t.ap(), in_=src_t.ap()).then_inc(SS["cc"].h, 16)
        g.wait_ge(SS["cc"].h, b_cc + 16)
    block.gpsimd(gp)
    SS["cc"].n = b_cc + 16
    barrier(block, SS)


def emit_resadd(nc, block, SS, G, res_t, delta_t, bvec_ap=None, src_t=None):
    if src_t is None:
        src_t = res_t
    nm, ntc = D // P, TLOC // TCH
    nidx = nm * ntc
    b_ldr = [SS["ldr0"].n, SS["ldr1"].n]
    b_ldv = SS["ldv"].n
    b_ev2 = SS["ev2"].n
    b_st = [SS["st0"].n, SS["st1"].n]
    has_bias = bvec_ap is not None
    with ExitStack() as cx:
        uid = _uid()
        ec = cx.enter_context
        rb = [ec(nc.sbuf_tensor(f"ra_r{j}_{uid}", [P, TCH], F32)) for j in range(2)]
        db = [ec(nc.sbuf_tensor(f"ra_d{j}_{uid}", [P, TCH], F32)) for j in range(2)]
        vb = ec(nc.sbuf_tensor(f"ra_vb_{uid}", [P, nm], F32))

        def sy(sy_):
            if has_bias:
                sy_.dma_start(out=vb[:], in_=bvec_ap).then_inc(SS["ldv"].h, 16)
            for idx in range(nidx):
                t, m = divmod(idx, nm)
                if idx >= 2:
                    sy_.wait_ge(SS[f"st{idx % 2}"].h, b_st[idx % 2] + 16 * ((idx - 2) // 2 + 1))
                sy_.dma_start(out=rb[idx % 2][:],
                              in_=src_t.ap()[m * P:(m + 1) * P, t * TCH:(t + 1) * TCH]).then_inc(SS[f"ldr{idx % 2}"].h, 16)
                sy_.dma_start(out=db[idx % 2][:],
                              in_=delta_t.ap()[m * P:(m + 1) * P, t * TCH:(t + 1) * TCH]).then_inc(SS[f"ldr{idx % 2}"].h, 16)
        block.sync(sy)

        def vc(v):
            if has_bias:
                v.wait_ge(SS["ldv"].h, b_ldv + 16)
            for idx in range(nidx):
                t, m = divmod(idx, nm)
                v.wait_ge(SS[f"ldr{idx % 2}"].h, b_ldr[idx % 2] + 32 * (idx // 2 + 1))
                if idx >= 2:
                    v.wait_ge(SS[f"st{idx % 2}"].h, b_st[idx % 2] + 16 * ((idx - 2) // 2 + 1))
                if has_bias:
                    v.scalar_tensor_tensor(rb[idx % 2][:], rb[idx % 2][:], vb[:, m:m + 1],
                                           db[idx % 2][:], ALU.add, ALU.add).then_inc(SS["ev2"].h, 1)
                else:
                    v.tensor_add(rb[idx % 2][:], rb[idx % 2][:], db[idx % 2][:]).then_inc(SS["ev2"].h, 1)
        block.vector(vc)

        def sc(s_):
            for idx in range(nidx):
                t, m = divmod(idx, nm)
                s_.wait_ge(SS["ev2"].h, b_ev2 + idx + 1)
                s_.dma_start(out=res_t.ap()[m * P:(m + 1) * P, t * TCH:(t + 1) * TCH],
                             in_=rb[idx % 2][:]).then_inc(SS[f"st{idx % 2}"].h, 16)
            s_.wait_ge(SS["st0"].h, b_st[0] + 16 * ((nidx + 1) // 2))
            s_.wait_ge(SS["st1"].h, b_st[1] + 16 * (nidx // 2))
        block.scalar(sc)

        SS["ldv"].n = b_ldv + (16 if has_bias else 0)
        SS["ldr0"].n = b_ldr[0] + 32 * ((nidx + 1) // 2)
        SS["ldr1"].n = b_ldr[1] + 32 * (nidx // 2)
        SS["ev2"].n = b_ev2 + nidx
        SS["st0"].n = b_st[0] + 16 * ((nidx + 1) // 2)
        SS["st1"].n = b_st[1] + 16 * (nidx // 2)
    barrier(block, SS)



def build_megakernel():
    nc = bass.Bass("TRN2", target_bir_lowering=False)
    x_fm = nc.dram_tensor("x_fm", [D, TLOC], F32, kind="ExternalInput")
    w_qkv0 = nc.dram_tensor("w_qkv0", [D, 3072], F32, kind="ExternalInput")
    b_qk0 = nc.dram_tensor("b_qk0", [P, 16], F32, kind="ExternalInput")
    b_v0 = nc.dram_tensor("b_v0", [1, 1024], F32, kind="ExternalInput")
    w_out0 = nc.dram_tensor("w_out0", [NH * P, D], F32, kind="ExternalInput")
    b_out0 = nc.dram_tensor("b_out0", [P, 16], F32, kind="ExternalInput")
    h2048 = nc.dram_tensor("h2048", [D, D], F32, kind="ExternalInput")
    su_ug = nc.dram_tensor("su_ug", [P, 16], F32, kind="ExternalInput")
    w_upT = nc.dram_tensor("w_upT", [D, 8192], F32, kind="ExternalInput")
    ws_u = nc.dram_tensor("ws_u", [P, 64], F32, kind="ExternalInput")
    su_dp = nc.dram_tensor("su_dp", [P, 64], F32, kind="ExternalInput")
    w_gateT = nc.dram_tensor("w_gateT", [D, 8192], F32, kind="ExternalInput")
    ws_g = nc.dram_tensor("ws_g", [P, 64], F32, kind="ExternalInput")
    wdn = nc.dram_tensor("wdn", [256, 8192], F32, kind="ExternalInput")
    ws_dp = nc.dram_tensor("ws_dp", [P, 16], F32, kind="ExternalInput")
    w_qkv1 = nc.dram_tensor("w_qkv1", [D, 3072], F32, kind="ExternalInput")
    w_o1 = nc.dram_tensor("w_o1", [NH * P, D], F32, kind="ExternalInput")
    b_o1 = nc.dram_tensor("b_o1", [P, 16], F32, kind="ExternalInput")
    w_up1T = nc.dram_tensor("w_up1T", [D, 8192], F32, kind="ExternalInput")
    us1 = nc.dram_tensor("us1", [P, 64], F32, kind="ExternalInput")
    w_dn1T = nc.dram_tensor("w_dn1T", [8192, D], F32, kind="ExternalInput")
    ds1 = nc.dram_tensor("ds1", [P, 16], F32, kind="ExternalInput")
    osc = nc.dram_tensor("osc", [P, 1], F32, kind="ExternalInput")
    masks_in = nc.dram_tensor("masks_in", [4 * P, TCH], F32, kind="ExternalInput")
    onesk_in = nc.dram_tensor("onesk_in", [P, 1], F32, kind="ExternalInput")
    onesb_in = nc.dram_tensor("onesb_in", [1, P], F32, kind="ExternalInput")
    eye_in = nc.dram_tensor("eye_in", [P, P], F32, kind="ExternalInput")
    eps_in = nc.dram_tensor("eps_in", [P, 1], F32, kind="ExternalInput")
    y = nc.dram_tensor("y", [D, TLOC], F32, kind="ExternalOutput")
    res = nc.dram_tensor("res", [D, TLOC], F32)
    xnc = nc.dram_tensor("xnc", [D, TLOC], BF16)
    xng = nc.dram_tensor("xng", [2 * D, TLOC], BF16)
    qk = nc.dram_tensor("qk", [2 * NH * P, S], BF16)
    vt = nc.dram_tensor("vt", [S, NH * P], BF16)
    ctxl = nc.dram_tensor("ctxl", [NH * P, S], BF16)
    prt = nc.dram_tensor("prt", [2 * D, TLOC], F32)
    rso = nc.dram_tensor("rso", [D, TLOC], F32)
    x2t = nc.dram_tensor("x2t", [D, TLOC], BF16)
    xupt = nc.dram_tensor("xupt", [8192, TLOC], BF16)
    x3t = nc.dram_tensor("x3t", [8192, TLOC], BF16)
    hidt = nc.dram_tensor("hidt", [8192, TLOC], BF16)
    tdc = nc.dram_tensor("tdc", [8192, 256], BF16)
    tdg = nc.dram_tensor("tdg", [8 * 8192, 256], BF16)

    PAIRS = [[0, 1], [2, 3], [4, 5], [6, 7]]
    ALL8 = [list(range(8))]

    with ExitStack() as cx:
        uid = _uid()
        ec = cx.enter_context
        ones_bf = ec(nc.sbuf_tensor(f"g_ones_{uid}", [P, 1], BF16))
        onesb_bf = ec(nc.sbuf_tensor(f"g_onesb_{uid}", [1, P], BF16))
        mbuf = ec(nc.sbuf_tensor(f"g_mbuf_{uid}", [P, 4, TCH], BF16))
        eye_bf = ec(nc.sbuf_tensor(f"g_eye_{uid}", [P, P], BF16))
        epst = ec(nc.sbuf_tensor(f"g_eps_{uid}", [P, 1], F32))
        sems = {}
        for nm in ("bar", "ldx", "ldw0", "ldw1", "ldv", "ldr0", "ldr1", "gq0", "gq1", "gldc", "xpms",
                   "vini", "pe", "ev", "ev2", "sg", "st0", "st1", "cc",
                   "cc0", "cc1", "cc2s", "cc3", "cc4"):
            sems[nm] = SemT(ec(nc.semaphore(nm)))
        block = ec(nc.Block())
        SS = sems
        G = {"ones_bf": ones_bf, "onesb_bf": onesb_bf, "mbuf": mbuf,
             "eye_bf": eye_bf, "epst": epst, "ldv_const": 80}

        def pro(g):
            g.dma_start(out=ones_bf[:], in_=onesk_in.ap()).then_inc(SS["gldc"].h, 16)
            g.dma_start(out=onesb_bf[:], in_=onesb_in.ap()).then_inc(SS["gldc"].h, 16)
            g.dma_start(out=mbuf[:], in_=masks_in.ap().rearrange("(i p) n -> p i n", p=P)).then_inc(SS["gldc"].h, 16)
            g.dma_start(out=eye_bf[:], in_=eye_in.ap()).then_inc(SS["gldc"].h, 16)
            g.dma_start(out=epst[:], in_=eps_in.ap()).then_inc(SS["gldc"].h, 16)
            g.wait_ge(SS["gldc"].h, 80)
        block.gpsimd(pro)
        SS["gldc"].n = 80
        barrier(block, SS)

        def fm_chunk(tensor):
            return lambda th: tensor.ap().rearrange("(k p) n -> p k n", p=P)

        def fm_chunk_q(tensor):
            return lambda th: tensor.ap()[:, th * TCH:(th + 1) * TCH].rearrange("(k p) n -> p k n", p=P)

        def xng_chunk(th, xb):
            return [(xb[:, :, r * TLOC:(r + 1) * TLOC],
                     xng.ap()[r * D:(r + 1) * D, :].rearrange("(k p) n -> p k n", p=P))
                    for r in range(2)]

        def xng_tile(tt):
            r, tl = divmod(tt, 16)
            return xng.ap()[r * D:(r + 1) * D, tl * P:(tl + 1) * P].rearrange("(k p) n -> p k n", p=P)

        def w_slab(w, c0=0):
            return lambda si: w.ap()[:, c0 + si * 512:c0 + (si + 1) * 512].rearrange("(k p) n -> p k n", p=P)

        def w_slab_dn1(si):
            return w_dn1T.ap()[:, si * 256:(si + 1) * 256].rearrange("(k p) n -> p k n", p=P)

        def ctxl_chunk(th):
            return ctxl.ap().rearrange("(k p) n -> p k n", p=P)

        def prt_tile(t, m):
            return prt.ap()[(t // 4) * D + m * P:(t // 4) * D + (m + 1) * P,
                            (t % 4) * TCH:(t % 4 + 1) * TCH]

        def w_tdg_slab(si):
            return tdg.ap()[si * 8192:(si + 1) * 8192, :].rearrange("(k p) n -> p k n", p=P)

        def tile_of(tensor):
            return lambda t, m: tensor.ap()[m * P:(m + 1) * P, t * TCH:(t + 1) * TCH]

        def qk_tile(t, m):
            return qk.ap()[m * P:(m + 1) * P, t * TCH:(t + 1) * TCH]

        def vt_tile(tt, mc):
            return vt.ap()[tt * P:(tt + 1) * P, mc * TCH:(mc + 1) * TCH]

        emit_ln(nc, block, SS, G, x_fm.ap(), xnc.ap())
        emit_collective(nc, block, SS, "AllGather", PAIRS, [xnc.ap()], [xng.ap()], sem="cc0")
        emit_matmul(nc, block, SS, G, D, 2048, S, xng_chunk, None, "cast",
                    qk_tile, out_dtype="bf16", has_bias=True, bvec=b_qk0.ap(),
                    wslab_ap=w_slab(w_qkv0), xnparts=2)
        emit_matmul_tm(nc, block, SS, G, D, 1024, S, xng_tile, w_qkv0.ap()[:, 2048:3072],
                       vt_tile, bias_row_ap=b_v0.ap())
        emit_attention(nc, block, SS, G, qk, vt, ctxl)
        emit_matmul(nc, block, SS, G, NH * P, D, S, ctxl_chunk, None, "cast",
                    prt_tile, out_dtype="f32", wslab_ap=w_slab(w_out0))
        emit_collective(nc, block, SS, "ReduceScatter", PAIRS, [prt.ap()], [rso.ap()], op=ALU.add, sem="cc1")
        emit_resadd(nc, block, SS, G, res, rso, bvec_ap=b_out0.ap(), src_t=x_fm)
        if DBG:
            d_res1 = nc.dram_tensor("d_res1", [D, TLOC], F32, kind="ExternalOutput")
            emit_dbgcopy(nc, block, SS, d_res1, res)
        emit_ln(nc, block, SS, G, res.ap(), xnc.ap())
        emit_matmul(nc, block, SS, G, D, D, TLOC, fm_chunk(xnc), None, "cast",
                    tile_of(x2t), out_dtype="bf16", cmul=1.0 / (64.0 * math.sqrt(D)),
                    xpm_ap=su_ug.ap(), wslab_ap=w_slab(h2048))
        emit_matmul(nc, block, SS, G, D, 8192, TLOC, fm_chunk(x2t), None, "cast",
                    tile_of(xupt), out_dtype="bf16", has_scale=True, svec_ap=ws_u.ap(),
                    cmul=64.0, sfold_ap=su_dp.ap(), wslab_ap=w_slab(w_upT))
        emit_matmul(nc, block, SS, G, D, 8192, TLOC, fm_chunk(x2t), None, "cast",
                    tile_of(x3t), out_dtype="bf16", act="silu", has_scale=True,
                    svec_ap=ws_g.ap(), cmul=64.0, mult_tile_ap=tile_of(xupt),
                    wslab_ap=w_slab(w_gateT))
        emit_fwht_T(nc, block, SS, G, wdn, tdc)
        emit_collective(nc, block, SS, "AllGather", ALL8, [tdc.ap()], [tdg.ap()], sem="cc2s")
        emit_matmul(nc, block, SS, G, 8192, D, TLOC, fm_chunk_q(x3t), None, "bf16",
                    tile_of(res), out_dtype="f32", has_scale=True, svec_ap=ws_dp.ap(),
                    cmul=1.0 / math.sqrt(8192.0), resid_tile_ap=tile_of(res),
                    wslab_ap=w_tdg_slab)
        if DBG:
            d_res2 = nc.dram_tensor("d_res2", [D, TLOC], F32, kind="ExternalOutput")
            emit_dbgcopy(nc, block, SS, d_res2, res)
        emit_ln(nc, block, SS, G, res.ap(), xnc.ap())
        emit_collective(nc, block, SS, "AllGather", PAIRS, [xnc.ap()], [xng.ap()], sem="cc3")
        emit_matmul(nc, block, SS, G, D, 2048, S, xng_chunk, None, "cast",
                    qk_tile, out_dtype="bf16", wslab_ap=w_slab(w_qkv1), xnparts=2)
        emit_matmul_tm(nc, block, SS, G, D, 1024, S, xng_tile, w_qkv1.ap()[:, 2048:3072],
                       vt_tile)
        emit_attention(nc, block, SS, G, qk, vt, ctxl)
        emit_matmul(nc, block, SS, G, NH * P, D, S, ctxl_chunk, None, "cast",
                    prt_tile, out_dtype="f32", wslab_ap=w_slab(w_o1))
        emit_collective(nc, block, SS, "ReduceScatter", PAIRS, [prt.ap()], [rso.ap()], op=ALU.add, sem="cc4")
        emit_resadd(nc, block, SS, G, res, rso, bvec_ap=b_o1.ap())
        emit_ln(nc, block, SS, G, res.ap(), xnc.ap())
        emit_matmul(nc, block, SS, G, D, 8192, TLOC, fm_chunk(xnc), None, "cast",
                    tile_of(hidt), out_dtype="bf16", act="relu", has_scale=True,
                    svec_ap=us1.ap(), sign_w=True, wslab_ap=w_slab(w_up1T))
        emit_matmul(nc, block, SS, G, 8192, D, TLOC, fm_chunk_q(hidt), None, "cast",
                    tile_of(y), out_dtype="f32", has_scale=True, svec_ap=ds1.ap(),
                    sign_w=True, psc_ap=osc.ap(), resid_tile_ap=tile_of(res),
                    wslab_ap=w_slab_dn1)
        if DBG:
            dbgs = [("d_res", res), ("d_rso", rso), ("d_x2", x2t), ("d_xup", xupt),
                    ("d_x3", x3t), ("d_qk", qk), ("d_vt", vt), ("d_ctxl", ctxl),
                    ("d_xnc", xnc), ("d_hid", hidt), ("d_tdg", tdg), ("d_xng", xng)]
            douts = {}
            for nm, t in dbgs:
                douts[nm] = nc.dram_tensor(nm, list(t.shape), t.dtype, kind="ExternalOutput")
            b_cc = SS["cc"].n

            def dump(g):
                for i, (nm, t) in enumerate(dbgs):
                    g.dma_start(out=douts[nm].ap(), in_=t.ap()).then_inc(SS["cc"].h, 16)
                g.wait_ge(SS["cc"].h, b_cc + 16 * len(dbgs))
            block.gpsimd(dump)
            SS["cc"].n = b_cc + 16 * len(dbgs)
            barrier(block, SS)
    return nc


def _hadamard(n):
    h = np.array([[1.0]], dtype=np.float32)
    while h.shape[0] < n:
        h = np.kron(h, np.array([[1.0, 1.0], [1.0, -1.0]], dtype=np.float32))
    return h


def _prelay(v, n):
    return np.ascontiguousarray(np.asarray(v, dtype=np.float32).reshape(n, P).T)


def make_inmaps(inputs):
    ac = np.ascontiguousarray
    f32 = lambda a: np.asarray(a, dtype=np.float32)
    x = f32(inputs["L0_attn_x"]).reshape(NTOK, D)
    H = _hadamard(D)
    mask = np.zeros((4 * P, TCH), np.float32)
    for j in range(4):
        for kp in range(P):
            mask[j * P + kp, j * P + kp:] = 1.0
    W_in = f32(inputs["L0_attn_W_in"])
    b_in = f32(inputs["L0_attn_b_in"])
    Wq, Wk, Wv = f32(inputs["L1_attn_Wq"]), f32(inputs["L1_attn_Wk"]), f32(inputs["L1_attn_Wv"])
    ws_ug = f32(inputs["L0_ffn_Wscale_ug"])
    in_maps = []
    for c in range(NCORES):
        b, r = divmod(c, 2)
        hs = r * NH
        qcols = slice(hs * P, (hs + NH) * P)
        kcols = slice(D + hs * P, D + (hs + NH) * P)
        vcols = slice(2 * D + hs * P, 2 * D + (hs + NH) * P)
        m = {
            "x_fm": ac(x[b * S + r * TLOC:(b * S + (r + 1) * TLOC)].T),
            "w_qkv0": ac(np.concatenate([W_in[:, qcols], W_in[:, kcols], W_in[:, vcols]], axis=1)),
            "b_qk0": _prelay(np.concatenate([b_in[qcols], b_in[kcols]]), 16),
            "b_v0": ac(b_in[vcols].reshape(1, 1024)),
            "w_out0": ac(f32(inputs["L0_attn_W_out"])[hs * P:(hs + NH) * P, :]),
            "b_out0": _prelay(f32(inputs["L0_attn_b_out"]), 16),
            "h2048": H,
            "su_ug": _prelay(f32(inputs["L0_ffn_SU_ug"]), 16),
            "w_upT": ac(f32(inputs["L0_ffn_W_up"]).T),
            "ws_u": _prelay(ws_ug[:8192], 64),
            "su_dp": _prelay(f32(inputs["L0_ffn_SU_dp"]), 64),
            "w_gateT": ac(f32(inputs["L0_ffn_W_gate"]).T),
            "ws_g": _prelay(ws_ug[8192:], 64),
            "wdn": ac(f32(inputs["L0_ffn_W_down"])[c * 256:(c + 1) * 256]),
            "ws_dp": _prelay(f32(inputs["L0_ffn_Wscale_dp"]), 16),
            "w_qkv1": ac(np.concatenate([Wq[:, qcols.start:qcols.stop],
                                         Wk[:, qcols.start:qcols.stop],
                                         Wv[:, qcols.start:qcols.stop]], axis=1)),
            "w_o1": ac(f32(inputs["L1_attn_Wo"])[hs * P:(hs + NH) * P, :]),
            "b_o1": _prelay(f32(inputs["L1_attn_bo"]), 16),
            "w_up1T": ac(f32(inputs["L1_ffn_up_weight"]).T),
            "us1": _prelay(f32(inputs["L1_ffn_up_scale"]), 64),
            "w_dn1T": ac(f32(inputs["L1_ffn_down_weight"]).T),
            "ds1": _prelay(f32(inputs["L1_ffn_down_scale"]), 16),
            "osc": ac(np.broadcast_to(f32(inputs["L1_ffn_output_scale"]).reshape(1, 1), (P, 1))),
            "masks_in": mask,
            "onesk_in": np.ones((P, 1), np.float32),
            "onesb_in": np.ones((1, P), np.float32),
            "eye_in": np.eye(P, dtype=np.float32),
            "eps_in": np.full((P, 1), EPS, np.float32),
        }
        in_maps.append(m)
    return in_maps


def kernel(**inputs):
    if "nc" not in _CACHE:
        _CACHE["nc"] = build_megakernel()
    nc = _CACHE["nc"]
    in_maps = make_inmaps(inputs)
    res = run_bass_kernel_spmd(nc, in_maps, core_ids=list(range(NCORES)))
    _CACHE["last_results"] = res
    out = np.empty((NTOK, D), np.float32)
    for c in range(NCORES):
        out[c * TLOC:(c + 1) * TLOC] = res.results[c]["y"].T
    return out.reshape(B, S, D)


# version 06
# speedup vs baseline: 1.1303x; 1.0940x over previous
import math
from contextlib import ExitStack

import numpy as np
import ml_dtypes
import concourse.bass as bass
import concourse.mybir as mybir
from concourse.bass_utils import run_bass_kernel_spmd

F32 = mybir.dt.float32
BF16 = mybir.dt.bfloat16
AF = mybir.ActivationFunctionType
ALU = mybir.AluOpType
AX = mybir.AxisListType

P = 128
NCORES = 8
B, S, D = 4, 4096, 2048
NTOK = B * S
TLOC = NTOK // NCORES
TCH = 512
EPS = 1e-5
NH = 8
NSUP = 8
SCL = 1.0 / math.sqrt(128.0)
DBG = False

_CACHE = {}
_UIDC = [0]


def _uid():
    _UIDC[0] += 1
    return _UIDC[0]


class SemT:
    def __init__(self, h):
        self.h = h
        self.n = 0


def barrier(block, SS):
    bar = SS["bar"]
    base = bar.n
    for name in ("sync", "gpsimd", "tensor", "scalar", "vector"):
        def mk(eng):
            eng.nop().then_inc(bar.h, 1)
            eng.wait_ge(bar.h, base + 5)
        getattr(block, name)(mk)
    bar.n += 5



def emit_ln(nc, block, SS, G, src_ap, dst_ap):
    nk = D // P
    ntc = TLOC // TCH
    VPT = 10 + 2 * nk
    SPT = nk + 1
    PPT = nk + 3
    b_ldx, b_pe, b_ev, b_ev2, b_st = SS["ldx"].n, SS["pe"].n, SS["ev"].n, SS["ev2"].n, SS["st0"].n
    with ExitStack() as cx:
        uid = _uid()
        ec = cx.enter_context
        xb = ec(nc.sbuf_tensor(f"ln_xb_{uid}", [P, nk, TCH], BF16))
        sqb = ec(nc.sbuf_tensor(f"ln_sqb_{uid}", [P, TCH], BF16))
        xnb = ec(nc.sbuf_tensor(f"ln_xnb_{uid}", [P, nk, TCH], BF16))
        mrow = ec(nc.sbuf_tensor(f"ln_m_{uid}", [1, TCH], F32))
        erow = ec(nc.sbuf_tensor(f"ln_e_{uid}", [1, TCH], F32))
        msq = ec(nc.sbuf_tensor(f"ln_msq_{uid}", [1, TCH], F32))
        var = ec(nc.sbuf_tensor(f"ln_var_{uid}", [1, TCH], F32))
        srt = ec(nc.sbuf_tensor(f"ln_srt_{uid}", [1, TCH], F32))
        rr = ec(nc.sbuf_tensor(f"ln_rr_{uid}", [1, TCH], F32))
        nmr = ec(nc.sbuf_tensor(f"ln_nmr_{uid}", [1, TCH], F32))
        a_bf = ec(nc.sbuf_tensor(f"ln_abf_{uid}", [1, TCH], BF16))
        bb_bf = ec(nc.sbuf_tensor(f"ln_bbf_{uid}", [1, TCH], BF16))
        a_sb = ec(nc.sbuf_tensor(f"ln_asb_{uid}", [P, TCH], BF16))
        b_sb = ec(nc.sbuf_tensor(f"ln_bsb_{uid}", [P, TCH], BF16))
        ps_m = ec(nc.psum_tensor(f"ln_psm_{uid}", [1, TCH], F32))
        ps_s = ec(nc.psum_tensor(f"ln_pss_{uid}", [1, TCH], F32))
        ps_a = ec(nc.psum_tensor(f"ln_psa_{uid}", [P, TCH], F32))
        ps_b = ec(nc.psum_tensor(f"ln_psb_{uid}", [P, TCH], F32))

        def gp(g):
            for t in range(ntc):
                if t >= 1:
                    g.wait_ge(SS["ev2"].h, b_ev2 + VPT * t)
                g.dma_start(out=xb[:], in_=src_ap[:, t * TCH:(t + 1) * TCH]
                            .rearrange("(k p) n -> p k n", p=P)).then_inc(SS["ldx"].h, 16)
        block.gpsimd(gp)

        def tns(tn):
            tn.wait_ge(SS["gldc"].h, G["ldv_const"])
            for t in range(ntc):
                pb = b_pe + PPT * t
                vbp = b_ev2 + VPT * (t - 1)
                tn.wait_ge(SS["ldx"].h, b_ldx + 16 * (t + 1))
                if t >= 1:
                    tn.wait_ge(SS["ev2"].h, vbp + 1)
                ins = None
                for k in range(nk):
                    ins = tn.matmul(ps_m[:], G["ones_bf"][:], xb[:, k, :],
                                    start=(k == 0), stop=(k == nk - 1))
                ins.then_inc(SS["pe"].h, 1)
                for k in range(nk):
                    tn.wait_ge(SS["ev"].h, b_ev + SPT * t + k + 1)
                    if t >= 1 and k == 0:
                        tn.wait_ge(SS["ev2"].h, vbp + 2)
                    tn.matmul(ps_s[:], G["ones_bf"][:], sqb[:],
                              start=(k == 0), stop=(k == nk - 1)).then_inc(SS["pe"].h, 1)
                vb = b_ev2 + VPT * t
                tn.wait_ge(SS["ev2"].h, vb + 7)
                if t >= 1:
                    tn.wait_ge(SS["ev2"].h, vbp + 9)
                tn.matmul(ps_a[:], G["onesb_bf"][:], a_bf[:], start=True, stop=True).then_inc(SS["pe"].h, 1)
                tn.wait_ge(SS["ev2"].h, vb + 8)
                if t >= 1:
                    tn.wait_ge(SS["ev2"].h, vbp + 10)
                tn.matmul(ps_b[:], G["onesb_bf"][:], bb_bf[:], start=True, stop=True).then_inc(SS["pe"].h, 1)
        block.tensor(tns)

        def sc(s_):
            for t in range(ntc):
                s_.wait_ge(SS["ldx"].h, b_ldx + 16 * (t + 1))
                for k in range(nk):
                    if k >= 1:
                        s_.wait_ge(SS["pe"].h, b_pe + PPT * t + 1 + k)
                    elif t >= 1:
                        s_.wait_ge(SS["pe"].h, b_pe + PPT * t)
                    s_.activation(sqb[:], xb[:, k, :], AF.Square).then_inc(SS["ev"].h, 1)
                s_.wait_ge(SS["ev2"].h, b_ev2 + VPT * t + 4)
                s_.activation(srt[:], var[:], AF.Sqrt, bias=G["epst"][0:1, :]).then_inc(SS["ev"].h, 1)
        block.scalar(sc)

        def vc(v):
            for t in range(ntc):
                vb = b_ev2 + VPT * t
                v.wait_ge(SS["pe"].h, b_pe + PPT * t + 1)
                v.tensor_scalar(mrow[:], ps_m[:], 1.0 / D, None, ALU.mult).then_inc(SS["ev2"].h, 1)
                v.wait_ge(SS["pe"].h, b_pe + PPT * t + 1 + nk)
                v.tensor_scalar(erow[:], ps_s[:], 1.0 / D, None, ALU.mult).then_inc(SS["ev2"].h, 1)
                v.wait_ge(SS["ev2"].h, vb + 2)
                v.tensor_mul(msq[:], mrow[:], mrow[:]).then_inc(SS["ev2"].h, 1)
                v.wait_ge(SS["ev2"].h, vb + 3)
                v.tensor_sub(var[:], erow[:], msq[:]).then_inc(SS["ev2"].h, 1)
                v.wait_ge(SS["ev"].h, b_ev + SPT * t + nk + 1)
                v.reciprocal(rr[:], srt[:]).then_inc(SS["ev2"].h, 1)
                v.wait_ge(SS["ev2"].h, vb + 5)
                v.tensor_mul(nmr[:], mrow[:], rr[:]).then_inc(SS["ev2"].h, 1)
                v.tensor_copy(a_bf[:], rr[:]).then_inc(SS["ev2"].h, 1)
                v.wait_ge(SS["ev2"].h, vb + 6)
                v.tensor_scalar(bb_bf[:], nmr[:], -1.0, None, ALU.mult).then_inc(SS["ev2"].h, 1)
                v.wait_ge(SS["pe"].h, b_pe + PPT * t + nk + 2)
                if t >= 1:
                    v.wait_ge(SS["ev2"].h, b_ev2 + VPT * t)
                v.tensor_copy(a_sb[:], ps_a[:]).then_inc(SS["ev2"].h, 1)
                v.wait_ge(SS["pe"].h, b_pe + PPT * t + nk + 3)
                v.tensor_copy(b_sb[:], ps_b[:]).then_inc(SS["ev2"].h, 1)
                v.wait_ge(SS["ev2"].h, vb + 10)
                if t >= 2:
                    v.wait_ge(SS["st0"].h, b_st + 16 * (t - 1))
                if t >= 1:
                    v.wait_ge(SS["st0"].h, b_st + 16 * t)
                for k in range(nk):
                    v.tensor_mul(xnb[:, k, :], xb[:, k, :], a_sb[:]).then_inc(SS["ev2"].h, 1)
                    v.wait_ge(SS["ev2"].h, vb + 11 + 2 * k)
                    v.tensor_add(xnb[:, k, :], xnb[:, k, :], b_sb[:]).then_inc(SS["ev2"].h, 1)
        block.vector(vc)

        def sy(sy_):
            for t in range(ntc):
                sy_.wait_ge(SS["ev2"].h, b_ev2 + VPT * (t + 1))
                sy_.dma_start(out=dst_ap[:, t * TCH:(t + 1) * TCH].rearrange("(k p) n -> p k n", p=P),
                              in_=xnb[:]).then_inc(SS["st0"].h, 16)
            sy_.wait_ge(SS["st0"].h, b_st + 16 * ntc)
        block.sync(sy)

        SS["ldx"].n = b_ldx + 16 * ntc
        SS["pe"].n = b_pe + PPT * ntc
        SS["ev"].n = b_ev + SPT * ntc
        SS["ev2"].n = b_ev2 + VPT * ntc
        SS["st0"].n = b_st + 16 * ntc
    barrier(block, SS)



def emit_matmul(nc, block, SS, G, K, M, ntok, xchunk, w_ap, w_mode, out_tile_ap,
                out_dtype="bf16", has_bias=False, bvec=None, act="copy",
                has_scale=False, svec_ap=None, cmul=1.0, sfold_ap=None,
                resid_tile_ap=None, sign_w=False, mult_tile_ap=None, psc_ap=None,
                xpm_ap=None, wslab_ap=None, slab_m=4, xnparts=1, pre_wait=None):
    nk, nm = K // P, M // P
    ntc = ntok // TCH
    if K >= 8192:
        nth = 4 if ntc >= 4 else ntc
    else:
        nth = 1
    ntcl = ntc // nth
    if K >= 8192:
        slab_m = 2
    nslab = nm // slab_m
    use_act = act != "copy"
    afunc = {"silu": AF.Silu, "relu": AF.Relu}.get(act)
    has_resid = resid_tile_ap is not None
    has_mult = mult_tile_ap is not None
    has_psc = psc_ap is not None
    has_xpm = xpm_ap is not None
    has_vec_op = (not use_act) or has_mult or has_resid or has_psc
    n_rm = int(has_resid) + int(has_mult)
    if has_psc or sfold_ap is not None:
        assert has_scale
    if use_act:
        assert not has_bias and not has_resid and not has_psc
    scale_pre = has_scale and (cmul != 1.0 or has_psc or sfold_ap is not None)
    n_vini = int(has_scale and cmul != 1.0) + int(has_psc) + int(sfold_ap is not None)
    nidx = ntc * nm

    b_ldx = SS["ldx"].n
    b_ldw = [SS["ldw0"].n, SS["ldw1"].n]
    b_ldv = SS["ldv"].n
    b_ldr = [SS["ldr0"].n, SS["ldr1"].n, SS["ldr2"].n, SS["ldr3"].n]
    b_pe = SS["pe"].n
    b_ev = SS["ev"].n
    b_ev2 = SS["ev2"].n
    b_sg = SS["sg"].n
    b_st = [SS["st0"].n, SS["st1"].n]
    b_xpms = SS["xpms"].n
    b_vini = SS["vini"].n
    odt = BF16 if out_dtype == "bf16" else F32

    NX = xnparts

    def order():
        gi = 0
        for th in range(nth):
            for m in range(nm):
                for tc in range(ntcl):
                    yield gi, th, m, tc, th * ntcl + tc
                    gi += 1

    with ExitStack() as cx:
        uid = _uid()
        ec = cx.enter_context
        xb = ec(nc.sbuf_tensor(f"mm_xb_{uid}", [P, nk, ntcl * TCH], BF16))
        wb = [ec(nc.sbuf_tensor(f"mm_wb{j}_{uid}", [P, nk, slab_m * P], BF16)) for j in range(2)]
        eb = [ec(nc.sbuf_tensor(f"mm_e{j}_{uid}", [P, TCH], odt)) for j in range(4)]
        rb = [ec(nc.sbuf_tensor(f"mm_r{j}_{uid}", [P, TCH], F32)) for j in range(4)]
        mb = [ec(nc.sbuf_tensor(f"mm_m{j}_{uid}", [P, TCH], BF16)) for j in range(4)]
        vecs = ec(nc.sbuf_tensor(f"mm_vecs_{uid}", [P, 3 * nm + nk + 2], F32))
        pb = [ec(nc.psum_tensor(f"mm_pb{j}_{uid}", [P, TCH], F32)) for j in range(4)]
        off_b, off_s, off_w, off_p, off_f = 0, nm, 2 * nm, 2 * nm + nk, 2 * nm + nk + 1
        nvec_dma = int(has_bias) + int(has_scale) + int(has_xpm) + int(has_psc) + int(sfold_ap is not None)

        def sy(sy_):
            if has_bias:
                sy_.dma_start(out=vecs[:, off_b:off_b + nm], in_=bvec).then_inc(SS["ldv"].h, 16)
            if has_scale:
                sy_.dma_start(out=vecs[:, off_s:off_s + nm], in_=svec_ap).then_inc(SS["ldv"].h, 16)
            if has_xpm:
                sy_.dma_start(out=vecs[:, off_w:off_w + nk], in_=xpm_ap).then_inc(SS["ldv"].h, 16)
            if has_psc:
                sy_.dma_start(out=vecs[:, off_p:off_p + 1], in_=psc_ap).then_inc(SS["ldv"].h, 16)
            if sfold_ap is not None:
                sy_.dma_start(out=vecs[:, off_f:off_f + nm], in_=sfold_ap).then_inc(SS["ldv"].h, 16)
            if n_rm:
                for gi, th, m, tc, t in order():
                    if gi >= 4:
                        sy_.wait_ge(SS[f"st{gi % 2}"].h, b_st[gi % 2] + 16 * ((gi - 4) // 2 + 1))
                    if has_resid:
                        sy_.dma_start(out=rb[gi % 4][:], in_=resid_tile_ap(t, m)).then_inc(SS[f"ldr{gi % 4}"].h, 16)
                    if has_mult:
                        sy_.dma_start(out=mb[gi % 4][:], in_=mult_tile_ap(t, m)).then_inc(SS[f"ldr{gi % 4}"].h, 16)
        block.sync(sy)

        def gp(g):
            if pre_wait is not None:
                g.wait_ge(SS[pre_wait].h, 1)
            for th in range(nth):
                if th >= 1:
                    g.wait_ge(SS["pe"].h, b_pe + th * ntcl * nm)
                parts = xchunk2(th)
                if not isinstance(parts, list):
                    parts = [(xb[:], parts)]
                for xo, xi in parts:
                    g.dma_start(out=xo, in_=xi).then_inc(SS["ldx"].h, 16)
                for si in range(nslab):
                    sidx = th * nslab + si
                    if sidx >= 2:
                        g.wait_ge(SS["pe"].h, b_pe + (sidx - 1) * slab_m * ntcl)
                    g.dma_start(out=wb[sidx % 2][:], in_=wslab_ap(si)).then_inc(SS[f"ldw{sidx % 2}"].h, 16)
        def xchunk2(th):
            try:
                return xchunk(th, xb)
            except TypeError:
                return xchunk(th)
        block.gpsimd(gp)

        def tns(tn):
            for gi, th, m, tc, t in order():
                sidx = th * nslab + m // slab_m
                if m == 0 and tc == 0:
                    if has_xpm:
                        tn.wait_ge(SS["xpms"].h, b_xpms + th + 1)
                    else:
                        tn.wait_ge(SS["ldx"].h, b_ldx + 16 * NX * (th + 1))
                if m % slab_m == 0 and tc == 0:
                    if sign_w:
                        tn.wait_ge(SS["sg"].h, b_sg + sidx + 1)
                    else:
                        tn.wait_ge(SS[f"ldw{sidx % 2}"].h, b_ldw[sidx % 2] + 16 * (sidx // 2 + 1))
                if gi >= 4:
                    if use_act:
                        tn.wait_ge(SS["ev"].h, b_ev + gi - 3)
                    else:
                        tn.wait_ge(SS["ev2"].h, b_ev2 + gi - 3)
                ins = None
                wloc = (m % slab_m) * P
                for k in range(nk):
                    ins = tn.matmul(pb[gi % 4][:], wb[sidx % 2][:, k, wloc:wloc + P],
                                    xb[:, k, tc * TCH:(tc + 1) * TCH],
                                    start=(k == 0), stop=(k == nk - 1))
                ins.then_inc(SS["pe"].h, 1)
        block.tensor(tns)

        def sc(s_):
            if use_act and has_scale:
                s_.wait_ge(SS["ldv"].h, b_ldv + 16 * nvec_dma)
                if scale_pre:
                    s_.wait_ge(SS["vini"].h, b_vini + n_vini)
            for gi, th, m, tc, t in order():
                sidx = th * nslab + m // slab_m
                if sign_w and m % slab_m == 0 and tc == 0:
                    s_.wait_ge(SS[f"ldw{sidx % 2}"].h, b_ldw[sidx % 2] + 16 * (sidx // 2 + 1))
                    s_.activation(wb[sidx % 2][:, :, :], wb[sidx % 2][:, :, :], AF.Sign).then_inc(SS["sg"].h, 1)
                if use_act:
                    s_.wait_ge(SS["pe"].h, b_pe + gi + 1)
                    if gi >= 4:
                        s_.wait_ge(SS[f"st{gi % 2}"].h, b_st[gi % 2] + 16 * ((gi - 4) // 2 + 1))
                    kw = {}
                    if has_scale:
                        kw["scale"] = vecs[:, off_s + m:off_s + m + 1]
                    elif cmul != 1.0:
                        kw["scale"] = float(cmul)
                    s_.activation(eb[gi % 4][:], pb[gi % 4][:], afunc, **kw).then_inc(SS["ev"].h, 1)
                if has_vec_op:
                    s_.wait_ge(SS["ev2"].h, b_ev2 + gi + 1)
                else:
                    s_.wait_ge(SS["ev"].h, b_ev + gi + 1)
                s_.dma_start(out=out_tile_ap(t, m), in_=eb[gi % 4][:]).then_inc(SS[f"st{gi % 2}"].h, 16)
            s_.wait_ge(SS["st0"].h, b_st[0] + 16 * ((nidx + 1) // 2))
            s_.wait_ge(SS["st1"].h, b_st[1] + 16 * (nidx // 2))
        block.scalar(sc)

        def vc(v):
            nvi = 0
            if scale_pre:
                v.wait_ge(SS["ldv"].h, b_ldv + 16 * nvec_dma)
                if cmul != 1.0:
                    v.tensor_scalar(vecs[:, off_s:off_s + nm], vecs[:, off_s:off_s + nm],
                                    float(cmul), None, ALU.mult).then_inc(SS["vini"].h, 1)
                    nvi += 1
                    v.wait_ge(SS["vini"].h, b_vini + nvi)
                if has_psc:
                    v.tensor_scalar(vecs[:, off_s:off_s + nm], vecs[:, off_s:off_s + nm],
                                    vecs[:, off_p:off_p + 1], None, ALU.mult).then_inc(SS["vini"].h, 1)
                    nvi += 1
                    v.wait_ge(SS["vini"].h, b_vini + nvi)
                if sfold_ap is not None:
                    v.tensor_mul(vecs[:, off_s:off_s + nm], vecs[:, off_s:off_s + nm],
                                 vecs[:, off_f:off_f + nm]).then_inc(SS["vini"].h, 1)
                    nvi += 1
                    v.wait_ge(SS["vini"].h, b_vini + nvi)
            for gi, th, m, tc, t in order():
                if has_xpm and m == 0 and tc == 0:
                    v.wait_ge(SS["ldv"].h, b_ldv + 16 * nvec_dma)
                    v.wait_ge(SS["ldx"].h, b_ldx + 16 * NX * (th + 1))
                    ins = None
                    for k in range(nk):
                        ins = v.tensor_scalar(xb[:, k, :], xb[:, k, :],
                                              vecs[:, off_w + k:off_w + k + 1], None, ALU.mult)
                    ins.then_inc(SS["xpms"].h, 1)
                if not has_vec_op:
                    continue
                if use_act:
                    v.wait_ge(SS["ev"].h, b_ev + gi + 1)
                    v.wait_ge(SS[f"ldr{gi % 4}"].h, b_ldr[gi % 4] + 16 * n_rm * (gi // 4 + 1))
                    v.tensor_mul(eb[gi % 4][:], eb[gi % 4][:], mb[gi % 4][:]).then_inc(SS["ev2"].h, 1)
                    continue
                v.wait_ge(SS["pe"].h, b_pe + gi + 1)
                if has_bias or has_scale:
                    v.wait_ge(SS["ldv"].h, b_ldv + 16 * nvec_dma)
                if scale_pre:
                    v.wait_ge(SS["vini"].h, b_vini + n_vini)
                if n_rm:
                    v.wait_ge(SS[f"ldr{gi % 4}"].h, b_ldr[gi % 4] + 16 * n_rm * (gi // 4 + 1))
                if gi >= 4:
                    v.wait_ge(SS[f"st{gi % 2}"].h, b_st[gi % 2] + 16 * ((gi - 4) // 2 + 1))
                pbank = pb[gi % 4]
                cur = eb[gi % 4]
                if has_resid:
                    if has_scale:
                        last = v.scalar_tensor_tensor(cur[:], pbank[:], vecs[:, off_s + m:off_s + m + 1],
                                                      rb[gi % 4][:], ALU.mult, ALU.add)
                    elif has_bias:
                        last = v.scalar_tensor_tensor(cur[:], pbank[:], vecs[:, off_b + m:off_b + m + 1],
                                                      rb[gi % 4][:], ALU.add, ALU.add)
                    else:
                        last = v.scalar_tensor_tensor(cur[:], pbank[:], float(cmul),
                                                      rb[gi % 4][:], ALU.mult, ALU.add)
                else:
                    if has_scale and has_bias:
                        last = v.tensor_scalar(cur[:], pbank[:], vecs[:, off_s + m:off_s + m + 1],
                                               vecs[:, off_b + m:off_b + m + 1], ALU.mult, ALU.add)
                    elif has_scale:
                        last = v.tensor_scalar(cur[:], pbank[:], vecs[:, off_s + m:off_s + m + 1],
                                               None, ALU.mult)
                    elif has_bias:
                        last = v.tensor_scalar(cur[:], pbank[:], vecs[:, off_b + m:off_b + m + 1],
                                               None, ALU.add)
                    elif cmul != 1.0:
                        last = v.tensor_scalar(cur[:], pbank[:], float(cmul), None, ALU.mult)
                    else:
                        last = v.tensor_copy(cur[:], pbank[:])
                last.then_inc(SS["ev2"].h, 1)
        block.vector(vc)

        SS["ldx"].n = b_ldx + 16 * NX * nth
        nslabs_tot = nth * nslab
        SS["ldw0"].n = b_ldw[0] + 16 * ((nslabs_tot + 1) // 2)
        SS["ldw1"].n = b_ldw[1] + 16 * (nslabs_tot // 2)
        SS["ldv"].n = b_ldv + 16 * nvec_dma
        for j in range(4):
            SS[f"ldr{j}"].n = b_ldr[j] + 16 * n_rm * ((nidx - j + 3) // 4)
        SS["pe"].n = b_pe + nidx
        SS["ev"].n = b_ev + (nidx if use_act else 0)
        SS["ev2"].n = b_ev2 + (nidx if has_vec_op else 0)
        SS["sg"].n = b_sg + (nslabs_tot if sign_w else 0)
        SS["st0"].n = b_st[0] + 16 * ((nidx + 1) // 2)
        SS["st1"].n = b_st[1] + 16 * (nidx // 2)
        SS["xpms"].n = b_xpms + (nth if has_xpm else 0)
        SS["vini"].n = b_vini + n_vini
    barrier(block, SS)



def emit_matmul_tm(nc, block, SS, G, K, M, ntok, xtile, w_ap_full, out_tile_ap,
                   bias_row_ap=None):
    nk, nmc = K // P, M // TCH
    ntt = ntok // P
    has_bias = bias_row_ap is not None
    b_ldx, b_ldw, b_ldv = SS["ldx"].n, SS["ldw0"].n, SS["ldw1"].n
    b_pe, b_ev2 = SS["pe"].n, SS["ev2"].n
    b_st = [SS["st0"].n, SS["st1"].n]
    with ExitStack() as cx:
        uid = _uid()
        ec = cx.enter_context
        xt = [ec(nc.sbuf_tensor(f"vt_x{j}_{uid}", [P, nk, P], BF16)) for j in range(2)]
        wsb = ec(nc.sbuf_tensor(f"vt_w_{uid}", [P, nk, M], BF16))
        brow = ec(nc.sbuf_tensor(f"vt_br_{uid}", [1, M], F32))
        bsb = ec(nc.sbuf_tensor(f"vt_bs_{uid}", [P, M], F32))
        eb = [ec(nc.sbuf_tensor(f"vt_e{j}_{uid}", [P, TCH], BF16)) for j in range(2)]
        brbf = ec(nc.sbuf_tensor(f"vt_brbf_{uid}", [1, M], BF16))
        pb = [ec(nc.psum_tensor(f"vt_pb{j}_{uid}", [P, TCH], F32)) for j in range(2)]
        ps_bc = ec(nc.psum_tensor(f"vt_psbc_{uid}", [P, TCH], F32))

        def gp(g):
            g.dma_start(out=wsb[:], in_=w_ap_full.rearrange("(k p) m -> p k m", p=P)).then_inc(SS["ldw0"].h, 16)
            if has_bias:
                g.dma_start(out=brow[:], in_=bias_row_ap).then_inc(SS["ldw1"].h, 16)
            for tt in range(ntt):
                if tt >= 2:
                    g.wait_ge(SS["pe"].h, b_pe + nbc + (tt - 1) * nmc)
                g.dma_start(out=xt[tt % 2][:], in_=xtile(tt)).then_inc(SS[f"gq{tt % 2}"].h, 16)
        nbc = (M // TCH) if has_bias else 0

        b_ldr = [SS["gq0"].n, SS["gq1"].n]
        block.gpsimd(gp)

        def tns(tn):
            tn.wait_ge(SS["ldw0"].h, b_ldw + 16)
            if has_bias:
                tn.wait_ge(SS["ldw1"].h, b_ldv + 16)
                for mc in range(M // TCH):
                    tn.wait_ge(SS["ev2"].h, b_ev2 + 1)
                    if mc > 0:
                        tn.wait_ge(SS["ev2"].h, b_ev2 + 1 + mc)
                    tn.matmul(ps_bc[:], G["onesb_bf"][:], brbf[:, mc * TCH:(mc + 1) * TCH],
                              start=True, stop=True).then_inc(SS["pe"].h, 1)
            for tt in range(ntt):
                for mc in range(nmc):
                    idx = tt * nmc + mc
                    if mc == 0:
                        tn.wait_ge(SS[f"gq{tt % 2}"].h, b_ldr[tt % 2] + 16 * (tt // 2 + 1))
                    if idx >= 2:
                        tn.wait_ge(SS["ev2"].h, b_ev2 + 1 + nbc + idx - 1)
                    ins = None
                    for k in range(nk):
                        ins = tn.matmul(pb[idx % 2][:], xt[tt % 2][:, k, :],
                                        wsb[:, k, mc * TCH:(mc + 1) * TCH],
                                        start=(k == 0), stop=(k == nk - 1))
                    ins.then_inc(SS["pe"].h, 1)
        block.tensor(tns)

        def vc(v):
            nb = 0
            if has_bias:
                v.wait_ge(SS["ldw1"].h, b_ldv + 16)
                v.tensor_copy(brbf[:], brow[:]).then_inc(SS["ev2"].h, 1)
                nb = 1
                for mc in range(M // TCH):
                    v.wait_ge(SS["pe"].h, b_pe + mc + 1)
                    v.tensor_copy(bsb[:, mc * TCH:(mc + 1) * TCH], ps_bc[:]).then_inc(SS["ev2"].h, 1)
                nb += M // TCH
            for tt in range(ntt):
                for mc in range(nmc):
                    idx = tt * nmc + mc
                    v.wait_ge(SS["pe"].h, b_pe + nbc + idx + 1)
                    if idx >= 2:
                        v.wait_ge(SS[f"st{idx % 2}"].h, b_st[idx % 2] + 16 * ((idx - 2) // 2 + 1))
                    if has_bias:
                        v.wait_ge(SS["ev2"].h, b_ev2 + 1 + nbc)
                        v.tensor_add(eb[idx % 2][:], pb[idx % 2][:], bsb[:, mc * TCH:(mc + 1) * TCH]).then_inc(SS["ev2"].h, 1)
                    else:
                        v.tensor_copy(eb[idx % 2][:], pb[idx % 2][:]).then_inc(SS["ev2"].h, 1)
        block.vector(vc)

        def sc(s_):
            for tt in range(ntt):
                for mc in range(nmc):
                    idx = tt * nmc + mc
                    s_.wait_ge(SS["ev2"].h, b_ev2 + (1 + nbc if has_bias else 0) + idx + 1)
                    s_.dma_start(out=out_tile_ap(tt, mc), in_=eb[idx % 2][:]).then_inc(SS[f"st{idx % 2}"].h, 16)
            s_.wait_ge(SS["st0"].h, b_st[0] + 16 * ((ntt * nmc + 1) // 2))
            s_.wait_ge(SS["st1"].h, b_st[1] + 16 * ((ntt * nmc) // 2))
        block.scalar(sc)

        nidx = ntt * nmc
        SS["ldw0"].n = b_ldw + 16
        SS["ldw1"].n = b_ldv + (16 if has_bias else 0)
        SS["gq0"].n = b_ldr[0] + 16 * ((ntt + 1) // 2)
        SS["gq1"].n = b_ldr[1] + 16 * (ntt // 2)
        SS["pe"].n = b_pe + nbc + nidx
        SS["ev2"].n = b_ev2 + (1 + nbc if has_bias else 0) + nidx
        SS["st0"].n = b_st[0] + 16 * ((nidx + 1) // 2)
        SS["st1"].n = b_st[1] + 16 * (nidx // 2)
    barrier(block, SS)



def _nkb(s):
    return 4 * (s + 1)


KBS_HEAD = sum(_nkb(s) for s in range(NSUP))
NPR_HEAD = KBS_HEAD // 2
A_PE_HEAD = 17 + sum(3 * _nkb(s) + 1 for s in range(NSUP))
A_SC_HEAD = 4 + NPR_HEAD
A_VC_HEAD = 28 + 9 * NSUP


def _a_pe_s(s):
    return 17 + sum(3 * _nkb(x) + 1 for x in range(s))


def _a_scp_s(s):
    return 4 + sum(_nkb(x) // 2 for x in range(s))


def _a_vc_s(s):
    return 28 + 9 * s


def _a_kbg(s, kb):
    return sum(_nkb(x) for x in range(s)) + kb


def _a_prg_to_spr(prg):
    s = 0
    while prg >= _nkb(s) // 2:
        prg -= _nkb(s) // 2
        s += 1
    return s, prg


def emit_attention(nc, block, SS, G, qk_t, vt_t, cc2_t):
    NKB = S // P
    b_pe, b_sc, b_vc = SS["pe"].n, SS["ev"].n, SS["ev2"].n
    b_kk, b_kv, b_q = SS["ldw0"].n, SS["ldw1"].n, SS["gq0"].n
    b_st = [SS["st0"].n, SS["st1"].n]

    def pe_st(h, s, kb):
        return b_pe + A_PE_HEAD * h + _a_pe_s(s) + 6 * (kb // 2) + 1 + (kb % 2)

    def pe_l(h, s, kb):
        return b_pe + A_PE_HEAD * h + _a_pe_s(s) + 6 * (kb // 2) + 3 + 2 * (kb % 2)

    def pe_pv(h, s, kb):
        return b_pe + A_PE_HEAD * h + _a_pe_s(s) + 6 * (kb // 2) + 4 + 2 * (kb % 2)

    def sc_exp(h, s, pr):
        return b_sc + A_SC_HEAD * h + _a_scp_s(s) + pr + 1

    with ExitStack() as cx:
        uid = _uid()
        ec = cx.enter_context
        kbuf = [ec(nc.sbuf_tensor(f"at_kbuf{j}_{uid}", [P, S], BF16)) for j in range(2)]
        vbuf = [ec(nc.sbuf_tensor(f"at_vbuf{j}_{uid}", [P, NKB, P], BF16)) for j in range(2)]
        qh = [ec(nc.sbuf_tensor(f"at_qh{j}_{uid}", [P, S], BF16)) for j in range(2)]
        ksqf = ec(nc.sbuf_tensor(f"at_ksqf_{uid}", [P, S], BF16))
        qsqf = ec(nc.sbuf_tensor(f"at_qsqf_{uid}", [P, S], BF16))
        ksn = ec(nc.sbuf_tensor(f"at_ksn_{uid}", [1, S], F32))
        qsn = ec(nc.sbuf_tensor(f"at_qsn_{uid}", [1, S], F32))
        km2 = ec(nc.sbuf_tensor(f"at_km2_{uid}", [1, 1], F32))
        kms = ec(nc.sbuf_tensor(f"at_kms_{uid}", [1, 1], F32))
        qm2 = ec(nc.sbuf_tensor(f"at_qm2_{uid}", [1, NSUP], F32))
        qmv = ec(nc.sbuf_tensor(f"at_qmv_{uid}", [1, NSUP], F32))
        negcb = ec(nc.sbuf_tensor(f"at_negcb_{uid}", [1, NSUP], F32))
        negc_bf = ec(nc.sbuf_tensor(f"at_negcbf_{uid}", [1, NSUP], BF16))
        bias_all = ec(nc.sbuf_tensor(f"at_bias_{uid}", [P, NSUP], F32))
        pts = [ec(nc.sbuf_tensor(f"at_pt{j}_{uid}", [P, 2 * TCH], BF16)) for j in range(2)]
        lsb = ec(nc.sbuf_tensor(f"at_lsb_{uid}", [1, TCH], F32))
        rlv = ec(nc.sbuf_tensor(f"at_rlv_{uid}", [1, TCH], F32))
        rl_bf = ec(nc.sbuf_tensor(f"at_rlbf_{uid}", [1, TCH], BF16))
        rb_s = ec(nc.sbuf_tensor(f"at_rbs_{uid}", [P, TCH], F32))
        cxs = [ec(nc.sbuf_tensor(f"at_cxs{j}_{uid}", [P, TCH], BF16)) for j in range(2)]
        pss = [ec(nc.psum_tensor(f"at_pss{j}_{uid}", [P, 2 * TCH], F32)) for j in range(2)]
        ps_o = [ec(nc.psum_tensor(f"at_pso{j}_{uid}", [P, TCH], F32)) for j in range(2)]
        ps_l = ec(nc.psum_tensor(f"at_psl_{uid}", [1, TCH], F32))
        ps_sm = ec(nc.psum_tensor(f"at_pssm_{uid}", [P, TCH], F32))

        def gp(g):
            for h in range(NH):
                if h >= 2:
                    g.wait_ge(SS["pe"].h, b_pe + A_PE_HEAD * (h - 1))
                g.dma_start(out=kbuf[h % 2][:], in_=qk_t.ap()[(NH + h) * P:(NH + h + 1) * P, :]).then_inc(SS["ldw0"].h, 16)
                g.dma_start(out=vbuf[h % 2][:], in_=vt_t.ap()[:, h * P:(h + 1) * P]
                            .rearrange("(b p) d -> p b d", p=P)).then_inc(SS["ldw1"].h, 16)
                g.dma_start(out=qh[h % 2][:], in_=qk_t.ap()[h * P:(h + 1) * P, :]).then_inc(SS["gq0"].h, 16)
        block.gpsimd(gp)

        def tns(tn):
            for h in range(NH):
                hb = b_pe + A_PE_HEAD * h
                vcb = b_vc + A_VC_HEAD * h
                tn.wait_ge(SS["ldw0"].h, b_kk + 16 * (h + 1))
                tn.wait_ge(SS["ldw1"].h, b_kv + 16 * (h + 1))
                tn.wait_ge(SS["gq0"].h, b_q + 16 * (h + 1))
                for c in range(8):
                    if c == 0:
                        tn.wait_ge(SS["ev"].h, b_sc + A_SC_HEAD * h + 1)
                        if h > 0:
                            tn.wait_ge(SS["ev2"].h, b_vc + A_VC_HEAD * h)
                    else:
                        tn.wait_ge(SS["ev2"].h, vcb + c)
                    tn.matmul(ps_sm[0:1, :], G["ones_bf"][:], ksqf[:, c * TCH:(c + 1) * TCH],
                              start=True, stop=True).then_inc(SS["pe"].h, 1)
                for c in range(8):
                    if c == 0:
                        tn.wait_ge(SS["ev"].h, b_sc + A_SC_HEAD * h + 2)
                        tn.wait_ge(SS["ev2"].h, vcb + 8)
                    else:
                        tn.wait_ge(SS["ev2"].h, vcb + 9 + c)
                    tn.matmul(ps_sm[0:1, :], G["ones_bf"][:], qsqf[:, c * TCH:(c + 1) * TCH],
                              start=True, stop=True).then_inc(SS["pe"].h, 1)
                tn.wait_ge(SS["ev2"].h, vcb + 27)
                tn.matmul(ps_sm[:, 0:NSUP], G["onesb_bf"][:], negc_bf[:], start=True, stop=True).then_inc(SS["pe"].h, 1)
                for s in range(NSUP):
                    sb = hb + _a_pe_s(s)
                    for pr in range(_nkb(s) // 2):
                        kb0, kb1 = 2 * pr, 2 * pr + 1
                        prg = _a_kbg(s, kb0) // 2
                        if prg >= 2:
                            s2, p2 = _a_prg_to_spr(prg - 2)
                            tn.wait_ge(SS["ev"].h, sc_exp(h, s2, p2))
                        elif h > 0:
                            s2, p2 = _a_prg_to_spr(NPR_HEAD - 2 + prg)
                            tn.wait_ge(SS["ev"].h, sc_exp(h - 1, s2, p2))
                        for e in (0, 1):
                            kb = 2 * pr + e
                            tn.matmul(pss[prg % 2][:, e * TCH:(e + 1) * TCH],
                                      kbuf[h % 2][:, kb * P:(kb + 1) * P],
                                      qh[h % 2][:, s * TCH:(s + 1) * TCH], start=True, stop=True).then_inc(SS["pe"].h, 1)
                        tn.wait_ge(SS["ev"].h, sc_exp(h, s, pr))
                        for e in (0, 1):
                            kb = 2 * pr + e
                            if kb >= 4 * s:
                                tn.wait_ge(SS["ev2"].h, vcb + _a_vc_s(s) + (kb - 4 * s) + 1)
                            if kb == 0:
                                if s >= 2:
                                    tn.wait_ge(SS["ev2"].h, vcb + _a_vc_s(s - 2) + 9)
                                elif h > 0:
                                    tn.wait_ge(SS["ev2"].h, b_vc + A_VC_HEAD * (h - 1) + _a_vc_s(NSUP - 2 + s) + 9)
                                if s >= 1:
                                    tn.wait_ge(SS["ev2"].h, vcb + _a_vc_s(s - 1) + 5)
                                elif h > 0:
                                    tn.wait_ge(SS["ev2"].h, b_vc + A_VC_HEAD * (h - 1) + _a_vc_s(NSUP - 1) + 5)
                            tn.matmul(ps_l[:], G["ones_bf"][:],
                                      pts[prg % 2][:, e * TCH:(e + 1) * TCH],
                                      start=(kb == 0), stop=(kb == _nkb(s) - 1)).then_inc(SS["pe"].h, 1)
                            tn.matmul(ps_o[s % 2][:], vbuf[h % 2][:, kb, :],
                                      pts[prg % 2][:, e * TCH:(e + 1) * TCH],
                                      start=(kb == 0), stop=(kb == _nkb(s) - 1)).then_inc(SS["pe"].h, 1)
                    tn.wait_ge(SS["ev2"].h, vcb + _a_vc_s(s) + 7)
                    if s > 0:
                        tn.wait_ge(SS["ev2"].h, vcb + _a_vc_s(s - 1) + 8)
                    tn.matmul(ps_sm[:], G["onesb_bf"][:], rl_bf[:], start=True, stop=True).then_inc(SS["pe"].h, 1)
        block.tensor(tns)

        def sc(s_):
            for h in range(NH):
                s_.wait_ge(SS["ldw0"].h, b_kk + 16 * (h + 1))
                if h >= 2:
                    s_.wait_ge(SS["pe"].h, b_pe + A_PE_HEAD * (h - 1))
                s_.activation(ksqf[:], kbuf[h % 2][:], AF.Square).then_inc(SS["ev"].h, 1)
                s_.wait_ge(SS["gq0"].h, b_q + 16 * (h + 1))
                s_.activation(qsqf[:], qh[h % 2][:], AF.Square).then_inc(SS["ev"].h, 1)
                s_.wait_ge(SS["ev2"].h, b_vc + A_VC_HEAD * h + 9)
                s_.activation(kms[:], km2[:], AF.Sqrt).then_inc(SS["ev"].h, 1)
                s_.wait_ge(SS["ev2"].h, b_vc + A_VC_HEAD * h + 25)
                s_.activation(qmv[:], qm2[:], AF.Sqrt).then_inc(SS["ev"].h, 1)
                for s in range(NSUP):
                    for pr in range(_nkb(s) // 2):
                        kb1 = 2 * pr + 1
                        prg = _a_kbg(s, 0) // 2 + pr
                        s_.wait_ge(SS["pe"].h, pe_st(h, s, kb1))
                        s_.wait_ge(SS["ev2"].h, b_vc + A_VC_HEAD * h + 28)
                        if prg >= 2:
                            s4, p4 = _a_prg_to_spr(prg - 2)
                            s_.wait_ge(SS["pe"].h, pe_pv(h, s4, 2 * p4 + 1))
                        elif h > 0:
                            s4, p4 = _a_prg_to_spr(NPR_HEAD - 2 + prg)
                            s_.wait_ge(SS["pe"].h, pe_pv(h - 1, s4, 2 * p4 + 1))
                        s_.activation(pts[prg % 2][:], pss[prg % 2][:], AF.Exp,
                                      bias=bias_all[:, s:s + 1], scale=SCL).then_inc(SS["ev"].h, 1)
        block.scalar(sc)

        def vc(v):
            for h in range(NH):
                vcb = b_vc + A_VC_HEAD * h
                hb = b_pe + A_PE_HEAD * h
                for c in range(8):
                    v.wait_ge(SS["pe"].h, hb + c + 1)
                    v.tensor_copy(ksn[:, c * TCH:(c + 1) * TCH], ps_sm[0:1, :]).then_inc(SS["ev2"].h, 1)
                v.wait_ge(SS["ev2"].h, vcb + 8)
                v.reduce_max(km2[:], ksn[:], axis=AX.X).then_inc(SS["ev2"].h, 1)
                for c in range(8):
                    v.wait_ge(SS["pe"].h, hb + 8 + c + 1)
                    v.tensor_copy(qsn[:, c * TCH:(c + 1) * TCH], ps_sm[0:1, :]).then_inc(SS["ev2"].h, 1)
                for c in range(8):
                    v.wait_ge(SS["ev2"].h, vcb + 9 + c + 1)
                    v.reduce_max(qm2[:, c:c + 1], qsn[:, c * TCH:(c + 1) * TCH], axis=AX.X).then_inc(SS["ev2"].h, 1)
                v.wait_ge(SS["ev"].h, b_sc + A_SC_HEAD * h + 4)
                v.tensor_scalar(negcb[:], qmv[:], kms[0:1, :], -SCL, ALU.mult, ALU.mult).then_inc(SS["ev2"].h, 1)
                v.wait_ge(SS["ev2"].h, vcb + 26)
                v.tensor_copy(negc_bf[:], negcb[:]).then_inc(SS["ev2"].h, 1)
                v.wait_ge(SS["pe"].h, hb + 17)
                if h > 0:
                    v.wait_ge(SS["ev"].h, b_sc + A_SC_HEAD * h)
                v.tensor_copy(bias_all[:], ps_sm[:, 0:NSUP]).then_inc(SS["ev2"].h, 1)
                for s in range(NSUP):
                    vb = vcb + _a_vc_s(s)
                    for j in range(4):
                        kb = 4 * s + j
                        kbg = _a_kbg(s, kb)
                        prg = kbg // 2
                        v.wait_ge(SS["ev"].h, sc_exp(h, s, kb // 2))
                        v.tensor_mul(pts[prg % 2][:, (kb % 2) * TCH:(kb % 2 + 1) * TCH],
                                     pts[prg % 2][:, (kb % 2) * TCH:(kb % 2 + 1) * TCH],
                                     G["mbuf"][:, j, :]).then_inc(SS["ev2"].h, 1)
                    v.wait_ge(SS["pe"].h, pe_l(h, s, _nkb(s) - 1))
                    v.tensor_copy(lsb[:], ps_l[0:1, :]).then_inc(SS["ev2"].h, 1)
                    v.wait_ge(SS["ev2"].h, vb + 5)
                    v.reciprocal(rlv[:], lsb[:]).then_inc(SS["ev2"].h, 1)
                    v.wait_ge(SS["ev2"].h, vb + 6)
                    v.tensor_copy(rl_bf[:], rlv[:]).then_inc(SS["ev2"].h, 1)
                    v.wait_ge(SS["pe"].h, hb + _a_pe_s(s) + 3 * _nkb(s) + 1)
                    v.tensor_copy(rb_s[:], ps_sm[:]).then_inc(SS["ev2"].h, 1)
                    gq = h * NSUP + s
                    v.wait_ge(SS["pe"].h, pe_pv(h, s, _nkb(s) - 1))
                    v.wait_ge(SS["ev2"].h, vb + 8)
                    if gq >= 2:
                        v.wait_ge(SS[f"st{gq % 2}"].h, b_st[gq % 2] + 16 * ((gq - 2) // 2 + 1))
                    v.tensor_mul(cxs[gq % 2][:], ps_o[s % 2][:], rb_s[:]).then_inc(SS["ev2"].h, 1)
        block.vector(vc)

        def sy(sy_):
            for h in range(NH):
                for s in range(NSUP):
                    gq = h * NSUP + s
                    sy_.wait_ge(SS["ev2"].h, b_vc + A_VC_HEAD * h + _a_vc_s(s) + 9)
                    sy_.dma_start(out=cc2_t.ap()[h * P:(h + 1) * P, s * TCH:(s + 1) * TCH],
                                  in_=cxs[gq % 2][:]).then_inc(SS[f"st{gq % 2}"].h, 16)
            sy_.wait_ge(SS["st0"].h, b_st[0] + 16 * 32)
            sy_.wait_ge(SS["st1"].h, b_st[1] + 16 * 32)
        block.sync(sy)

        SS["pe"].n = b_pe + A_PE_HEAD * NH
        SS["ev"].n = b_sc + A_SC_HEAD * NH
        SS["ev2"].n = b_vc + A_VC_HEAD * NH
        SS["ldw0"].n = b_kk + 16 * NH
        SS["ldw1"].n = b_kv + 16 * NH
        SS["gq0"].n = b_q + 16 * NH
        SS["st0"].n = b_st[0] + 16 * 32
        SS["st1"].n = b_st[1] + 16 * 32
    barrier(block, SS)



def emit_fwht_T(nc, block, SS, G, wdn_t, tdc_t):
    n = 8192
    nst = 13
    b_ldx = SS["ldx"].n
    b_pe = SS["pe"].n
    b_vc = SS["ev2"].n
    b_st = SS["st0"].n
    VPT = 2 * nst + 1 + 8
    PPT = 64
    with ExitStack() as cx:
        uid = _uid()
        ec = cx.enter_context
        ba = ec(nc.sbuf_tensor(f"ft_ba_{uid}", [P, n], F32))
        bb = ec(nc.sbuf_tensor(f"ft_bb_{uid}", [P, n], F32))
        tcb = ec(nc.sbuf_tensor(f"ft_tcb_{uid}", [P, n], BF16))
        tout = ec(nc.sbuf_tensor(f"ft_tout_{uid}", [P, 8, P], BF16))
        ptr = [ec(nc.psum_tensor(f"ft_ptr{j}_{uid}", [P, 8 * P], BF16)) for j in range(2)]

        def gp(g):
            for i in range(2):
                if i >= 1:
                    g.wait_ge(SS["ev2"].h, b_vc + VPT * i - 8)
                g.dma_start(out=ba[:], in_=wdn_t.ap()[i * P:(i + 1) * P, :]).then_inc(SS["ldx"].h, 16)
        block.gpsimd(gp)

        def vc(v):
            for i in range(2):
                vb = b_vc + VPT * i
                cnt = 0
                v.wait_ge(SS["ldx"].h, b_ldx + 16 * (i + 1))
                if i >= 1:
                    v.wait_ge(SS["st0"].h, b_st + 16 * 8 * i)
                src_, dst = ba, bb
                h = 1
                for stg in range(nst):
                    sv = src_[:].rearrange("p (g two h) -> p g two h", two=2, h=h)
                    dv = dst[:].rearrange("p (g two h) -> p g two h", two=2, h=h)
                    v.tensor_add(dv[:, :, 0, :], sv[:, :, 0, :], sv[:, :, 1, :]).then_inc(SS["ev2"].h, 1)
                    v.tensor_sub(dv[:, :, 1, :], sv[:, :, 0, :], sv[:, :, 1, :]).then_inc(SS["ev2"].h, 1)
                    cnt += 2
                    v.wait_ge(SS["ev2"].h, vb + cnt)
                    src_, dst = dst, src_
                    h *= 2
                v.tensor_copy(tcb[:], src_[:]).then_inc(SS["ev2"].h, 1)
                cnt += 1
                v.wait_ge(SS["ev2"].h, vb + cnt)
                for grp in range(8):
                    v.wait_ge(SS["pe"].h, b_pe + PPT * i + grp * 8 + 8)
                    if i > 0 or grp > 0:
                        v.wait_ge(SS["st0"].h, b_st + 16 * (8 * i + grp))
                    v.tensor_copy(tout[:], ptr[grp % 2][:].rearrange("p (j m) -> p j m", m=P)).then_inc(SS["ev2"].h, 1)
        block.vector(vc)

        def tns(tn):
            for i in range(2):
                tn.wait_ge(SS["ev2"].h, b_vc + VPT * i + 2 * nst + 1)
                for grp in range(8):
                    if i > 0 or grp > 1:
                        gidx = 8 * i + grp
                        pv, gv = divmod(gidx - 2, 8)
                        tn.wait_ge(SS["ev2"].h, b_vc + VPT * pv + 2 * nst + 1 + gv + 1)
                    for j in range(8):
                        tn.transpose(ptr[grp % 2][:, j * P:(j + 1) * P],
                                     tcb[:, (grp * 8 + j) * P:(grp * 8 + j + 1) * P],
                                     G["eye_bf"][:]).then_inc(SS["pe"].h, 1)
        block.tensor(tns)

        def sy(sy_):
            for i in range(2):
                for grp in range(8):
                    sy_.wait_ge(SS["ev2"].h, b_vc + VPT * i + 2 * nst + 1 + grp + 1)
                    sy_.dma_start(out=tdc_t.ap()[grp * 1024:(grp + 1) * 1024, i * P:(i + 1) * P]
                                  .rearrange("(j p) m -> p j m", p=P),
                                  in_=tout[:]).then_inc(SS["st0"].h, 16)
            sy_.wait_ge(SS["st0"].h, b_st + 16 * 16)
        block.sync(sy)

        SS["ldx"].n = b_ldx + 32
        SS["pe"].n = b_pe + 2 * PPT
        SS["ev2"].n = b_vc + 2 * VPT
        SS["st0"].n = b_st + 16 * 16
    barrier(block, SS)


def emit_collective(nc, block, SS, kind, groups, ins, outs, op=None, sem="cc0",
                    fire_only=False):
    from concourse.aot_env import allow_ici_all_to_all

    if op is None:
        op = ALU.bypass

    def gp(g):
        with allow_ici_all_to_all():
            g.collective_compute(kind, op, replica_groups=groups,
                                 ins=ins, outs=outs).then_inc(SS[sem].h, 1)
        if not fire_only:
            g.wait_ge(SS[sem].h, 1)
    block.gpsimd(gp)
    SS[sem].n = 1
    if not fire_only:
        barrier(block, SS)


def emit_dbgcopy(nc, block, SS, dst_t, src_t):
    b_cc = SS["cc"].n

    def gp(g):
        g.dma_start(out=dst_t.ap(), in_=src_t.ap()).then_inc(SS["cc"].h, 16)
        g.wait_ge(SS["cc"].h, b_cc + 16)
    block.gpsimd(gp)
    SS["cc"].n = b_cc + 16
    barrier(block, SS)


def emit_resadd(nc, block, SS, G, res_t, delta_t, bvec_ap=None, src_t=None):
    if src_t is None:
        src_t = res_t
    nm, ntc = D // P, TLOC // TCH
    nidx = nm * ntc
    b_ldr = [SS["ldr0"].n, SS["ldr1"].n]
    b_ldv = SS["ldv"].n
    b_ev2 = SS["ev2"].n
    b_st = [SS["st0"].n, SS["st1"].n]
    has_bias = bvec_ap is not None
    with ExitStack() as cx:
        uid = _uid()
        ec = cx.enter_context
        rb = [ec(nc.sbuf_tensor(f"ra_r{j}_{uid}", [P, TCH], F32)) for j in range(2)]
        db = [ec(nc.sbuf_tensor(f"ra_d{j}_{uid}", [P, TCH], BF16)) for j in range(2)]
        vb = ec(nc.sbuf_tensor(f"ra_vb_{uid}", [P, nm], F32))

        def sy(sy_):
            if has_bias:
                sy_.dma_start(out=vb[:], in_=bvec_ap).then_inc(SS["ldv"].h, 16)
            for idx in range(nidx):
                t, m = divmod(idx, nm)
                if idx >= 2:
                    sy_.wait_ge(SS[f"st{idx % 2}"].h, b_st[idx % 2] + 16 * ((idx - 2) // 2 + 1))
                sy_.dma_start(out=rb[idx % 2][:],
                              in_=src_t.ap()[m * P:(m + 1) * P, t * TCH:(t + 1) * TCH]).then_inc(SS[f"ldr{idx % 2}"].h, 16)
                sy_.dma_start(out=db[idx % 2][:],
                              in_=delta_t.ap()[m * P:(m + 1) * P, t * TCH:(t + 1) * TCH]).then_inc(SS[f"ldr{idx % 2}"].h, 16)
        block.sync(sy)

        def vc(v):
            if has_bias:
                v.wait_ge(SS["ldv"].h, b_ldv + 16)
            for idx in range(nidx):
                t, m = divmod(idx, nm)
                v.wait_ge(SS[f"ldr{idx % 2}"].h, b_ldr[idx % 2] + 32 * (idx // 2 + 1))
                if idx >= 2:
                    v.wait_ge(SS[f"st{idx % 2}"].h, b_st[idx % 2] + 16 * ((idx - 2) // 2 + 1))
                if has_bias:
                    v.scalar_tensor_tensor(rb[idx % 2][:], rb[idx % 2][:], vb[:, m:m + 1],
                                           db[idx % 2][:], ALU.add, ALU.add).then_inc(SS["ev2"].h, 1)
                else:
                    v.tensor_add(rb[idx % 2][:], rb[idx % 2][:], db[idx % 2][:]).then_inc(SS["ev2"].h, 1)
        block.vector(vc)

        def sc(s_):
            for idx in range(nidx):
                t, m = divmod(idx, nm)
                s_.wait_ge(SS["ev2"].h, b_ev2 + idx + 1)
                s_.dma_start(out=res_t.ap()[m * P:(m + 1) * P, t * TCH:(t + 1) * TCH],
                             in_=rb[idx % 2][:]).then_inc(SS[f"st{idx % 2}"].h, 16)
            s_.wait_ge(SS["st0"].h, b_st[0] + 16 * ((nidx + 1) // 2))
            s_.wait_ge(SS["st1"].h, b_st[1] + 16 * (nidx // 2))
        block.scalar(sc)

        SS["ldv"].n = b_ldv + (16 if has_bias else 0)
        SS["ldr0"].n = b_ldr[0] + 32 * ((nidx + 1) // 2)
        SS["ldr1"].n = b_ldr[1] + 32 * (nidx // 2)
        SS["ev2"].n = b_ev2 + nidx
        SS["st0"].n = b_st[0] + 16 * ((nidx + 1) // 2)
        SS["st1"].n = b_st[1] + 16 * (nidx // 2)
    barrier(block, SS)



def build_megakernel():
    nc = bass.Bass("TRN2", target_bir_lowering=False)
    x_fm = nc.dram_tensor("x_fm", [D, TLOC], F32, kind="ExternalInput")
    w_qkv0 = nc.dram_tensor("w_qkv0", [D, 3072], F32, kind="ExternalInput")
    b_qk0 = nc.dram_tensor("b_qk0", [P, 16], F32, kind="ExternalInput")
    b_v0 = nc.dram_tensor("b_v0", [1, 1024], F32, kind="ExternalInput")
    w_out0 = nc.dram_tensor("w_out0", [NH * P, D], F32, kind="ExternalInput")
    b_out0 = nc.dram_tensor("b_out0", [P, 16], F32, kind="ExternalInput")
    h2048 = nc.dram_tensor("h2048", [D, D], F32, kind="ExternalInput")
    su_ug = nc.dram_tensor("su_ug", [P, 16], F32, kind="ExternalInput")
    w_upT = nc.dram_tensor("w_upT", [D, 8192], F32, kind="ExternalInput")
    ws_u = nc.dram_tensor("ws_u", [P, 64], F32, kind="ExternalInput")
    su_dp = nc.dram_tensor("su_dp", [P, 64], F32, kind="ExternalInput")
    w_gateT = nc.dram_tensor("w_gateT", [D, 8192], F32, kind="ExternalInput")
    ws_g = nc.dram_tensor("ws_g", [P, 64], F32, kind="ExternalInput")
    wdn = nc.dram_tensor("wdn", [256, 8192], F32, kind="ExternalInput")
    ws_dp = nc.dram_tensor("ws_dp", [P, 16], F32, kind="ExternalInput")
    w_qkv1 = nc.dram_tensor("w_qkv1", [D, 3072], F32, kind="ExternalInput")
    w_o1 = nc.dram_tensor("w_o1", [NH * P, D], F32, kind="ExternalInput")
    b_o1 = nc.dram_tensor("b_o1", [P, 16], F32, kind="ExternalInput")
    w_up1T = nc.dram_tensor("w_up1T", [D, 8192], F32, kind="ExternalInput")
    us1 = nc.dram_tensor("us1", [P, 64], F32, kind="ExternalInput")
    w_dn1T = nc.dram_tensor("w_dn1T", [8192, D], F32, kind="ExternalInput")
    ds1 = nc.dram_tensor("ds1", [P, 16], F32, kind="ExternalInput")
    osc = nc.dram_tensor("osc", [P, 1], F32, kind="ExternalInput")
    masks_in = nc.dram_tensor("masks_in", [4 * P, TCH], F32, kind="ExternalInput")
    onesk_in = nc.dram_tensor("onesk_in", [P, 1], F32, kind="ExternalInput")
    onesb_in = nc.dram_tensor("onesb_in", [1, P], F32, kind="ExternalInput")
    eye_in = nc.dram_tensor("eye_in", [P, P], F32, kind="ExternalInput")
    eps_in = nc.dram_tensor("eps_in", [P, 1], F32, kind="ExternalInput")
    y = nc.dram_tensor("y", [D, TLOC], F32, kind="ExternalOutput")
    res = nc.dram_tensor("res", [D, TLOC], F32)
    xnc = nc.dram_tensor("xnc", [D, TLOC], BF16)
    xng = nc.dram_tensor("xng", [2 * D, TLOC], BF16)
    qk = nc.dram_tensor("qk", [2 * NH * P, S], BF16)
    vt = nc.dram_tensor("vt", [S, NH * P], BF16)
    ctxl = nc.dram_tensor("ctxl", [NH * P, S], BF16)
    prt = nc.dram_tensor("prt", [2 * D, TLOC], BF16)
    rso = nc.dram_tensor("rso", [D, TLOC], BF16)
    x2t = nc.dram_tensor("x2t", [D, TLOC], BF16)
    xupt = nc.dram_tensor("xupt", [8192, TLOC], BF16)
    x3t = nc.dram_tensor("x3t", [8192, TLOC], BF16)
    hidt = nc.dram_tensor("hidt", [8192, TLOC], BF16)
    tdc = nc.dram_tensor("tdc", [8192, 256], BF16)
    tdg = nc.dram_tensor("tdg", [8 * 8192, 256], BF16)

    PAIRS = [[0, 1], [2, 3], [4, 5], [6, 7]]
    ALL8 = [list(range(8))]

    with ExitStack() as cx:
        uid = _uid()
        ec = cx.enter_context
        ones_bf = ec(nc.sbuf_tensor(f"g_ones_{uid}", [P, 1], BF16))
        onesb_bf = ec(nc.sbuf_tensor(f"g_onesb_{uid}", [1, P], BF16))
        mbuf = ec(nc.sbuf_tensor(f"g_mbuf_{uid}", [P, 4, TCH], BF16))
        eye_bf = ec(nc.sbuf_tensor(f"g_eye_{uid}", [P, P], BF16))
        epst = ec(nc.sbuf_tensor(f"g_eps_{uid}", [P, 1], F32))
        sems = {}
        for nm in ("bar", "ldx", "ldw0", "ldw1", "ldv", "ldr0", "ldr1", "gq0", "gq1", "gldc", "xpms",
                   "vini", "pe", "ev", "ev2", "sg", "st0", "st1", "cc",
                   "ldr2", "ldr3", "cc0", "cc1", "cc2s", "cc3", "cc4"):
            sems[nm] = SemT(ec(nc.semaphore(nm)))
        block = ec(nc.Block())
        SS = sems
        G = {"ones_bf": ones_bf, "onesb_bf": onesb_bf, "mbuf": mbuf,
             "eye_bf": eye_bf, "epst": epst, "ldv_const": 80}

        def pro(g):
            g.dma_start(out=ones_bf[:], in_=onesk_in.ap()).then_inc(SS["gldc"].h, 16)
            g.dma_start(out=onesb_bf[:], in_=onesb_in.ap()).then_inc(SS["gldc"].h, 16)
            g.dma_start(out=mbuf[:], in_=masks_in.ap().rearrange("(i p) n -> p i n", p=P)).then_inc(SS["gldc"].h, 16)
            g.dma_start(out=eye_bf[:], in_=eye_in.ap()).then_inc(SS["gldc"].h, 16)
            g.dma_start(out=epst[:], in_=eps_in.ap()).then_inc(SS["gldc"].h, 16)
            g.wait_ge(SS["gldc"].h, 80)
        block.gpsimd(pro)
        SS["gldc"].n = 80
        barrier(block, SS)

        def fm_chunk(tensor):
            return lambda th: tensor.ap().rearrange("(k p) n -> p k n", p=P)

        def fm_chunk_q(tensor):
            return lambda th: tensor.ap()[:, th * TCH:(th + 1) * TCH].rearrange("(k p) n -> p k n", p=P)

        def xng_chunk(th, xb):
            return [(xb[:, :, r * TLOC:(r + 1) * TLOC],
                     xng.ap()[r * D:(r + 1) * D, :].rearrange("(k p) n -> p k n", p=P))
                    for r in range(2)]

        def xng_tile(tt):
            r, tl = divmod(tt, 16)
            return xng.ap()[r * D:(r + 1) * D, tl * P:(tl + 1) * P].rearrange("(k p) n -> p k n", p=P)

        def w_slab(w, c0=0):
            return lambda si: w.ap()[:, c0 + si * 512:c0 + (si + 1) * 512].rearrange("(k p) n -> p k n", p=P)

        def w_slab_dn1(si):
            return w_dn1T.ap()[:, si * 256:(si + 1) * 256].rearrange("(k p) n -> p k n", p=P)

        def ctxl_chunk(th):
            return ctxl.ap().rearrange("(k p) n -> p k n", p=P)

        def prt_tile(t, m):
            return prt.ap()[(t // 4) * D + m * P:(t // 4) * D + (m + 1) * P,
                            (t % 4) * TCH:(t % 4 + 1) * TCH]

        def w_tdg_slab(si):
            return tdg.ap()[si * 8192:(si + 1) * 8192, :].rearrange("(k p) n -> p k n", p=P)

        def tile_of(tensor):
            return lambda t, m: tensor.ap()[m * P:(m + 1) * P, t * TCH:(t + 1) * TCH]

        def qk_tile(t, m):
            return qk.ap()[m * P:(m + 1) * P, t * TCH:(t + 1) * TCH]

        def vt_tile(tt, mc):
            return vt.ap()[tt * P:(tt + 1) * P, mc * TCH:(mc + 1) * TCH]

        emit_ln(nc, block, SS, G, x_fm.ap(), xnc.ap())
        emit_collective(nc, block, SS, "AllGather", PAIRS, [xnc.ap()], [xng.ap()], sem="cc0")
        emit_matmul(nc, block, SS, G, D, 2048, S, xng_chunk, None, "cast",
                    qk_tile, out_dtype="bf16", has_bias=True, bvec=b_qk0.ap(),
                    wslab_ap=w_slab(w_qkv0), xnparts=2)
        emit_matmul_tm(nc, block, SS, G, D, 1024, S, xng_tile, w_qkv0.ap()[:, 2048:3072],
                       vt_tile, bias_row_ap=b_v0.ap())
        emit_attention(nc, block, SS, G, qk, vt, ctxl)
        emit_matmul(nc, block, SS, G, NH * P, D, S, ctxl_chunk, None, "cast",
                    prt_tile, out_dtype="bf16", wslab_ap=w_slab(w_out0))
        emit_collective(nc, block, SS, "ReduceScatter", PAIRS, [prt.ap()], [rso.ap()], op=ALU.add, sem="cc1")
        emit_resadd(nc, block, SS, G, res, rso, bvec_ap=b_out0.ap(), src_t=x_fm)
        if DBG:
            d_res1 = nc.dram_tensor("d_res1", [D, TLOC], F32, kind="ExternalOutput")
            emit_dbgcopy(nc, block, SS, d_res1, res)
        emit_ln(nc, block, SS, G, res.ap(), xnc.ap())
        emit_matmul(nc, block, SS, G, D, D, TLOC, fm_chunk(xnc), None, "cast",
                    tile_of(x2t), out_dtype="bf16", cmul=1.0 / (64.0 * math.sqrt(D)),
                    xpm_ap=su_ug.ap(), wslab_ap=w_slab(h2048))
        emit_matmul(nc, block, SS, G, D, 8192, TLOC, fm_chunk(x2t), None, "cast",
                    tile_of(xupt), out_dtype="bf16", has_scale=True, svec_ap=ws_u.ap(),
                    cmul=64.0, sfold_ap=su_dp.ap(), wslab_ap=w_slab(w_upT))
        emit_matmul(nc, block, SS, G, D, 8192, TLOC, fm_chunk(x2t), None, "cast",
                    tile_of(x3t), out_dtype="bf16", act="silu", has_scale=True,
                    svec_ap=ws_g.ap(), cmul=64.0, mult_tile_ap=tile_of(xupt),
                    wslab_ap=w_slab(w_gateT))
        emit_fwht_T(nc, block, SS, G, wdn, tdc)
        emit_collective(nc, block, SS, "AllGather", ALL8, [tdc.ap()], [tdg.ap()], sem="cc2s",
                        fire_only=True)
        emit_matmul(nc, block, SS, G, 8192, D, TLOC, fm_chunk_q(x3t), None, "bf16",
                    tile_of(res), out_dtype="f32", has_scale=True, svec_ap=ws_dp.ap(),
                    cmul=1.0 / math.sqrt(8192.0), resid_tile_ap=tile_of(res),
                    wslab_ap=w_tdg_slab, pre_wait="cc2s")
        if DBG:
            d_res2 = nc.dram_tensor("d_res2", [D, TLOC], F32, kind="ExternalOutput")
            emit_dbgcopy(nc, block, SS, d_res2, res)
        emit_ln(nc, block, SS, G, res.ap(), xnc.ap())
        emit_collective(nc, block, SS, "AllGather", PAIRS, [xnc.ap()], [xng.ap()], sem="cc3")
        emit_matmul(nc, block, SS, G, D, 2048, S, xng_chunk, None, "cast",
                    qk_tile, out_dtype="bf16", wslab_ap=w_slab(w_qkv1), xnparts=2)
        emit_matmul_tm(nc, block, SS, G, D, 1024, S, xng_tile, w_qkv1.ap()[:, 2048:3072],
                       vt_tile)
        emit_attention(nc, block, SS, G, qk, vt, ctxl)
        emit_matmul(nc, block, SS, G, NH * P, D, S, ctxl_chunk, None, "cast",
                    prt_tile, out_dtype="bf16", wslab_ap=w_slab(w_o1))
        emit_collective(nc, block, SS, "ReduceScatter", PAIRS, [prt.ap()], [rso.ap()], op=ALU.add, sem="cc4")
        emit_resadd(nc, block, SS, G, res, rso, bvec_ap=b_o1.ap())
        emit_ln(nc, block, SS, G, res.ap(), xnc.ap())
        emit_matmul(nc, block, SS, G, D, 8192, TLOC, fm_chunk(xnc), None, "cast",
                    tile_of(hidt), out_dtype="bf16", act="relu", has_scale=True,
                    svec_ap=us1.ap(), sign_w=True, wslab_ap=w_slab(w_up1T))
        emit_matmul(nc, block, SS, G, 8192, D, TLOC, fm_chunk_q(hidt), None, "cast",
                    tile_of(y), out_dtype="f32", has_scale=True, svec_ap=ds1.ap(),
                    sign_w=True, psc_ap=osc.ap(), resid_tile_ap=tile_of(res),
                    wslab_ap=w_slab_dn1)
        if DBG:
            dbgs = [("d_res", res), ("d_rso", rso), ("d_x2", x2t), ("d_xup", xupt),
                    ("d_x3", x3t), ("d_qk", qk), ("d_vt", vt), ("d_ctxl", ctxl),
                    ("d_xnc", xnc), ("d_hid", hidt), ("d_tdg", tdg), ("d_xng", xng)]
            douts = {}
            for nm, t in dbgs:
                douts[nm] = nc.dram_tensor(nm, list(t.shape), t.dtype, kind="ExternalOutput")
            b_cc = SS["cc"].n

            def dump(g):
                for i, (nm, t) in enumerate(dbgs):
                    g.dma_start(out=douts[nm].ap(), in_=t.ap()).then_inc(SS["cc"].h, 16)
                g.wait_ge(SS["cc"].h, b_cc + 16 * len(dbgs))
            block.gpsimd(dump)
            SS["cc"].n = b_cc + 16 * len(dbgs)
            barrier(block, SS)
    return nc


def _hadamard(n):
    h = np.array([[1.0]], dtype=np.float32)
    while h.shape[0] < n:
        h = np.kron(h, np.array([[1.0, 1.0], [1.0, -1.0]], dtype=np.float32))
    return h


def _prelay(v, n):
    return np.ascontiguousarray(np.asarray(v, dtype=np.float32).reshape(n, P).T)


def make_inmaps(inputs):
    ac = np.ascontiguousarray
    f32 = lambda a: np.asarray(a, dtype=np.float32)
    x = f32(inputs["L0_attn_x"]).reshape(NTOK, D)
    H = _hadamard(D)
    mask = np.zeros((4 * P, TCH), np.float32)
    for j in range(4):
        for kp in range(P):
            mask[j * P + kp, j * P + kp:] = 1.0
    W_in = f32(inputs["L0_attn_W_in"])
    b_in = f32(inputs["L0_attn_b_in"])
    Wq, Wk, Wv = f32(inputs["L1_attn_Wq"]), f32(inputs["L1_attn_Wk"]), f32(inputs["L1_attn_Wv"])
    ws_ug = f32(inputs["L0_ffn_Wscale_ug"])
    in_maps = []
    for c in range(NCORES):
        b, r = divmod(c, 2)
        hs = r * NH
        qcols = slice(hs * P, (hs + NH) * P)
        kcols = slice(D + hs * P, D + (hs + NH) * P)
        vcols = slice(2 * D + hs * P, 2 * D + (hs + NH) * P)
        m = {
            "x_fm": ac(x[b * S + r * TLOC:(b * S + (r + 1) * TLOC)].T),
            "w_qkv0": ac(np.concatenate([W_in[:, qcols], W_in[:, kcols], W_in[:, vcols]], axis=1)),
            "b_qk0": _prelay(np.concatenate([b_in[qcols], b_in[kcols]]), 16),
            "b_v0": ac(b_in[vcols].reshape(1, 1024)),
            "w_out0": ac(f32(inputs["L0_attn_W_out"])[hs * P:(hs + NH) * P, :]),
            "b_out0": _prelay(f32(inputs["L0_attn_b_out"]), 16),
            "h2048": H,
            "su_ug": _prelay(f32(inputs["L0_ffn_SU_ug"]), 16),
            "w_upT": ac(f32(inputs["L0_ffn_W_up"]).T),
            "ws_u": _prelay(ws_ug[:8192], 64),
            "su_dp": _prelay(f32(inputs["L0_ffn_SU_dp"]), 64),
            "w_gateT": ac(f32(inputs["L0_ffn_W_gate"]).T),
            "ws_g": _prelay(ws_ug[8192:], 64),
            "wdn": ac(f32(inputs["L0_ffn_W_down"])[c * 256:(c + 1) * 256]),
            "ws_dp": _prelay(f32(inputs["L0_ffn_Wscale_dp"]), 16),
            "w_qkv1": ac(np.concatenate([Wq[:, qcols.start:qcols.stop],
                                         Wk[:, qcols.start:qcols.stop],
                                         Wv[:, qcols.start:qcols.stop]], axis=1)),
            "w_o1": ac(f32(inputs["L1_attn_Wo"])[hs * P:(hs + NH) * P, :]),
            "b_o1": _prelay(f32(inputs["L1_attn_bo"]), 16),
            "w_up1T": ac(f32(inputs["L1_ffn_up_weight"]).T),
            "us1": _prelay(f32(inputs["L1_ffn_up_scale"]), 64),
            "w_dn1T": ac(f32(inputs["L1_ffn_down_weight"]).T),
            "ds1": _prelay(f32(inputs["L1_ffn_down_scale"]), 16),
            "osc": ac(np.broadcast_to(f32(inputs["L1_ffn_output_scale"]).reshape(1, 1), (P, 1))),
            "masks_in": mask,
            "onesk_in": np.ones((P, 1), np.float32),
            "onesb_in": np.ones((1, P), np.float32),
            "eye_in": np.eye(P, dtype=np.float32),
            "eps_in": np.full((P, 1), EPS, np.float32),
        }
        in_maps.append(m)
    return in_maps


def kernel(**inputs):
    if "nc" not in _CACHE:
        _CACHE["nc"] = build_megakernel()
    nc = _CACHE["nc"]
    in_maps = make_inmaps(inputs)
    res = run_bass_kernel_spmd(nc, in_maps, core_ids=list(range(NCORES)))
    _CACHE["last_results"] = res
    out = np.empty((NTOK, D), np.float32)
    for c in range(NCORES):
        out[c * TLOC:(c + 1) * TLOC] = res.results[c]["y"].T
    return out.reshape(B, S, D)


# version 08
# speedup vs baseline: 1.1550x; 1.0219x over previous
import math
from contextlib import ExitStack

import numpy as np
import ml_dtypes
import concourse.bass as bass
import concourse.mybir as mybir
from concourse.bass_utils import run_bass_kernel_spmd

F32 = mybir.dt.float32
BF16 = mybir.dt.bfloat16
AF = mybir.ActivationFunctionType
ALU = mybir.AluOpType
AX = mybir.AxisListType

P = 128
NCORES = 8
B, S, D = 4, 4096, 2048
NTOK = B * S
TLOC = NTOK // NCORES
TCH = 512
EPS = 1e-5
NH = 8
NSUP = 8
SCL = 1.0 / math.sqrt(128.0)
DBG = False

_CACHE = {}
_UIDC = [0]


def _uid():
    _UIDC[0] += 1
    return _UIDC[0]


class SemT:
    def __init__(self, h):
        self.h = h
        self.n = 0


def barrier(block, SS):
    bar = SS["bar"]
    base = bar.n
    for name in ("sync", "gpsimd", "tensor", "scalar", "vector"):
        def mk(eng):
            eng.nop().then_inc(bar.h, 1)
            eng.wait_ge(bar.h, base + 5)
        getattr(block, name)(mk)
    bar.n += 5



def emit_ln(nc, block, SS, G, src_ap, dst_ap):
    nk = D // P
    ntc = TLOC // TCH
    VPT = 10 + 2 * nk
    SPT = nk + 1
    PPT = nk + 3
    b_ldx, b_pe, b_ev, b_ev2, b_st = SS["ldx"].n, SS["pe"].n, SS["ev"].n, SS["ev2"].n, SS["st0"].n
    with ExitStack() as cx:
        uid = _uid()
        ec = cx.enter_context
        xb = ec(nc.sbuf_tensor(f"ln_xb_{uid}", [P, nk, TCH], BF16))
        sqb = ec(nc.sbuf_tensor(f"ln_sqb_{uid}", [P, TCH], BF16))
        xnb = ec(nc.sbuf_tensor(f"ln_xnb_{uid}", [P, nk, TCH], BF16))
        mrow = ec(nc.sbuf_tensor(f"ln_m_{uid}", [1, TCH], F32))
        erow = ec(nc.sbuf_tensor(f"ln_e_{uid}", [1, TCH], F32))
        msq = ec(nc.sbuf_tensor(f"ln_msq_{uid}", [1, TCH], F32))
        var = ec(nc.sbuf_tensor(f"ln_var_{uid}", [1, TCH], F32))
        srt = ec(nc.sbuf_tensor(f"ln_srt_{uid}", [1, TCH], F32))
        rr = ec(nc.sbuf_tensor(f"ln_rr_{uid}", [1, TCH], F32))
        nmr = ec(nc.sbuf_tensor(f"ln_nmr_{uid}", [1, TCH], F32))
        a_bf = ec(nc.sbuf_tensor(f"ln_abf_{uid}", [1, TCH], BF16))
        bb_bf = ec(nc.sbuf_tensor(f"ln_bbf_{uid}", [1, TCH], BF16))
        a_sb = ec(nc.sbuf_tensor(f"ln_asb_{uid}", [P, TCH], BF16))
        b_sb = ec(nc.sbuf_tensor(f"ln_bsb_{uid}", [P, TCH], BF16))
        ps_m = ec(nc.psum_tensor(f"ln_psm_{uid}", [1, TCH], F32))
        ps_s = ec(nc.psum_tensor(f"ln_pss_{uid}", [1, TCH], F32))
        ps_a = ec(nc.psum_tensor(f"ln_psa_{uid}", [P, TCH], F32))
        ps_b = ec(nc.psum_tensor(f"ln_psb_{uid}", [P, TCH], F32))

        def gp(g):
            for t in range(ntc):
                if t >= 1:
                    g.wait_ge(SS["ev2"].h, b_ev2 + VPT * t)
                g.dma_start(out=xb[:], in_=src_ap[:, t * TCH:(t + 1) * TCH]
                            .rearrange("(k p) n -> p k n", p=P)).then_inc(SS["ldx"].h, 16)
        block.gpsimd(gp)

        def tns(tn):
            tn.wait_ge(SS["gldc"].h, G["ldv_const"])
            for t in range(ntc):
                pb = b_pe + PPT * t
                vbp = b_ev2 + VPT * (t - 1)
                tn.wait_ge(SS["ldx"].h, b_ldx + 16 * (t + 1))
                if t >= 1:
                    tn.wait_ge(SS["ev2"].h, vbp + 1)
                ins = None
                for k in range(nk):
                    ins = tn.matmul(ps_m[:], G["ones_bf"][:], xb[:, k, :],
                                    start=(k == 0), stop=(k == nk - 1))
                ins.then_inc(SS["pe"].h, 1)
                for k in range(nk):
                    tn.wait_ge(SS["ev"].h, b_ev + SPT * t + k + 1)
                    if t >= 1 and k == 0:
                        tn.wait_ge(SS["ev2"].h, vbp + 2)
                    tn.matmul(ps_s[:], G["ones_bf"][:], sqb[:],
                              start=(k == 0), stop=(k == nk - 1)).then_inc(SS["pe"].h, 1)
                vb = b_ev2 + VPT * t
                tn.wait_ge(SS["ev2"].h, vb + 7)
                if t >= 1:
                    tn.wait_ge(SS["ev2"].h, vbp + 9)
                tn.matmul(ps_a[:], G["onesb_bf"][:], a_bf[:], start=True, stop=True).then_inc(SS["pe"].h, 1)
                tn.wait_ge(SS["ev2"].h, vb + 8)
                if t >= 1:
                    tn.wait_ge(SS["ev2"].h, vbp + 10)
                tn.matmul(ps_b[:], G["onesb_bf"][:], bb_bf[:], start=True, stop=True).then_inc(SS["pe"].h, 1)
        block.tensor(tns)

        def sc(s_):
            for t in range(ntc):
                s_.wait_ge(SS["ldx"].h, b_ldx + 16 * (t + 1))
                for k in range(nk):
                    if k >= 1:
                        s_.wait_ge(SS["pe"].h, b_pe + PPT * t + 1 + k)
                    elif t >= 1:
                        s_.wait_ge(SS["pe"].h, b_pe + PPT * t)
                    s_.activation(sqb[:], xb[:, k, :], AF.Square).then_inc(SS["ev"].h, 1)
                s_.wait_ge(SS["ev2"].h, b_ev2 + VPT * t + 4)
                s_.activation(srt[:], var[:], AF.Sqrt, bias=G["epst"][0:1, :]).then_inc(SS["ev"].h, 1)
        block.scalar(sc)

        def vc(v):
            for t in range(ntc):
                vb = b_ev2 + VPT * t
                v.wait_ge(SS["pe"].h, b_pe + PPT * t + 1)
                v.tensor_scalar(mrow[:], ps_m[:], 1.0 / D, None, ALU.mult).then_inc(SS["ev2"].h, 1)
                v.wait_ge(SS["pe"].h, b_pe + PPT * t + 1 + nk)
                v.tensor_scalar(erow[:], ps_s[:], 1.0 / D, None, ALU.mult).then_inc(SS["ev2"].h, 1)
                v.wait_ge(SS["ev2"].h, vb + 2)
                v.tensor_mul(msq[:], mrow[:], mrow[:]).then_inc(SS["ev2"].h, 1)
                v.wait_ge(SS["ev2"].h, vb + 3)
                v.tensor_sub(var[:], erow[:], msq[:]).then_inc(SS["ev2"].h, 1)
                v.wait_ge(SS["ev"].h, b_ev + SPT * t + nk + 1)
                v.reciprocal(rr[:], srt[:]).then_inc(SS["ev2"].h, 1)
                v.wait_ge(SS["ev2"].h, vb + 5)
                v.tensor_mul(nmr[:], mrow[:], rr[:]).then_inc(SS["ev2"].h, 1)
                v.tensor_copy(a_bf[:], rr[:]).then_inc(SS["ev2"].h, 1)
                v.wait_ge(SS["ev2"].h, vb + 6)
                v.tensor_scalar(bb_bf[:], nmr[:], -1.0, None, ALU.mult).then_inc(SS["ev2"].h, 1)
                v.wait_ge(SS["pe"].h, b_pe + PPT * t + nk + 2)
                if t >= 1:
                    v.wait_ge(SS["ev2"].h, b_ev2 + VPT * t)
                v.tensor_copy(a_sb[:], ps_a[:]).then_inc(SS["ev2"].h, 1)
                v.wait_ge(SS["pe"].h, b_pe + PPT * t + nk + 3)
                v.tensor_copy(b_sb[:], ps_b[:]).then_inc(SS["ev2"].h, 1)
                v.wait_ge(SS["ev2"].h, vb + 10)
                if t >= 2:
                    v.wait_ge(SS["st0"].h, b_st + 16 * (t - 1))
                if t >= 1:
                    v.wait_ge(SS["st0"].h, b_st + 16 * t)
                for k in range(nk):
                    v.tensor_mul(xnb[:, k, :], xb[:, k, :], a_sb[:]).then_inc(SS["ev2"].h, 1)
                    v.wait_ge(SS["ev2"].h, vb + 11 + 2 * k)
                    v.tensor_add(xnb[:, k, :], xnb[:, k, :], b_sb[:]).then_inc(SS["ev2"].h, 1)
        block.vector(vc)

        def sy(sy_):
            for t in range(ntc):
                sy_.wait_ge(SS["ev2"].h, b_ev2 + VPT * (t + 1))
                sy_.dma_start(out=dst_ap[:, t * TCH:(t + 1) * TCH].rearrange("(k p) n -> p k n", p=P),
                              in_=xnb[:]).then_inc(SS["st0"].h, 16)
            sy_.wait_ge(SS["st0"].h, b_st + 16 * ntc)
        block.sync(sy)

        SS["ldx"].n = b_ldx + 16 * ntc
        SS["pe"].n = b_pe + PPT * ntc
        SS["ev"].n = b_ev + SPT * ntc
        SS["ev2"].n = b_ev2 + VPT * ntc
        SS["st0"].n = b_st + 16 * ntc
    barrier(block, SS)



def emit_matmul(nc, block, SS, G, K, M, ntok, xchunk, w_ap, w_mode, out_tile_ap,
                out_dtype="bf16", has_bias=False, bvec=None, act="copy",
                has_scale=False, svec_ap=None, cmul=1.0, sfold_ap=None,
                resid_tile_ap=None, sign_w=False, mult_tile_ap=None, psc_ap=None,
                xpm_ap=None, wslab_ap=None, slab_m=4, xnparts=1, pre_wait=None):
    nk, nm = K // P, M // P
    ntc = ntok // TCH
    if K >= 8192:
        nth = 2 if ntc >= 2 else ntc
    else:
        nth = 1
    ntcl = ntc // nth
    if K >= 8192:
        slab_m = 1
    nslab = nm // slab_m
    use_act = act != "copy"
    afunc = {"silu": AF.Silu, "relu": AF.Relu}.get(act)
    has_resid = resid_tile_ap is not None
    has_mult = mult_tile_ap is not None
    has_psc = psc_ap is not None
    has_xpm = xpm_ap is not None
    has_vec_op = (not use_act) or has_mult or has_resid or has_psc
    n_rm = int(has_resid) + int(has_mult)
    if has_psc or sfold_ap is not None:
        assert has_scale
    if use_act:
        assert not has_bias and not has_resid and not has_psc
    scale_pre = has_scale and (cmul != 1.0 or has_psc or sfold_ap is not None)
    n_vini = int(has_scale and cmul != 1.0) + int(has_psc) + int(sfold_ap is not None)
    nidx = ntc * nm

    b_ldx = SS["ldx"].n
    b_ldw = [SS["ldw0"].n, SS["ldw1"].n]
    b_ldv = SS["ldv"].n
    b_ldr = [SS["ldr0"].n, SS["ldr1"].n, SS["ldr2"].n, SS["ldr3"].n]
    b_pe = SS["pe"].n
    b_ev = SS["ev"].n
    b_ev2 = SS["ev2"].n
    b_sg = SS["sg"].n
    b_st = [SS["st0"].n, SS["st1"].n]
    b_xpms = SS["xpms"].n
    b_vini = SS["vini"].n
    odt = BF16 if out_dtype == "bf16" else F32

    NX = xnparts

    def order():
        gi = 0
        for th in range(nth):
            for m in range(nm):
                for tc in range(ntcl):
                    yield gi, th, m, tc, th * ntcl + tc
                    gi += 1

    with ExitStack() as cx:
        uid = _uid()
        ec = cx.enter_context
        xb = ec(nc.sbuf_tensor(f"mm_xb_{uid}", [P, nk, ntcl * TCH], BF16))
        wb = [ec(nc.sbuf_tensor(f"mm_wb{j}_{uid}", [P, nk, slab_m * P], BF16)) for j in range(2)]
        eb = [ec(nc.sbuf_tensor(f"mm_e{j}_{uid}", [P, TCH], odt)) for j in range(4)]
        rb = [ec(nc.sbuf_tensor(f"mm_r{j}_{uid}", [P, TCH], F32)) for j in range(4)]
        mb = [ec(nc.sbuf_tensor(f"mm_m{j}_{uid}", [P, TCH], BF16)) for j in range(4)]
        vecs = ec(nc.sbuf_tensor(f"mm_vecs_{uid}", [P, 3 * nm + nk + 2], F32))
        pb = [ec(nc.psum_tensor(f"mm_pb{j}_{uid}", [P, TCH], F32)) for j in range(4)]
        off_b, off_s, off_w, off_p, off_f = 0, nm, 2 * nm, 2 * nm + nk, 2 * nm + nk + 1
        nvec_dma = int(has_bias) + int(has_scale) + int(has_xpm) + int(has_psc) + int(sfold_ap is not None)

        def sy(sy_):
            if has_bias:
                sy_.dma_start(out=vecs[:, off_b:off_b + nm], in_=bvec).then_inc(SS["ldv"].h, 16)
            if has_scale:
                sy_.dma_start(out=vecs[:, off_s:off_s + nm], in_=svec_ap).then_inc(SS["ldv"].h, 16)
            if has_xpm:
                sy_.dma_start(out=vecs[:, off_w:off_w + nk], in_=xpm_ap).then_inc(SS["ldv"].h, 16)
            if has_psc:
                sy_.dma_start(out=vecs[:, off_p:off_p + 1], in_=psc_ap).then_inc(SS["ldv"].h, 16)
            if sfold_ap is not None:
                sy_.dma_start(out=vecs[:, off_f:off_f + nm], in_=sfold_ap).then_inc(SS["ldv"].h, 16)
            if n_rm:
                for gi, th, m, tc, t in order():
                    if gi >= 4:
                        sy_.wait_ge(SS[f"st{gi % 2}"].h, b_st[gi % 2] + 16 * ((gi - 4) // 2 + 1))
                    if has_resid:
                        sy_.dma_start(out=rb[gi % 4][:], in_=resid_tile_ap(t, m)).then_inc(SS[f"ldr{gi % 4}"].h, 16)
                    if has_mult:
                        sy_.dma_start(out=mb[gi % 4][:], in_=mult_tile_ap(t, m)).then_inc(SS[f"ldr{gi % 4}"].h, 16)
        block.sync(sy)

        def gp(g):
            if pre_wait is not None:
                g.wait_ge(SS[pre_wait].h, 1)
            for th in range(nth):
                if th >= 1:
                    g.wait_ge(SS["pe"].h, b_pe + th * ntcl * nm)
                parts = xchunk2(th)
                if not isinstance(parts, list):
                    parts = [(xb[:], parts)]
                for xo, xi in parts:
                    g.dma_start(out=xo, in_=xi).then_inc(SS["ldx"].h, 16)
                for si in range(nslab):
                    sidx = th * nslab + si
                    if sidx >= 2:
                        g.wait_ge(SS["pe"].h, b_pe + (sidx - 1) * slab_m * ntcl)
                    g.dma_start(out=wb[sidx % 2][:], in_=wslab_ap(si)).then_inc(SS[f"ldw{sidx % 2}"].h, 16)
        def xchunk2(th):
            try:
                return xchunk(th, xb)
            except TypeError:
                return xchunk(th)
        block.gpsimd(gp)

        def tns(tn):
            for gi, th, m, tc, t in order():
                sidx = th * nslab + m // slab_m
                if m == 0 and tc == 0:
                    if has_xpm:
                        tn.wait_ge(SS["xpms"].h, b_xpms + th + 1)
                    else:
                        tn.wait_ge(SS["ldx"].h, b_ldx + 16 * NX * (th + 1))
                if m % slab_m == 0 and tc == 0:
                    if sign_w:
                        tn.wait_ge(SS["sg"].h, b_sg + sidx + 1)
                    else:
                        tn.wait_ge(SS[f"ldw{sidx % 2}"].h, b_ldw[sidx % 2] + 16 * (sidx // 2 + 1))
                if gi >= 4:
                    if use_act:
                        tn.wait_ge(SS["ev"].h, b_ev + gi - 3)
                    else:
                        tn.wait_ge(SS["ev2"].h, b_ev2 + gi - 3)
                ins = None
                wloc = (m % slab_m) * P
                for k in range(nk):
                    ins = tn.matmul(pb[gi % 4][:], wb[sidx % 2][:, k, wloc:wloc + P],
                                    xb[:, k, tc * TCH:(tc + 1) * TCH],
                                    start=(k == 0), stop=(k == nk - 1))
                ins.then_inc(SS["pe"].h, 1)
        block.tensor(tns)

        def sc(s_):
            if use_act and has_scale:
                s_.wait_ge(SS["ldv"].h, b_ldv + 16 * nvec_dma)
                if scale_pre:
                    s_.wait_ge(SS["vini"].h, b_vini + n_vini)
            for gi, th, m, tc, t in order():
                sidx = th * nslab + m // slab_m
                if sign_w and m % slab_m == 0 and tc == 0:
                    s_.wait_ge(SS[f"ldw{sidx % 2}"].h, b_ldw[sidx % 2] + 16 * (sidx // 2 + 1))
                    s_.activation(wb[sidx % 2][:, :, :], wb[sidx % 2][:, :, :], AF.Sign).then_inc(SS["sg"].h, 1)
                if use_act:
                    s_.wait_ge(SS["pe"].h, b_pe + gi + 1)
                    if gi >= 4:
                        s_.wait_ge(SS[f"st{gi % 2}"].h, b_st[gi % 2] + 16 * ((gi - 4) // 2 + 1))
                    kw = {}
                    if has_scale:
                        kw["scale"] = vecs[:, off_s + m:off_s + m + 1]
                    elif cmul != 1.0:
                        kw["scale"] = float(cmul)
                    s_.activation(eb[gi % 4][:], pb[gi % 4][:], afunc, **kw).then_inc(SS["ev"].h, 1)
                if has_vec_op:
                    s_.wait_ge(SS["ev2"].h, b_ev2 + gi + 1)
                else:
                    s_.wait_ge(SS["ev"].h, b_ev + gi + 1)
                s_.dma_start(out=out_tile_ap(t, m), in_=eb[gi % 4][:]).then_inc(SS[f"st{gi % 2}"].h, 16)
            s_.wait_ge(SS["st0"].h, b_st[0] + 16 * ((nidx + 1) // 2))
            s_.wait_ge(SS["st1"].h, b_st[1] + 16 * (nidx // 2))
        block.scalar(sc)

        def vc(v):
            nvi = 0
            if scale_pre:
                v.wait_ge(SS["ldv"].h, b_ldv + 16 * nvec_dma)
                if cmul != 1.0:
                    v.tensor_scalar(vecs[:, off_s:off_s + nm], vecs[:, off_s:off_s + nm],
                                    float(cmul), None, ALU.mult).then_inc(SS["vini"].h, 1)
                    nvi += 1
                    v.wait_ge(SS["vini"].h, b_vini + nvi)
                if has_psc:
                    v.tensor_scalar(vecs[:, off_s:off_s + nm], vecs[:, off_s:off_s + nm],
                                    vecs[:, off_p:off_p + 1], None, ALU.mult).then_inc(SS["vini"].h, 1)
                    nvi += 1
                    v.wait_ge(SS["vini"].h, b_vini + nvi)
                if sfold_ap is not None:
                    v.tensor_mul(vecs[:, off_s:off_s + nm], vecs[:, off_s:off_s + nm],
                                 vecs[:, off_f:off_f + nm]).then_inc(SS["vini"].h, 1)
                    nvi += 1
                    v.wait_ge(SS["vini"].h, b_vini + nvi)
            for gi, th, m, tc, t in order():
                if has_xpm and m == 0 and tc == 0:
                    v.wait_ge(SS["ldv"].h, b_ldv + 16 * nvec_dma)
                    v.wait_ge(SS["ldx"].h, b_ldx + 16 * NX * (th + 1))
                    ins = None
                    for k in range(nk):
                        ins = v.tensor_scalar(xb[:, k, :], xb[:, k, :],
                                              vecs[:, off_w + k:off_w + k + 1], None, ALU.mult)
                    ins.then_inc(SS["xpms"].h, 1)
                if not has_vec_op:
                    continue
                if use_act:
                    v.wait_ge(SS["ev"].h, b_ev + gi + 1)
                    v.wait_ge(SS[f"ldr{gi % 4}"].h, b_ldr[gi % 4] + 16 * n_rm * (gi // 4 + 1))
                    v.tensor_mul(eb[gi % 4][:], eb[gi % 4][:], mb[gi % 4][:]).then_inc(SS["ev2"].h, 1)
                    continue
                v.wait_ge(SS["pe"].h, b_pe + gi + 1)
                if has_bias or has_scale:
                    v.wait_ge(SS["ldv"].h, b_ldv + 16 * nvec_dma)
                if scale_pre:
                    v.wait_ge(SS["vini"].h, b_vini + n_vini)
                if n_rm:
                    v.wait_ge(SS[f"ldr{gi % 4}"].h, b_ldr[gi % 4] + 16 * n_rm * (gi // 4 + 1))
                if gi >= 4:
                    v.wait_ge(SS[f"st{gi % 2}"].h, b_st[gi % 2] + 16 * ((gi - 4) // 2 + 1))
                pbank = pb[gi % 4]
                cur = eb[gi % 4]
                if has_resid:
                    if has_scale:
                        last = v.scalar_tensor_tensor(cur[:], pbank[:], vecs[:, off_s + m:off_s + m + 1],
                                                      rb[gi % 4][:], ALU.mult, ALU.add)
                    elif has_bias:
                        last = v.scalar_tensor_tensor(cur[:], pbank[:], vecs[:, off_b + m:off_b + m + 1],
                                                      rb[gi % 4][:], ALU.add, ALU.add)
                    else:
                        last = v.scalar_tensor_tensor(cur[:], pbank[:], float(cmul),
                                                      rb[gi % 4][:], ALU.mult, ALU.add)
                else:
                    if has_scale and has_bias:
                        last = v.tensor_scalar(cur[:], pbank[:], vecs[:, off_s + m:off_s + m + 1],
                                               vecs[:, off_b + m:off_b + m + 1], ALU.mult, ALU.add)
                    elif has_scale:
                        last = v.tensor_scalar(cur[:], pbank[:], vecs[:, off_s + m:off_s + m + 1],
                                               None, ALU.mult)
                    elif has_bias:
                        last = v.tensor_scalar(cur[:], pbank[:], vecs[:, off_b + m:off_b + m + 1],
                                               None, ALU.add)
                    elif cmul != 1.0:
                        last = v.tensor_scalar(cur[:], pbank[:], float(cmul), None, ALU.mult)
                    else:
                        last = v.tensor_copy(cur[:], pbank[:])
                last.then_inc(SS["ev2"].h, 1)
        block.vector(vc)

        SS["ldx"].n = b_ldx + 16 * NX * nth
        nslabs_tot = nth * nslab
        SS["ldw0"].n = b_ldw[0] + 16 * ((nslabs_tot + 1) // 2)
        SS["ldw1"].n = b_ldw[1] + 16 * (nslabs_tot // 2)
        SS["ldv"].n = b_ldv + 16 * nvec_dma
        for j in range(4):
            SS[f"ldr{j}"].n = b_ldr[j] + 16 * n_rm * ((nidx - j + 3) // 4)
        SS["pe"].n = b_pe + nidx
        SS["ev"].n = b_ev + (nidx if use_act else 0)
        SS["ev2"].n = b_ev2 + (nidx if has_vec_op else 0)
        SS["sg"].n = b_sg + (nslabs_tot if sign_w else 0)
        SS["st0"].n = b_st[0] + 16 * ((nidx + 1) // 2)
        SS["st1"].n = b_st[1] + 16 * (nidx // 2)
        SS["xpms"].n = b_xpms + (nth if has_xpm else 0)
        SS["vini"].n = b_vini + n_vini
    barrier(block, SS)



def emit_matmul_tm(nc, block, SS, G, K, M, ntok, xtile, w_ap_full, out_tile_ap,
                   bias_row_ap=None):
    nk, nmc = K // P, M // TCH
    ntt = ntok // P
    has_bias = bias_row_ap is not None
    b_ldx, b_ldw, b_ldv = SS["ldx"].n, SS["ldw0"].n, SS["ldw1"].n
    b_pe, b_ev2 = SS["pe"].n, SS["ev2"].n
    b_st = [SS["st0"].n, SS["st1"].n]
    with ExitStack() as cx:
        uid = _uid()
        ec = cx.enter_context
        xt = [ec(nc.sbuf_tensor(f"vt_x{j}_{uid}", [P, nk, P], BF16)) for j in range(2)]
        wsb = ec(nc.sbuf_tensor(f"vt_w_{uid}", [P, nk, M], BF16))
        brow = ec(nc.sbuf_tensor(f"vt_br_{uid}", [1, M], F32))
        bsb = ec(nc.sbuf_tensor(f"vt_bs_{uid}", [P, M], F32))
        eb = [ec(nc.sbuf_tensor(f"vt_e{j}_{uid}", [P, TCH], BF16)) for j in range(2)]
        brbf = ec(nc.sbuf_tensor(f"vt_brbf_{uid}", [1, M], BF16))
        pb = [ec(nc.psum_tensor(f"vt_pb{j}_{uid}", [P, TCH], F32)) for j in range(2)]
        ps_bc = ec(nc.psum_tensor(f"vt_psbc_{uid}", [P, TCH], F32))

        def gp(g):
            g.dma_start(out=wsb[:], in_=w_ap_full.rearrange("(k p) m -> p k m", p=P)).then_inc(SS["ldw0"].h, 16)
            if has_bias:
                g.dma_start(out=brow[:], in_=bias_row_ap).then_inc(SS["ldw1"].h, 16)
            for tt in range(ntt):
                if tt >= 2:
                    g.wait_ge(SS["pe"].h, b_pe + nbc + (tt - 1) * nmc)
                g.dma_start(out=xt[tt % 2][:], in_=xtile(tt)).then_inc(SS[f"gq{tt % 2}"].h, 16)
        nbc = (M // TCH) if has_bias else 0

        b_ldr = [SS["gq0"].n, SS["gq1"].n]
        block.gpsimd(gp)

        def tns(tn):
            tn.wait_ge(SS["ldw0"].h, b_ldw + 16)
            if has_bias:
                tn.wait_ge(SS["ldw1"].h, b_ldv + 16)
                for mc in range(M // TCH):
                    tn.wait_ge(SS["ev2"].h, b_ev2 + 1)
                    if mc > 0:
                        tn.wait_ge(SS["ev2"].h, b_ev2 + 1 + mc)
                    tn.matmul(ps_bc[:], G["onesb_bf"][:], brbf[:, mc * TCH:(mc + 1) * TCH],
                              start=True, stop=True).then_inc(SS["pe"].h, 1)
            for tt in range(ntt):
                for mc in range(nmc):
                    idx = tt * nmc + mc
                    if mc == 0:
                        tn.wait_ge(SS[f"gq{tt % 2}"].h, b_ldr[tt % 2] + 16 * (tt // 2 + 1))
                    if idx >= 2:
                        tn.wait_ge(SS["ev2"].h, b_ev2 + 1 + nbc + idx - 1)
                    ins = None
                    for k in range(nk):
                        ins = tn.matmul(pb[idx % 2][:], xt[tt % 2][:, k, :],
                                        wsb[:, k, mc * TCH:(mc + 1) * TCH],
                                        start=(k == 0), stop=(k == nk - 1))
                    ins.then_inc(SS["pe"].h, 1)
        block.tensor(tns)

        def vc(v):
            nb = 0
            if has_bias:
                v.wait_ge(SS["ldw1"].h, b_ldv + 16)
                v.tensor_copy(brbf[:], brow[:]).then_inc(SS["ev2"].h, 1)
                nb = 1
                for mc in range(M // TCH):
                    v.wait_ge(SS["pe"].h, b_pe + mc + 1)
                    v.tensor_copy(bsb[:, mc * TCH:(mc + 1) * TCH], ps_bc[:]).then_inc(SS["ev2"].h, 1)
                nb += M // TCH
            for tt in range(ntt):
                for mc in range(nmc):
                    idx = tt * nmc + mc
                    v.wait_ge(SS["pe"].h, b_pe + nbc + idx + 1)
                    if idx >= 2:
                        v.wait_ge(SS[f"st{idx % 2}"].h, b_st[idx % 2] + 16 * ((idx - 2) // 2 + 1))
                    if has_bias:
                        v.wait_ge(SS["ev2"].h, b_ev2 + 1 + nbc)
                        v.tensor_add(eb[idx % 2][:], pb[idx % 2][:], bsb[:, mc * TCH:(mc + 1) * TCH]).then_inc(SS["ev2"].h, 1)
                    else:
                        v.tensor_copy(eb[idx % 2][:], pb[idx % 2][:]).then_inc(SS["ev2"].h, 1)
        block.vector(vc)

        def sc(s_):
            for tt in range(ntt):
                for mc in range(nmc):
                    idx = tt * nmc + mc
                    s_.wait_ge(SS["ev2"].h, b_ev2 + (1 + nbc if has_bias else 0) + idx + 1)
                    s_.dma_start(out=out_tile_ap(tt, mc), in_=eb[idx % 2][:]).then_inc(SS[f"st{idx % 2}"].h, 16)
            s_.wait_ge(SS["st0"].h, b_st[0] + 16 * ((ntt * nmc + 1) // 2))
            s_.wait_ge(SS["st1"].h, b_st[1] + 16 * ((ntt * nmc) // 2))
        block.scalar(sc)

        nidx = ntt * nmc
        SS["ldw0"].n = b_ldw + 16
        SS["ldw1"].n = b_ldv + (16 if has_bias else 0)
        SS["gq0"].n = b_ldr[0] + 16 * ((ntt + 1) // 2)
        SS["gq1"].n = b_ldr[1] + 16 * (ntt // 2)
        SS["pe"].n = b_pe + nbc + nidx
        SS["ev2"].n = b_ev2 + (1 + nbc if has_bias else 0) + nidx
        SS["st0"].n = b_st[0] + 16 * ((nidx + 1) // 2)
        SS["st1"].n = b_st[1] + 16 * (nidx // 2)
    barrier(block, SS)



def _nkb(s):
    return 4 * (s + 1)


KBS_HEAD = sum(_nkb(s) for s in range(NSUP))
NPR_HEAD = KBS_HEAD // 2
A_PE_HEAD = 17 + sum(3 * _nkb(s) + 1 for s in range(NSUP))
A_SC_HEAD = 4 + NPR_HEAD
A_VC_HEAD = 28 + 9 * NSUP


def _a_pe_s(s):
    return 17 + sum(3 * _nkb(x) + 1 for x in range(s))


def _a_scp_s(s):
    return 4 + sum(_nkb(x) // 2 for x in range(s))


def _a_vc_s(s):
    return 28 + 9 * s


def _a_kbg(s, kb):
    return sum(_nkb(x) for x in range(s)) + kb


def _a_prg_to_spr(prg):
    s = 0
    while prg >= _nkb(s) // 2:
        prg -= _nkb(s) // 2
        s += 1
    return s, prg


def emit_attention(nc, block, SS, G, qk_t, vt_t, cc2_t):
    NKB = S // P
    b_pe, b_sc, b_vc = SS["pe"].n, SS["ev"].n, SS["ev2"].n
    b_kk, b_kv, b_q = SS["ldw0"].n, SS["ldw1"].n, SS["gq0"].n
    b_st = [SS["st0"].n, SS["st1"].n]

    def pe_st(h, s, kb):
        return b_pe + A_PE_HEAD * h + _a_pe_s(s) + 6 * (kb // 2) + 1 + (kb % 2)

    def pe_l(h, s, kb):
        return b_pe + A_PE_HEAD * h + _a_pe_s(s) + 6 * (kb // 2) + 3 + 2 * (kb % 2)

    def pe_pv(h, s, kb):
        return b_pe + A_PE_HEAD * h + _a_pe_s(s) + 6 * (kb // 2) + 4 + 2 * (kb % 2)

    def sc_exp(h, s, pr):
        return b_sc + A_SC_HEAD * h + _a_scp_s(s) + pr + 1

    with ExitStack() as cx:
        uid = _uid()
        ec = cx.enter_context
        kbuf = [ec(nc.sbuf_tensor(f"at_kbuf{j}_{uid}", [P, S], BF16)) for j in range(2)]
        vbuf = [ec(nc.sbuf_tensor(f"at_vbuf{j}_{uid}", [P, NKB, P], BF16)) for j in range(2)]
        qh = [ec(nc.sbuf_tensor(f"at_qh{j}_{uid}", [P, S], BF16)) for j in range(2)]
        ksqf = ec(nc.sbuf_tensor(f"at_ksqf_{uid}", [P, S], BF16))
        qsqf = ec(nc.sbuf_tensor(f"at_qsqf_{uid}", [P, S], BF16))
        ksn = ec(nc.sbuf_tensor(f"at_ksn_{uid}", [1, S], F32))
        qsn = ec(nc.sbuf_tensor(f"at_qsn_{uid}", [1, S], F32))
        km2 = ec(nc.sbuf_tensor(f"at_km2_{uid}", [1, 1], F32))
        kms = ec(nc.sbuf_tensor(f"at_kms_{uid}", [1, 1], F32))
        qm2 = ec(nc.sbuf_tensor(f"at_qm2_{uid}", [1, NSUP], F32))
        qmv = ec(nc.sbuf_tensor(f"at_qmv_{uid}", [1, NSUP], F32))
        negcb = ec(nc.sbuf_tensor(f"at_negcb_{uid}", [1, NSUP], F32))
        negc_bf = ec(nc.sbuf_tensor(f"at_negcbf_{uid}", [1, NSUP], BF16))
        bias_all = ec(nc.sbuf_tensor(f"at_bias_{uid}", [P, NSUP], F32))
        pts = [ec(nc.sbuf_tensor(f"at_pt{j}_{uid}", [P, 2 * TCH], BF16)) for j in range(2)]
        lsb = ec(nc.sbuf_tensor(f"at_lsb_{uid}", [1, TCH], F32))
        rlv = ec(nc.sbuf_tensor(f"at_rlv_{uid}", [1, TCH], F32))
        rl_bf = ec(nc.sbuf_tensor(f"at_rlbf_{uid}", [1, TCH], BF16))
        rb_s = ec(nc.sbuf_tensor(f"at_rbs_{uid}", [P, TCH], F32))
        cxs = [ec(nc.sbuf_tensor(f"at_cxs{j}_{uid}", [P, TCH], BF16)) for j in range(2)]
        pss = [ec(nc.psum_tensor(f"at_pss{j}_{uid}", [P, 2 * TCH], F32)) for j in range(2)]
        ps_o = [ec(nc.psum_tensor(f"at_pso{j}_{uid}", [P, TCH], F32)) for j in range(2)]
        ps_l = ec(nc.psum_tensor(f"at_psl_{uid}", [1, TCH], F32))
        ps_sm = ec(nc.psum_tensor(f"at_pssm_{uid}", [P, TCH], F32))

        def gp(g):
            for h in range(NH):
                if h >= 2:
                    g.wait_ge(SS["pe"].h, b_pe + A_PE_HEAD * (h - 1))
                g.dma_start(out=kbuf[h % 2][:], in_=qk_t.ap()[(NH + h) * P:(NH + h + 1) * P, :]).then_inc(SS["ldw0"].h, 16)
                g.dma_start(out=vbuf[h % 2][:], in_=vt_t.ap()[:, h * P:(h + 1) * P]
                            .rearrange("(b p) d -> p b d", p=P)).then_inc(SS["ldw1"].h, 16)
                g.dma_start(out=qh[h % 2][:], in_=qk_t.ap()[h * P:(h + 1) * P, :]).then_inc(SS["gq0"].h, 16)
        block.gpsimd(gp)

        def tns(tn):
            for h in range(NH):
                hb = b_pe + A_PE_HEAD * h
                vcb = b_vc + A_VC_HEAD * h
                tn.wait_ge(SS["ldw0"].h, b_kk + 16 * (h + 1))
                tn.wait_ge(SS["ldw1"].h, b_kv + 16 * (h + 1))
                tn.wait_ge(SS["gq0"].h, b_q + 16 * (h + 1))
                for c in range(8):
                    if c == 0:
                        tn.wait_ge(SS["ev"].h, b_sc + A_SC_HEAD * h + 1)
                        if h > 0:
                            tn.wait_ge(SS["ev2"].h, b_vc + A_VC_HEAD * h)
                    else:
                        tn.wait_ge(SS["ev2"].h, vcb + c)
                    tn.matmul(ps_sm[0:1, :], G["ones_bf"][:], ksqf[:, c * TCH:(c + 1) * TCH],
                              start=True, stop=True).then_inc(SS["pe"].h, 1)
                for c in range(8):
                    if c == 0:
                        tn.wait_ge(SS["ev"].h, b_sc + A_SC_HEAD * h + 2)
                        tn.wait_ge(SS["ev2"].h, vcb + 8)
                    else:
                        tn.wait_ge(SS["ev2"].h, vcb + 9 + c)
                    tn.matmul(ps_sm[0:1, :], G["ones_bf"][:], qsqf[:, c * TCH:(c + 1) * TCH],
                              start=True, stop=True).then_inc(SS["pe"].h, 1)
                tn.wait_ge(SS["ev2"].h, vcb + 27)
                tn.matmul(ps_sm[:, 0:NSUP], G["onesb_bf"][:], negc_bf[:], start=True, stop=True).then_inc(SS["pe"].h, 1)
                for s in range(NSUP):
                    sb = hb + _a_pe_s(s)
                    for pr in range(_nkb(s) // 2):
                        kb0, kb1 = 2 * pr, 2 * pr + 1
                        prg = _a_kbg(s, kb0) // 2
                        if prg >= 2:
                            s2, p2 = _a_prg_to_spr(prg - 2)
                            tn.wait_ge(SS["ev"].h, sc_exp(h, s2, p2))
                        elif h > 0:
                            s2, p2 = _a_prg_to_spr(NPR_HEAD - 2 + prg)
                            tn.wait_ge(SS["ev"].h, sc_exp(h - 1, s2, p2))
                        for e in (0, 1):
                            kb = 2 * pr + e
                            tn.matmul(pss[prg % 2][:, e * TCH:(e + 1) * TCH],
                                      kbuf[h % 2][:, kb * P:(kb + 1) * P],
                                      qh[h % 2][:, s * TCH:(s + 1) * TCH], start=True, stop=True).then_inc(SS["pe"].h, 1)
                        tn.wait_ge(SS["ev"].h, sc_exp(h, s, pr))
                        for e in (0, 1):
                            kb = 2 * pr + e
                            if kb >= 4 * s:
                                tn.wait_ge(SS["ev2"].h, vcb + _a_vc_s(s) + (kb - 4 * s) + 1)
                            if kb == 0:
                                if s >= 2:
                                    tn.wait_ge(SS["ev2"].h, vcb + _a_vc_s(s - 2) + 9)
                                elif h > 0:
                                    tn.wait_ge(SS["ev2"].h, b_vc + A_VC_HEAD * (h - 1) + _a_vc_s(NSUP - 2 + s) + 9)
                                if s >= 1:
                                    tn.wait_ge(SS["ev2"].h, vcb + _a_vc_s(s - 1) + 5)
                                elif h > 0:
                                    tn.wait_ge(SS["ev2"].h, b_vc + A_VC_HEAD * (h - 1) + _a_vc_s(NSUP - 1) + 5)
                            tn.matmul(ps_l[:], G["ones_bf"][:],
                                      pts[prg % 2][:, e * TCH:(e + 1) * TCH],
                                      start=(kb == 0), stop=(kb == _nkb(s) - 1)).then_inc(SS["pe"].h, 1)
                            tn.matmul(ps_o[s % 2][:], vbuf[h % 2][:, kb, :],
                                      pts[prg % 2][:, e * TCH:(e + 1) * TCH],
                                      start=(kb == 0), stop=(kb == _nkb(s) - 1)).then_inc(SS["pe"].h, 1)
                    tn.wait_ge(SS["ev2"].h, vcb + _a_vc_s(s) + 7)
                    if s > 0:
                        tn.wait_ge(SS["ev2"].h, vcb + _a_vc_s(s - 1) + 8)
                    tn.matmul(ps_sm[:], G["onesb_bf"][:], rl_bf[:], start=True, stop=True).then_inc(SS["pe"].h, 1)
        block.tensor(tns)

        def sc(s_):
            for h in range(NH):
                s_.wait_ge(SS["ldw0"].h, b_kk + 16 * (h + 1))
                if h >= 2:
                    s_.wait_ge(SS["pe"].h, b_pe + A_PE_HEAD * (h - 1))
                s_.activation(ksqf[:], kbuf[h % 2][:], AF.Square).then_inc(SS["ev"].h, 1)
                s_.wait_ge(SS["gq0"].h, b_q + 16 * (h + 1))
                s_.activation(qsqf[:], qh[h % 2][:], AF.Square).then_inc(SS["ev"].h, 1)
                s_.wait_ge(SS["ev2"].h, b_vc + A_VC_HEAD * h + 9)
                s_.activation(kms[:], km2[:], AF.Sqrt).then_inc(SS["ev"].h, 1)
                s_.wait_ge(SS["ev2"].h, b_vc + A_VC_HEAD * h + 25)
                s_.activation(qmv[:], qm2[:], AF.Sqrt).then_inc(SS["ev"].h, 1)
                for s in range(NSUP):
                    for pr in range(_nkb(s) // 2):
                        kb1 = 2 * pr + 1
                        prg = _a_kbg(s, 0) // 2 + pr
                        s_.wait_ge(SS["pe"].h, pe_st(h, s, kb1))
                        s_.wait_ge(SS["ev2"].h, b_vc + A_VC_HEAD * h + 28)
                        if prg >= 2:
                            s4, p4 = _a_prg_to_spr(prg - 2)
                            s_.wait_ge(SS["pe"].h, pe_pv(h, s4, 2 * p4 + 1))
                        elif h > 0:
                            s4, p4 = _a_prg_to_spr(NPR_HEAD - 2 + prg)
                            s_.wait_ge(SS["pe"].h, pe_pv(h - 1, s4, 2 * p4 + 1))
                        s_.activation(pts[prg % 2][:], pss[prg % 2][:], AF.Exp,
                                      bias=bias_all[:, s:s + 1], scale=SCL).then_inc(SS["ev"].h, 1)
        block.scalar(sc)

        def vc(v):
            for h in range(NH):
                vcb = b_vc + A_VC_HEAD * h
                hb = b_pe + A_PE_HEAD * h
                for c in range(8):
                    v.wait_ge(SS["pe"].h, hb + c + 1)
                    v.tensor_copy(ksn[:, c * TCH:(c + 1) * TCH], ps_sm[0:1, :]).then_inc(SS["ev2"].h, 1)
                v.wait_ge(SS["ev2"].h, vcb + 8)
                v.reduce_max(km2[:], ksn[:], axis=AX.X).then_inc(SS["ev2"].h, 1)
                for c in range(8):
                    v.wait_ge(SS["pe"].h, hb + 8 + c + 1)
                    v.tensor_copy(qsn[:, c * TCH:(c + 1) * TCH], ps_sm[0:1, :]).then_inc(SS["ev2"].h, 1)
                for c in range(8):
                    v.wait_ge(SS["ev2"].h, vcb + 9 + c + 1)
                    v.reduce_max(qm2[:, c:c + 1], qsn[:, c * TCH:(c + 1) * TCH], axis=AX.X).then_inc(SS["ev2"].h, 1)
                v.wait_ge(SS["ev"].h, b_sc + A_SC_HEAD * h + 4)
                v.tensor_scalar(negcb[:], qmv[:], kms[0:1, :], -SCL, ALU.mult, ALU.mult).then_inc(SS["ev2"].h, 1)
                v.wait_ge(SS["ev2"].h, vcb + 26)
                v.tensor_copy(negc_bf[:], negcb[:]).then_inc(SS["ev2"].h, 1)
                v.wait_ge(SS["pe"].h, hb + 17)
                if h > 0:
                    v.wait_ge(SS["ev"].h, b_sc + A_SC_HEAD * h)
                v.tensor_copy(bias_all[:], ps_sm[:, 0:NSUP]).then_inc(SS["ev2"].h, 1)
                for s in range(NSUP):
                    vb = vcb + _a_vc_s(s)
                    for j in range(4):
                        kb = 4 * s + j
                        kbg = _a_kbg(s, kb)
                        prg = kbg // 2
                        v.wait_ge(SS["ev"].h, sc_exp(h, s, kb // 2))
                        v.tensor_mul(pts[prg % 2][:, (kb % 2) * TCH:(kb % 2 + 1) * TCH],
                                     pts[prg % 2][:, (kb % 2) * TCH:(kb % 2 + 1) * TCH],
                                     G["mbuf"][:, j, :]).then_inc(SS["ev2"].h, 1)
                    v.wait_ge(SS["pe"].h, pe_l(h, s, _nkb(s) - 1))
                    v.tensor_copy(lsb[:], ps_l[0:1, :]).then_inc(SS["ev2"].h, 1)
                    v.wait_ge(SS["ev2"].h, vb + 5)
                    v.reciprocal(rlv[:], lsb[:]).then_inc(SS["ev2"].h, 1)
                    v.wait_ge(SS["ev2"].h, vb + 6)
                    v.tensor_copy(rl_bf[:], rlv[:]).then_inc(SS["ev2"].h, 1)
                    v.wait_ge(SS["pe"].h, hb + _a_pe_s(s) + 3 * _nkb(s) + 1)
                    v.tensor_copy(rb_s[:], ps_sm[:]).then_inc(SS["ev2"].h, 1)
                    gq = h * NSUP + s
                    v.wait_ge(SS["pe"].h, pe_pv(h, s, _nkb(s) - 1))
                    v.wait_ge(SS["ev2"].h, vb + 8)
                    if gq >= 2:
                        v.wait_ge(SS[f"st{gq % 2}"].h, b_st[gq % 2] + 16 * ((gq - 2) // 2 + 1))
                    v.tensor_mul(cxs[gq % 2][:], ps_o[s % 2][:], rb_s[:]).then_inc(SS["ev2"].h, 1)
        block.vector(vc)

        def sy(sy_):
            for h in range(NH):
                for s in range(NSUP):
                    gq = h * NSUP + s
                    sy_.wait_ge(SS["ev2"].h, b_vc + A_VC_HEAD * h + _a_vc_s(s) + 9)
                    sy_.dma_start(out=cc2_t.ap()[h * P:(h + 1) * P, s * TCH:(s + 1) * TCH],
                                  in_=cxs[gq % 2][:]).then_inc(SS[f"st{gq % 2}"].h, 16)
            sy_.wait_ge(SS["st0"].h, b_st[0] + 16 * 32)
            sy_.wait_ge(SS["st1"].h, b_st[1] + 16 * 32)
        block.sync(sy)

        SS["pe"].n = b_pe + A_PE_HEAD * NH
        SS["ev"].n = b_sc + A_SC_HEAD * NH
        SS["ev2"].n = b_vc + A_VC_HEAD * NH
        SS["ldw0"].n = b_kk + 16 * NH
        SS["ldw1"].n = b_kv + 16 * NH
        SS["gq0"].n = b_q + 16 * NH
        SS["st0"].n = b_st[0] + 16 * 32
        SS["st1"].n = b_st[1] + 16 * 32
    barrier(block, SS)



def emit_fwht_T(nc, block, SS, G, wdn_t, tdc_t):
    n = 8192
    nst = 13
    b_ldx = SS["ldx"].n
    b_pe = SS["pe"].n
    b_vc = SS["ev2"].n
    b_st = SS["st0"].n
    VPT = 2 * nst + 1 + 8
    PPT = 64
    with ExitStack() as cx:
        uid = _uid()
        ec = cx.enter_context
        ba = ec(nc.sbuf_tensor(f"ft_ba_{uid}", [P, n], F32))
        bb = ec(nc.sbuf_tensor(f"ft_bb_{uid}", [P, n], F32))
        tcb = ec(nc.sbuf_tensor(f"ft_tcb_{uid}", [P, n], BF16))
        tout = ec(nc.sbuf_tensor(f"ft_tout_{uid}", [P, 8, P], BF16))
        ptr = [ec(nc.psum_tensor(f"ft_ptr{j}_{uid}", [P, 8 * P], BF16)) for j in range(2)]

        def gp(g):
            for i in range(2):
                if i >= 1:
                    g.wait_ge(SS["ev2"].h, b_vc + VPT * i - 8)
                g.dma_start(out=ba[:], in_=wdn_t.ap()[i * P:(i + 1) * P, :]).then_inc(SS["ldx"].h, 16)
        block.gpsimd(gp)

        def vc(v):
            for i in range(2):
                vb = b_vc + VPT * i
                cnt = 0
                v.wait_ge(SS["ldx"].h, b_ldx + 16 * (i + 1))
                if i >= 1:
                    v.wait_ge(SS["st0"].h, b_st + 16 * 8 * i)
                src_, dst = ba, bb
                h = 1
                for stg in range(nst):
                    sv = src_[:].rearrange("p (g two h) -> p g two h", two=2, h=h)
                    dv = dst[:].rearrange("p (g two h) -> p g two h", two=2, h=h)
                    v.tensor_add(dv[:, :, 0, :], sv[:, :, 0, :], sv[:, :, 1, :]).then_inc(SS["ev2"].h, 1)
                    v.tensor_sub(dv[:, :, 1, :], sv[:, :, 0, :], sv[:, :, 1, :]).then_inc(SS["ev2"].h, 1)
                    cnt += 2
                    v.wait_ge(SS["ev2"].h, vb + cnt)
                    src_, dst = dst, src_
                    h *= 2
                v.tensor_copy(tcb[:], src_[:]).then_inc(SS["ev2"].h, 1)
                cnt += 1
                v.wait_ge(SS["ev2"].h, vb + cnt)
                for grp in range(8):
                    v.wait_ge(SS["pe"].h, b_pe + PPT * i + grp * 8 + 8)
                    if i > 0 or grp > 0:
                        v.wait_ge(SS["st0"].h, b_st + 16 * (8 * i + grp))
                    v.tensor_copy(tout[:], ptr[grp % 2][:].rearrange("p (j m) -> p j m", m=P)).then_inc(SS["ev2"].h, 1)
        block.vector(vc)

        def tns(tn):
            for i in range(2):
                tn.wait_ge(SS["ev2"].h, b_vc + VPT * i + 2 * nst + 1)
                for grp in range(8):
                    if i > 0 or grp > 1:
                        gidx = 8 * i + grp
                        pv, gv = divmod(gidx - 2, 8)
                        tn.wait_ge(SS["ev2"].h, b_vc + VPT * pv + 2 * nst + 1 + gv + 1)
                    for j in range(8):
                        tn.transpose(ptr[grp % 2][:, j * P:(j + 1) * P],
                                     tcb[:, (grp * 8 + j) * P:(grp * 8 + j + 1) * P],
                                     G["eye_bf"][:]).then_inc(SS["pe"].h, 1)
        block.tensor(tns)

        def sy(sy_):
            for i in range(2):
                for grp in range(8):
                    sy_.wait_ge(SS["ev2"].h, b_vc + VPT * i + 2 * nst + 1 + grp + 1)
                    sy_.dma_start(out=tdc_t.ap()[grp * 1024:(grp + 1) * 1024, i * P:(i + 1) * P]
                                  .rearrange("(j p) m -> p j m", p=P),
                                  in_=tout[:]).then_inc(SS["st0"].h, 16)
            sy_.wait_ge(SS["st0"].h, b_st + 16 * 16)
        block.sync(sy)

        SS["ldx"].n = b_ldx + 32
        SS["pe"].n = b_pe + 2 * PPT
        SS["ev2"].n = b_vc + 2 * VPT
        SS["st0"].n = b_st + 16 * 16
    barrier(block, SS)


def emit_collective(nc, block, SS, kind, groups, ins, outs, op=None, sem="cc0",
                    fire_only=False):
    from concourse.aot_env import allow_ici_all_to_all

    if op is None:
        op = ALU.bypass

    def gp(g):
        with allow_ici_all_to_all():
            g.collective_compute(kind, op, replica_groups=groups,
                                 ins=ins, outs=outs).then_inc(SS[sem].h, 1)
        if not fire_only:
            g.wait_ge(SS[sem].h, 1)
    block.gpsimd(gp)
    SS[sem].n = 1
    if not fire_only:
        barrier(block, SS)


def emit_dbgcopy(nc, block, SS, dst_t, src_t):
    b_cc = SS["cc"].n

    def gp(g):
        g.dma_start(out=dst_t.ap(), in_=src_t.ap()).then_inc(SS["cc"].h, 16)
        g.wait_ge(SS["cc"].h, b_cc + 16)
    block.gpsimd(gp)
    SS["cc"].n = b_cc + 16
    barrier(block, SS)


def emit_resadd(nc, block, SS, G, res_t, delta_t, bvec_ap=None, src_t=None):
    if src_t is None:
        src_t = res_t
    nm, ntc = D // P, TLOC // TCH
    nidx = nm * ntc
    b_ldr = [SS["ldr0"].n, SS["ldr1"].n]
    b_ldv = SS["ldv"].n
    b_ev2 = SS["ev2"].n
    b_st = [SS["st0"].n, SS["st1"].n]
    has_bias = bvec_ap is not None
    with ExitStack() as cx:
        uid = _uid()
        ec = cx.enter_context
        rb = [ec(nc.sbuf_tensor(f"ra_r{j}_{uid}", [P, TCH], F32)) for j in range(2)]
        db = [ec(nc.sbuf_tensor(f"ra_d{j}_{uid}", [P, TCH], BF16)) for j in range(2)]
        vb = ec(nc.sbuf_tensor(f"ra_vb_{uid}", [P, nm], F32))

        def sy(sy_):
            if has_bias:
                sy_.dma_start(out=vb[:], in_=bvec_ap).then_inc(SS["ldv"].h, 16)
            for idx in range(nidx):
                t, m = divmod(idx, nm)
                if idx >= 2:
                    sy_.wait_ge(SS[f"st{idx % 2}"].h, b_st[idx % 2] + 16 * ((idx - 2) // 2 + 1))
                sy_.dma_start(out=rb[idx % 2][:],
                              in_=src_t.ap()[m * P:(m + 1) * P, t * TCH:(t + 1) * TCH]).then_inc(SS[f"ldr{idx % 2}"].h, 16)
                sy_.dma_start(out=db[idx % 2][:],
                              in_=delta_t.ap()[m * P:(m + 1) * P, t * TCH:(t + 1) * TCH]).then_inc(SS[f"ldr{idx % 2}"].h, 16)
        block.sync(sy)

        def vc(v):
            if has_bias:
                v.wait_ge(SS["ldv"].h, b_ldv + 16)
            for idx in range(nidx):
                t, m = divmod(idx, nm)
                v.wait_ge(SS[f"ldr{idx % 2}"].h, b_ldr[idx % 2] + 32 * (idx // 2 + 1))
                if idx >= 2:
                    v.wait_ge(SS[f"st{idx % 2}"].h, b_st[idx % 2] + 16 * ((idx - 2) // 2 + 1))
                if has_bias:
                    v.scalar_tensor_tensor(rb[idx % 2][:], rb[idx % 2][:], vb[:, m:m + 1],
                                           db[idx % 2][:], ALU.add, ALU.add).then_inc(SS["ev2"].h, 1)
                else:
                    v.tensor_add(rb[idx % 2][:], rb[idx % 2][:], db[idx % 2][:]).then_inc(SS["ev2"].h, 1)
        block.vector(vc)

        def sc(s_):
            for idx in range(nidx):
                t, m = divmod(idx, nm)
                s_.wait_ge(SS["ev2"].h, b_ev2 + idx + 1)
                s_.dma_start(out=res_t.ap()[m * P:(m + 1) * P, t * TCH:(t + 1) * TCH],
                             in_=rb[idx % 2][:]).then_inc(SS[f"st{idx % 2}"].h, 16)
            s_.wait_ge(SS["st0"].h, b_st[0] + 16 * ((nidx + 1) // 2))
            s_.wait_ge(SS["st1"].h, b_st[1] + 16 * (nidx // 2))
        block.scalar(sc)

        SS["ldv"].n = b_ldv + (16 if has_bias else 0)
        SS["ldr0"].n = b_ldr[0] + 32 * ((nidx + 1) // 2)
        SS["ldr1"].n = b_ldr[1] + 32 * (nidx // 2)
        SS["ev2"].n = b_ev2 + nidx
        SS["st0"].n = b_st[0] + 16 * ((nidx + 1) // 2)
        SS["st1"].n = b_st[1] + 16 * (nidx // 2)
    barrier(block, SS)



def build_megakernel():
    nc = bass.Bass("TRN2", target_bir_lowering=False)
    x_fm = nc.dram_tensor("x_fm", [D, TLOC], F32, kind="ExternalInput")
    w_qkv0 = nc.dram_tensor("w_qkv0", [D, 3072], F32, kind="ExternalInput")
    b_qk0 = nc.dram_tensor("b_qk0", [P, 16], F32, kind="ExternalInput")
    b_v0 = nc.dram_tensor("b_v0", [1, 1024], F32, kind="ExternalInput")
    w_out0 = nc.dram_tensor("w_out0", [NH * P, D], F32, kind="ExternalInput")
    b_out0 = nc.dram_tensor("b_out0", [P, 16], F32, kind="ExternalInput")
    h2048 = nc.dram_tensor("h2048", [D, D], F32, kind="ExternalInput")
    su_ug = nc.dram_tensor("su_ug", [P, 16], F32, kind="ExternalInput")
    w_upT = nc.dram_tensor("w_upT", [D, 8192], F32, kind="ExternalInput")
    ws_u = nc.dram_tensor("ws_u", [P, 64], F32, kind="ExternalInput")
    su_dp = nc.dram_tensor("su_dp", [P, 64], F32, kind="ExternalInput")
    w_gateT = nc.dram_tensor("w_gateT", [D, 8192], F32, kind="ExternalInput")
    ws_g = nc.dram_tensor("ws_g", [P, 64], F32, kind="ExternalInput")
    wdn = nc.dram_tensor("wdn", [256, 8192], F32, kind="ExternalInput")
    ws_dp = nc.dram_tensor("ws_dp", [P, 16], F32, kind="ExternalInput")
    w_qkv1 = nc.dram_tensor("w_qkv1", [D, 3072], F32, kind="ExternalInput")
    w_o1 = nc.dram_tensor("w_o1", [NH * P, D], F32, kind="ExternalInput")
    b_o1 = nc.dram_tensor("b_o1", [P, 16], F32, kind="ExternalInput")
    w_up1T = nc.dram_tensor("w_up1T", [D, 8192], F32, kind="ExternalInput")
    us1 = nc.dram_tensor("us1", [P, 64], F32, kind="ExternalInput")
    w_dn1T = nc.dram_tensor("w_dn1T", [8192, D], F32, kind="ExternalInput")
    ds1 = nc.dram_tensor("ds1", [P, 16], F32, kind="ExternalInput")
    osc = nc.dram_tensor("osc", [P, 1], F32, kind="ExternalInput")
    masks_in = nc.dram_tensor("masks_in", [4 * P, TCH], F32, kind="ExternalInput")
    onesk_in = nc.dram_tensor("onesk_in", [P, 1], F32, kind="ExternalInput")
    onesb_in = nc.dram_tensor("onesb_in", [1, P], F32, kind="ExternalInput")
    eye_in = nc.dram_tensor("eye_in", [P, P], F32, kind="ExternalInput")
    eps_in = nc.dram_tensor("eps_in", [P, 1], F32, kind="ExternalInput")
    y = nc.dram_tensor("y", [D, TLOC], F32, kind="ExternalOutput")
    res = nc.dram_tensor("res", [D, TLOC], F32)
    xnc = nc.dram_tensor("xnc", [D, TLOC], BF16)
    xng = nc.dram_tensor("xng", [2 * D, TLOC], BF16)
    qk = nc.dram_tensor("qk", [2 * NH * P, S], BF16)
    vt = nc.dram_tensor("vt", [S, NH * P], BF16)
    ctxl = nc.dram_tensor("ctxl", [NH * P, S], BF16)
    prt = nc.dram_tensor("prt", [2 * D, TLOC], BF16)
    rso = nc.dram_tensor("rso", [D, TLOC], BF16)
    x2t = nc.dram_tensor("x2t", [D, TLOC], BF16)
    xupt = nc.dram_tensor("xupt", [8192, TLOC], BF16)
    x3t = nc.dram_tensor("x3t", [8192, TLOC], BF16)
    hidt = nc.dram_tensor("hidt", [8192, TLOC], BF16)
    tdc = nc.dram_tensor("tdc", [8192, 256], BF16)
    tdg = nc.dram_tensor("tdg", [8 * 8192, 256], BF16)

    PAIRS = [[0, 1], [2, 3], [4, 5], [6, 7]]
    ALL8 = [list(range(8))]

    with ExitStack() as cx:
        uid = _uid()
        ec = cx.enter_context
        ones_bf = ec(nc.sbuf_tensor(f"g_ones_{uid}", [P, 1], BF16))
        onesb_bf = ec(nc.sbuf_tensor(f"g_onesb_{uid}", [1, P], BF16))
        mbuf = ec(nc.sbuf_tensor(f"g_mbuf_{uid}", [P, 4, TCH], BF16))
        eye_bf = ec(nc.sbuf_tensor(f"g_eye_{uid}", [P, P], BF16))
        epst = ec(nc.sbuf_tensor(f"g_eps_{uid}", [P, 1], F32))
        sems = {}
        for nm in ("bar", "ldx", "ldw0", "ldw1", "ldv", "ldr0", "ldr1", "gq0", "gq1", "gldc", "xpms",
                   "vini", "pe", "ev", "ev2", "sg", "st0", "st1", "cc",
                   "ldr2", "ldr3", "cc0", "cc1", "cc2s", "cc3", "cc4"):
            sems[nm] = SemT(ec(nc.semaphore(nm)))
        block = ec(nc.Block())
        SS = sems
        G = {"ones_bf": ones_bf, "onesb_bf": onesb_bf, "mbuf": mbuf,
             "eye_bf": eye_bf, "epst": epst, "ldv_const": 80}

        def pro(g):
            g.dma_start(out=ones_bf[:], in_=onesk_in.ap()).then_inc(SS["gldc"].h, 16)
            g.dma_start(out=onesb_bf[:], in_=onesb_in.ap()).then_inc(SS["gldc"].h, 16)
            g.dma_start(out=mbuf[:], in_=masks_in.ap().rearrange("(i p) n -> p i n", p=P)).then_inc(SS["gldc"].h, 16)
            g.dma_start(out=eye_bf[:], in_=eye_in.ap()).then_inc(SS["gldc"].h, 16)
            g.dma_start(out=epst[:], in_=eps_in.ap()).then_inc(SS["gldc"].h, 16)
            g.wait_ge(SS["gldc"].h, 80)
        block.gpsimd(pro)
        SS["gldc"].n = 80
        barrier(block, SS)

        def fm_chunk(tensor):
            return lambda th: tensor.ap().rearrange("(k p) n -> p k n", p=P)

        def fm_chunk_q(tensor):
            return lambda th: tensor.ap()[:, th * 2 * TCH:(th + 1) * 2 * TCH].rearrange("(k p) n -> p k n", p=P)

        def xng_chunk(th, xb):
            return [(xb[:, :, r * TLOC:(r + 1) * TLOC],
                     xng.ap()[r * D:(r + 1) * D, :].rearrange("(k p) n -> p k n", p=P))
                    for r in range(2)]

        def xng_tile(tt):
            r, tl = divmod(tt, 16)
            return xng.ap()[r * D:(r + 1) * D, tl * P:(tl + 1) * P].rearrange("(k p) n -> p k n", p=P)

        def w_slab(w, c0=0):
            return lambda si: w.ap()[:, c0 + si * 512:c0 + (si + 1) * 512].rearrange("(k p) n -> p k n", p=P)

        def w_slab_dn1(si):
            return w_dn1T.ap()[:, si * P:(si + 1) * P].rearrange("(k p) n -> p k n", p=P)

        def ctxl_chunk(th):
            return ctxl.ap().rearrange("(k p) n -> p k n", p=P)

        def prt_tile(t, m):
            return prt.ap()[(t // 4) * D + m * P:(t // 4) * D + (m + 1) * P,
                            (t % 4) * TCH:(t % 4 + 1) * TCH]

        def w_tdg_slab(si):
            r, ml = divmod(si, 2)
            return tdg.ap()[r * 8192:(r + 1) * 8192, ml * P:(ml + 1) * P].rearrange("(k p) n -> p k n", p=P)

        def tile_of(tensor):
            return lambda t, m: tensor.ap()[m * P:(m + 1) * P, t * TCH:(t + 1) * TCH]

        def qk_tile(t, m):
            return qk.ap()[m * P:(m + 1) * P, t * TCH:(t + 1) * TCH]

        def vt_tile(tt, mc):
            return vt.ap()[tt * P:(tt + 1) * P, mc * TCH:(mc + 1) * TCH]

        emit_ln(nc, block, SS, G, x_fm.ap(), xnc.ap())
        emit_collective(nc, block, SS, "AllGather", PAIRS, [xnc.ap()], [xng.ap()], sem="cc0")
        emit_matmul(nc, block, SS, G, D, 2048, S, xng_chunk, None, "cast",
                    qk_tile, out_dtype="bf16", has_bias=True, bvec=b_qk0.ap(),
                    wslab_ap=w_slab(w_qkv0), xnparts=2)
        emit_matmul_tm(nc, block, SS, G, D, 1024, S, xng_tile, w_qkv0.ap()[:, 2048:3072],
                       vt_tile, bias_row_ap=b_v0.ap())
        emit_attention(nc, block, SS, G, qk, vt, ctxl)
        emit_matmul(nc, block, SS, G, NH * P, D, S, ctxl_chunk, None, "cast",
                    prt_tile, out_dtype="bf16", wslab_ap=w_slab(w_out0))
        emit_collective(nc, block, SS, "ReduceScatter", PAIRS, [prt.ap()], [rso.ap()], op=ALU.add, sem="cc1")
        emit_resadd(nc, block, SS, G, res, rso, bvec_ap=b_out0.ap(), src_t=x_fm)
        if DBG:
            d_res1 = nc.dram_tensor("d_res1", [D, TLOC], F32, kind="ExternalOutput")
            emit_dbgcopy(nc, block, SS, d_res1, res)
        emit_ln(nc, block, SS, G, res.ap(), xnc.ap())
        emit_matmul(nc, block, SS, G, D, D, TLOC, fm_chunk(xnc), None, "cast",
                    tile_of(x2t), out_dtype="bf16", cmul=1.0 / (64.0 * math.sqrt(D)),
                    xpm_ap=su_ug.ap(), wslab_ap=w_slab(h2048))
        emit_matmul(nc, block, SS, G, D, 8192, TLOC, fm_chunk(x2t), None, "cast",
                    tile_of(xupt), out_dtype="bf16", has_scale=True, svec_ap=ws_u.ap(),
                    cmul=64.0, sfold_ap=su_dp.ap(), wslab_ap=w_slab(w_upT))
        emit_matmul(nc, block, SS, G, D, 8192, TLOC, fm_chunk(x2t), None, "cast",
                    tile_of(x3t), out_dtype="bf16", act="silu", has_scale=True,
                    svec_ap=ws_g.ap(), cmul=64.0, mult_tile_ap=tile_of(xupt),
                    wslab_ap=w_slab(w_gateT))
        emit_fwht_T(nc, block, SS, G, wdn, tdc)
        emit_collective(nc, block, SS, "AllGather", ALL8, [tdc.ap()], [tdg.ap()], sem="cc2s",
                        fire_only=True)
        emit_matmul(nc, block, SS, G, 8192, D, TLOC, fm_chunk_q(x3t), None, "bf16",
                    tile_of(res), out_dtype="f32", has_scale=True, svec_ap=ws_dp.ap(),
                    cmul=1.0 / math.sqrt(8192.0), resid_tile_ap=tile_of(res),
                    wslab_ap=w_tdg_slab, pre_wait="cc2s")
        if DBG:
            d_res2 = nc.dram_tensor("d_res2", [D, TLOC], F32, kind="ExternalOutput")
            emit_dbgcopy(nc, block, SS, d_res2, res)
        emit_ln(nc, block, SS, G, res.ap(), xnc.ap())
        emit_collective(nc, block, SS, "AllGather", PAIRS, [xnc.ap()], [xng.ap()], sem="cc3")
        emit_matmul(nc, block, SS, G, D, 2048, S, xng_chunk, None, "cast",
                    qk_tile, out_dtype="bf16", wslab_ap=w_slab(w_qkv1), xnparts=2)
        emit_matmul_tm(nc, block, SS, G, D, 1024, S, xng_tile, w_qkv1.ap()[:, 2048:3072],
                       vt_tile)
        emit_attention(nc, block, SS, G, qk, vt, ctxl)
        emit_matmul(nc, block, SS, G, NH * P, D, S, ctxl_chunk, None, "cast",
                    prt_tile, out_dtype="bf16", wslab_ap=w_slab(w_o1))
        emit_collective(nc, block, SS, "ReduceScatter", PAIRS, [prt.ap()], [rso.ap()], op=ALU.add, sem="cc4")
        emit_resadd(nc, block, SS, G, res, rso, bvec_ap=b_o1.ap())
        emit_ln(nc, block, SS, G, res.ap(), xnc.ap())
        emit_matmul(nc, block, SS, G, D, 8192, TLOC, fm_chunk(xnc), None, "cast",
                    tile_of(hidt), out_dtype="bf16", act="relu", has_scale=True,
                    svec_ap=us1.ap(), sign_w=True, wslab_ap=w_slab(w_up1T))
        emit_matmul(nc, block, SS, G, 8192, D, TLOC, fm_chunk_q(hidt), None, "cast",
                    tile_of(y), out_dtype="f32", has_scale=True, svec_ap=ds1.ap(),
                    sign_w=True, psc_ap=osc.ap(), resid_tile_ap=tile_of(res),
                    wslab_ap=w_slab_dn1)
        if DBG:
            dbgs = [("d_res", res), ("d_rso", rso), ("d_x2", x2t), ("d_xup", xupt),
                    ("d_x3", x3t), ("d_qk", qk), ("d_vt", vt), ("d_ctxl", ctxl),
                    ("d_xnc", xnc), ("d_hid", hidt), ("d_tdg", tdg), ("d_xng", xng)]
            douts = {}
            for nm, t in dbgs:
                douts[nm] = nc.dram_tensor(nm, list(t.shape), t.dtype, kind="ExternalOutput")
            b_cc = SS["cc"].n

            def dump(g):
                for i, (nm, t) in enumerate(dbgs):
                    g.dma_start(out=douts[nm].ap(), in_=t.ap()).then_inc(SS["cc"].h, 16)
                g.wait_ge(SS["cc"].h, b_cc + 16 * len(dbgs))
            block.gpsimd(dump)
            SS["cc"].n = b_cc + 16 * len(dbgs)
            barrier(block, SS)
    return nc


def _hadamard(n):
    h = np.array([[1.0]], dtype=np.float32)
    while h.shape[0] < n:
        h = np.kron(h, np.array([[1.0, 1.0], [1.0, -1.0]], dtype=np.float32))
    return h


def _prelay(v, n):
    return np.ascontiguousarray(np.asarray(v, dtype=np.float32).reshape(n, P).T)


def make_inmaps(inputs):
    ac = np.ascontiguousarray
    f32 = lambda a: np.asarray(a, dtype=np.float32)
    x = f32(inputs["L0_attn_x"]).reshape(NTOK, D)
    H = _hadamard(D)
    mask = np.zeros((4 * P, TCH), np.float32)
    for j in range(4):
        for kp in range(P):
            mask[j * P + kp, j * P + kp:] = 1.0
    W_in = f32(inputs["L0_attn_W_in"])
    b_in = f32(inputs["L0_attn_b_in"])
    Wq, Wk, Wv = f32(inputs["L1_attn_Wq"]), f32(inputs["L1_attn_Wk"]), f32(inputs["L1_attn_Wv"])
    ws_ug = f32(inputs["L0_ffn_Wscale_ug"])
    in_maps = []
    for c in range(NCORES):
        b, r = divmod(c, 2)
        hs = r * NH
        qcols = slice(hs * P, (hs + NH) * P)
        kcols = slice(D + hs * P, D + (hs + NH) * P)
        vcols = slice(2 * D + hs * P, 2 * D + (hs + NH) * P)
        m = {
            "x_fm": ac(x[b * S + r * TLOC:(b * S + (r + 1) * TLOC)].T),
            "w_qkv0": ac(np.concatenate([W_in[:, qcols], W_in[:, kcols], W_in[:, vcols]], axis=1)),
            "b_qk0": _prelay(np.concatenate([b_in[qcols], b_in[kcols]]), 16),
            "b_v0": ac(b_in[vcols].reshape(1, 1024)),
            "w_out0": ac(f32(inputs["L0_attn_W_out"])[hs * P:(hs + NH) * P, :]),
            "b_out0": _prelay(f32(inputs["L0_attn_b_out"]), 16),
            "h2048": H,
            "su_ug": _prelay(f32(inputs["L0_ffn_SU_ug"]), 16),
            "w_upT": ac(f32(inputs["L0_ffn_W_up"]).T),
            "ws_u": _prelay(ws_ug[:8192], 64),
            "su_dp": _prelay(f32(inputs["L0_ffn_SU_dp"]), 64),
            "w_gateT": ac(f32(inputs["L0_ffn_W_gate"]).T),
            "ws_g": _prelay(ws_ug[8192:], 64),
            "wdn": ac(f32(inputs["L0_ffn_W_down"])[c * 256:(c + 1) * 256]),
            "ws_dp": _prelay(f32(inputs["L0_ffn_Wscale_dp"]), 16),
            "w_qkv1": ac(np.concatenate([Wq[:, qcols.start:qcols.stop],
                                         Wk[:, qcols.start:qcols.stop],
                                         Wv[:, qcols.start:qcols.stop]], axis=1)),
            "w_o1": ac(f32(inputs["L1_attn_Wo"])[hs * P:(hs + NH) * P, :]),
            "b_o1": _prelay(f32(inputs["L1_attn_bo"]), 16),
            "w_up1T": ac(f32(inputs["L1_ffn_up_weight"]).T),
            "us1": _prelay(f32(inputs["L1_ffn_up_scale"]), 64),
            "w_dn1T": ac(f32(inputs["L1_ffn_down_weight"]).T),
            "ds1": _prelay(f32(inputs["L1_ffn_down_scale"]), 16),
            "osc": ac(np.broadcast_to(f32(inputs["L1_ffn_output_scale"]).reshape(1, 1), (P, 1))),
            "masks_in": mask,
            "onesk_in": np.ones((P, 1), np.float32),
            "onesb_in": np.ones((1, P), np.float32),
            "eye_in": np.eye(P, dtype=np.float32),
            "eps_in": np.full((P, 1), EPS, np.float32),
        }
        in_maps.append(m)
    return in_maps


def kernel(**inputs):
    if "nc" not in _CACHE:
        _CACHE["nc"] = build_megakernel()
    nc = _CACHE["nc"]
    in_maps = make_inmaps(inputs)
    res = run_bass_kernel_spmd(nc, in_maps, core_ids=list(range(NCORES)))
    _CACHE["last_results"] = res
    out = np.empty((NTOK, D), np.float32)
    for c in range(NCORES):
        out[c * TLOC:(c + 1) * TLOC] = res.results[c]["y"].T
    return out.reshape(B, S, D)


# version 10
# speedup vs baseline: 1.1837x; 1.0248x over previous
import math
from contextlib import ExitStack

import numpy as np
import ml_dtypes
import concourse.bass as bass
import concourse.mybir as mybir
from concourse.bass_utils import run_bass_kernel_spmd

F32 = mybir.dt.float32
BF16 = mybir.dt.bfloat16
AF = mybir.ActivationFunctionType
ALU = mybir.AluOpType
AX = mybir.AxisListType

P = 128
NCORES = 8
B, S, D = 4, 4096, 2048
NTOK = B * S
TLOC = NTOK // NCORES
TCH = 512
EPS = 1e-5
NH = 8
NSUP = 8
SCL = 1.0 / math.sqrt(128.0)
DBG = False

_CACHE = {}
_UIDC = [0]


def _uid():
    _UIDC[0] += 1
    return _UIDC[0]


class SemT:
    def __init__(self, h):
        self.h = h
        self.n = 0


def barrier(block, SS):
    bar = SS["bar"]
    base = bar.n
    for name in ("sync", "gpsimd", "tensor", "scalar", "vector"):
        def mk(eng):
            eng.nop().then_inc(bar.h, 1)
            eng.wait_ge(bar.h, base + 5)
        getattr(block, name)(mk)
    bar.n += 5



def emit_ln(nc, block, SS, G, src_ap, dst_ap, fold=None):
    nk = D // P
    ntc = TLOC // TCH
    F = 2 * nk if fold else 0
    VPT = F + 10 + 2 * nk
    SPT = nk + 1
    PPT = nk + 3
    NXD = 2 if fold else 1
    b_ldx, b_pe, b_ev, b_ev2, b_st = SS["ldx"].n, SS["pe"].n, SS["ev"].n, SS["ev2"].n, SS["st0"].n
    b_ldv = SS["ldv"].n
    nst_per = 2 if fold else 1
    with ExitStack() as cx:
        uid = _uid()
        ec = cx.enter_context
        xb = ec(nc.sbuf_tensor(f"ln_xb_{uid}", [P, nk, TCH], BF16))
        if fold:
            r32 = ec(nc.sbuf_tensor(f"ln_r32_{uid}", [P, nk, TCH], F32))
            db = ec(nc.sbuf_tensor(f"ln_db_{uid}", [P, nk, TCH], BF16))
            bvv = ec(nc.sbuf_tensor(f"ln_bv_{uid}", [P, nk], F32))
        sqb = ec(nc.sbuf_tensor(f"ln_sqb_{uid}", [P, TCH], BF16))
        xnb = ec(nc.sbuf_tensor(f"ln_xnb_{uid}", [P, nk, TCH], BF16))
        mrow = ec(nc.sbuf_tensor(f"ln_m_{uid}", [1, TCH], F32))
        erow = ec(nc.sbuf_tensor(f"ln_e_{uid}", [1, TCH], F32))
        msq = ec(nc.sbuf_tensor(f"ln_msq_{uid}", [1, TCH], F32))
        var = ec(nc.sbuf_tensor(f"ln_var_{uid}", [1, TCH], F32))
        srt = ec(nc.sbuf_tensor(f"ln_srt_{uid}", [1, TCH], F32))
        rr = ec(nc.sbuf_tensor(f"ln_rr_{uid}", [1, TCH], F32))
        nmr = ec(nc.sbuf_tensor(f"ln_nmr_{uid}", [1, TCH], F32))
        a_bf = ec(nc.sbuf_tensor(f"ln_abf_{uid}", [1, TCH], BF16))
        bb_bf = ec(nc.sbuf_tensor(f"ln_bbf_{uid}", [1, TCH], BF16))
        a_sb = ec(nc.sbuf_tensor(f"ln_asb_{uid}", [P, TCH], BF16))
        b_sb = ec(nc.sbuf_tensor(f"ln_bsb_{uid}", [P, TCH], BF16))
        ps_m = ec(nc.psum_tensor(f"ln_psm_{uid}", [1, TCH], F32))
        ps_s = ec(nc.psum_tensor(f"ln_pss_{uid}", [1, TCH], F32))
        ps_a = ec(nc.psum_tensor(f"ln_psa_{uid}", [P, TCH], F32))
        ps_b = ec(nc.psum_tensor(f"ln_psb_{uid}", [P, TCH], F32))

        def gp(g):
            for t in range(ntc):
                if t >= 1:
                    g.wait_ge(SS["ev2"].h, b_ev2 + VPT * t)
                    if fold:
                        g.wait_ge(SS["st0"].h, b_st + 16 * nst_per * t)
                if fold:
                    g.dma_start(out=r32[:], in_=fold[0](t)).then_inc(SS["ldx"].h, 16)
                    g.dma_start(out=db[:], in_=fold[1].ap()[:, t * TCH:(t + 1) * TCH]
                                .rearrange("(k p) n -> p k n", p=P)).then_inc(SS["ldx"].h, 16)
                else:
                    g.dma_start(out=xb[:], in_=src_ap[:, t * TCH:(t + 1) * TCH]
                                .rearrange("(k p) n -> p k n", p=P)).then_inc(SS["ldx"].h, 16)
        block.gpsimd(gp)

        def tns(tn):
            tn.wait_ge(SS["gldc"].h, G["ldv_const"])
            for t in range(ntc):
                pb = b_pe + PPT * t
                vbp = b_ev2 + VPT * (t - 1)
                vb = b_ev2 + VPT * t
                if fold:
                    tn.wait_ge(SS["ev2"].h, vb + F)
                else:
                    tn.wait_ge(SS["ldx"].h, b_ldx + 16 * NXD * (t + 1))
                if t >= 1:
                    tn.wait_ge(SS["ev2"].h, vbp + F + 1)
                ins = None
                for k in range(nk):
                    ins = tn.matmul(ps_m[:], G["ones_bf"][:], xb[:, k, :],
                                    start=(k == 0), stop=(k == nk - 1))
                ins.then_inc(SS["pe"].h, 1)
                for k in range(nk):
                    tn.wait_ge(SS["ev"].h, b_ev + SPT * t + k + 1)
                    if t >= 1 and k == 0:
                        tn.wait_ge(SS["ev2"].h, vbp + F + 2)
                    tn.matmul(ps_s[:], G["ones_bf"][:], sqb[:],
                              start=(k == 0), stop=(k == nk - 1)).then_inc(SS["pe"].h, 1)
                tn.wait_ge(SS["ev2"].h, vb + F + 7)
                if t >= 1:
                    tn.wait_ge(SS["ev2"].h, vbp + F + 9)
                tn.matmul(ps_a[:], G["onesb_bf"][:], a_bf[:], start=True, stop=True).then_inc(SS["pe"].h, 1)
                tn.wait_ge(SS["ev2"].h, vb + F + 8)
                if t >= 1:
                    tn.wait_ge(SS["ev2"].h, vbp + F + 10)
                tn.matmul(ps_b[:], G["onesb_bf"][:], bb_bf[:], start=True, stop=True).then_inc(SS["pe"].h, 1)
        block.tensor(tns)

        def sc(s_):
            for t in range(ntc):
                vb = b_ev2 + VPT * t
                if fold:
                    s_.wait_ge(SS["ev2"].h, vb + F)
                else:
                    s_.wait_ge(SS["ldx"].h, b_ldx + 16 * NXD * (t + 1))
                for k in range(nk):
                    if k >= 1:
                        s_.wait_ge(SS["pe"].h, b_pe + PPT * t + 1 + k)
                    elif t >= 1:
                        s_.wait_ge(SS["pe"].h, b_pe + PPT * t)
                    s_.activation(sqb[:], xb[:, k, :], AF.Square).then_inc(SS["ev"].h, 1)
                s_.wait_ge(SS["ev2"].h, vb + F + 4)
                s_.activation(srt[:], var[:], AF.Sqrt, bias=G["epst"][0:1, :]).then_inc(SS["ev"].h, 1)
        block.scalar(sc)

        def vc(v):
            if fold:
                v.wait_ge(SS["ldv"].h, b_ldv + 16)
            for t in range(ntc):
                vb = b_ev2 + VPT * t
                if fold:
                    v.wait_ge(SS["ldx"].h, b_ldx + 16 * NXD * (t + 1))
                    for k in range(nk):
                        v.scalar_tensor_tensor(r32[:, k, :], r32[:, k, :], bvv[:, k:k + 1],
                                               db[:, k, :], ALU.add, ALU.add).then_inc(SS["ev2"].h, 1)
                    v.wait_ge(SS["ev2"].h, vb + nk)
                    for k in range(nk):
                        v.tensor_copy(xb[:, k, :], r32[:, k, :]).then_inc(SS["ev2"].h, 1)
                v.wait_ge(SS["pe"].h, b_pe + PPT * t + 1)
                v.tensor_scalar(mrow[:], ps_m[:], 1.0 / D, None, ALU.mult).then_inc(SS["ev2"].h, 1)
                v.wait_ge(SS["pe"].h, b_pe + PPT * t + 1 + nk)
                v.tensor_scalar(erow[:], ps_s[:], 1.0 / D, None, ALU.mult).then_inc(SS["ev2"].h, 1)
                v.wait_ge(SS["ev2"].h, vb + F + 2)
                v.tensor_mul(msq[:], mrow[:], mrow[:]).then_inc(SS["ev2"].h, 1)
                v.wait_ge(SS["ev2"].h, vb + F + 3)
                v.tensor_sub(var[:], erow[:], msq[:]).then_inc(SS["ev2"].h, 1)
                v.wait_ge(SS["ev"].h, b_ev + SPT * t + nk + 1)
                v.reciprocal(rr[:], srt[:]).then_inc(SS["ev2"].h, 1)
                v.wait_ge(SS["ev2"].h, vb + F + 5)
                v.tensor_mul(nmr[:], mrow[:], rr[:]).then_inc(SS["ev2"].h, 1)
                v.tensor_copy(a_bf[:], rr[:]).then_inc(SS["ev2"].h, 1)
                v.wait_ge(SS["ev2"].h, vb + F + 6)
                v.tensor_scalar(bb_bf[:], nmr[:], -1.0, None, ALU.mult).then_inc(SS["ev2"].h, 1)
                v.wait_ge(SS["pe"].h, b_pe + PPT * t + nk + 2)
                if t >= 1:
                    v.wait_ge(SS["ev2"].h, b_ev2 + VPT * t)
                v.tensor_copy(a_sb[:], ps_a[:]).then_inc(SS["ev2"].h, 1)
                v.wait_ge(SS["pe"].h, b_pe + PPT * t + nk + 3)
                v.tensor_copy(b_sb[:], ps_b[:]).then_inc(SS["ev2"].h, 1)
                v.wait_ge(SS["ev2"].h, vb + F + 10)
                if t >= 1:
                    v.wait_ge(SS["st0"].h, b_st + 16 * nst_per * t)
                for k in range(nk):
                    v.tensor_mul(xnb[:, k, :], xb[:, k, :], a_sb[:]).then_inc(SS["ev2"].h, 1)
                    v.wait_ge(SS["ev2"].h, vb + F + 11 + 2 * k)
                    v.tensor_add(xnb[:, k, :], xnb[:, k, :], b_sb[:]).then_inc(SS["ev2"].h, 1)
        block.vector(vc)

        def sy(sy_):
            if fold:
                sy_.dma_start(out=bvv[:], in_=fold[2]).then_inc(SS["ldv"].h, 16)
            for t in range(ntc):
                if fold:
                    sy_.wait_ge(SS["ev2"].h, b_ev2 + VPT * t + nk)
                    sy_.dma_start(out=fold[3].ap()[:, t * TCH:(t + 1) * TCH]
                                  .rearrange("(k p) n -> p k n", p=P),
                                  in_=r32[:]).then_inc(SS["st0"].h, 16)
                sy_.wait_ge(SS["ev2"].h, b_ev2 + VPT * (t + 1))
                sy_.dma_start(out=dst_ap[:, t * TCH:(t + 1) * TCH].rearrange("(k p) n -> p k n", p=P),
                              in_=xnb[:]).then_inc(SS["st0"].h, 16)
            sy_.wait_ge(SS["st0"].h, b_st + 16 * nst_per * ntc)
        block.sync(sy)

        SS["ldx"].n = b_ldx + 16 * NXD * ntc
        SS["ldv"].n = b_ldv + (16 if fold else 0)
        SS["pe"].n = b_pe + PPT * ntc
        SS["ev"].n = b_ev + SPT * ntc
        SS["ev2"].n = b_ev2 + VPT * ntc
        SS["st0"].n = b_st + 16 * nst_per * ntc
    barrier(block, SS)


def emit_matmul(nc, block, SS, G, K, M, ntok, xchunk, w_ap, w_mode, out_tile_ap,
                out_dtype="bf16", has_bias=False, bvec=None, act="copy",
                has_scale=False, svec_ap=None, cmul=1.0, sfold_ap=None,
                resid_tile_ap=None, sign_w=False, mult_tile_ap=None, psc_ap=None,
                xpm_ap=None, wslab_ap=None, slab_m=4, xnparts=1, pre_wait=None):
    nk, nm = K // P, M // P
    ntc = ntok // TCH
    if K >= 8192:
        nth = 2 if ntc >= 2 else ntc
    else:
        nth = 1
    ntcl = ntc // nth
    if K >= 8192:
        slab_m = 1
    nslab = nm // slab_m
    use_act = act != "copy"
    afunc = {"silu": AF.Silu, "relu": AF.Relu}.get(act)
    has_resid = resid_tile_ap is not None
    has_mult = mult_tile_ap is not None
    has_psc = psc_ap is not None
    has_xpm = xpm_ap is not None
    has_vec_op = (not use_act) or has_mult or has_resid or has_psc
    n_rm = int(has_resid) + int(has_mult)
    if has_psc or sfold_ap is not None:
        assert has_scale
    if use_act:
        assert not has_bias and not has_resid and not has_psc
    scale_pre = has_scale and (cmul != 1.0 or has_psc or sfold_ap is not None)
    n_vini = int(has_scale and cmul != 1.0) + int(has_psc) + int(sfold_ap is not None)
    nidx = ntc * nm

    b_ldx = SS["ldx"].n
    b_ldw = [SS["ldw0"].n, SS["ldw1"].n]
    b_ldv = SS["ldv"].n
    b_ldr = [SS["ldr0"].n, SS["ldr1"].n, SS["ldr2"].n, SS["ldr3"].n]
    b_pe = SS["pe"].n
    b_ev = SS["ev"].n
    b_ev2 = SS["ev2"].n
    b_sg = SS["sg"].n
    b_st = [SS["st0"].n, SS["st1"].n]
    b_xpms = SS["xpms"].n
    b_vini = SS["vini"].n
    odt = BF16 if out_dtype == "bf16" else F32

    NX = xnparts

    def order():
        gi = 0
        for th in range(nth):
            for m in range(nm):
                for tc in range(ntcl):
                    yield gi, th, m, tc, th * ntcl + tc
                    gi += 1

    with ExitStack() as cx:
        uid = _uid()
        ec = cx.enter_context
        xb = ec(nc.sbuf_tensor(f"mm_xb_{uid}", [P, nk, ntcl * TCH], BF16))
        wb = [ec(nc.sbuf_tensor(f"mm_wb{j}_{uid}", [P, nk, slab_m * P], BF16)) for j in range(2)]
        eb = [ec(nc.sbuf_tensor(f"mm_e{j}_{uid}", [P, TCH], odt)) for j in range(4)]
        rb = [ec(nc.sbuf_tensor(f"mm_r{j}_{uid}", [P, TCH], F32)) for j in range(4)]
        mb = [ec(nc.sbuf_tensor(f"mm_m{j}_{uid}", [P, TCH], BF16)) for j in range(4)]
        vecs = ec(nc.sbuf_tensor(f"mm_vecs_{uid}", [P, 3 * nm + nk + 2], F32))
        pb = [ec(nc.psum_tensor(f"mm_pb{j}_{uid}", [P, TCH], F32)) for j in range(4)]
        off_b, off_s, off_w, off_p, off_f = 0, nm, 2 * nm, 2 * nm + nk, 2 * nm + nk + 1
        nvec_dma = int(has_bias) + int(has_scale) + int(has_xpm) + int(has_psc) + int(sfold_ap is not None)

        def sy(sy_):
            if has_bias:
                sy_.dma_start(out=vecs[:, off_b:off_b + nm], in_=bvec).then_inc(SS["ldv"].h, 16)
            if has_scale:
                sy_.dma_start(out=vecs[:, off_s:off_s + nm], in_=svec_ap).then_inc(SS["ldv"].h, 16)
            if has_xpm:
                sy_.dma_start(out=vecs[:, off_w:off_w + nk], in_=xpm_ap).then_inc(SS["ldv"].h, 16)
            if has_psc:
                sy_.dma_start(out=vecs[:, off_p:off_p + 1], in_=psc_ap).then_inc(SS["ldv"].h, 16)
            if sfold_ap is not None:
                sy_.dma_start(out=vecs[:, off_f:off_f + nm], in_=sfold_ap).then_inc(SS["ldv"].h, 16)
            if n_rm:
                for gi, th, m, tc, t in order():
                    if gi >= 4:
                        sy_.wait_ge(SS[f"st{gi % 2}"].h, b_st[gi % 2] + 16 * ((gi - 4) // 2 + 1))
                    if has_resid:
                        sy_.dma_start(out=rb[gi % 4][:], in_=resid_tile_ap(t, m)).then_inc(SS[f"ldr{gi % 4}"].h, 16)
                    if has_mult:
                        sy_.dma_start(out=mb[gi % 4][:], in_=mult_tile_ap(t, m)).then_inc(SS[f"ldr{gi % 4}"].h, 16)
        block.sync(sy)

        def gp(g):
            if pre_wait is not None:
                g.wait_ge(SS[pre_wait].h, 1)
            for th in range(nth):
                if th >= 1:
                    g.wait_ge(SS["pe"].h, b_pe + th * ntcl * nm)
                parts = xchunk2(th)
                if not isinstance(parts, list):
                    parts = [(xb[:], parts)]
                for xo, xi in parts:
                    g.dma_start(out=xo, in_=xi).then_inc(SS["ldx"].h, 16)
                for si in range(nslab):
                    sidx = th * nslab + si
                    if sidx >= 2:
                        g.wait_ge(SS["pe"].h, b_pe + (sidx - 1) * slab_m * ntcl)
                    g.dma_start(out=wb[sidx % 2][:], in_=wslab_ap(si)).then_inc(SS[f"ldw{sidx % 2}"].h, 16)
        def xchunk2(th):
            try:
                return xchunk(th, xb)
            except TypeError:
                return xchunk(th)
        block.gpsimd(gp)

        def tns(tn):
            for gi, th, m, tc, t in order():
                sidx = th * nslab + m // slab_m
                if m == 0 and tc == 0:
                    if has_xpm:
                        tn.wait_ge(SS["xpms"].h, b_xpms + th + 1)
                    else:
                        tn.wait_ge(SS["ldx"].h, b_ldx + 16 * NX * (th + 1))
                if m % slab_m == 0 and tc == 0:
                    if sign_w:
                        tn.wait_ge(SS["sg"].h, b_sg + sidx + 1)
                    else:
                        tn.wait_ge(SS[f"ldw{sidx % 2}"].h, b_ldw[sidx % 2] + 16 * (sidx // 2 + 1))
                if gi >= 4:
                    if use_act:
                        tn.wait_ge(SS["ev"].h, b_ev + gi - 3)
                    else:
                        tn.wait_ge(SS["ev2"].h, b_ev2 + gi - 3)
                ins = None
                wloc = (m % slab_m) * P
                for k in range(nk):
                    ins = tn.matmul(pb[gi % 4][:], wb[sidx % 2][:, k, wloc:wloc + P],
                                    xb[:, k, tc * TCH:(tc + 1) * TCH],
                                    start=(k == 0), stop=(k == nk - 1))
                ins.then_inc(SS["pe"].h, 1)
        block.tensor(tns)

        def sc(s_):
            if use_act and has_scale:
                s_.wait_ge(SS["ldv"].h, b_ldv + 16 * nvec_dma)
                if scale_pre:
                    s_.wait_ge(SS["vini"].h, b_vini + n_vini)
            ordl = list(order())

            def do_store(gj):
                t_, m_ = ordl[gj][4], ordl[gj][2]
                if has_vec_op:
                    s_.wait_ge(SS["ev2"].h, b_ev2 + gj + 1)
                else:
                    s_.wait_ge(SS["ev"].h, b_ev + gj + 1)
                s_.dma_start(out=out_tile_ap(t_, m_), in_=eb[gj % 4][:]).then_inc(SS[f"st{gj % 2}"].h, 16)

            for gi, th, m, tc, t in ordl:
                sidx = th * nslab + m // slab_m
                if sign_w and m % slab_m == 0 and tc == 0:
                    s_.wait_ge(SS[f"ldw{sidx % 2}"].h, b_ldw[sidx % 2] + 16 * (sidx // 2 + 1))
                    s_.activation(wb[sidx % 2][:, :, :], wb[sidx % 2][:, :, :], AF.Sign).then_inc(SS["sg"].h, 1)
                if use_act:
                    s_.wait_ge(SS["pe"].h, b_pe + gi + 1)
                    if gi >= 4:
                        s_.wait_ge(SS[f"st{gi % 2}"].h, b_st[gi % 2] + 16 * ((gi - 4) // 2 + 1))
                    kw = {}
                    if has_scale:
                        kw["scale"] = vecs[:, off_s + m:off_s + m + 1]
                    elif cmul != 1.0:
                        kw["scale"] = float(cmul)
                    s_.activation(eb[gi % 4][:], pb[gi % 4][:], afunc, **kw).then_inc(SS["ev"].h, 1)
                if gi >= 2:
                    do_store(gi - 2)
            for gj in (nidx - 2, nidx - 1):
                if gj >= 0:
                    do_store(gj)
            s_.wait_ge(SS["st0"].h, b_st[0] + 16 * ((nidx + 1) // 2))
            s_.wait_ge(SS["st1"].h, b_st[1] + 16 * (nidx // 2))
        block.scalar(sc)

        def vc(v):
            nvi = 0
            if scale_pre:
                v.wait_ge(SS["ldv"].h, b_ldv + 16 * nvec_dma)
                if cmul != 1.0:
                    v.tensor_scalar(vecs[:, off_s:off_s + nm], vecs[:, off_s:off_s + nm],
                                    float(cmul), None, ALU.mult).then_inc(SS["vini"].h, 1)
                    nvi += 1
                    v.wait_ge(SS["vini"].h, b_vini + nvi)
                if has_psc:
                    v.tensor_scalar(vecs[:, off_s:off_s + nm], vecs[:, off_s:off_s + nm],
                                    vecs[:, off_p:off_p + 1], None, ALU.mult).then_inc(SS["vini"].h, 1)
                    nvi += 1
                    v.wait_ge(SS["vini"].h, b_vini + nvi)
                if sfold_ap is not None:
                    v.tensor_mul(vecs[:, off_s:off_s + nm], vecs[:, off_s:off_s + nm],
                                 vecs[:, off_f:off_f + nm]).then_inc(SS["vini"].h, 1)
                    nvi += 1
                    v.wait_ge(SS["vini"].h, b_vini + nvi)
            for gi, th, m, tc, t in order():
                if has_xpm and m == 0 and tc == 0:
                    v.wait_ge(SS["ldv"].h, b_ldv + 16 * nvec_dma)
                    v.wait_ge(SS["ldx"].h, b_ldx + 16 * NX * (th + 1))
                    ins = None
                    for k in range(nk):
                        ins = v.tensor_scalar(xb[:, k, :], xb[:, k, :],
                                              vecs[:, off_w + k:off_w + k + 1], None, ALU.mult)
                    ins.then_inc(SS["xpms"].h, 1)
                if not has_vec_op:
                    continue
                if use_act:
                    v.wait_ge(SS["ev"].h, b_ev + gi + 1)
                    v.wait_ge(SS[f"ldr{gi % 4}"].h, b_ldr[gi % 4] + 16 * n_rm * (gi // 4 + 1))
                    v.tensor_mul(eb[gi % 4][:], eb[gi % 4][:], mb[gi % 4][:]).then_inc(SS["ev2"].h, 1)
                    continue
                v.wait_ge(SS["pe"].h, b_pe + gi + 1)
                if has_bias or has_scale:
                    v.wait_ge(SS["ldv"].h, b_ldv + 16 * nvec_dma)
                if scale_pre:
                    v.wait_ge(SS["vini"].h, b_vini + n_vini)
                if n_rm:
                    v.wait_ge(SS[f"ldr{gi % 4}"].h, b_ldr[gi % 4] + 16 * n_rm * (gi // 4 + 1))
                if gi >= 4:
                    v.wait_ge(SS[f"st{gi % 2}"].h, b_st[gi % 2] + 16 * ((gi - 4) // 2 + 1))
                pbank = pb[gi % 4]
                cur = eb[gi % 4]
                if has_resid:
                    if has_scale:
                        last = v.scalar_tensor_tensor(cur[:], pbank[:], vecs[:, off_s + m:off_s + m + 1],
                                                      rb[gi % 4][:], ALU.mult, ALU.add)
                    elif has_bias:
                        last = v.scalar_tensor_tensor(cur[:], pbank[:], vecs[:, off_b + m:off_b + m + 1],
                                                      rb[gi % 4][:], ALU.add, ALU.add)
                    else:
                        last = v.scalar_tensor_tensor(cur[:], pbank[:], float(cmul),
                                                      rb[gi % 4][:], ALU.mult, ALU.add)
                else:
                    if has_scale and has_bias:
                        last = v.tensor_scalar(cur[:], pbank[:], vecs[:, off_s + m:off_s + m + 1],
                                               vecs[:, off_b + m:off_b + m + 1], ALU.mult, ALU.add)
                    elif has_scale:
                        last = v.tensor_scalar(cur[:], pbank[:], vecs[:, off_s + m:off_s + m + 1],
                                               None, ALU.mult)
                    elif has_bias:
                        last = v.tensor_scalar(cur[:], pbank[:], vecs[:, off_b + m:off_b + m + 1],
                                               None, ALU.add)
                    elif cmul != 1.0:
                        last = v.tensor_scalar(cur[:], pbank[:], float(cmul), None, ALU.mult)
                    else:
                        last = v.tensor_copy(cur[:], pbank[:])
                last.then_inc(SS["ev2"].h, 1)
        block.vector(vc)

        SS["ldx"].n = b_ldx + 16 * NX * nth
        nslabs_tot = nth * nslab
        SS["ldw0"].n = b_ldw[0] + 16 * ((nslabs_tot + 1) // 2)
        SS["ldw1"].n = b_ldw[1] + 16 * (nslabs_tot // 2)
        SS["ldv"].n = b_ldv + 16 * nvec_dma
        for j in range(4):
            SS[f"ldr{j}"].n = b_ldr[j] + 16 * n_rm * ((nidx - j + 3) // 4)
        SS["pe"].n = b_pe + nidx
        SS["ev"].n = b_ev + (nidx if use_act else 0)
        SS["ev2"].n = b_ev2 + (nidx if has_vec_op else 0)
        SS["sg"].n = b_sg + (nslabs_tot if sign_w else 0)
        SS["st0"].n = b_st[0] + 16 * ((nidx + 1) // 2)
        SS["st1"].n = b_st[1] + 16 * (nidx // 2)
        SS["xpms"].n = b_xpms + (nth if has_xpm else 0)
        SS["vini"].n = b_vini + n_vini
    barrier(block, SS)



def emit_matmul_tm(nc, block, SS, G, K, M, ntok, xtile, w_ap_full, out_tile_ap,
                   bias_row_ap=None):
    nk, nmc = K // P, M // TCH
    ntt = ntok // P
    has_bias = bias_row_ap is not None
    b_ldx, b_ldw, b_ldv = SS["ldx"].n, SS["ldw0"].n, SS["ldw1"].n
    b_pe, b_ev2 = SS["pe"].n, SS["ev2"].n
    b_st = [SS["st0"].n, SS["st1"].n]
    with ExitStack() as cx:
        uid = _uid()
        ec = cx.enter_context
        xt = [ec(nc.sbuf_tensor(f"vt_x{j}_{uid}", [P, nk, P], BF16)) for j in range(2)]
        wsb = ec(nc.sbuf_tensor(f"vt_w_{uid}", [P, nk, M], BF16))
        brow = ec(nc.sbuf_tensor(f"vt_br_{uid}", [1, M], F32))
        bsb = ec(nc.sbuf_tensor(f"vt_bs_{uid}", [P, M], F32))
        eb = [ec(nc.sbuf_tensor(f"vt_e{j}_{uid}", [P, TCH], BF16)) for j in range(2)]
        brbf = ec(nc.sbuf_tensor(f"vt_brbf_{uid}", [1, M], BF16))
        pb = [ec(nc.psum_tensor(f"vt_pb{j}_{uid}", [P, TCH], F32)) for j in range(2)]
        ps_bc = ec(nc.psum_tensor(f"vt_psbc_{uid}", [P, TCH], F32))

        def gp(g):
            g.dma_start(out=wsb[:], in_=w_ap_full.rearrange("(k p) m -> p k m", p=P)).then_inc(SS["ldw0"].h, 16)
            if has_bias:
                g.dma_start(out=brow[:], in_=bias_row_ap).then_inc(SS["ldw1"].h, 16)
            for tt in range(ntt):
                if tt >= 2:
                    g.wait_ge(SS["pe"].h, b_pe + nbc + (tt - 1) * nmc)
                g.dma_start(out=xt[tt % 2][:], in_=xtile(tt)).then_inc(SS[f"gq{tt % 2}"].h, 16)
        nbc = (M // TCH) if has_bias else 0

        b_ldr = [SS["gq0"].n, SS["gq1"].n]
        block.gpsimd(gp)

        def tns(tn):
            tn.wait_ge(SS["ldw0"].h, b_ldw + 16)
            if has_bias:
                tn.wait_ge(SS["ldw1"].h, b_ldv + 16)
                for mc in range(M // TCH):
                    tn.wait_ge(SS["ev2"].h, b_ev2 + 1)
                    if mc > 0:
                        tn.wait_ge(SS["ev2"].h, b_ev2 + 1 + mc)
                    tn.matmul(ps_bc[:], G["onesb_bf"][:], brbf[:, mc * TCH:(mc + 1) * TCH],
                              start=True, stop=True).then_inc(SS["pe"].h, 1)
            for tt in range(ntt):
                for mc in range(nmc):
                    idx = tt * nmc + mc
                    if mc == 0:
                        tn.wait_ge(SS[f"gq{tt % 2}"].h, b_ldr[tt % 2] + 16 * (tt // 2 + 1))
                    if idx >= 2:
                        tn.wait_ge(SS["ev2"].h, b_ev2 + 1 + nbc + idx - 1)
                    ins = None
                    for k in range(nk):
                        ins = tn.matmul(pb[idx % 2][:], xt[tt % 2][:, k, :],
                                        wsb[:, k, mc * TCH:(mc + 1) * TCH],
                                        start=(k == 0), stop=(k == nk - 1))
                    ins.then_inc(SS["pe"].h, 1)
        block.tensor(tns)

        def vc(v):
            nb = 0
            if has_bias:
                v.wait_ge(SS["ldw1"].h, b_ldv + 16)
                v.tensor_copy(brbf[:], brow[:]).then_inc(SS["ev2"].h, 1)
                nb = 1
                for mc in range(M // TCH):
                    v.wait_ge(SS["pe"].h, b_pe + mc + 1)
                    v.tensor_copy(bsb[:, mc * TCH:(mc + 1) * TCH], ps_bc[:]).then_inc(SS["ev2"].h, 1)
                nb += M // TCH
            for tt in range(ntt):
                for mc in range(nmc):
                    idx = tt * nmc + mc
                    v.wait_ge(SS["pe"].h, b_pe + nbc + idx + 1)
                    if idx >= 2:
                        v.wait_ge(SS[f"st{idx % 2}"].h, b_st[idx % 2] + 16 * ((idx - 2) // 2 + 1))
                    if has_bias:
                        v.wait_ge(SS["ev2"].h, b_ev2 + 1 + nbc)
                        v.tensor_add(eb[idx % 2][:], pb[idx % 2][:], bsb[:, mc * TCH:(mc + 1) * TCH]).then_inc(SS["ev2"].h, 1)
                    else:
                        v.tensor_copy(eb[idx % 2][:], pb[idx % 2][:]).then_inc(SS["ev2"].h, 1)
        block.vector(vc)

        def sc(s_):
            for tt in range(ntt):
                for mc in range(nmc):
                    idx = tt * nmc + mc
                    s_.wait_ge(SS["ev2"].h, b_ev2 + (1 + nbc if has_bias else 0) + idx + 1)
                    s_.dma_start(out=out_tile_ap(tt, mc), in_=eb[idx % 2][:]).then_inc(SS[f"st{idx % 2}"].h, 16)
            s_.wait_ge(SS["st0"].h, b_st[0] + 16 * ((ntt * nmc + 1) // 2))
            s_.wait_ge(SS["st1"].h, b_st[1] + 16 * ((ntt * nmc) // 2))
        block.scalar(sc)

        nidx = ntt * nmc
        SS["ldw0"].n = b_ldw + 16
        SS["ldw1"].n = b_ldv + (16 if has_bias else 0)
        SS["gq0"].n = b_ldr[0] + 16 * ((ntt + 1) // 2)
        SS["gq1"].n = b_ldr[1] + 16 * (ntt // 2)
        SS["pe"].n = b_pe + nbc + nidx
        SS["ev2"].n = b_ev2 + (1 + nbc if has_bias else 0) + nidx
        SS["st0"].n = b_st[0] + 16 * ((nidx + 1) // 2)
        SS["st1"].n = b_st[1] + 16 * (nidx // 2)
    barrier(block, SS)



def _nkb(s):
    return 4 * (s + 1)


KBS_HEAD = sum(_nkb(s) for s in range(NSUP))
NPR_HEAD = KBS_HEAD // 2
A_PE_HEAD = 17 + sum(3 * _nkb(s) + 1 for s in range(NSUP))
A_SC_HEAD = 4 + NPR_HEAD
A_VC_HEAD = 28 + 9 * NSUP


def _a_pe_s(s):
    return 17 + sum(3 * _nkb(x) + 1 for x in range(s))


def _a_scp_s(s):
    return 4 + sum(_nkb(x) // 2 for x in range(s))


def _a_vc_s(s):
    return 28 + 9 * s


def _a_kbg(s, kb):
    return sum(_nkb(x) for x in range(s)) + kb


def _a_prg_to_spr(prg):
    s = 0
    while prg >= _nkb(s) // 2:
        prg -= _nkb(s) // 2
        s += 1
    return s, prg


def emit_attention(nc, block, SS, G, qk_t, vt_t, cc2_t):
    NKB = S // P
    b_pe, b_sc, b_vc = SS["pe"].n, SS["ev"].n, SS["ev2"].n
    b_kk, b_kv, b_q = SS["ldw0"].n, SS["ldw1"].n, SS["gq0"].n
    b_st = [SS["st0"].n, SS["st1"].n]

    def pe_st(h, s, kb):
        return b_pe + A_PE_HEAD * h + _a_pe_s(s) + 6 * (kb // 2) + 1 + (kb % 2)

    def pe_l(h, s, kb):
        return b_pe + A_PE_HEAD * h + _a_pe_s(s) + 6 * (kb // 2) + 3 + 2 * (kb % 2)

    def pe_pv(h, s, kb):
        return b_pe + A_PE_HEAD * h + _a_pe_s(s) + 6 * (kb // 2) + 4 + 2 * (kb % 2)

    def sc_exp(h, s, pr):
        return b_sc + A_SC_HEAD * h + _a_scp_s(s) + pr + 1

    with ExitStack() as cx:
        uid = _uid()
        ec = cx.enter_context
        kbuf = [ec(nc.sbuf_tensor(f"at_kbuf{j}_{uid}", [P, S], BF16)) for j in range(2)]
        vbuf = [ec(nc.sbuf_tensor(f"at_vbuf{j}_{uid}", [P, NKB, P], BF16)) for j in range(2)]
        qh = [ec(nc.sbuf_tensor(f"at_qh{j}_{uid}", [P, S], BF16)) for j in range(2)]
        ksqf = ec(nc.sbuf_tensor(f"at_ksqf_{uid}", [P, S], BF16))
        qsqf = ec(nc.sbuf_tensor(f"at_qsqf_{uid}", [P, S], BF16))
        ksn = ec(nc.sbuf_tensor(f"at_ksn_{uid}", [1, S], F32))
        qsn = ec(nc.sbuf_tensor(f"at_qsn_{uid}", [1, S], F32))
        km2 = ec(nc.sbuf_tensor(f"at_km2_{uid}", [1, 1], F32))
        kms = ec(nc.sbuf_tensor(f"at_kms_{uid}", [1, 1], F32))
        qm2 = ec(nc.sbuf_tensor(f"at_qm2_{uid}", [1, NSUP], F32))
        qmv = ec(nc.sbuf_tensor(f"at_qmv_{uid}", [1, NSUP], F32))
        negcb = ec(nc.sbuf_tensor(f"at_negcb_{uid}", [1, NSUP], F32))
        negc_bf = ec(nc.sbuf_tensor(f"at_negcbf_{uid}", [1, NSUP], BF16))
        bias_all = ec(nc.sbuf_tensor(f"at_bias_{uid}", [P, NSUP], F32))
        pts = [ec(nc.sbuf_tensor(f"at_pt{j}_{uid}", [P, 2 * TCH], BF16)) for j in range(2)]
        lsb = ec(nc.sbuf_tensor(f"at_lsb_{uid}", [1, TCH], F32))
        rlv = ec(nc.sbuf_tensor(f"at_rlv_{uid}", [1, TCH], F32))
        rl_bf = ec(nc.sbuf_tensor(f"at_rlbf_{uid}", [1, TCH], BF16))
        rb_s = ec(nc.sbuf_tensor(f"at_rbs_{uid}", [P, TCH], F32))
        cxs = [ec(nc.sbuf_tensor(f"at_cxs{j}_{uid}", [P, TCH], BF16)) for j in range(2)]
        pss = [ec(nc.psum_tensor(f"at_pss{j}_{uid}", [P, 2 * TCH], F32)) for j in range(2)]
        ps_o = [ec(nc.psum_tensor(f"at_pso{j}_{uid}", [P, TCH], F32)) for j in range(2)]
        ps_l = ec(nc.psum_tensor(f"at_psl_{uid}", [1, TCH], F32))
        ps_sm = ec(nc.psum_tensor(f"at_pssm_{uid}", [P, TCH], F32))

        def gp(g):
            for h in range(NH):
                if h >= 2:
                    g.wait_ge(SS["pe"].h, b_pe + A_PE_HEAD * (h - 1))
                g.dma_start(out=kbuf[h % 2][:], in_=qk_t.ap()[(NH + h) * P:(NH + h + 1) * P, :]).then_inc(SS["ldw0"].h, 16)
                g.dma_start(out=vbuf[h % 2][:], in_=vt_t.ap()[:, h * P:(h + 1) * P]
                            .rearrange("(b p) d -> p b d", p=P)).then_inc(SS["ldw1"].h, 16)
                g.dma_start(out=qh[h % 2][:], in_=qk_t.ap()[h * P:(h + 1) * P, :]).then_inc(SS["gq0"].h, 16)
        block.gpsimd(gp)

        def tns(tn):
            for h in range(NH):
                hb = b_pe + A_PE_HEAD * h
                vcb = b_vc + A_VC_HEAD * h
                tn.wait_ge(SS["ldw0"].h, b_kk + 16 * (h + 1))
                tn.wait_ge(SS["ldw1"].h, b_kv + 16 * (h + 1))
                tn.wait_ge(SS["gq0"].h, b_q + 16 * (h + 1))
                for c in range(8):
                    if c == 0:
                        tn.wait_ge(SS["ev"].h, b_sc + A_SC_HEAD * h + 1)
                        if h > 0:
                            tn.wait_ge(SS["ev2"].h, b_vc + A_VC_HEAD * h)
                    else:
                        tn.wait_ge(SS["ev2"].h, vcb + c)
                    tn.matmul(ps_sm[0:1, :], G["ones_bf"][:], ksqf[:, c * TCH:(c + 1) * TCH],
                              start=True, stop=True).then_inc(SS["pe"].h, 1)
                for c in range(8):
                    if c == 0:
                        tn.wait_ge(SS["ev"].h, b_sc + A_SC_HEAD * h + 2)
                        tn.wait_ge(SS["ev2"].h, vcb + 8)
                    else:
                        tn.wait_ge(SS["ev2"].h, vcb + 9 + c)
                    tn.matmul(ps_sm[0:1, :], G["ones_bf"][:], qsqf[:, c * TCH:(c + 1) * TCH],
                              start=True, stop=True).then_inc(SS["pe"].h, 1)
                tn.wait_ge(SS["ev2"].h, vcb + 27)
                tn.matmul(ps_sm[:, 0:NSUP], G["onesb_bf"][:], negc_bf[:], start=True, stop=True).then_inc(SS["pe"].h, 1)
                for s in range(NSUP):
                    sb = hb + _a_pe_s(s)
                    for pr in range(_nkb(s) // 2):
                        kb0, kb1 = 2 * pr, 2 * pr + 1
                        prg = _a_kbg(s, kb0) // 2
                        if prg >= 2:
                            s2, p2 = _a_prg_to_spr(prg - 2)
                            tn.wait_ge(SS["ev"].h, sc_exp(h, s2, p2))
                        elif h > 0:
                            s2, p2 = _a_prg_to_spr(NPR_HEAD - 2 + prg)
                            tn.wait_ge(SS["ev"].h, sc_exp(h - 1, s2, p2))
                        for e in (0, 1):
                            kb = 2 * pr + e
                            tn.matmul(pss[prg % 2][:, e * TCH:(e + 1) * TCH],
                                      kbuf[h % 2][:, kb * P:(kb + 1) * P],
                                      qh[h % 2][:, s * TCH:(s + 1) * TCH], start=True, stop=True).then_inc(SS["pe"].h, 1)
                        tn.wait_ge(SS["ev"].h, sc_exp(h, s, pr))
                        for e in (0, 1):
                            kb = 2 * pr + e
                            if kb >= 4 * s:
                                tn.wait_ge(SS["ev2"].h, vcb + _a_vc_s(s) + (kb - 4 * s) + 1)
                            if kb == 0:
                                if s >= 2:
                                    tn.wait_ge(SS["ev2"].h, vcb + _a_vc_s(s - 2) + 9)
                                elif h > 0:
                                    tn.wait_ge(SS["ev2"].h, b_vc + A_VC_HEAD * (h - 1) + _a_vc_s(NSUP - 2 + s) + 9)
                                if s >= 1:
                                    tn.wait_ge(SS["ev2"].h, vcb + _a_vc_s(s - 1) + 5)
                                elif h > 0:
                                    tn.wait_ge(SS["ev2"].h, b_vc + A_VC_HEAD * (h - 1) + _a_vc_s(NSUP - 1) + 5)
                            tn.matmul(ps_l[:], G["ones_bf"][:],
                                      pts[prg % 2][:, e * TCH:(e + 1) * TCH],
                                      start=(kb == 0), stop=(kb == _nkb(s) - 1)).then_inc(SS["pe"].h, 1)
                            tn.matmul(ps_o[s % 2][:], vbuf[h % 2][:, kb, :],
                                      pts[prg % 2][:, e * TCH:(e + 1) * TCH],
                                      start=(kb == 0), stop=(kb == _nkb(s) - 1)).then_inc(SS["pe"].h, 1)
                    tn.wait_ge(SS["ev2"].h, vcb + _a_vc_s(s) + 7)
                    if s > 0:
                        tn.wait_ge(SS["ev2"].h, vcb + _a_vc_s(s - 1) + 8)
                    tn.matmul(ps_sm[:], G["onesb_bf"][:], rl_bf[:], start=True, stop=True).then_inc(SS["pe"].h, 1)
        block.tensor(tns)

        def sc(s_):
            for h in range(NH):
                s_.wait_ge(SS["ldw0"].h, b_kk + 16 * (h + 1))
                if h >= 2:
                    s_.wait_ge(SS["pe"].h, b_pe + A_PE_HEAD * (h - 1))
                s_.activation(ksqf[:], kbuf[h % 2][:], AF.Square).then_inc(SS["ev"].h, 1)
                s_.wait_ge(SS["gq0"].h, b_q + 16 * (h + 1))
                s_.activation(qsqf[:], qh[h % 2][:], AF.Square).then_inc(SS["ev"].h, 1)
                s_.wait_ge(SS["ev2"].h, b_vc + A_VC_HEAD * h + 9)
                s_.activation(kms[:], km2[:], AF.Sqrt).then_inc(SS["ev"].h, 1)
                s_.wait_ge(SS["ev2"].h, b_vc + A_VC_HEAD * h + 25)
                s_.activation(qmv[:], qm2[:], AF.Sqrt).then_inc(SS["ev"].h, 1)
                for s in range(NSUP):
                    for pr in range(_nkb(s) // 2):
                        kb1 = 2 * pr + 1
                        prg = _a_kbg(s, 0) // 2 + pr
                        s_.wait_ge(SS["pe"].h, pe_st(h, s, kb1))
                        s_.wait_ge(SS["ev2"].h, b_vc + A_VC_HEAD * h + 28)
                        if prg >= 2:
                            s4, p4 = _a_prg_to_spr(prg - 2)
                            s_.wait_ge(SS["pe"].h, pe_pv(h, s4, 2 * p4 + 1))
                        elif h > 0:
                            s4, p4 = _a_prg_to_spr(NPR_HEAD - 2 + prg)
                            s_.wait_ge(SS["pe"].h, pe_pv(h - 1, s4, 2 * p4 + 1))
                        s_.activation(pts[prg % 2][:], pss[prg % 2][:], AF.Exp,
                                      bias=bias_all[:, s:s + 1], scale=SCL).then_inc(SS["ev"].h, 1)
        block.scalar(sc)

        def vc(v):
            for h in range(NH):
                vcb = b_vc + A_VC_HEAD * h
                hb = b_pe + A_PE_HEAD * h
                for c in range(8):
                    v.wait_ge(SS["pe"].h, hb + c + 1)
                    v.tensor_copy(ksn[:, c * TCH:(c + 1) * TCH], ps_sm[0:1, :]).then_inc(SS["ev2"].h, 1)
                v.wait_ge(SS["ev2"].h, vcb + 8)
                v.reduce_max(km2[:], ksn[:], axis=AX.X).then_inc(SS["ev2"].h, 1)
                for c in range(8):
                    v.wait_ge(SS["pe"].h, hb + 8 + c + 1)
                    v.tensor_copy(qsn[:, c * TCH:(c + 1) * TCH], ps_sm[0:1, :]).then_inc(SS["ev2"].h, 1)
                for c in range(8):
                    v.wait_ge(SS["ev2"].h, vcb + 9 + c + 1)
                    v.reduce_max(qm2[:, c:c + 1], qsn[:, c * TCH:(c + 1) * TCH], axis=AX.X).then_inc(SS["ev2"].h, 1)
                v.wait_ge(SS["ev"].h, b_sc + A_SC_HEAD * h + 4)
                v.tensor_scalar(negcb[:], qmv[:], kms[0:1, :], -SCL, ALU.mult, ALU.mult).then_inc(SS["ev2"].h, 1)
                v.wait_ge(SS["ev2"].h, vcb + 26)
                v.tensor_copy(negc_bf[:], negcb[:]).then_inc(SS["ev2"].h, 1)
                v.wait_ge(SS["pe"].h, hb + 17)
                if h > 0:
                    v.wait_ge(SS["ev"].h, b_sc + A_SC_HEAD * h)
                v.tensor_copy(bias_all[:], ps_sm[:, 0:NSUP]).then_inc(SS["ev2"].h, 1)
                for s in range(NSUP):
                    vb = vcb + _a_vc_s(s)
                    for j in range(4):
                        kb = 4 * s + j
                        kbg = _a_kbg(s, kb)
                        prg = kbg // 2
                        v.wait_ge(SS["ev"].h, sc_exp(h, s, kb // 2))
                        v.tensor_mul(pts[prg % 2][:, (kb % 2) * TCH:(kb % 2 + 1) * TCH],
                                     pts[prg % 2][:, (kb % 2) * TCH:(kb % 2 + 1) * TCH],
                                     G["mbuf"][:, j, :]).then_inc(SS["ev2"].h, 1)
                    v.wait_ge(SS["pe"].h, pe_l(h, s, _nkb(s) - 1))
                    v.tensor_copy(lsb[:], ps_l[0:1, :]).then_inc(SS["ev2"].h, 1)
                    v.wait_ge(SS["ev2"].h, vb + 5)
                    v.reciprocal(rlv[:], lsb[:]).then_inc(SS["ev2"].h, 1)
                    v.wait_ge(SS["ev2"].h, vb + 6)
                    v.tensor_copy(rl_bf[:], rlv[:]).then_inc(SS["ev2"].h, 1)
                    v.wait_ge(SS["pe"].h, hb + _a_pe_s(s) + 3 * _nkb(s) + 1)
                    v.tensor_copy(rb_s[:], ps_sm[:]).then_inc(SS["ev2"].h, 1)
                    gq = h * NSUP + s
                    v.wait_ge(SS["pe"].h, pe_pv(h, s, _nkb(s) - 1))
                    v.wait_ge(SS["ev2"].h, vb + 8)
                    if gq >= 2:
                        v.wait_ge(SS[f"st{gq % 2}"].h, b_st[gq % 2] + 16 * ((gq - 2) // 2 + 1))
                    v.tensor_mul(cxs[gq % 2][:], ps_o[s % 2][:], rb_s[:]).then_inc(SS["ev2"].h, 1)
        block.vector(vc)

        def sy(sy_):
            for h in range(NH):
                for s in range(NSUP):
                    gq = h * NSUP + s
                    sy_.wait_ge(SS["ev2"].h, b_vc + A_VC_HEAD * h + _a_vc_s(s) + 9)
                    sy_.dma_start(out=cc2_t.ap()[h * P:(h + 1) * P, s * TCH:(s + 1) * TCH],
                                  in_=cxs[gq % 2][:]).then_inc(SS[f"st{gq % 2}"].h, 16)
            sy_.wait_ge(SS["st0"].h, b_st[0] + 16 * 32)
            sy_.wait_ge(SS["st1"].h, b_st[1] + 16 * 32)
        block.sync(sy)

        SS["pe"].n = b_pe + A_PE_HEAD * NH
        SS["ev"].n = b_sc + A_SC_HEAD * NH
        SS["ev2"].n = b_vc + A_VC_HEAD * NH
        SS["ldw0"].n = b_kk + 16 * NH
        SS["ldw1"].n = b_kv + 16 * NH
        SS["gq0"].n = b_q + 16 * NH
        SS["st0"].n = b_st[0] + 16 * 32
        SS["st1"].n = b_st[1] + 16 * 32
    barrier(block, SS)



def emit_fwht_T(nc, block, SS, G, wdn_t, tdc_t):
    n = 8192
    nst = 13
    b_ldx = SS["ldx"].n
    b_pe = SS["pe"].n
    b_vc = SS["ev2"].n
    b_st = SS["st0"].n
    VPT = 2 * nst + 1 + 8
    PPT = 64
    with ExitStack() as cx:
        uid = _uid()
        ec = cx.enter_context
        ba = ec(nc.sbuf_tensor(f"ft_ba_{uid}", [P, n], F32))
        bb = ec(nc.sbuf_tensor(f"ft_bb_{uid}", [P, n], F32))
        tcb = ec(nc.sbuf_tensor(f"ft_tcb_{uid}", [P, n], BF16))
        tout = ec(nc.sbuf_tensor(f"ft_tout_{uid}", [P, 8, P], BF16))
        ptr = [ec(nc.psum_tensor(f"ft_ptr{j}_{uid}", [P, 8 * P], BF16)) for j in range(2)]

        def gp(g):
            for i in range(2):
                if i >= 1:
                    g.wait_ge(SS["ev2"].h, b_vc + VPT * i - 8)
                g.dma_start(out=ba[:], in_=wdn_t.ap()[i * P:(i + 1) * P, :]).then_inc(SS["ldx"].h, 16)
        block.gpsimd(gp)

        def vc(v):
            for i in range(2):
                vb = b_vc + VPT * i
                cnt = 0
                v.wait_ge(SS["ldx"].h, b_ldx + 16 * (i + 1))
                if i >= 1:
                    v.wait_ge(SS["st0"].h, b_st + 16 * 8 * i)
                src_, dst = ba, bb
                h = 1
                for stg in range(nst):
                    sv = src_[:].rearrange("p (g two h) -> p g two h", two=2, h=h)
                    dv = dst[:].rearrange("p (g two h) -> p g two h", two=2, h=h)
                    v.tensor_add(dv[:, :, 0, :], sv[:, :, 0, :], sv[:, :, 1, :]).then_inc(SS["ev2"].h, 1)
                    v.tensor_sub(dv[:, :, 1, :], sv[:, :, 0, :], sv[:, :, 1, :]).then_inc(SS["ev2"].h, 1)
                    cnt += 2
                    v.wait_ge(SS["ev2"].h, vb + cnt)
                    src_, dst = dst, src_
                    h *= 2
                v.tensor_copy(tcb[:], src_[:]).then_inc(SS["ev2"].h, 1)
                cnt += 1
                v.wait_ge(SS["ev2"].h, vb + cnt)
                for grp in range(8):
                    v.wait_ge(SS["pe"].h, b_pe + PPT * i + grp * 8 + 8)
                    if i > 0 or grp > 0:
                        v.wait_ge(SS["st0"].h, b_st + 16 * (8 * i + grp))
                    v.tensor_copy(tout[:], ptr[grp % 2][:].rearrange("p (j m) -> p j m", m=P)).then_inc(SS["ev2"].h, 1)
        block.vector(vc)

        def tns(tn):
            for i in range(2):
                tn.wait_ge(SS["ev2"].h, b_vc + VPT * i + 2 * nst + 1)
                for grp in range(8):
                    if i > 0 or grp > 1:
                        gidx = 8 * i + grp
                        pv, gv = divmod(gidx - 2, 8)
                        tn.wait_ge(SS["ev2"].h, b_vc + VPT * pv + 2 * nst + 1 + gv + 1)
                    for j in range(8):
                        tn.transpose(ptr[grp % 2][:, j * P:(j + 1) * P],
                                     tcb[:, (grp * 8 + j) * P:(grp * 8 + j + 1) * P],
                                     G["eye_bf"][:]).then_inc(SS["pe"].h, 1)
        block.tensor(tns)

        def sy(sy_):
            for i in range(2):
                for grp in range(8):
                    sy_.wait_ge(SS["ev2"].h, b_vc + VPT * i + 2 * nst + 1 + grp + 1)
                    sy_.dma_start(out=tdc_t.ap()[grp * 1024:(grp + 1) * 1024, i * P:(i + 1) * P]
                                  .rearrange("(j p) m -> p j m", p=P),
                                  in_=tout[:]).then_inc(SS["st0"].h, 16)
            sy_.wait_ge(SS["st0"].h, b_st + 16 * 16)
        block.sync(sy)

        SS["ldx"].n = b_ldx + 32
        SS["pe"].n = b_pe + 2 * PPT
        SS["ev2"].n = b_vc + 2 * VPT
        SS["st0"].n = b_st + 16 * 16
    barrier(block, SS)


def emit_collective(nc, block, SS, kind, groups, ins, outs, op=None, sem="cc0",
                    fire_only=False):
    from concourse.aot_env import allow_ici_all_to_all

    if op is None:
        op = ALU.bypass

    def gp(g):
        with allow_ici_all_to_all():
            g.collective_compute(kind, op, replica_groups=groups,
                                 ins=ins, outs=outs).then_inc(SS[sem].h, 1)
        if not fire_only:
            g.wait_ge(SS[sem].h, 1)
    block.gpsimd(gp)
    SS[sem].n = 1
    if not fire_only:
        barrier(block, SS)


def emit_dbgcopy(nc, block, SS, dst_t, src_t):
    b_cc = SS["cc"].n

    def gp(g):
        g.dma_start(out=dst_t.ap(), in_=src_t.ap()).then_inc(SS["cc"].h, 16)
        g.wait_ge(SS["cc"].h, b_cc + 16)
    block.gpsimd(gp)
    SS["cc"].n = b_cc + 16
    barrier(block, SS)


def emit_resadd(nc, block, SS, G, res_t, delta_t, bvec_ap=None, src_t=None):
    if src_t is None:
        src_t = res_t
    nm, ntc = D // P, TLOC // TCH
    nidx = nm * ntc
    b_ldr = [SS["ldr0"].n, SS["ldr1"].n]
    b_ldv = SS["ldv"].n
    b_ev2 = SS["ev2"].n
    b_st = [SS["st0"].n, SS["st1"].n]
    has_bias = bvec_ap is not None
    with ExitStack() as cx:
        uid = _uid()
        ec = cx.enter_context
        rb = [ec(nc.sbuf_tensor(f"ra_r{j}_{uid}", [P, TCH], F32)) for j in range(2)]
        db = [ec(nc.sbuf_tensor(f"ra_d{j}_{uid}", [P, TCH], BF16)) for j in range(2)]
        vb = ec(nc.sbuf_tensor(f"ra_vb_{uid}", [P, nm], F32))

        def sy(sy_):
            if has_bias:
                sy_.dma_start(out=vb[:], in_=bvec_ap).then_inc(SS["ldv"].h, 16)
            for idx in range(nidx):
                t, m = divmod(idx, nm)
                if idx >= 2:
                    sy_.wait_ge(SS[f"st{idx % 2}"].h, b_st[idx % 2] + 16 * ((idx - 2) // 2 + 1))
                sy_.dma_start(out=rb[idx % 2][:],
                              in_=src_t.ap()[m * P:(m + 1) * P, t * TCH:(t + 1) * TCH]).then_inc(SS[f"ldr{idx % 2}"].h, 16)
                sy_.dma_start(out=db[idx % 2][:],
                              in_=delta_t.ap()[m * P:(m + 1) * P, t * TCH:(t + 1) * TCH]).then_inc(SS[f"ldr{idx % 2}"].h, 16)
        block.sync(sy)

        def vc(v):
            if has_bias:
                v.wait_ge(SS["ldv"].h, b_ldv + 16)
            for idx in range(nidx):
                t, m = divmod(idx, nm)
                v.wait_ge(SS[f"ldr{idx % 2}"].h, b_ldr[idx % 2] + 32 * (idx // 2 + 1))
                if idx >= 2:
                    v.wait_ge(SS[f"st{idx % 2}"].h, b_st[idx % 2] + 16 * ((idx - 2) // 2 + 1))
                if has_bias:
                    v.scalar_tensor_tensor(rb[idx % 2][:], rb[idx % 2][:], vb[:, m:m + 1],
                                           db[idx % 2][:], ALU.add, ALU.add).then_inc(SS["ev2"].h, 1)
                else:
                    v.tensor_add(rb[idx % 2][:], rb[idx % 2][:], db[idx % 2][:]).then_inc(SS["ev2"].h, 1)
        block.vector(vc)

        def sc(s_):
            for idx in range(nidx):
                t, m = divmod(idx, nm)
                s_.wait_ge(SS["ev2"].h, b_ev2 + idx + 1)
                s_.dma_start(out=res_t.ap()[m * P:(m + 1) * P, t * TCH:(t + 1) * TCH],
                             in_=rb[idx % 2][:]).then_inc(SS[f"st{idx % 2}"].h, 16)
            s_.wait_ge(SS["st0"].h, b_st[0] + 16 * ((nidx + 1) // 2))
            s_.wait_ge(SS["st1"].h, b_st[1] + 16 * (nidx // 2))
        block.scalar(sc)

        SS["ldv"].n = b_ldv + (16 if has_bias else 0)
        SS["ldr0"].n = b_ldr[0] + 32 * ((nidx + 1) // 2)
        SS["ldr1"].n = b_ldr[1] + 32 * (nidx // 2)
        SS["ev2"].n = b_ev2 + nidx
        SS["st0"].n = b_st[0] + 16 * ((nidx + 1) // 2)
        SS["st1"].n = b_st[1] + 16 * (nidx // 2)
    barrier(block, SS)



def build_megakernel():
    nc = bass.Bass("TRN2", target_bir_lowering=False)
    x_fm = nc.dram_tensor("x_fm", [D, TLOC], F32, kind="ExternalInput")
    w_qkv0 = nc.dram_tensor("w_qkv0", [D, 3072], F32, kind="ExternalInput")
    b_qk0 = nc.dram_tensor("b_qk0", [P, 16], F32, kind="ExternalInput")
    b_v0 = nc.dram_tensor("b_v0", [1, 1024], F32, kind="ExternalInput")
    w_out0 = nc.dram_tensor("w_out0", [NH * P, D], F32, kind="ExternalInput")
    b_out0 = nc.dram_tensor("b_out0", [P, 16], F32, kind="ExternalInput")
    h2048 = nc.dram_tensor("h2048", [D, D], F32, kind="ExternalInput")
    su_ug = nc.dram_tensor("su_ug", [P, 16], F32, kind="ExternalInput")
    w_upT = nc.dram_tensor("w_upT", [D, 8192], F32, kind="ExternalInput")
    ws_u = nc.dram_tensor("ws_u", [P, 64], F32, kind="ExternalInput")
    su_dp = nc.dram_tensor("su_dp", [P, 64], F32, kind="ExternalInput")
    w_gateT = nc.dram_tensor("w_gateT", [D, 8192], F32, kind="ExternalInput")
    ws_g = nc.dram_tensor("ws_g", [P, 64], F32, kind="ExternalInput")
    wdn = nc.dram_tensor("wdn", [256, 8192], F32, kind="ExternalInput")
    ws_dp = nc.dram_tensor("ws_dp", [P, 16], F32, kind="ExternalInput")
    w_qkv1 = nc.dram_tensor("w_qkv1", [D, 3072], F32, kind="ExternalInput")
    w_o1 = nc.dram_tensor("w_o1", [NH * P, D], F32, kind="ExternalInput")
    b_o1 = nc.dram_tensor("b_o1", [P, 16], F32, kind="ExternalInput")
    w_up1T = nc.dram_tensor("w_up1T", [D, 8192], F32, kind="ExternalInput")
    us1 = nc.dram_tensor("us1", [P, 64], F32, kind="ExternalInput")
    w_dn1T = nc.dram_tensor("w_dn1T", [8192, D], F32, kind="ExternalInput")
    ds1 = nc.dram_tensor("ds1", [P, 16], F32, kind="ExternalInput")
    osc = nc.dram_tensor("osc", [P, 1], F32, kind="ExternalInput")
    masks_in = nc.dram_tensor("masks_in", [4 * P, TCH], F32, kind="ExternalInput")
    onesk_in = nc.dram_tensor("onesk_in", [P, 1], F32, kind="ExternalInput")
    onesb_in = nc.dram_tensor("onesb_in", [1, P], F32, kind="ExternalInput")
    eye_in = nc.dram_tensor("eye_in", [P, P], F32, kind="ExternalInput")
    eps_in = nc.dram_tensor("eps_in", [P, 1], F32, kind="ExternalInput")
    y = nc.dram_tensor("y", [D, TLOC], F32, kind="ExternalOutput")
    res = nc.dram_tensor("res", [D, TLOC], F32)
    xnc = nc.dram_tensor("xnc", [D, TLOC], BF16)
    xng = nc.dram_tensor("xng", [2 * D, TLOC], BF16)
    qk = nc.dram_tensor("qk", [2 * NH * P, S], BF16)
    vt = nc.dram_tensor("vt", [S, NH * P], BF16)
    ctxl = nc.dram_tensor("ctxl", [NH * P, S], BF16)
    prt = nc.dram_tensor("prt", [2 * D, TLOC], BF16)
    rso = nc.dram_tensor("rso", [D, TLOC], BF16)
    x2t = nc.dram_tensor("x2t", [D, TLOC], BF16)
    xupt = nc.dram_tensor("xupt", [8192, TLOC], BF16)
    x3t = nc.dram_tensor("x3t", [8192, TLOC], BF16)
    hidt = nc.dram_tensor("hidt", [8192, TLOC], BF16)
    tdc = nc.dram_tensor("tdc", [8192, 256], BF16)
    tdg = nc.dram_tensor("tdg", [8 * 8192, 256], BF16)

    PAIRS = [[0, 1], [2, 3], [4, 5], [6, 7]]
    ALL8 = [list(range(8))]

    with ExitStack() as cx:
        uid = _uid()
        ec = cx.enter_context
        ones_bf = ec(nc.sbuf_tensor(f"g_ones_{uid}", [P, 1], BF16))
        onesb_bf = ec(nc.sbuf_tensor(f"g_onesb_{uid}", [1, P], BF16))
        mbuf = ec(nc.sbuf_tensor(f"g_mbuf_{uid}", [P, 4, TCH], BF16))
        eye_bf = ec(nc.sbuf_tensor(f"g_eye_{uid}", [P, P], BF16))
        epst = ec(nc.sbuf_tensor(f"g_eps_{uid}", [P, 1], F32))
        sems = {}
        for nm in ("bar", "ldx", "ldw0", "ldw1", "ldv", "ldr0", "ldr1", "gq0", "gq1", "gldc", "xpms",
                   "vini", "pe", "ev", "ev2", "sg", "st0", "st1", "cc",
                   "ldr2", "ldr3", "cc0", "cc1", "cc2s", "cc3", "cc4"):
            sems[nm] = SemT(ec(nc.semaphore(nm)))
        block = ec(nc.Block())
        SS = sems
        G = {"ones_bf": ones_bf, "onesb_bf": onesb_bf, "mbuf": mbuf,
             "eye_bf": eye_bf, "epst": epst, "ldv_const": 80}

        def pro(g):
            g.dma_start(out=ones_bf[:], in_=onesk_in.ap()).then_inc(SS["gldc"].h, 16)
            g.dma_start(out=onesb_bf[:], in_=onesb_in.ap()).then_inc(SS["gldc"].h, 16)
            g.dma_start(out=mbuf[:], in_=masks_in.ap().rearrange("(i p) n -> p i n", p=P)).then_inc(SS["gldc"].h, 16)
            g.dma_start(out=eye_bf[:], in_=eye_in.ap()).then_inc(SS["gldc"].h, 16)
            g.dma_start(out=epst[:], in_=eps_in.ap()).then_inc(SS["gldc"].h, 16)
            g.wait_ge(SS["gldc"].h, 80)
        block.gpsimd(pro)
        SS["gldc"].n = 80
        barrier(block, SS)

        def fm_chunk(tensor):
            return lambda th: tensor.ap().rearrange("(k p) n -> p k n", p=P)

        def fm_chunk_q(tensor):
            return lambda th: tensor.ap()[:, th * 2 * TCH:(th + 1) * 2 * TCH].rearrange("(k p) n -> p k n", p=P)

        def xng_chunk(th, xb):
            return [(xb[:, :, r * TLOC:(r + 1) * TLOC],
                     xng.ap()[r * D:(r + 1) * D, :].rearrange("(k p) n -> p k n", p=P))
                    for r in range(2)]

        def xng_tile(tt):
            r, tl = divmod(tt, 16)
            return xng.ap()[r * D:(r + 1) * D, tl * P:(tl + 1) * P].rearrange("(k p) n -> p k n", p=P)

        def w_slab(w, c0=0):
            return lambda si: w.ap()[:, c0 + si * 512:c0 + (si + 1) * 512].rearrange("(k p) n -> p k n", p=P)

        def w_slab_dn1(si):
            return w_dn1T.ap()[:, si * P:(si + 1) * P].rearrange("(k p) n -> p k n", p=P)

        def ctxl_chunk(th):
            return ctxl.ap().rearrange("(k p) n -> p k n", p=P)

        def prt_tile(t, m):
            return prt.ap()[(t // 4) * D + m * P:(t // 4) * D + (m + 1) * P,
                            (t % 4) * TCH:(t % 4 + 1) * TCH]

        def w_tdg_slab(si):
            r, ml = divmod(si, 2)
            return tdg.ap()[r * 8192:(r + 1) * 8192, ml * P:(ml + 1) * P].rearrange("(k p) n -> p k n", p=P)

        def tile_of(tensor):
            return lambda t, m: tensor.ap()[m * P:(m + 1) * P, t * TCH:(t + 1) * TCH]

        def qk_tile(t, m):
            return qk.ap()[m * P:(m + 1) * P, t * TCH:(t + 1) * TCH]

        def vt_tile(tt, mc):
            return vt.ap()[tt * P:(tt + 1) * P, mc * TCH:(mc + 1) * TCH]

        emit_ln(nc, block, SS, G, x_fm.ap(), xnc.ap())
        emit_fwht_T(nc, block, SS, G, wdn, tdc)
        emit_collective(nc, block, SS, "AllGather", ALL8, [tdc.ap()], [tdg.ap()], sem="cc2s",
                        fire_only=True)
        emit_collective(nc, block, SS, "AllGather", PAIRS, [xnc.ap()], [xng.ap()], sem="cc0")
        emit_matmul(nc, block, SS, G, D, 2048, S, xng_chunk, None, "cast",
                    qk_tile, out_dtype="bf16", has_bias=True, bvec=b_qk0.ap(),
                    wslab_ap=w_slab(w_qkv0), xnparts=2)
        emit_matmul_tm(nc, block, SS, G, D, 1024, S, xng_tile, w_qkv0.ap()[:, 2048:3072],
                       vt_tile, bias_row_ap=b_v0.ap())
        emit_attention(nc, block, SS, G, qk, vt, ctxl)
        emit_matmul(nc, block, SS, G, NH * P, D, S, ctxl_chunk, None, "cast",
                    prt_tile, out_dtype="bf16", wslab_ap=w_slab(w_out0))
        emit_collective(nc, block, SS, "ReduceScatter", PAIRS, [prt.ap()], [rso.ap()], op=ALU.add, sem="cc1")
        emit_ln(nc, block, SS, G, None, xnc.ap(),
                fold=(lambda t: x_fm.ap()[:, t * TCH:(t + 1) * TCH].rearrange("(k p) n -> p k n", p=P),
                      rso, b_out0.ap(), res))
        if DBG:
            d_res1 = nc.dram_tensor("d_res1", [D, TLOC], F32, kind="ExternalOutput")
            emit_dbgcopy(nc, block, SS, d_res1, res)
        emit_matmul(nc, block, SS, G, D, D, TLOC, fm_chunk(xnc), None, "cast",
                    tile_of(x2t), out_dtype="bf16", cmul=1.0 / (64.0 * math.sqrt(D)),
                    xpm_ap=su_ug.ap(), wslab_ap=w_slab(h2048))
        emit_matmul(nc, block, SS, G, D, 8192, TLOC, fm_chunk(x2t), None, "cast",
                    tile_of(xupt), out_dtype="bf16", has_scale=True, svec_ap=ws_u.ap(),
                    cmul=64.0, sfold_ap=su_dp.ap(), wslab_ap=w_slab(w_upT))
        emit_matmul(nc, block, SS, G, D, 8192, TLOC, fm_chunk(x2t), None, "cast",
                    tile_of(x3t), out_dtype="bf16", act="silu", has_scale=True,
                    svec_ap=ws_g.ap(), cmul=64.0, mult_tile_ap=tile_of(xupt),
                    wslab_ap=w_slab(w_gateT))
        emit_matmul(nc, block, SS, G, 8192, D, TLOC, fm_chunk_q(x3t), None, "bf16",
                    tile_of(res), out_dtype="f32", has_scale=True, svec_ap=ws_dp.ap(),
                    cmul=1.0 / math.sqrt(8192.0), resid_tile_ap=tile_of(res),
                    wslab_ap=w_tdg_slab, pre_wait="cc2s")
        if DBG:
            d_res2 = nc.dram_tensor("d_res2", [D, TLOC], F32, kind="ExternalOutput")
            emit_dbgcopy(nc, block, SS, d_res2, res)
        emit_ln(nc, block, SS, G, res.ap(), xnc.ap())
        emit_collective(nc, block, SS, "AllGather", PAIRS, [xnc.ap()], [xng.ap()], sem="cc3")
        emit_matmul(nc, block, SS, G, D, 2048, S, xng_chunk, None, "cast",
                    qk_tile, out_dtype="bf16", wslab_ap=w_slab(w_qkv1), xnparts=2)
        emit_matmul_tm(nc, block, SS, G, D, 1024, S, xng_tile, w_qkv1.ap()[:, 2048:3072],
                       vt_tile)
        emit_attention(nc, block, SS, G, qk, vt, ctxl)
        emit_matmul(nc, block, SS, G, NH * P, D, S, ctxl_chunk, None, "cast",
                    prt_tile, out_dtype="bf16", wslab_ap=w_slab(w_o1))
        emit_collective(nc, block, SS, "ReduceScatter", PAIRS, [prt.ap()], [rso.ap()], op=ALU.add, sem="cc4")
        emit_ln(nc, block, SS, G, None, xnc.ap(),
                fold=(lambda t: res.ap()[:, t * TCH:(t + 1) * TCH].rearrange("(k p) n -> p k n", p=P),
                      rso, b_o1.ap(), res))
        emit_matmul(nc, block, SS, G, D, 8192, TLOC, fm_chunk(xnc), None, "cast",
                    tile_of(hidt), out_dtype="bf16", act="relu", has_scale=True,
                    svec_ap=us1.ap(), sign_w=True, wslab_ap=w_slab(w_up1T))
        emit_matmul(nc, block, SS, G, 8192, D, TLOC, fm_chunk_q(hidt), None, "cast",
                    tile_of(y), out_dtype="f32", has_scale=True, svec_ap=ds1.ap(),
                    sign_w=True, psc_ap=osc.ap(), resid_tile_ap=tile_of(res),
                    wslab_ap=w_slab_dn1)
        if DBG:
            dbgs = [("d_res", res), ("d_rso", rso), ("d_x2", x2t), ("d_xup", xupt),
                    ("d_x3", x3t), ("d_qk", qk), ("d_vt", vt), ("d_ctxl", ctxl),
                    ("d_xnc", xnc), ("d_hid", hidt), ("d_tdg", tdg), ("d_xng", xng)]
            douts = {}
            for nm, t in dbgs:
                douts[nm] = nc.dram_tensor(nm, list(t.shape), t.dtype, kind="ExternalOutput")
            b_cc = SS["cc"].n

            def dump(g):
                for i, (nm, t) in enumerate(dbgs):
                    g.dma_start(out=douts[nm].ap(), in_=t.ap()).then_inc(SS["cc"].h, 16)
                g.wait_ge(SS["cc"].h, b_cc + 16 * len(dbgs))
            block.gpsimd(dump)
            SS["cc"].n = b_cc + 16 * len(dbgs)
            barrier(block, SS)
    return nc


def _hadamard(n):
    h = np.array([[1.0]], dtype=np.float32)
    while h.shape[0] < n:
        h = np.kron(h, np.array([[1.0, 1.0], [1.0, -1.0]], dtype=np.float32))
    return h


def _prelay(v, n):
    return np.ascontiguousarray(np.asarray(v, dtype=np.float32).reshape(n, P).T)


def make_inmaps(inputs):
    ac = np.ascontiguousarray
    f32 = lambda a: np.asarray(a, dtype=np.float32)
    x = f32(inputs["L0_attn_x"]).reshape(NTOK, D)
    H = _hadamard(D)
    mask = np.zeros((4 * P, TCH), np.float32)
    for j in range(4):
        for kp in range(P):
            mask[j * P + kp, j * P + kp:] = 1.0
    W_in = f32(inputs["L0_attn_W_in"])
    b_in = f32(inputs["L0_attn_b_in"])
    Wq, Wk, Wv = f32(inputs["L1_attn_Wq"]), f32(inputs["L1_attn_Wk"]), f32(inputs["L1_attn_Wv"])
    ws_ug = f32(inputs["L0_ffn_Wscale_ug"])
    in_maps = []
    for c in range(NCORES):
        b, r = divmod(c, 2)
        hs = r * NH
        qcols = slice(hs * P, (hs + NH) * P)
        kcols = slice(D + hs * P, D + (hs + NH) * P)
        vcols = slice(2 * D + hs * P, 2 * D + (hs + NH) * P)
        m = {
            "x_fm": ac(x[b * S + r * TLOC:(b * S + (r + 1) * TLOC)].T),
            "w_qkv0": ac(np.concatenate([W_in[:, qcols], W_in[:, kcols], W_in[:, vcols]], axis=1)),
            "b_qk0": _prelay(np.concatenate([b_in[qcols], b_in[kcols]]), 16),
            "b_v0": ac(b_in[vcols].reshape(1, 1024)),
            "w_out0": ac(f32(inputs["L0_attn_W_out"])[hs * P:(hs + NH) * P, :]),
            "b_out0": _prelay(f32(inputs["L0_attn_b_out"]), 16),
            "h2048": H,
            "su_ug": _prelay(f32(inputs["L0_ffn_SU_ug"]), 16),
            "w_upT": ac(f32(inputs["L0_ffn_W_up"]).T),
            "ws_u": _prelay(ws_ug[:8192], 64),
            "su_dp": _prelay(f32(inputs["L0_ffn_SU_dp"]), 64),
            "w_gateT": ac(f32(inputs["L0_ffn_W_gate"]).T),
            "ws_g": _prelay(ws_ug[8192:], 64),
            "wdn": ac(f32(inputs["L0_ffn_W_down"])[c * 256:(c + 1) * 256]),
            "ws_dp": _prelay(f32(inputs["L0_ffn_Wscale_dp"]), 16),
            "w_qkv1": ac(np.concatenate([Wq[:, qcols.start:qcols.stop],
                                         Wk[:, qcols.start:qcols.stop],
                                         Wv[:, qcols.start:qcols.stop]], axis=1)),
            "w_o1": ac(f32(inputs["L1_attn_Wo"])[hs * P:(hs + NH) * P, :]),
            "b_o1": _prelay(f32(inputs["L1_attn_bo"]), 16),
            "w_up1T": ac(f32(inputs["L1_ffn_up_weight"]).T),
            "us1": _prelay(f32(inputs["L1_ffn_up_scale"]), 64),
            "w_dn1T": ac(f32(inputs["L1_ffn_down_weight"]).T),
            "ds1": _prelay(f32(inputs["L1_ffn_down_scale"]), 16),
            "osc": ac(np.broadcast_to(f32(inputs["L1_ffn_output_scale"]).reshape(1, 1), (P, 1))),
            "masks_in": mask,
            "onesk_in": np.ones((P, 1), np.float32),
            "onesb_in": np.ones((1, P), np.float32),
            "eye_in": np.eye(P, dtype=np.float32),
            "eps_in": np.full((P, 1), EPS, np.float32),
        }
        in_maps.append(m)
    return in_maps


def kernel(**inputs):
    if "nc" not in _CACHE:
        _CACHE["nc"] = build_megakernel()
    nc = _CACHE["nc"]
    in_maps = make_inmaps(inputs)
    res = run_bass_kernel_spmd(nc, in_maps, core_ids=list(range(NCORES)))
    _CACHE["last_results"] = res
    out = np.empty((NTOK, D), np.float32)
    for c in range(NCORES):
        out[c * TLOC:(c + 1) * TLOC] = res.results[c]["y"].T
    return out.reshape(B, S, D)


# version 11
# speedup vs baseline: 1.1881x; 1.0037x over previous
import math
from contextlib import ExitStack

import numpy as np
import ml_dtypes
import concourse.bass as bass
import concourse.mybir as mybir
from concourse.bass_utils import run_bass_kernel_spmd

F32 = mybir.dt.float32
BF16 = mybir.dt.bfloat16
AF = mybir.ActivationFunctionType
ALU = mybir.AluOpType
AX = mybir.AxisListType

P = 128
NCORES = 8
B, S, D = 4, 4096, 2048
NTOK = B * S
TLOC = NTOK // NCORES
TCH = 512
EPS = 1e-5
NH = 8
NSUP = 8
SCL = 1.0 / math.sqrt(128.0)
DBG = False

_CACHE = {}
_UIDC = [0]


def _uid():
    _UIDC[0] += 1
    return _UIDC[0]


class SemT:
    def __init__(self, h):
        self.h = h
        self.n = 0


def barrier(block, SS):
    bar = SS["bar"]
    base = bar.n
    for name in ("sync", "gpsimd", "tensor", "scalar", "vector"):
        def mk(eng):
            eng.nop().then_inc(bar.h, 1)
            eng.wait_ge(bar.h, base + 5)
        getattr(block, name)(mk)
    bar.n += 5



def emit_ln(nc, block, SS, G, src_ap, dst_ap, fold=None):
    nk = D // P
    ntc = TLOC // TCH
    F = 2 * nk if fold else 0
    VPT = F + 10 + 2 * nk
    SPT = nk + 1
    PPT = nk + 3
    NXD = 2 if fold else 1
    b_ldx, b_pe, b_ev, b_ev2, b_st = SS["ldx"].n, SS["pe"].n, SS["ev"].n, SS["ev2"].n, SS["st0"].n
    b_ldv = SS["ldv"].n
    nst_per = 2 if fold else 1
    with ExitStack() as cx:
        uid = _uid()
        ec = cx.enter_context
        xb = ec(nc.sbuf_tensor(f"ln_xb_{uid}", [P, nk, TCH], BF16))
        if fold:
            r32 = ec(nc.sbuf_tensor(f"ln_r32_{uid}", [P, nk, TCH], F32))
            db = ec(nc.sbuf_tensor(f"ln_db_{uid}", [P, nk, TCH], BF16))
            bvv = ec(nc.sbuf_tensor(f"ln_bv_{uid}", [P, nk], F32))
        sqb = ec(nc.sbuf_tensor(f"ln_sqb_{uid}", [P, TCH], BF16))
        xnb = ec(nc.sbuf_tensor(f"ln_xnb_{uid}", [P, nk, TCH], BF16))
        mrow = ec(nc.sbuf_tensor(f"ln_m_{uid}", [1, TCH], F32))
        erow = ec(nc.sbuf_tensor(f"ln_e_{uid}", [1, TCH], F32))
        msq = ec(nc.sbuf_tensor(f"ln_msq_{uid}", [1, TCH], F32))
        var = ec(nc.sbuf_tensor(f"ln_var_{uid}", [1, TCH], F32))
        srt = ec(nc.sbuf_tensor(f"ln_srt_{uid}", [1, TCH], F32))
        rr = ec(nc.sbuf_tensor(f"ln_rr_{uid}", [1, TCH], F32))
        nmr = ec(nc.sbuf_tensor(f"ln_nmr_{uid}", [1, TCH], F32))
        a_bf = ec(nc.sbuf_tensor(f"ln_abf_{uid}", [1, TCH], BF16))
        bb_bf = ec(nc.sbuf_tensor(f"ln_bbf_{uid}", [1, TCH], BF16))
        a_sb = ec(nc.sbuf_tensor(f"ln_asb_{uid}", [P, TCH], BF16))
        b_sb = ec(nc.sbuf_tensor(f"ln_bsb_{uid}", [P, TCH], BF16))
        ps_m = ec(nc.psum_tensor(f"ln_psm_{uid}", [1, TCH], F32))
        ps_s = ec(nc.psum_tensor(f"ln_pss_{uid}", [1, TCH], F32))
        ps_a = ec(nc.psum_tensor(f"ln_psa_{uid}", [P, TCH], F32))
        ps_b = ec(nc.psum_tensor(f"ln_psb_{uid}", [P, TCH], F32))

        def gp(g):
            for t in range(ntc):
                if t >= 1:
                    g.wait_ge(SS["ev2"].h, b_ev2 + VPT * t)
                    if fold:
                        g.wait_ge(SS["st0"].h, b_st + 16 * nst_per * t)
                if fold:
                    g.dma_start(out=r32[:], in_=fold[0](t)).then_inc(SS["ldx"].h, 16)
                    g.dma_start(out=db[:], in_=fold[1].ap()[:, t * TCH:(t + 1) * TCH]
                                .rearrange("(k p) n -> p k n", p=P)).then_inc(SS["ldx"].h, 16)
                else:
                    g.dma_start(out=xb[:], in_=src_ap[:, t * TCH:(t + 1) * TCH]
                                .rearrange("(k p) n -> p k n", p=P)).then_inc(SS["ldx"].h, 16)
        block.gpsimd(gp)

        def tns(tn):
            tn.wait_ge(SS["gldc"].h, G["ldv_const"])
            for t in range(ntc):
                pb = b_pe + PPT * t
                vbp = b_ev2 + VPT * (t - 1)
                vb = b_ev2 + VPT * t
                if fold:
                    tn.wait_ge(SS["ev2"].h, vb + F)
                else:
                    tn.wait_ge(SS["ldx"].h, b_ldx + 16 * NXD * (t + 1))
                if t >= 1:
                    tn.wait_ge(SS["ev2"].h, vbp + F + 1)
                ins = None
                for k in range(nk):
                    ins = tn.matmul(ps_m[:], G["ones_bf"][:], xb[:, k, :],
                                    start=(k == 0), stop=(k == nk - 1))
                ins.then_inc(SS["pe"].h, 1)
                for k in range(nk):
                    tn.wait_ge(SS["ev"].h, b_ev + SPT * t + k + 1)
                    if t >= 1 and k == 0:
                        tn.wait_ge(SS["ev2"].h, vbp + F + 2)
                    tn.matmul(ps_s[:], G["ones_bf"][:], sqb[:],
                              start=(k == 0), stop=(k == nk - 1)).then_inc(SS["pe"].h, 1)
                tn.wait_ge(SS["ev2"].h, vb + F + 7)
                if t >= 1:
                    tn.wait_ge(SS["ev2"].h, vbp + F + 9)
                tn.matmul(ps_a[:], G["onesb_bf"][:], a_bf[:], start=True, stop=True).then_inc(SS["pe"].h, 1)
                tn.wait_ge(SS["ev2"].h, vb + F + 8)
                if t >= 1:
                    tn.wait_ge(SS["ev2"].h, vbp + F + 10)
                tn.matmul(ps_b[:], G["onesb_bf"][:], bb_bf[:], start=True, stop=True).then_inc(SS["pe"].h, 1)
        block.tensor(tns)

        def sc(s_):
            for t in range(ntc):
                vb = b_ev2 + VPT * t
                if fold:
                    s_.wait_ge(SS["ev2"].h, vb + F)
                else:
                    s_.wait_ge(SS["ldx"].h, b_ldx + 16 * NXD * (t + 1))
                for k in range(nk):
                    if k >= 1:
                        s_.wait_ge(SS["pe"].h, b_pe + PPT * t + 1 + k)
                    elif t >= 1:
                        s_.wait_ge(SS["pe"].h, b_pe + PPT * t)
                    s_.activation(sqb[:], xb[:, k, :], AF.Square).then_inc(SS["ev"].h, 1)
                s_.wait_ge(SS["ev2"].h, vb + F + 4)
                s_.activation(srt[:], var[:], AF.Sqrt, bias=G["epst"][0:1, :]).then_inc(SS["ev"].h, 1)
        block.scalar(sc)

        def vc(v):
            if fold:
                v.wait_ge(SS["ldv"].h, b_ldv + 16)
            for t in range(ntc):
                vb = b_ev2 + VPT * t
                if fold:
                    v.wait_ge(SS["ldx"].h, b_ldx + 16 * NXD * (t + 1))
                    for k in range(nk):
                        v.scalar_tensor_tensor(r32[:, k, :], r32[:, k, :], bvv[:, k:k + 1],
                                               db[:, k, :], ALU.add, ALU.add).then_inc(SS["ev2"].h, 1)
                    v.wait_ge(SS["ev2"].h, vb + nk)
                    for k in range(nk):
                        v.tensor_copy(xb[:, k, :], r32[:, k, :]).then_inc(SS["ev2"].h, 1)
                v.wait_ge(SS["pe"].h, b_pe + PPT * t + 1)
                v.tensor_scalar(mrow[:], ps_m[:], 1.0 / D, None, ALU.mult).then_inc(SS["ev2"].h, 1)
                v.wait_ge(SS["pe"].h, b_pe + PPT * t + 1 + nk)
                v.tensor_scalar(erow[:], ps_s[:], 1.0 / D, None, ALU.mult).then_inc(SS["ev2"].h, 1)
                v.wait_ge(SS["ev2"].h, vb + F + 2)
                v.tensor_mul(msq[:], mrow[:], mrow[:]).then_inc(SS["ev2"].h, 1)
                v.wait_ge(SS["ev2"].h, vb + F + 3)
                v.tensor_sub(var[:], erow[:], msq[:]).then_inc(SS["ev2"].h, 1)
                v.wait_ge(SS["ev"].h, b_ev + SPT * t + nk + 1)
                v.reciprocal(rr[:], srt[:]).then_inc(SS["ev2"].h, 1)
                v.wait_ge(SS["ev2"].h, vb + F + 5)
                v.tensor_mul(nmr[:], mrow[:], rr[:]).then_inc(SS["ev2"].h, 1)
                v.tensor_copy(a_bf[:], rr[:]).then_inc(SS["ev2"].h, 1)
                v.wait_ge(SS["ev2"].h, vb + F + 6)
                v.tensor_scalar(bb_bf[:], nmr[:], -1.0, None, ALU.mult).then_inc(SS["ev2"].h, 1)
                v.wait_ge(SS["pe"].h, b_pe + PPT * t + nk + 2)
                if t >= 1:
                    v.wait_ge(SS["ev2"].h, b_ev2 + VPT * t)
                v.tensor_copy(a_sb[:], ps_a[:]).then_inc(SS["ev2"].h, 1)
                v.wait_ge(SS["pe"].h, b_pe + PPT * t + nk + 3)
                v.tensor_copy(b_sb[:], ps_b[:]).then_inc(SS["ev2"].h, 1)
                v.wait_ge(SS["ev2"].h, vb + F + 10)
                if t >= 1:
                    v.wait_ge(SS["st0"].h, b_st + 16 * nst_per * t)
                for k in range(nk):
                    v.tensor_mul(xnb[:, k, :], xb[:, k, :], a_sb[:]).then_inc(SS["ev2"].h, 1)
                    v.wait_ge(SS["ev2"].h, vb + F + 11 + 2 * k)
                    v.tensor_add(xnb[:, k, :], xnb[:, k, :], b_sb[:]).then_inc(SS["ev2"].h, 1)
        block.vector(vc)

        def sy(sy_):
            if fold:
                sy_.dma_start(out=bvv[:], in_=fold[2]).then_inc(SS["ldv"].h, 16)
            for t in range(ntc):
                if fold:
                    sy_.wait_ge(SS["ev2"].h, b_ev2 + VPT * t + nk)
                    sy_.dma_start(out=fold[3].ap()[:, t * TCH:(t + 1) * TCH]
                                  .rearrange("(k p) n -> p k n", p=P),
                                  in_=r32[:]).then_inc(SS["st0"].h, 16)
                sy_.wait_ge(SS["ev2"].h, b_ev2 + VPT * (t + 1))
                sy_.dma_start(out=dst_ap[:, t * TCH:(t + 1) * TCH].rearrange("(k p) n -> p k n", p=P),
                              in_=xnb[:]).then_inc(SS["st0"].h, 16)
            sy_.wait_ge(SS["st0"].h, b_st + 16 * nst_per * ntc)
        block.sync(sy)

        SS["ldx"].n = b_ldx + 16 * NXD * ntc
        SS["ldv"].n = b_ldv + (16 if fold else 0)
        SS["pe"].n = b_pe + PPT * ntc
        SS["ev"].n = b_ev + SPT * ntc
        SS["ev2"].n = b_ev2 + VPT * ntc
        SS["st0"].n = b_st + 16 * nst_per * ntc
    barrier(block, SS)


def emit_matmul(nc, block, SS, G, K, M, ntok, xchunk, w_ap, w_mode, out_tile_ap,
                out_dtype="bf16", has_bias=False, bvec=None, act="copy",
                has_scale=False, svec_ap=None, cmul=1.0, sfold_ap=None,
                resid_tile_ap=None, sign_w=False, mult_tile_ap=None, psc_ap=None,
                xpm_ap=None, wslab_ap=None, slab_m=4, xnparts=1, pre_wait=None):
    nk, nm = K // P, M // P
    ntc = ntok // TCH
    if K >= 8192:
        nth = 2 if ntc >= 2 else ntc
    else:
        nth = 1
    ntcl = ntc // nth
    if K >= 8192:
        slab_m = 1
    nslab = nm // slab_m
    use_act = act != "copy"
    afunc = {"silu": AF.Silu, "relu": AF.Relu}.get(act)
    has_resid = resid_tile_ap is not None
    has_mult = mult_tile_ap is not None
    has_psc = psc_ap is not None
    has_xpm = xpm_ap is not None
    has_vec_op = (not use_act) or has_mult or has_resid or has_psc
    n_rm = int(has_resid) + int(has_mult)
    if has_psc or sfold_ap is not None:
        assert has_scale
    if use_act:
        assert not has_bias and not has_resid and not has_psc
    scale_pre = has_scale and (cmul != 1.0 or has_psc or sfold_ap is not None)
    n_vini = int(has_scale and cmul != 1.0) + int(has_psc) + int(sfold_ap is not None)
    nidx = ntc * nm

    b_ldx = SS["ldx"].n
    b_ldw = [SS["ldw0"].n, SS["ldw1"].n]
    b_ldv = SS["ldv"].n
    b_ldr = [SS["ldr0"].n, SS["ldr1"].n, SS["ldr2"].n, SS["ldr3"].n]
    b_pe = SS["pe"].n
    b_ev = SS["ev"].n
    b_ev2 = SS["ev2"].n
    b_sg = SS["sg"].n
    b_st = [SS["st0"].n, SS["st1"].n]
    b_xpms = SS["xpms"].n
    b_vini = SS["vini"].n
    odt = BF16 if out_dtype == "bf16" else F32

    NX = xnparts

    def order():
        gi = 0
        for th in range(nth):
            for m in range(nm):
                for tc in range(ntcl):
                    yield gi, th, m, tc, th * ntcl + tc
                    gi += 1

    with ExitStack() as cx:
        uid = _uid()
        ec = cx.enter_context
        xb = ec(nc.sbuf_tensor(f"mm_xb_{uid}", [P, nk, ntcl * TCH], BF16))
        wb = [ec(nc.sbuf_tensor(f"mm_wb{j}_{uid}", [P, nk, slab_m * P], BF16)) for j in range(2)]
        eb = [ec(nc.sbuf_tensor(f"mm_e{j}_{uid}", [P, TCH], odt)) for j in range(4)]
        rb = [ec(nc.sbuf_tensor(f"mm_r{j}_{uid}", [P, TCH], F32)) for j in range(4)]
        mb = [ec(nc.sbuf_tensor(f"mm_m{j}_{uid}", [P, TCH], BF16)) for j in range(4)]
        vecs = ec(nc.sbuf_tensor(f"mm_vecs_{uid}", [P, 3 * nm + nk + 2], F32))
        pb = [ec(nc.psum_tensor(f"mm_pb{j}_{uid}", [P, TCH], F32)) for j in range(4)]
        off_b, off_s, off_w, off_p, off_f = 0, nm, 2 * nm, 2 * nm + nk, 2 * nm + nk + 1
        nvec_dma = int(has_bias) + int(has_scale) + int(has_xpm) + int(has_psc) + int(sfold_ap is not None)

        def sy(sy_):
            if has_bias:
                sy_.dma_start(out=vecs[:, off_b:off_b + nm], in_=bvec).then_inc(SS["ldv"].h, 16)
            if has_scale:
                sy_.dma_start(out=vecs[:, off_s:off_s + nm], in_=svec_ap).then_inc(SS["ldv"].h, 16)
            if has_xpm:
                sy_.dma_start(out=vecs[:, off_w:off_w + nk], in_=xpm_ap).then_inc(SS["ldv"].h, 16)
            if has_psc:
                sy_.dma_start(out=vecs[:, off_p:off_p + 1], in_=psc_ap).then_inc(SS["ldv"].h, 16)
            if sfold_ap is not None:
                sy_.dma_start(out=vecs[:, off_f:off_f + nm], in_=sfold_ap).then_inc(SS["ldv"].h, 16)
            if n_rm:
                for gi, th, m, tc, t in order():
                    if gi >= 4:
                        sy_.wait_ge(SS[f"st{gi % 2}"].h, b_st[gi % 2] + 16 * ((gi - 4) // 2 + 1))
                    if has_resid:
                        sy_.dma_start(out=rb[gi % 4][:], in_=resid_tile_ap(t, m)).then_inc(SS[f"ldr{gi % 4}"].h, 16)
                    if has_mult:
                        sy_.dma_start(out=mb[gi % 4][:], in_=mult_tile_ap(t, m)).then_inc(SS[f"ldr{gi % 4}"].h, 16)
        block.sync(sy)

        def gp(g):
            if pre_wait is not None:
                g.wait_ge(SS[pre_wait].h, 1)
            for th in range(nth):
                if th >= 1:
                    g.wait_ge(SS["pe"].h, b_pe + th * ntcl * nm)
                parts = xchunk2(th)
                if not isinstance(parts, list):
                    parts = [(xb[:], parts)]
                for xo, xi in parts:
                    g.dma_start(out=xo, in_=xi).then_inc(SS["ldx"].h, 16)
                for si in range(nslab):
                    sidx = th * nslab + si
                    if sidx >= 2:
                        g.wait_ge(SS["pe"].h, b_pe + (sidx - 1) * slab_m * ntcl)
                    g.dma_start(out=wb[sidx % 2][:], in_=wslab_ap(si)).then_inc(SS[f"ldw{sidx % 2}"].h, 16)
        def xchunk2(th):
            try:
                return xchunk(th, xb)
            except TypeError:
                return xchunk(th)
        block.gpsimd(gp)

        def tns(tn):
            for gi, th, m, tc, t in order():
                sidx = th * nslab + m // slab_m
                if m == 0 and tc == 0:
                    if has_xpm:
                        tn.wait_ge(SS["xpms"].h, b_xpms + th + 1)
                    else:
                        tn.wait_ge(SS["ldx"].h, b_ldx + 16 * NX * (th + 1))
                if m % slab_m == 0 and tc == 0:
                    if sign_w:
                        tn.wait_ge(SS["sg"].h, b_sg + sidx + 1)
                    else:
                        tn.wait_ge(SS[f"ldw{sidx % 2}"].h, b_ldw[sidx % 2] + 16 * (sidx // 2 + 1))
                if gi >= 4:
                    if use_act:
                        tn.wait_ge(SS["ev"].h, b_ev + gi - 3)
                    else:
                        tn.wait_ge(SS["ev2"].h, b_ev2 + gi - 3)
                ins = None
                wloc = (m % slab_m) * P
                for k in range(nk):
                    ins = tn.matmul(pb[gi % 4][:], wb[sidx % 2][:, k, wloc:wloc + P],
                                    xb[:, k, tc * TCH:(tc + 1) * TCH],
                                    start=(k == 0), stop=(k == nk - 1))
                ins.then_inc(SS["pe"].h, 1)
        block.tensor(tns)

        def sc(s_):
            if use_act and has_scale:
                s_.wait_ge(SS["ldv"].h, b_ldv + 16 * nvec_dma)
                if scale_pre:
                    s_.wait_ge(SS["vini"].h, b_vini + n_vini)
            ordl = list(order())

            def do_store(gj):
                t_, m_ = ordl[gj][4], ordl[gj][2]
                if has_vec_op:
                    s_.wait_ge(SS["ev2"].h, b_ev2 + gj + 1)
                else:
                    s_.wait_ge(SS["ev"].h, b_ev + gj + 1)
                s_.dma_start(out=out_tile_ap(t_, m_), in_=eb[gj % 4][:]).then_inc(SS[f"st{gj % 2}"].h, 16)

            for gi, th, m, tc, t in ordl:
                sidx = th * nslab + m // slab_m
                if sign_w and m % slab_m == 0 and tc == 0:
                    s_.wait_ge(SS[f"ldw{sidx % 2}"].h, b_ldw[sidx % 2] + 16 * (sidx // 2 + 1))
                    s_.activation(wb[sidx % 2][:, :, :], wb[sidx % 2][:, :, :], AF.Sign).then_inc(SS["sg"].h, 1)
                if use_act:
                    s_.wait_ge(SS["pe"].h, b_pe + gi + 1)
                    if gi >= 4:
                        s_.wait_ge(SS[f"st{gi % 2}"].h, b_st[gi % 2] + 16 * ((gi - 4) // 2 + 1))
                    kw = {}
                    if has_scale:
                        kw["scale"] = vecs[:, off_s + m:off_s + m + 1]
                    elif cmul != 1.0:
                        kw["scale"] = float(cmul)
                    s_.activation(eb[gi % 4][:], pb[gi % 4][:], afunc, **kw).then_inc(SS["ev"].h, 1)
                if use_act:
                    if gi >= 2:
                        do_store(gi - 2)
                else:
                    do_store(gi)
            if use_act:
                for gj in (nidx - 2, nidx - 1):
                    if gj >= 0:
                        do_store(gj)
            s_.wait_ge(SS["st0"].h, b_st[0] + 16 * ((nidx + 1) // 2))
            s_.wait_ge(SS["st1"].h, b_st[1] + 16 * (nidx // 2))
        block.scalar(sc)

        def vc(v):
            nvi = 0
            if scale_pre:
                v.wait_ge(SS["ldv"].h, b_ldv + 16 * nvec_dma)
                if cmul != 1.0:
                    v.tensor_scalar(vecs[:, off_s:off_s + nm], vecs[:, off_s:off_s + nm],
                                    float(cmul), None, ALU.mult).then_inc(SS["vini"].h, 1)
                    nvi += 1
                    v.wait_ge(SS["vini"].h, b_vini + nvi)
                if has_psc:
                    v.tensor_scalar(vecs[:, off_s:off_s + nm], vecs[:, off_s:off_s + nm],
                                    vecs[:, off_p:off_p + 1], None, ALU.mult).then_inc(SS["vini"].h, 1)
                    nvi += 1
                    v.wait_ge(SS["vini"].h, b_vini + nvi)
                if sfold_ap is not None:
                    v.tensor_mul(vecs[:, off_s:off_s + nm], vecs[:, off_s:off_s + nm],
                                 vecs[:, off_f:off_f + nm]).then_inc(SS["vini"].h, 1)
                    nvi += 1
                    v.wait_ge(SS["vini"].h, b_vini + nvi)
            for gi, th, m, tc, t in order():
                if has_xpm and m == 0 and tc == 0:
                    v.wait_ge(SS["ldv"].h, b_ldv + 16 * nvec_dma)
                    v.wait_ge(SS["ldx"].h, b_ldx + 16 * NX * (th + 1))
                    ins = None
                    for k in range(nk):
                        ins = v.tensor_scalar(xb[:, k, :], xb[:, k, :],
                                              vecs[:, off_w + k:off_w + k + 1], None, ALU.mult)
                    ins.then_inc(SS["xpms"].h, 1)
                if not has_vec_op:
                    continue
                if use_act:
                    v.wait_ge(SS["ev"].h, b_ev + gi + 1)
                    v.wait_ge(SS[f"ldr{gi % 4}"].h, b_ldr[gi % 4] + 16 * n_rm * (gi // 4 + 1))
                    v.tensor_mul(eb[gi % 4][:], eb[gi % 4][:], mb[gi % 4][:]).then_inc(SS["ev2"].h, 1)
                    continue
                v.wait_ge(SS["pe"].h, b_pe + gi + 1)
                if has_bias or has_scale:
                    v.wait_ge(SS["ldv"].h, b_ldv + 16 * nvec_dma)
                if scale_pre:
                    v.wait_ge(SS["vini"].h, b_vini + n_vini)
                if n_rm:
                    v.wait_ge(SS[f"ldr{gi % 4}"].h, b_ldr[gi % 4] + 16 * n_rm * (gi // 4 + 1))
                if gi >= 4:
                    v.wait_ge(SS[f"st{gi % 2}"].h, b_st[gi % 2] + 16 * ((gi - 4) // 2 + 1))
                pbank = pb[gi % 4]
                cur = eb[gi % 4]
                if has_resid:
                    if has_scale:
                        last = v.scalar_tensor_tensor(cur[:], pbank[:], vecs[:, off_s + m:off_s + m + 1],
                                                      rb[gi % 4][:], ALU.mult, ALU.add)
                    elif has_bias:
                        last = v.scalar_tensor_tensor(cur[:], pbank[:], vecs[:, off_b + m:off_b + m + 1],
                                                      rb[gi % 4][:], ALU.add, ALU.add)
                    else:
                        last = v.scalar_tensor_tensor(cur[:], pbank[:], float(cmul),
                                                      rb[gi % 4][:], ALU.mult, ALU.add)
                else:
                    if has_scale and has_bias:
                        last = v.tensor_scalar(cur[:], pbank[:], vecs[:, off_s + m:off_s + m + 1],
                                               vecs[:, off_b + m:off_b + m + 1], ALU.mult, ALU.add)
                    elif has_scale:
                        last = v.tensor_scalar(cur[:], pbank[:], vecs[:, off_s + m:off_s + m + 1],
                                               None, ALU.mult)
                    elif has_bias:
                        last = v.tensor_scalar(cur[:], pbank[:], vecs[:, off_b + m:off_b + m + 1],
                                               None, ALU.add)
                    elif cmul != 1.0:
                        last = v.tensor_scalar(cur[:], pbank[:], float(cmul), None, ALU.mult)
                    else:
                        last = v.tensor_copy(cur[:], pbank[:])
                last.then_inc(SS["ev2"].h, 1)
        block.vector(vc)

        SS["ldx"].n = b_ldx + 16 * NX * nth
        nslabs_tot = nth * nslab
        SS["ldw0"].n = b_ldw[0] + 16 * ((nslabs_tot + 1) // 2)
        SS["ldw1"].n = b_ldw[1] + 16 * (nslabs_tot // 2)
        SS["ldv"].n = b_ldv + 16 * nvec_dma
        for j in range(4):
            SS[f"ldr{j}"].n = b_ldr[j] + 16 * n_rm * ((nidx - j + 3) // 4)
        SS["pe"].n = b_pe + nidx
        SS["ev"].n = b_ev + (nidx if use_act else 0)
        SS["ev2"].n = b_ev2 + (nidx if has_vec_op else 0)
        SS["sg"].n = b_sg + (nslabs_tot if sign_w else 0)
        SS["st0"].n = b_st[0] + 16 * ((nidx + 1) // 2)
        SS["st1"].n = b_st[1] + 16 * (nidx // 2)
        SS["xpms"].n = b_xpms + (nth if has_xpm else 0)
        SS["vini"].n = b_vini + n_vini
    barrier(block, SS)



def emit_matmul_tm(nc, block, SS, G, K, M, ntok, xtile, w_ap_full, out_tile_ap,
                   bias_row_ap=None):
    nk, nmc = K // P, M // TCH
    ntt = ntok // P
    has_bias = bias_row_ap is not None
    b_ldx, b_ldw, b_ldv = SS["ldx"].n, SS["ldw0"].n, SS["ldw1"].n
    b_pe, b_ev2 = SS["pe"].n, SS["ev2"].n
    b_st = [SS["st0"].n, SS["st1"].n]
    with ExitStack() as cx:
        uid = _uid()
        ec = cx.enter_context
        xt = [ec(nc.sbuf_tensor(f"vt_x{j}_{uid}", [P, nk, P], BF16)) for j in range(2)]
        wsb = ec(nc.sbuf_tensor(f"vt_w_{uid}", [P, nk, M], BF16))
        brow = ec(nc.sbuf_tensor(f"vt_br_{uid}", [1, M], F32))
        bsb = ec(nc.sbuf_tensor(f"vt_bs_{uid}", [P, M], F32))
        eb = [ec(nc.sbuf_tensor(f"vt_e{j}_{uid}", [P, TCH], BF16)) for j in range(2)]
        brbf = ec(nc.sbuf_tensor(f"vt_brbf_{uid}", [1, M], BF16))
        pb = [ec(nc.psum_tensor(f"vt_pb{j}_{uid}", [P, TCH], F32)) for j in range(2)]
        ps_bc = ec(nc.psum_tensor(f"vt_psbc_{uid}", [P, TCH], F32))

        def gp(g):
            g.dma_start(out=wsb[:], in_=w_ap_full.rearrange("(k p) m -> p k m", p=P)).then_inc(SS["ldw0"].h, 16)
            if has_bias:
                g.dma_start(out=brow[:], in_=bias_row_ap).then_inc(SS["ldw1"].h, 16)
            for tt in range(ntt):
                if tt >= 2:
                    g.wait_ge(SS["pe"].h, b_pe + nbc + (tt - 1) * nmc)
                g.dma_start(out=xt[tt % 2][:], in_=xtile(tt)).then_inc(SS[f"gq{tt % 2}"].h, 16)
        nbc = (M // TCH) if has_bias else 0

        b_ldr = [SS["gq0"].n, SS["gq1"].n]
        block.gpsimd(gp)

        def tns(tn):
            tn.wait_ge(SS["ldw0"].h, b_ldw + 16)
            if has_bias:
                tn.wait_ge(SS["ldw1"].h, b_ldv + 16)
                for mc in range(M // TCH):
                    tn.wait_ge(SS["ev2"].h, b_ev2 + 1)
                    if mc > 0:
                        tn.wait_ge(SS["ev2"].h, b_ev2 + 1 + mc)
                    tn.matmul(ps_bc[:], G["onesb_bf"][:], brbf[:, mc * TCH:(mc + 1) * TCH],
                              start=True, stop=True).then_inc(SS["pe"].h, 1)
            for tt in range(ntt):
                for mc in range(nmc):
                    idx = tt * nmc + mc
                    if mc == 0:
                        tn.wait_ge(SS[f"gq{tt % 2}"].h, b_ldr[tt % 2] + 16 * (tt // 2 + 1))
                    if idx >= 2:
                        tn.wait_ge(SS["ev2"].h, b_ev2 + 1 + nbc + idx - 1)
                    ins = None
                    for k in range(nk):
                        ins = tn.matmul(pb[idx % 2][:], xt[tt % 2][:, k, :],
                                        wsb[:, k, mc * TCH:(mc + 1) * TCH],
                                        start=(k == 0), stop=(k == nk - 1))
                    ins.then_inc(SS["pe"].h, 1)
        block.tensor(tns)

        def vc(v):
            nb = 0
            if has_bias:
                v.wait_ge(SS["ldw1"].h, b_ldv + 16)
                v.tensor_copy(brbf[:], brow[:]).then_inc(SS["ev2"].h, 1)
                nb = 1
                for mc in range(M // TCH):
                    v.wait_ge(SS["pe"].h, b_pe + mc + 1)
                    v.tensor_copy(bsb[:, mc * TCH:(mc + 1) * TCH], ps_bc[:]).then_inc(SS["ev2"].h, 1)
                nb += M // TCH
            for tt in range(ntt):
                for mc in range(nmc):
                    idx = tt * nmc + mc
                    v.wait_ge(SS["pe"].h, b_pe + nbc + idx + 1)
                    if idx >= 2:
                        v.wait_ge(SS[f"st{idx % 2}"].h, b_st[idx % 2] + 16 * ((idx - 2) // 2 + 1))
                    if has_bias:
                        v.wait_ge(SS["ev2"].h, b_ev2 + 1 + nbc)
                        v.tensor_add(eb[idx % 2][:], pb[idx % 2][:], bsb[:, mc * TCH:(mc + 1) * TCH]).then_inc(SS["ev2"].h, 1)
                    else:
                        v.tensor_copy(eb[idx % 2][:], pb[idx % 2][:]).then_inc(SS["ev2"].h, 1)
        block.vector(vc)

        def sc(s_):
            for tt in range(ntt):
                for mc in range(nmc):
                    idx = tt * nmc + mc
                    s_.wait_ge(SS["ev2"].h, b_ev2 + (1 + nbc if has_bias else 0) + idx + 1)
                    s_.dma_start(out=out_tile_ap(tt, mc), in_=eb[idx % 2][:]).then_inc(SS[f"st{idx % 2}"].h, 16)
            s_.wait_ge(SS["st0"].h, b_st[0] + 16 * ((ntt * nmc + 1) // 2))
            s_.wait_ge(SS["st1"].h, b_st[1] + 16 * ((ntt * nmc) // 2))
        block.scalar(sc)

        nidx = ntt * nmc
        SS["ldw0"].n = b_ldw + 16
        SS["ldw1"].n = b_ldv + (16 if has_bias else 0)
        SS["gq0"].n = b_ldr[0] + 16 * ((ntt + 1) // 2)
        SS["gq1"].n = b_ldr[1] + 16 * (ntt // 2)
        SS["pe"].n = b_pe + nbc + nidx
        SS["ev2"].n = b_ev2 + (1 + nbc if has_bias else 0) + nidx
        SS["st0"].n = b_st[0] + 16 * ((nidx + 1) // 2)
        SS["st1"].n = b_st[1] + 16 * (nidx // 2)
    barrier(block, SS)



def _nkb(s):
    return 4 * (s + 1)


KBS_HEAD = sum(_nkb(s) for s in range(NSUP))
NPR_HEAD = KBS_HEAD // 2
A_PE_HEAD = 17 + sum(3 * _nkb(s) + 1 for s in range(NSUP))
A_SC_HEAD = 4 + NPR_HEAD
A_VC_HEAD = 28 + 9 * NSUP


def _a_pe_s(s):
    return 17 + sum(3 * _nkb(x) + 1 for x in range(s))


def _a_scp_s(s):
    return 4 + sum(_nkb(x) // 2 for x in range(s))


def _a_vc_s(s):
    return 28 + 9 * s


def _a_kbg(s, kb):
    return sum(_nkb(x) for x in range(s)) + kb


def _a_prg_to_spr(prg):
    s = 0
    while prg >= _nkb(s) // 2:
        prg -= _nkb(s) // 2
        s += 1
    return s, prg


def emit_attention(nc, block, SS, G, qk_t, vt_t, cc2_t):
    NKB = S // P
    b_pe, b_sc, b_vc = SS["pe"].n, SS["ev"].n, SS["ev2"].n
    b_kk, b_kv, b_q = SS["ldw0"].n, SS["ldw1"].n, SS["gq0"].n
    b_st = [SS["st0"].n, SS["st1"].n]

    def pe_st(h, s, kb):
        return b_pe + A_PE_HEAD * h + _a_pe_s(s) + 6 * (kb // 2) + 1 + (kb % 2)

    def pe_l(h, s, kb):
        return b_pe + A_PE_HEAD * h + _a_pe_s(s) + 6 * (kb // 2) + 3 + 2 * (kb % 2)

    def pe_pv(h, s, kb):
        return b_pe + A_PE_HEAD * h + _a_pe_s(s) + 6 * (kb // 2) + 4 + 2 * (kb % 2)

    def sc_exp(h, s, pr):
        return b_sc + A_SC_HEAD * h + _a_scp_s(s) + pr + 1

    with ExitStack() as cx:
        uid = _uid()
        ec = cx.enter_context
        kbuf = [ec(nc.sbuf_tensor(f"at_kbuf{j}_{uid}", [P, S], BF16)) for j in range(2)]
        vbuf = [ec(nc.sbuf_tensor(f"at_vbuf{j}_{uid}", [P, NKB, P], BF16)) for j in range(2)]
        qh = [ec(nc.sbuf_tensor(f"at_qh{j}_{uid}", [P, S], BF16)) for j in range(2)]
        ksqf = ec(nc.sbuf_tensor(f"at_ksqf_{uid}", [P, S], BF16))
        qsqf = ec(nc.sbuf_tensor(f"at_qsqf_{uid}", [P, S], BF16))
        ksn = ec(nc.sbuf_tensor(f"at_ksn_{uid}", [1, S], F32))
        qsn = ec(nc.sbuf_tensor(f"at_qsn_{uid}", [1, S], F32))
        km2 = ec(nc.sbuf_tensor(f"at_km2_{uid}", [1, 1], F32))
        kms = ec(nc.sbuf_tensor(f"at_kms_{uid}", [1, 1], F32))
        qm2 = ec(nc.sbuf_tensor(f"at_qm2_{uid}", [1, NSUP], F32))
        qmv = ec(nc.sbuf_tensor(f"at_qmv_{uid}", [1, NSUP], F32))
        negcb = ec(nc.sbuf_tensor(f"at_negcb_{uid}", [1, NSUP], F32))
        negc_bf = ec(nc.sbuf_tensor(f"at_negcbf_{uid}", [1, NSUP], BF16))
        bias_all = ec(nc.sbuf_tensor(f"at_bias_{uid}", [P, NSUP], F32))
        pts = [ec(nc.sbuf_tensor(f"at_pt{j}_{uid}", [P, 2 * TCH], BF16)) for j in range(2)]
        lsb = ec(nc.sbuf_tensor(f"at_lsb_{uid}", [1, TCH], F32))
        rlv = ec(nc.sbuf_tensor(f"at_rlv_{uid}", [1, TCH], F32))
        rl_bf = ec(nc.sbuf_tensor(f"at_rlbf_{uid}", [1, TCH], BF16))
        rb_s = ec(nc.sbuf_tensor(f"at_rbs_{uid}", [P, TCH], F32))
        cxs = [ec(nc.sbuf_tensor(f"at_cxs{j}_{uid}", [P, TCH], BF16)) for j in range(2)]
        pss = [ec(nc.psum_tensor(f"at_pss{j}_{uid}", [P, 2 * TCH], F32)) for j in range(2)]
        ps_o = [ec(nc.psum_tensor(f"at_pso{j}_{uid}", [P, TCH], F32)) for j in range(2)]
        ps_l = ec(nc.psum_tensor(f"at_psl_{uid}", [1, TCH], F32))
        ps_sm = ec(nc.psum_tensor(f"at_pssm_{uid}", [P, TCH], F32))

        def gp(g):
            for h in range(NH):
                if h >= 2:
                    g.wait_ge(SS["pe"].h, b_pe + A_PE_HEAD * (h - 1))
                g.dma_start(out=kbuf[h % 2][:], in_=qk_t.ap()[(NH + h) * P:(NH + h + 1) * P, :]).then_inc(SS["ldw0"].h, 16)
                g.dma_start(out=vbuf[h % 2][:], in_=vt_t.ap()[:, h * P:(h + 1) * P]
                            .rearrange("(b p) d -> p b d", p=P)).then_inc(SS["ldw1"].h, 16)
                g.dma_start(out=qh[h % 2][:], in_=qk_t.ap()[h * P:(h + 1) * P, :]).then_inc(SS["gq0"].h, 16)
        block.gpsimd(gp)

        def tns(tn):
            for h in range(NH):
                hb = b_pe + A_PE_HEAD * h
                vcb = b_vc + A_VC_HEAD * h
                tn.wait_ge(SS["ldw0"].h, b_kk + 16 * (h + 1))
                tn.wait_ge(SS["ldw1"].h, b_kv + 16 * (h + 1))
                tn.wait_ge(SS["gq0"].h, b_q + 16 * (h + 1))
                for c in range(8):
                    if c == 0:
                        tn.wait_ge(SS["ev"].h, b_sc + A_SC_HEAD * h + 1)
                        if h > 0:
                            tn.wait_ge(SS["ev2"].h, b_vc + A_VC_HEAD * h)
                    else:
                        tn.wait_ge(SS["ev2"].h, vcb + c)
                    tn.matmul(ps_sm[0:1, :], G["ones_bf"][:], ksqf[:, c * TCH:(c + 1) * TCH],
                              start=True, stop=True).then_inc(SS["pe"].h, 1)
                for c in range(8):
                    if c == 0:
                        tn.wait_ge(SS["ev"].h, b_sc + A_SC_HEAD * h + 2)
                        tn.wait_ge(SS["ev2"].h, vcb + 8)
                    else:
                        tn.wait_ge(SS["ev2"].h, vcb + 9 + c)
                    tn.matmul(ps_sm[0:1, :], G["ones_bf"][:], qsqf[:, c * TCH:(c + 1) * TCH],
                              start=True, stop=True).then_inc(SS["pe"].h, 1)
                tn.wait_ge(SS["ev2"].h, vcb + 27)
                tn.matmul(ps_sm[:, 0:NSUP], G["onesb_bf"][:], negc_bf[:], start=True, stop=True).then_inc(SS["pe"].h, 1)
                for s in range(NSUP):
                    sb = hb + _a_pe_s(s)
                    for pr in range(_nkb(s) // 2):
                        kb0, kb1 = 2 * pr, 2 * pr + 1
                        prg = _a_kbg(s, kb0) // 2
                        if prg >= 2:
                            s2, p2 = _a_prg_to_spr(prg - 2)
                            tn.wait_ge(SS["ev"].h, sc_exp(h, s2, p2))
                        elif h > 0:
                            s2, p2 = _a_prg_to_spr(NPR_HEAD - 2 + prg)
                            tn.wait_ge(SS["ev"].h, sc_exp(h - 1, s2, p2))
                        for e in (0, 1):
                            kb = 2 * pr + e
                            tn.matmul(pss[prg % 2][:, e * TCH:(e + 1) * TCH],
                                      kbuf[h % 2][:, kb * P:(kb + 1) * P],
                                      qh[h % 2][:, s * TCH:(s + 1) * TCH], start=True, stop=True).then_inc(SS["pe"].h, 1)
                        tn.wait_ge(SS["ev"].h, sc_exp(h, s, pr))
                        for e in (0, 1):
                            kb = 2 * pr + e
                            if kb >= 4 * s:
                                tn.wait_ge(SS["ev2"].h, vcb + _a_vc_s(s) + (kb - 4 * s) + 1)
                            if kb == 0:
                                if s >= 2:
                                    tn.wait_ge(SS["ev2"].h, vcb + _a_vc_s(s - 2) + 9)
                                elif h > 0:
                                    tn.wait_ge(SS["ev2"].h, b_vc + A_VC_HEAD * (h - 1) + _a_vc_s(NSUP - 2 + s) + 9)
                                if s >= 1:
                                    tn.wait_ge(SS["ev2"].h, vcb + _a_vc_s(s - 1) + 5)
                                elif h > 0:
                                    tn.wait_ge(SS["ev2"].h, b_vc + A_VC_HEAD * (h - 1) + _a_vc_s(NSUP - 1) + 5)
                            tn.matmul(ps_l[:], G["ones_bf"][:],
                                      pts[prg % 2][:, e * TCH:(e + 1) * TCH],
                                      start=(kb == 0), stop=(kb == _nkb(s) - 1)).then_inc(SS["pe"].h, 1)
                            tn.matmul(ps_o[s % 2][:], vbuf[h % 2][:, kb, :],
                                      pts[prg % 2][:, e * TCH:(e + 1) * TCH],
                                      start=(kb == 0), stop=(kb == _nkb(s) - 1)).then_inc(SS["pe"].h, 1)
                    tn.wait_ge(SS["ev2"].h, vcb + _a_vc_s(s) + 7)
                    if s > 0:
                        tn.wait_ge(SS["ev2"].h, vcb + _a_vc_s(s - 1) + 8)
                    tn.matmul(ps_sm[:], G["onesb_bf"][:], rl_bf[:], start=True, stop=True).then_inc(SS["pe"].h, 1)
        block.tensor(tns)

        def sc(s_):
            for h in range(NH):
                s_.wait_ge(SS["ldw0"].h, b_kk + 16 * (h + 1))
                if h >= 2:
                    s_.wait_ge(SS["pe"].h, b_pe + A_PE_HEAD * (h - 1))
                s_.activation(ksqf[:], kbuf[h % 2][:], AF.Square).then_inc(SS["ev"].h, 1)
                s_.wait_ge(SS["gq0"].h, b_q + 16 * (h + 1))
                s_.activation(qsqf[:], qh[h % 2][:], AF.Square).then_inc(SS["ev"].h, 1)
                s_.wait_ge(SS["ev2"].h, b_vc + A_VC_HEAD * h + 9)
                s_.activation(kms[:], km2[:], AF.Sqrt).then_inc(SS["ev"].h, 1)
                s_.wait_ge(SS["ev2"].h, b_vc + A_VC_HEAD * h + 25)
                s_.activation(qmv[:], qm2[:], AF.Sqrt).then_inc(SS["ev"].h, 1)
                for s in range(NSUP):
                    for pr in range(_nkb(s) // 2):
                        kb1 = 2 * pr + 1
                        prg = _a_kbg(s, 0) // 2 + pr
                        s_.wait_ge(SS["pe"].h, pe_st(h, s, kb1))
                        s_.wait_ge(SS["ev2"].h, b_vc + A_VC_HEAD * h + 28)
                        if prg >= 2:
                            s4, p4 = _a_prg_to_spr(prg - 2)
                            s_.wait_ge(SS["pe"].h, pe_pv(h, s4, 2 * p4 + 1))
                        elif h > 0:
                            s4, p4 = _a_prg_to_spr(NPR_HEAD - 2 + prg)
                            s_.wait_ge(SS["pe"].h, pe_pv(h - 1, s4, 2 * p4 + 1))
                        s_.activation(pts[prg % 2][:], pss[prg % 2][:], AF.Exp,
                                      bias=bias_all[:, s:s + 1], scale=SCL).then_inc(SS["ev"].h, 1)
        block.scalar(sc)

        def vc(v):
            for h in range(NH):
                vcb = b_vc + A_VC_HEAD * h
                hb = b_pe + A_PE_HEAD * h
                for c in range(8):
                    v.wait_ge(SS["pe"].h, hb + c + 1)
                    v.tensor_copy(ksn[:, c * TCH:(c + 1) * TCH], ps_sm[0:1, :]).then_inc(SS["ev2"].h, 1)
                v.wait_ge(SS["ev2"].h, vcb + 8)
                v.reduce_max(km2[:], ksn[:], axis=AX.X).then_inc(SS["ev2"].h, 1)
                for c in range(8):
                    v.wait_ge(SS["pe"].h, hb + 8 + c + 1)
                    v.tensor_copy(qsn[:, c * TCH:(c + 1) * TCH], ps_sm[0:1, :]).then_inc(SS["ev2"].h, 1)
                for c in range(8):
                    v.wait_ge(SS["ev2"].h, vcb + 9 + c + 1)
                    v.reduce_max(qm2[:, c:c + 1], qsn[:, c * TCH:(c + 1) * TCH], axis=AX.X).then_inc(SS["ev2"].h, 1)
                v.wait_ge(SS["ev"].h, b_sc + A_SC_HEAD * h + 4)
                v.tensor_scalar(negcb[:], qmv[:], kms[0:1, :], -SCL, ALU.mult, ALU.mult).then_inc(SS["ev2"].h, 1)
                v.wait_ge(SS["ev2"].h, vcb + 26)
                v.tensor_copy(negc_bf[:], negcb[:]).then_inc(SS["ev2"].h, 1)
                v.wait_ge(SS["pe"].h, hb + 17)
                if h > 0:
                    v.wait_ge(SS["ev"].h, b_sc + A_SC_HEAD * h)
                v.tensor_copy(bias_all[:], ps_sm[:, 0:NSUP]).then_inc(SS["ev2"].h, 1)
                for s in range(NSUP):
                    vb = vcb + _a_vc_s(s)
                    for j in range(4):
                        kb = 4 * s + j
                        kbg = _a_kbg(s, kb)
                        prg = kbg // 2
                        v.wait_ge(SS["ev"].h, sc_exp(h, s, kb // 2))
                        v.tensor_mul(pts[prg % 2][:, (kb % 2) * TCH:(kb % 2 + 1) * TCH],
                                     pts[prg % 2][:, (kb % 2) * TCH:(kb % 2 + 1) * TCH],
                                     G["mbuf"][:, j, :]).then_inc(SS["ev2"].h, 1)
                    v.wait_ge(SS["pe"].h, pe_l(h, s, _nkb(s) - 1))
                    v.tensor_copy(lsb[:], ps_l[0:1, :]).then_inc(SS["ev2"].h, 1)
                    v.wait_ge(SS["ev2"].h, vb + 5)
                    v.reciprocal(rlv[:], lsb[:]).then_inc(SS["ev2"].h, 1)
                    v.wait_ge(SS["ev2"].h, vb + 6)
                    v.tensor_copy(rl_bf[:], rlv[:]).then_inc(SS["ev2"].h, 1)
                    v.wait_ge(SS["pe"].h, hb + _a_pe_s(s) + 3 * _nkb(s) + 1)
                    v.tensor_copy(rb_s[:], ps_sm[:]).then_inc(SS["ev2"].h, 1)
                    gq = h * NSUP + s
                    v.wait_ge(SS["pe"].h, pe_pv(h, s, _nkb(s) - 1))
                    v.wait_ge(SS["ev2"].h, vb + 8)
                    if gq >= 2:
                        v.wait_ge(SS[f"st{gq % 2}"].h, b_st[gq % 2] + 16 * ((gq - 2) // 2 + 1))
                    v.tensor_mul(cxs[gq % 2][:], ps_o[s % 2][:], rb_s[:]).then_inc(SS["ev2"].h, 1)
        block.vector(vc)

        def sy(sy_):
            for h in range(NH):
                for s in range(NSUP):
                    gq = h * NSUP + s
                    sy_.wait_ge(SS["ev2"].h, b_vc + A_VC_HEAD * h + _a_vc_s(s) + 9)
                    sy_.dma_start(out=cc2_t.ap()[h * P:(h + 1) * P, s * TCH:(s + 1) * TCH],
                                  in_=cxs[gq % 2][:]).then_inc(SS[f"st{gq % 2}"].h, 16)
            sy_.wait_ge(SS["st0"].h, b_st[0] + 16 * 32)
            sy_.wait_ge(SS["st1"].h, b_st[1] + 16 * 32)
        block.sync(sy)

        SS["pe"].n = b_pe + A_PE_HEAD * NH
        SS["ev"].n = b_sc + A_SC_HEAD * NH
        SS["ev2"].n = b_vc + A_VC_HEAD * NH
        SS["ldw0"].n = b_kk + 16 * NH
        SS["ldw1"].n = b_kv + 16 * NH
        SS["gq0"].n = b_q + 16 * NH
        SS["st0"].n = b_st[0] + 16 * 32
        SS["st1"].n = b_st[1] + 16 * 32
    barrier(block, SS)



def emit_fwht_T(nc, block, SS, G, wdn_t, tdc_t):
    n = 8192
    nst = 13
    b_ldx = SS["ldx"].n
    b_pe = SS["pe"].n
    b_vc = SS["ev2"].n
    b_st = SS["st0"].n
    VPT = 2 * nst + 1 + 8
    PPT = 64
    with ExitStack() as cx:
        uid = _uid()
        ec = cx.enter_context
        ba = ec(nc.sbuf_tensor(f"ft_ba_{uid}", [P, n], F32))
        bb = ec(nc.sbuf_tensor(f"ft_bb_{uid}", [P, n], F32))
        tcb = ec(nc.sbuf_tensor(f"ft_tcb_{uid}", [P, n], BF16))
        tout = ec(nc.sbuf_tensor(f"ft_tout_{uid}", [P, 8, P], BF16))
        ptr = [ec(nc.psum_tensor(f"ft_ptr{j}_{uid}", [P, 8 * P], BF16)) for j in range(2)]

        def gp(g):
            for i in range(2):
                if i >= 1:
                    g.wait_ge(SS["ev2"].h, b_vc + VPT * i - 8)
                g.dma_start(out=ba[:], in_=wdn_t.ap()[i * P:(i + 1) * P, :]).then_inc(SS["ldx"].h, 16)
        block.gpsimd(gp)

        def vc(v):
            for i in range(2):
                vb = b_vc + VPT * i
                cnt = 0
                v.wait_ge(SS["ldx"].h, b_ldx + 16 * (i + 1))
                if i >= 1:
                    v.wait_ge(SS["st0"].h, b_st + 16 * 8 * i)
                src_, dst = ba, bb
                h = 1
                for stg in range(nst):
                    sv = src_[:].rearrange("p (g two h) -> p g two h", two=2, h=h)
                    dv = dst[:].rearrange("p (g two h) -> p g two h", two=2, h=h)
                    v.tensor_add(dv[:, :, 0, :], sv[:, :, 0, :], sv[:, :, 1, :]).then_inc(SS["ev2"].h, 1)
                    v.tensor_sub(dv[:, :, 1, :], sv[:, :, 0, :], sv[:, :, 1, :]).then_inc(SS["ev2"].h, 1)
                    cnt += 2
                    v.wait_ge(SS["ev2"].h, vb + cnt)
                    src_, dst = dst, src_
                    h *= 2
                v.tensor_copy(tcb[:], src_[:]).then_inc(SS["ev2"].h, 1)
                cnt += 1
                v.wait_ge(SS["ev2"].h, vb + cnt)
                for grp in range(8):
                    v.wait_ge(SS["pe"].h, b_pe + PPT * i + grp * 8 + 8)
                    if i > 0 or grp > 0:
                        v.wait_ge(SS["st0"].h, b_st + 16 * (8 * i + grp))
                    v.tensor_copy(tout[:], ptr[grp % 2][:].rearrange("p (j m) -> p j m", m=P)).then_inc(SS["ev2"].h, 1)
        block.vector(vc)

        def tns(tn):
            for i in range(2):
                tn.wait_ge(SS["ev2"].h, b_vc + VPT * i + 2 * nst + 1)
                for grp in range(8):
                    if i > 0 or grp > 1:
                        gidx = 8 * i + grp
                        pv, gv = divmod(gidx - 2, 8)
                        tn.wait_ge(SS["ev2"].h, b_vc + VPT * pv + 2 * nst + 1 + gv + 1)
                    for j in range(8):
                        tn.transpose(ptr[grp % 2][:, j * P:(j + 1) * P],
                                     tcb[:, (grp * 8 + j) * P:(grp * 8 + j + 1) * P],
                                     G["eye_bf"][:]).then_inc(SS["pe"].h, 1)
        block.tensor(tns)

        def sy(sy_):
            for i in range(2):
                for grp in range(8):
                    sy_.wait_ge(SS["ev2"].h, b_vc + VPT * i + 2 * nst + 1 + grp + 1)
                    sy_.dma_start(out=tdc_t.ap()[grp * 1024:(grp + 1) * 1024, i * P:(i + 1) * P]
                                  .rearrange("(j p) m -> p j m", p=P),
                                  in_=tout[:]).then_inc(SS["st0"].h, 16)
            sy_.wait_ge(SS["st0"].h, b_st + 16 * 16)
        block.sync(sy)

        SS["ldx"].n = b_ldx + 32
        SS["pe"].n = b_pe + 2 * PPT
        SS["ev2"].n = b_vc + 2 * VPT
        SS["st0"].n = b_st + 16 * 16
    barrier(block, SS)


def emit_collective(nc, block, SS, kind, groups, ins, outs, op=None, sem="cc0",
                    fire_only=False):
    from concourse.aot_env import allow_ici_all_to_all

    if op is None:
        op = ALU.bypass

    def gp(g):
        with allow_ici_all_to_all():
            g.collective_compute(kind, op, replica_groups=groups,
                                 ins=ins, outs=outs).then_inc(SS[sem].h, 1)
        if not fire_only:
            g.wait_ge(SS[sem].h, 1)
    block.gpsimd(gp)
    SS[sem].n = 1
    if not fire_only:
        barrier(block, SS)


def emit_dbgcopy(nc, block, SS, dst_t, src_t):
    b_cc = SS["cc"].n

    def gp(g):
        g.dma_start(out=dst_t.ap(), in_=src_t.ap()).then_inc(SS["cc"].h, 16)
        g.wait_ge(SS["cc"].h, b_cc + 16)
    block.gpsimd(gp)
    SS["cc"].n = b_cc + 16
    barrier(block, SS)


def emit_resadd(nc, block, SS, G, res_t, delta_t, bvec_ap=None, src_t=None):
    if src_t is None:
        src_t = res_t
    nm, ntc = D // P, TLOC // TCH
    nidx = nm * ntc
    b_ldr = [SS["ldr0"].n, SS["ldr1"].n]
    b_ldv = SS["ldv"].n
    b_ev2 = SS["ev2"].n
    b_st = [SS["st0"].n, SS["st1"].n]
    has_bias = bvec_ap is not None
    with ExitStack() as cx:
        uid = _uid()
        ec = cx.enter_context
        rb = [ec(nc.sbuf_tensor(f"ra_r{j}_{uid}", [P, TCH], F32)) for j in range(2)]
        db = [ec(nc.sbuf_tensor(f"ra_d{j}_{uid}", [P, TCH], BF16)) for j in range(2)]
        vb = ec(nc.sbuf_tensor(f"ra_vb_{uid}", [P, nm], F32))

        def sy(sy_):
            if has_bias:
                sy_.dma_start(out=vb[:], in_=bvec_ap).then_inc(SS["ldv"].h, 16)
            for idx in range(nidx):
                t, m = divmod(idx, nm)
                if idx >= 2:
                    sy_.wait_ge(SS[f"st{idx % 2}"].h, b_st[idx % 2] + 16 * ((idx - 2) // 2 + 1))
                sy_.dma_start(out=rb[idx % 2][:],
                              in_=src_t.ap()[m * P:(m + 1) * P, t * TCH:(t + 1) * TCH]).then_inc(SS[f"ldr{idx % 2}"].h, 16)
                sy_.dma_start(out=db[idx % 2][:],
                              in_=delta_t.ap()[m * P:(m + 1) * P, t * TCH:(t + 1) * TCH]).then_inc(SS[f"ldr{idx % 2}"].h, 16)
        block.sync(sy)

        def vc(v):
            if has_bias:
                v.wait_ge(SS["ldv"].h, b_ldv + 16)
            for idx in range(nidx):
                t, m = divmod(idx, nm)
                v.wait_ge(SS[f"ldr{idx % 2}"].h, b_ldr[idx % 2] + 32 * (idx // 2 + 1))
                if idx >= 2:
                    v.wait_ge(SS[f"st{idx % 2}"].h, b_st[idx % 2] + 16 * ((idx - 2) // 2 + 1))
                if has_bias:
                    v.scalar_tensor_tensor(rb[idx % 2][:], rb[idx % 2][:], vb[:, m:m + 1],
                                           db[idx % 2][:], ALU.add, ALU.add).then_inc(SS["ev2"].h, 1)
                else:
                    v.tensor_add(rb[idx % 2][:], rb[idx % 2][:], db[idx % 2][:]).then_inc(SS["ev2"].h, 1)
        block.vector(vc)

        def sc(s_):
            for idx in range(nidx):
                t, m = divmod(idx, nm)
                s_.wait_ge(SS["ev2"].h, b_ev2 + idx + 1)
                s_.dma_start(out=res_t.ap()[m * P:(m + 1) * P, t * TCH:(t + 1) * TCH],
                             in_=rb[idx % 2][:]).then_inc(SS[f"st{idx % 2}"].h, 16)
            s_.wait_ge(SS["st0"].h, b_st[0] + 16 * ((nidx + 1) // 2))
            s_.wait_ge(SS["st1"].h, b_st[1] + 16 * (nidx // 2))
        block.scalar(sc)

        SS["ldv"].n = b_ldv + (16 if has_bias else 0)
        SS["ldr0"].n = b_ldr[0] + 32 * ((nidx + 1) // 2)
        SS["ldr1"].n = b_ldr[1] + 32 * (nidx // 2)
        SS["ev2"].n = b_ev2 + nidx
        SS["st0"].n = b_st[0] + 16 * ((nidx + 1) // 2)
        SS["st1"].n = b_st[1] + 16 * (nidx // 2)
    barrier(block, SS)



def build_megakernel():
    nc = bass.Bass("TRN2", target_bir_lowering=False)
    x_fm = nc.dram_tensor("x_fm", [D, TLOC], F32, kind="ExternalInput")
    w_qkv0 = nc.dram_tensor("w_qkv0", [D, 3072], F32, kind="ExternalInput")
    b_qk0 = nc.dram_tensor("b_qk0", [P, 16], F32, kind="ExternalInput")
    b_v0 = nc.dram_tensor("b_v0", [1, 1024], F32, kind="ExternalInput")
    w_out0 = nc.dram_tensor("w_out0", [NH * P, D], F32, kind="ExternalInput")
    b_out0 = nc.dram_tensor("b_out0", [P, 16], F32, kind="ExternalInput")
    h2048 = nc.dram_tensor("h2048", [D, D], F32, kind="ExternalInput")
    su_ug = nc.dram_tensor("su_ug", [P, 16], F32, kind="ExternalInput")
    w_upT = nc.dram_tensor("w_upT", [D, 8192], F32, kind="ExternalInput")
    ws_u = nc.dram_tensor("ws_u", [P, 64], F32, kind="ExternalInput")
    su_dp = nc.dram_tensor("su_dp", [P, 64], F32, kind="ExternalInput")
    w_gateT = nc.dram_tensor("w_gateT", [D, 8192], F32, kind="ExternalInput")
    ws_g = nc.dram_tensor("ws_g", [P, 64], F32, kind="ExternalInput")
    wdn = nc.dram_tensor("wdn", [256, 8192], F32, kind="ExternalInput")
    ws_dp = nc.dram_tensor("ws_dp", [P, 16], F32, kind="ExternalInput")
    w_qkv1 = nc.dram_tensor("w_qkv1", [D, 3072], F32, kind="ExternalInput")
    w_o1 = nc.dram_tensor("w_o1", [NH * P, D], F32, kind="ExternalInput")
    b_o1 = nc.dram_tensor("b_o1", [P, 16], F32, kind="ExternalInput")
    w_up1T = nc.dram_tensor("w_up1T", [D, 8192], F32, kind="ExternalInput")
    us1 = nc.dram_tensor("us1", [P, 64], F32, kind="ExternalInput")
    w_dn1T = nc.dram_tensor("w_dn1T", [8192, D], F32, kind="ExternalInput")
    ds1 = nc.dram_tensor("ds1", [P, 16], F32, kind="ExternalInput")
    osc = nc.dram_tensor("osc", [P, 1], F32, kind="ExternalInput")
    masks_in = nc.dram_tensor("masks_in", [4 * P, TCH], F32, kind="ExternalInput")
    onesk_in = nc.dram_tensor("onesk_in", [P, 1], F32, kind="ExternalInput")
    onesb_in = nc.dram_tensor("onesb_in", [1, P], F32, kind="ExternalInput")
    eye_in = nc.dram_tensor("eye_in", [P, P], F32, kind="ExternalInput")
    eps_in = nc.dram_tensor("eps_in", [P, 1], F32, kind="ExternalInput")
    y = nc.dram_tensor("y", [D, TLOC], F32, kind="ExternalOutput")
    res = nc.dram_tensor("res", [D, TLOC], F32)
    xnc = nc.dram_tensor("xnc", [D, TLOC], BF16)
    xng = nc.dram_tensor("xng", [2 * D, TLOC], BF16)
    qk = nc.dram_tensor("qk", [2 * NH * P, S], BF16)
    vt = nc.dram_tensor("vt", [S, NH * P], BF16)
    ctxl = nc.dram_tensor("ctxl", [NH * P, S], BF16)
    prt = nc.dram_tensor("prt", [2 * D, TLOC], BF16)
    rso = nc.dram_tensor("rso", [D, TLOC], BF16)
    x2t = nc.dram_tensor("x2t", [D, TLOC], BF16)
    xupt = nc.dram_tensor("xupt", [8192, TLOC], BF16)
    x3t = nc.dram_tensor("x3t", [8192, TLOC], BF16)
    hidt = nc.dram_tensor("hidt", [8192, TLOC], BF16)
    tdc = nc.dram_tensor("tdc", [8192, 256], BF16)
    tdg = nc.dram_tensor("tdg", [8 * 8192, 256], BF16)

    PAIRS = [[0, 1], [2, 3], [4, 5], [6, 7]]
    ALL8 = [list(range(8))]

    with ExitStack() as cx:
        uid = _uid()
        ec = cx.enter_context
        ones_bf = ec(nc.sbuf_tensor(f"g_ones_{uid}", [P, 1], BF16))
        onesb_bf = ec(nc.sbuf_tensor(f"g_onesb_{uid}", [1, P], BF16))
        mbuf = ec(nc.sbuf_tensor(f"g_mbuf_{uid}", [P, 4, TCH], BF16))
        eye_bf = ec(nc.sbuf_tensor(f"g_eye_{uid}", [P, P], BF16))
        epst = ec(nc.sbuf_tensor(f"g_eps_{uid}", [P, 1], F32))
        sems = {}
        for nm in ("bar", "ldx", "ldw0", "ldw1", "ldv", "ldr0", "ldr1", "gq0", "gq1", "gldc", "xpms",
                   "vini", "pe", "ev", "ev2", "sg", "st0", "st1", "cc",
                   "ldr2", "ldr3", "cc0", "cc1", "cc2s", "cc3", "cc4"):
            sems[nm] = SemT(ec(nc.semaphore(nm)))
        block = ec(nc.Block())
        SS = sems
        G = {"ones_bf": ones_bf, "onesb_bf": onesb_bf, "mbuf": mbuf,
             "eye_bf": eye_bf, "epst": epst, "ldv_const": 80}

        def pro(g):
            g.dma_start(out=ones_bf[:], in_=onesk_in.ap()).then_inc(SS["gldc"].h, 16)
            g.dma_start(out=onesb_bf[:], in_=onesb_in.ap()).then_inc(SS["gldc"].h, 16)
            g.dma_start(out=mbuf[:], in_=masks_in.ap().rearrange("(i p) n -> p i n", p=P)).then_inc(SS["gldc"].h, 16)
            g.dma_start(out=eye_bf[:], in_=eye_in.ap()).then_inc(SS["gldc"].h, 16)
            g.dma_start(out=epst[:], in_=eps_in.ap()).then_inc(SS["gldc"].h, 16)
            g.wait_ge(SS["gldc"].h, 80)
        block.gpsimd(pro)
        SS["gldc"].n = 80
        barrier(block, SS)

        def fm_chunk(tensor):
            return lambda th: tensor.ap().rearrange("(k p) n -> p k n", p=P)

        def fm_chunk_q(tensor):
            return lambda th: tensor.ap()[:, th * 2 * TCH:(th + 1) * 2 * TCH].rearrange("(k p) n -> p k n", p=P)

        def xng_chunk(th, xb):
            return [(xb[:, :, r * TLOC:(r + 1) * TLOC],
                     xng.ap()[r * D:(r + 1) * D, :].rearrange("(k p) n -> p k n", p=P))
                    for r in range(2)]

        def xng_tile(tt):
            r, tl = divmod(tt, 16)
            return xng.ap()[r * D:(r + 1) * D, tl * P:(tl + 1) * P].rearrange("(k p) n -> p k n", p=P)

        def w_slab(w, c0=0):
            return lambda si: w.ap()[:, c0 + si * 512:c0 + (si + 1) * 512].rearrange("(k p) n -> p k n", p=P)

        def w_slab_dn1(si):
            return w_dn1T.ap()[:, si * P:(si + 1) * P].rearrange("(k p) n -> p k n", p=P)

        def ctxl_chunk(th):
            return ctxl.ap().rearrange("(k p) n -> p k n", p=P)

        def prt_tile(t, m):
            return prt.ap()[(t // 4) * D + m * P:(t // 4) * D + (m + 1) * P,
                            (t % 4) * TCH:(t % 4 + 1) * TCH]

        def w_tdg_slab(si):
            r, ml = divmod(si, 2)
            return tdg.ap()[r * 8192:(r + 1) * 8192, ml * P:(ml + 1) * P].rearrange("(k p) n -> p k n", p=P)

        def tile_of(tensor):
            return lambda t, m: tensor.ap()[m * P:(m + 1) * P, t * TCH:(t + 1) * TCH]

        def qk_tile(t, m):
            return qk.ap()[m * P:(m + 1) * P, t * TCH:(t + 1) * TCH]

        def vt_tile(tt, mc):
            return vt.ap()[tt * P:(tt + 1) * P, mc * TCH:(mc + 1) * TCH]

        emit_ln(nc, block, SS, G, x_fm.ap(), xnc.ap())
        emit_fwht_T(nc, block, SS, G, wdn, tdc)
        emit_collective(nc, block, SS, "AllGather", ALL8, [tdc.ap()], [tdg.ap()], sem="cc2s",
                        fire_only=True)
        emit_collective(nc, block, SS, "AllGather", PAIRS, [xnc.ap()], [xng.ap()], sem="cc0")
        emit_matmul(nc, block, SS, G, D, 2048, S, xng_chunk, None, "cast",
                    qk_tile, out_dtype="bf16", has_bias=True, bvec=b_qk0.ap(),
                    wslab_ap=w_slab(w_qkv0), xnparts=2)
        emit_matmul_tm(nc, block, SS, G, D, 1024, S, xng_tile, w_qkv0.ap()[:, 2048:3072],
                       vt_tile, bias_row_ap=b_v0.ap())
        emit_attention(nc, block, SS, G, qk, vt, ctxl)
        emit_matmul(nc, block, SS, G, NH * P, D, S, ctxl_chunk, None, "cast",
                    prt_tile, out_dtype="bf16", wslab_ap=w_slab(w_out0))
        emit_collective(nc, block, SS, "ReduceScatter", PAIRS, [prt.ap()], [rso.ap()], op=ALU.add, sem="cc1")
        emit_ln(nc, block, SS, G, None, xnc.ap(),
                fold=(lambda t: x_fm.ap()[:, t * TCH:(t + 1) * TCH].rearrange("(k p) n -> p k n", p=P),
                      rso, b_out0.ap(), res))
        if DBG:
            d_res1 = nc.dram_tensor("d_res1", [D, TLOC], F32, kind="ExternalOutput")
            emit_dbgcopy(nc, block, SS, d_res1, res)
        emit_matmul(nc, block, SS, G, D, D, TLOC, fm_chunk(xnc), None, "cast",
                    tile_of(x2t), out_dtype="bf16", cmul=1.0 / (64.0 * math.sqrt(D)),
                    xpm_ap=su_ug.ap(), wslab_ap=w_slab(h2048))
        emit_matmul(nc, block, SS, G, D, 8192, TLOC, fm_chunk(x2t), None, "cast",
                    tile_of(xupt), out_dtype="bf16", has_scale=True, svec_ap=ws_u.ap(),
                    cmul=64.0, sfold_ap=su_dp.ap(), wslab_ap=w_slab(w_upT))
        emit_matmul(nc, block, SS, G, D, 8192, TLOC, fm_chunk(x2t), None, "cast",
                    tile_of(x3t), out_dtype="bf16", act="silu", has_scale=True,
                    svec_ap=ws_g.ap(), cmul=64.0, mult_tile_ap=tile_of(xupt),
                    wslab_ap=w_slab(w_gateT))
        emit_matmul(nc, block, SS, G, 8192, D, TLOC, fm_chunk_q(x3t), None, "bf16",
                    tile_of(res), out_dtype="f32", has_scale=True, svec_ap=ws_dp.ap(),
                    cmul=1.0 / math.sqrt(8192.0), resid_tile_ap=tile_of(res),
                    wslab_ap=w_tdg_slab, pre_wait="cc2s")
        if DBG:
            d_res2 = nc.dram_tensor("d_res2", [D, TLOC], F32, kind="ExternalOutput")
            emit_dbgcopy(nc, block, SS, d_res2, res)
        emit_ln(nc, block, SS, G, res.ap(), xnc.ap())
        emit_collective(nc, block, SS, "AllGather", PAIRS, [xnc.ap()], [xng.ap()], sem="cc3")
        emit_matmul(nc, block, SS, G, D, 2048, S, xng_chunk, None, "cast",
                    qk_tile, out_dtype="bf16", wslab_ap=w_slab(w_qkv1), xnparts=2)
        emit_matmul_tm(nc, block, SS, G, D, 1024, S, xng_tile, w_qkv1.ap()[:, 2048:3072],
                       vt_tile)
        emit_attention(nc, block, SS, G, qk, vt, ctxl)
        emit_matmul(nc, block, SS, G, NH * P, D, S, ctxl_chunk, None, "cast",
                    prt_tile, out_dtype="bf16", wslab_ap=w_slab(w_o1))
        emit_collective(nc, block, SS, "ReduceScatter", PAIRS, [prt.ap()], [rso.ap()], op=ALU.add, sem="cc4")
        emit_ln(nc, block, SS, G, None, xnc.ap(),
                fold=(lambda t: res.ap()[:, t * TCH:(t + 1) * TCH].rearrange("(k p) n -> p k n", p=P),
                      rso, b_o1.ap(), res))
        emit_matmul(nc, block, SS, G, D, 8192, TLOC, fm_chunk(xnc), None, "cast",
                    tile_of(hidt), out_dtype="bf16", act="relu", has_scale=True,
                    svec_ap=us1.ap(), sign_w=True, wslab_ap=w_slab(w_up1T))
        emit_matmul(nc, block, SS, G, 8192, D, TLOC, fm_chunk_q(hidt), None, "cast",
                    tile_of(y), out_dtype="f32", has_scale=True, svec_ap=ds1.ap(),
                    sign_w=True, psc_ap=osc.ap(), resid_tile_ap=tile_of(res),
                    wslab_ap=w_slab_dn1)
        if DBG:
            dbgs = [("d_res", res), ("d_rso", rso), ("d_x2", x2t), ("d_xup", xupt),
                    ("d_x3", x3t), ("d_qk", qk), ("d_vt", vt), ("d_ctxl", ctxl),
                    ("d_xnc", xnc), ("d_hid", hidt), ("d_tdg", tdg), ("d_xng", xng)]
            douts = {}
            for nm, t in dbgs:
                douts[nm] = nc.dram_tensor(nm, list(t.shape), t.dtype, kind="ExternalOutput")
            b_cc = SS["cc"].n

            def dump(g):
                for i, (nm, t) in enumerate(dbgs):
                    g.dma_start(out=douts[nm].ap(), in_=t.ap()).then_inc(SS["cc"].h, 16)
                g.wait_ge(SS["cc"].h, b_cc + 16 * len(dbgs))
            block.gpsimd(dump)
            SS["cc"].n = b_cc + 16 * len(dbgs)
            barrier(block, SS)
    return nc


def _hadamard(n):
    h = np.array([[1.0]], dtype=np.float32)
    while h.shape[0] < n:
        h = np.kron(h, np.array([[1.0, 1.0], [1.0, -1.0]], dtype=np.float32))
    return h


def _prelay(v, n):
    return np.ascontiguousarray(np.asarray(v, dtype=np.float32).reshape(n, P).T)


def make_inmaps(inputs):
    ac = np.ascontiguousarray
    f32 = lambda a: np.asarray(a, dtype=np.float32)
    x = f32(inputs["L0_attn_x"]).reshape(NTOK, D)
    H = _hadamard(D)
    mask = np.zeros((4 * P, TCH), np.float32)
    for j in range(4):
        for kp in range(P):
            mask[j * P + kp, j * P + kp:] = 1.0
    W_in = f32(inputs["L0_attn_W_in"])
    b_in = f32(inputs["L0_attn_b_in"])
    Wq, Wk, Wv = f32(inputs["L1_attn_Wq"]), f32(inputs["L1_attn_Wk"]), f32(inputs["L1_attn_Wv"])
    ws_ug = f32(inputs["L0_ffn_Wscale_ug"])
    in_maps = []
    for c in range(NCORES):
        b, r = divmod(c, 2)
        hs = r * NH
        qcols = slice(hs * P, (hs + NH) * P)
        kcols = slice(D + hs * P, D + (hs + NH) * P)
        vcols = slice(2 * D + hs * P, 2 * D + (hs + NH) * P)
        m = {
            "x_fm": ac(x[b * S + r * TLOC:(b * S + (r + 1) * TLOC)].T),
            "w_qkv0": ac(np.concatenate([W_in[:, qcols], W_in[:, kcols], W_in[:, vcols]], axis=1)),
            "b_qk0": _prelay(np.concatenate([b_in[qcols], b_in[kcols]]), 16),
            "b_v0": ac(b_in[vcols].reshape(1, 1024)),
            "w_out0": ac(f32(inputs["L0_attn_W_out"])[hs * P:(hs + NH) * P, :]),
            "b_out0": _prelay(f32(inputs["L0_attn_b_out"]), 16),
            "h2048": H,
            "su_ug": _prelay(f32(inputs["L0_ffn_SU_ug"]), 16),
            "w_upT": ac(f32(inputs["L0_ffn_W_up"]).T),
            "ws_u": _prelay(ws_ug[:8192], 64),
            "su_dp": _prelay(f32(inputs["L0_ffn_SU_dp"]), 64),
            "w_gateT": ac(f32(inputs["L0_ffn_W_gate"]).T),
            "ws_g": _prelay(ws_ug[8192:], 64),
            "wdn": ac(f32(inputs["L0_ffn_W_down"])[c * 256:(c + 1) * 256]),
            "ws_dp": _prelay(f32(inputs["L0_ffn_Wscale_dp"]), 16),
            "w_qkv1": ac(np.concatenate([Wq[:, qcols.start:qcols.stop],
                                         Wk[:, qcols.start:qcols.stop],
                                         Wv[:, qcols.start:qcols.stop]], axis=1)),
            "w_o1": ac(f32(inputs["L1_attn_Wo"])[hs * P:(hs + NH) * P, :]),
            "b_o1": _prelay(f32(inputs["L1_attn_bo"]), 16),
            "w_up1T": ac(f32(inputs["L1_ffn_up_weight"]).T),
            "us1": _prelay(f32(inputs["L1_ffn_up_scale"]), 64),
            "w_dn1T": ac(f32(inputs["L1_ffn_down_weight"]).T),
            "ds1": _prelay(f32(inputs["L1_ffn_down_scale"]), 16),
            "osc": ac(np.broadcast_to(f32(inputs["L1_ffn_output_scale"]).reshape(1, 1), (P, 1))),
            "masks_in": mask,
            "onesk_in": np.ones((P, 1), np.float32),
            "onesb_in": np.ones((1, P), np.float32),
            "eye_in": np.eye(P, dtype=np.float32),
            "eps_in": np.full((P, 1), EPS, np.float32),
        }
        in_maps.append(m)
    return in_maps


def kernel(**inputs):
    if "nc" not in _CACHE:
        _CACHE["nc"] = build_megakernel()
    nc = _CACHE["nc"]
    in_maps = make_inmaps(inputs)
    res = run_bass_kernel_spmd(nc, in_maps, core_ids=list(range(NCORES)))
    _CACHE["last_results"] = res
    out = np.empty((NTOK, D), np.float32)
    for c in range(NCORES):
        out[c * TLOC:(c + 1) * TLOC] = res.results[c]["y"].T
    return out.reshape(B, S, D)
